# version 04
# baseline (speedup 1.0000x reference)
import numpy as np
import contextlib
import concourse.bass as bass
import concourse.mybir as mybir
from concourse.alu_op_type import AluOpType as OP
from concourse.bass_utils import run_bass_kernel_spmd

F32 = mybir.dt.float32
BF16 = mybir.dt.bfloat16
I32 = mybir.dt.int32
AF = mybir.ActivationFunctionType
AX = mybir.AxisListType.X

B, NSEQ, E = 4, 4096, 1024
TOK = 2048
TCH = 16
T4 = 4
ECH = 8
H, D, M = 16, 64, 256
IFF, ICH = 2816, 22
ISLOT = 3
NC = 8
RATIO = 0.0625
EPS = 1e-4
SQD = float(64.0 ** -0.25)
LN_EPS = 1e-5
QP = 127.49995
QN = -128.49995
NW_G = 1024 * ISLOT * 128
NW_D = 2816 * 128
NW = 2 * NW_G + NW_D

A1, A2, A3, A4 = 0, 32768, 65536, 98304
R0 = 131072
S0 = R0 + 61440
TOTAL = 212800

SY, GP, VE, SC, TE = "sync", "gpsimd", "vector", "scalar", "tensor"


def build_nc():
    nc = bass.Bass("TRN2", target_bir_lowering=False, detect_race_conditions=False)

    d_in = lambda n, s, dt=F32: nc.dram_tensor(n, list(s), dt, kind="ExternalInput").ap()
    x_h = d_in("x", (TOK, E))
    wqT_h = d_in("wqT", (E, E)); wkT_h = d_in("wkT", (E, E))
    wvT_h = d_in("wvT", (E, E)); woT_h = d_in("woT", (E, E))
    bqr_h = d_in("bqr", (1, E)); bkr_h = d_in("bkr", (1, E))
    bvr_h = d_in("bvr", (1, E)); bor_h = d_in("bor", (1, E))
    projT_h = d_in("projT", (D, M))
    clipx_h = d_in("clipx", (1, 1)); clipd_h = d_in("clipd", (1, 1))
    wg_h = d_in("wg_nat", (ISLOT * 128, E))
    wu_h = d_in("wu_nat", (ISLOT * 128, E))
    wd_h = d_in("wd_nat", (128, IFF))
    idbf_h = d_in("idbf", (128, 128), BF16)
    idf_h = d_in("idf", (128, 128), F32)
    y_h = nc.dram_tensor("y", [TOK, E], F32, kind="ExternalOutput").ap()


    xn32h = nc.dram_tensor("xn32h", [TOK, E], F32).ap()
    hh = nc.dram_tensor("hh", [ICH, 128, TOK], F32).ap()
    agw_in = nc.dram_tensor("agw_in", [1, NW], BF16)
    agw_out = nc.dram_tensor("agw_out", [NC, NW], BF16)
    ags_in = nc.dram_tensor("ags_in", [1, 768], F32)
    ags_out = nc.dram_tensor("ags_out", [NC, 768], F32)
    arx_in = nc.dram_tensor("arx_in", [1, 1], F32)
    arx_out = nc.dram_tensor("arx_out", [1, 1], F32)
    ars_in = nc.dram_tensor("ars_in", [1, 1], F32)
    ars_out = nc.dram_tensor("ars_out", [1, 1], F32)
    arh_in = nc.dram_tensor("arh_in", [1, 1], F32)
    arh_out = nc.dram_tensor("arh_out", [1, 1], F32)
    arp_in = nc.dram_tensor("arp_in", [129, 2080], F32)
    arp_out = nc.dram_tensor("arp_out", [129, 2080], F32)


    es = contextlib.ExitStack()
    big = es.enter_context(nc.sbuf_tensor("big", [128, TOTAL], mybir.dt.uint8))
    base = nc.lookup_mloc(big).addr

    def place(name, shape, dtype, off):
        per_part = int(np.prod(shape[1:])) * mybir.dt.size(dtype)
        assert off % 32 == 0 and off + per_part <= TOTAL, (name, off, per_part)
        h = nc.virtual_sbuf_tensor(name, list(shape), dtype)
        nc._tensor(h.name, h.shape, h.dtype, type="SB")
        mloc = nc.lookup_mloc(h)
        mloc.allocated = True
        mloc.addr = base + off
        return h

    a1 = place("a1", [128, 16384], BF16, A1)
    a2 = place("a2", [128, 16384], BF16, A2)
    a3 = place("a3", [128, 16384], BF16, A3)
    a4 = place("a4", [128, 16384], BF16, A4)

    msc = place("msc", [128, 4096], F32, R0)
    xbf = place("xbf", [128, 2048], BF16, R0 + 16384)
    wi32 = place("wi32", [128, 1024], I32, R0 + 20480)
    ttile = place("ttile", [128, 128], BF16, R0 + 24576)
    vw = place("vw", [128, 8192], BF16, R0)
    qkw = place("qkw", [128, 2048], BF16, R0 + 16384)
    ek = place("ek", [128, 4096], BF16, R0 + 20480)
    ksq = place("ksq", [128, 4096], BF16, R0 + 28672)
    xnr = place("xnr", [128, 2048], F32, R0)
    i32s = place("i32s", [128, 2048], I32, R0 + 8192)
    xbf2 = place("xbf2", [128, 1024], BF16, R0 + 16384)
    hst = place("hst", [128, 8192], F32, R0)
    gub = place("gub", [128, 8192], BF16, R0 + 32768)
    sgs = place("sgs", [128, 1024], F32, R0 + 49152)
    eq = place("eq", [128, 4096], BF16, R0 + 8192)
    aqs = place("aqs", [128, 1024], BF16, R0 + 40960)
    negrow = place("negrow", [1, 2048], F32, R0 + 43008)
    nhi = place("nhi", [1, 2048], BF16, R0 + 36864)
    nlo = place("nlo", [1, 2048], BF16, R0 + 51200)
    idr = place("idr", [1, 2048], F32, R0 + 16384)
    densb = place("densb", [65, 2048], F32, R0 + 51200)
    ibf = place("ibf", [65, 2048], BF16, R0 + 55296)
    i32s5 = place("i32s5", [128, 2048], I32, R0 + 16384)
    dbuf = place("dbuf", [128, 11264], BF16, R0)
    wobuf = place("wobuf", [128, 4096], BF16, R0 + 22528)
    xst = place("xst", [128, 1024], F32, R0 + 30720)
    tmpst = place("tmpst", [128, 1024], F32, R0 + 34816)
    yst = place("yst", [128, 1024], F32, R0 + 38912)
    ust = place("ust", [128, 130], F32, R0 + 36864)
    rm_all = place("rm_all", [128, 256], F32, R0 + 37440)
    upst = place("upst", [128, 130], F32, R0 + 0)
    ct_all = place("ct_all", [128, 2080], BF16, R0 + 1024)

    off = [S0]
    def sp(name, shape, dtype):
        per_part = int(np.prod(shape[1:])) * mybir.dt.size(dtype)
        h = place(name, shape, dtype, off[0])
        off[0] += (per_part + 63) // 64 * 64
        return h
    projs = sp("projs", [128, 256], BF16)
    idbf = sp("idbf_s", [128, 128], BF16)
    idf = sp("idf_s", [128, 128], F32)
    bqrb = sp("bqrb", [1, 1024], BF16)
    bkrb = sp("bkrb", [1, 1024], BF16)
    stats = sp("stats", [128, 96], F32)
    acx = sp("acx", [128, 16], F32)
    ach = sp("ach", [128, 88], F32)
    seffg = sp("seffg", [128, 22], F32)
    seffu = sp("seffu", [128, 22], F32)
    negb = sp("negb", [128, 16], F32)
    stq = sp("stq", [128, 16], F32)
    dcols = sp("dcols", [128, 16], F32)
    negtr = sp("negtr", [16, 128], F32)
    pp = sp("pp", [128, 16], F32)
    dencol = sp("dencol", [128, 16], F32)
    rme = sp("rme", [128, 16], F32)
    ones1b = sp("ones1b", [1, 128], BF16)
    onescb = sp("onescb", [128, 1], BF16)
    onesn16 = sp("onesn16", [128, 1], BF16)
    onesepsf = sp("onesepsf", [1, 128], F32)
    ones64f = sp("ones64f", [1, 64], F32)
    ones512b = sp("ones512b", [1, 512], BF16)
    ones65 = sp("ones65", [65, 128], BF16)
    csr = sp("csr", [1, 1040], BF16)
    rcell = sp("rcell", [1, 16], BF16)
    vss = sp("vss", [1, 80], F32)
    vsr = sp("vsr", [1, 80], F32)
    rowx = sp("rowx", [1, 128], F32)
    cells = sp("cells", [1, 32], F32)
    epscol = sp("epscol", [128, 1], F32)
    bvrb = sp("bvrb", [1, 1024], BF16)
    borb = sp("borb", [1, 1024], BF16)
    prf32 = place("prf32", [128, 256], F32, R0 + 49152)
    assert off[0] <= TOTAL, off[0]

    PA = es.enter_context(nc.psum_tensor("PA", [128, 512], F32))
    PB = es.enter_context(nc.psum_tensor("PB", [128, 512], F32))
    PC = es.enter_context(nc.psum_tensor("PC", [128, 512], F32))
    PD = es.enter_context(nc.psum_tensor("PD", [128, 512], F32))
    PE = es.enter_context(nc.psum_tensor("PE", [128, 512], F32))
    PF = es.enter_context(nc.psum_tensor("PF", [128, 512], F32))
    PG = es.enter_context(nc.psum_tensor("PG", [128, 512], F32))
    PT = es.enter_context(nc.psum_tensor("PT", [128, 1024], BF16))

    class Sem:
        def __init__(self, name):
            self.h = es.enter_context(nc.semaphore(name))
            self.v = 0
    LX, LG, VV, SS, TT = (Sem(n) for n in ("LX", "LG", "VV", "SS", "TT"))
    CC = [Sem(f"CC{i}") for i in range(6)]
    S_X = [Sem("SX0"), Sem("SX1")]
    S_SP = [Sem("SSP0"), Sem("SSP1"), Sem("SSP2"), Sem("SSP3")]
    S_G = [Sem("SG0"), Sem("SG1"), Sem("SG2"), Sem("SG3")]
    S_QK = [Sem("SQK0"), Sem("SQK1")]
    S_O = [Sem("SO0"), Sem("SO1")]
    S_DB = Sem("SDB")
    S_W = Sem("SW")
    S_VW = Sem("SVW")
    S_WO = Sem("SWO")
    S_AGW = Sem("SAGW")
    S_AGS = Sem("SAGS")
    S_ARP = Sem("SARP")
    S_CTL = Sem("SCTL")
    S_SEF = Sem("SSEF")
    BR = Sem("BR")
    SY_SEMS = [LX, S_X[0], S_X[1], S_SP[0], S_SP[1], S_SP[2], S_SP[3],
               S_G[0], S_G[1], S_G[2], S_G[3], S_O[0], S_O[1], S_DB, S_W]
    GP_SEMS = [LG, S_QK[0], S_QK[1], S_VW, S_WO, S_AGW, S_AGS, S_ARP, S_CTL, S_SEF]

    PROG = []
    def em(eng, fn):
        PROG.append((eng, fn))
    def wt(eng, sem, val):
        if val and val > 0:
            em(eng, lambda e, s=sem.h, v=val: e.wait_ge(s, v))
    def dma(eng, sem, out, in_):
        sem.v += 16
        em(eng, lambda e, o=out, i=in_, s=sem.h: e.dma_start(out=o, in_=i).then_inc(s, 16))
        return sem.v
    def op(eng, fn, sem=None):
        if sem is None:
            em(eng, fn)
            return None
        sem.v += 1
        em(eng, lambda e, f=fn, s=sem.h: f(e).then_inc(s, 1))
        return sem.v

    def ve_step(fn):
        v = op(VE, fn, VV)
        wt(VE, VV, v)
        return v

    class Guard:
        def __init__(self):
            self.conds = []
        def acquire(self, eng):
            for s, v in self.conds:
                wt(eng, s, v)
            self.conds = []
        def release(self, sem, val):
            if val:
                self.conds.append((sem, val))
    gPA, gPB, gPC, gPD, gPE, gPF, gPG, gPT = (Guard() for _ in range(8))

    br_round = [0]

    def barrier():
        for s in SY_SEMS:
            wt(SY, s, s.v)
        em(SY, lambda e: e.sem_inc(BR.h, 1)); BR.v += 1
        for s in GP_SEMS:
            wt(GP, s, s.v)
        em(GP, lambda e: e.sem_inc(BR.h, 1)); BR.v += 1
        br_round[0] = BR.v
        vals = [(BR, BR.v), (VV, VV.v), (SS, SS.v), (TT, TT.v)]
        for eng in (SY, GP, VE, SC, TE):
            for s, v in vals:
                wt(eng, s, v)

    op(VE, lambda e: e.memset(ones1b[:, :], 1.0))
    op(VE, lambda e: e.memset(onescb[:, :], 1.0))
    op(VE, lambda e: e.memset(onesn16[:, :], -0.0625))
    op(VE, lambda e: e.memset(onesepsf[:, :], RATIO * EPS))
    op(VE, lambda e: e.memset(ones64f[:, :], 1.0))
    op(VE, lambda e: e.memset(ones512b[:, :], 1.0))
    op(VE, lambda e: e.memset(ones65[:, :], 1.0))
    op(VE, lambda e: e.memset(vss[0:1, 64:65], float(TOK)))
    op(VE, lambda e: e.memset(epscol[:, :], LN_EPS), VV)
    dma(SY, LX, idbf[:, :], idbf_h)
    dma(SY, LX, idf[:, :], idf_h)
    dma(SY, LX, msc[0:1, 2048:3072], bqr_h)
    dma(SY, LX, msc[0:1, 3072:4096], bkr_h)
    dma(SY, LX, prf32[0:64, :], projT_h)
    dma(SY, LX, prf32[64:128, :], projT_h)
    dma(SY, LX, cells[0:1, 12:13], clipx_h)
    dma(SY, LX, cells[0:1, 13:14], clipd_h)
    dma(SY, LX, msc[0:1, 0:1024], bvr_h)
    l_pre = dma(SY, LX, msc[0:1, 1024:2048], bor_h)
    wt(SC, LX, l_pre)
    op(SC, lambda e: e.activation(projs[0:64, :], prf32[0:64, :], AF.Copy, scale=SQD))
    op(SC, lambda e: e.activation(projs[64:128, :], prf32[64:128, :], AF.Copy, scale=SQD))
    op(SC, lambda e: e.activation(bvrb[:, :], msc[0:1, 0:1024], AF.Copy))
    op(SC, lambda e: e.activation(borb[:, :], msc[0:1, 1024:2048], AF.Copy))
    op(SC, lambda e: e.activation(bqrb[:, :], msc[0:1, 2048:3072], AF.Copy))
    s_pre = op(SC, lambda e: e.activation(bkrb[:, :], msc[0:1, 3072:4096], AF.Copy), SS)

    slot_free = [None, None]
    for c in range(TCH):
        sl = (c % 2) * 1024
        if slot_free[c % 2] is not None:
            vv_f, ss_f = slot_free[c % 2]
            wt(SY, VV, vv_f); wt(SY, SS, ss_f)
        if c == 0:
            wt(SY, SS, s_pre)
        lx = dma(SY, S_X[c % 2], msc[:, sl:sl + 1024], x_h[c * 128:(c + 1) * 128, :])
        wt(VE, S_X[c % 2], lx)
        v_r = op(VE, lambda e, sl=sl, c=c: e.reduce_sum(
            stats[:, c:c + 1], msc[:, sl:sl + 1024], AX), VV)
        wt(SC, S_X[c % 2], lx)
        s_r = op(SC, lambda e, sl=sl, c=c: e.activation(
            xbf[:, sl:sl + 1024], msc[:, sl:sl + 1024], AF.Square,
            accum_out=stats[:, 16 + c:17 + c]), SS)
        slot_free[c % 2] = (v_r, s_r)
    wt(VE, SS, SS.v)
    wt(VE, VV, VV.v)
    ve_step(lambda e: e.tensor_scalar(stats[:, 32:48], stats[:, 0:16], 1.0 / E, None, OP.mult))
    ve_step(lambda e: e.tensor_scalar(stats[:, 48:64], stats[:, 16:32], 1.0 / E, None, OP.mult))
    ve_step(lambda e: e.tensor_tensor(stats[:, 64:80], stats[:, 32:48], stats[:, 32:48], OP.mult))
    v_var = ve_step(lambda e: e.tensor_tensor(stats[:, 64:80], stats[:, 48:64], stats[:, 64:80], OP.subtract))
    wt(SC, VV, v_var)
    s_sd = op(SC, lambda e: e.activation(stats[:, 80:96], stats[:, 64:80], AF.Sqrt, bias=epscol[:, 0:1]), SS)
    wt(VE, SS, s_sd)
    v_rstd = ve_step(lambda e: e.reciprocal(stats[:, 80:96], stats[:, 80:96]))

    xslot_free = [None, None]
    nslot_free = [None, None]
    g_xbf = [Guard(), Guard()]
    wt(SY, S_X[0], S_X[0].v)
    wt(SY, S_X[1], S_X[1].v)
    for c in range(TCH):
        slx = 2048 + (c % 2) * 1024
        sln = (c % 2) * 1024
        if xslot_free[c % 2] is not None:
            wt(SY, VV, xslot_free[c % 2])
        lx = dma(SY, S_X[c % 2], msc[:, slx:slx + 1024], x_h[c * 128:(c + 1) * 128, :])
        wt(VE, S_X[c % 2], lx)
        if c == 0:
            wt(VE, VV, v_rstd)
        if nslot_free[c % 2] is not None:
            lx_sp, ss_ca = nslot_free[c % 2]
            wt(VE, S_SP[c % 2], lx_sp); wt(VE, SS, ss_ca)
        v_n = op(VE, lambda e, slx=slx, sln=sln, c=c: e.tensor_scalar(
            msc[:, sln:sln + 1024], msc[:, slx:slx + 1024],
            stats[:, 32 + c:33 + c], stats[:, 80 + c:81 + c],
            OP.subtract, OP.mult), VV)
        xslot_free[c % 2] = v_n
        v_am = op(VE, lambda e, sln=sln, c=c: e.tensor_reduce(
            acx[:, c:c + 1], msc[:, sln:sln + 1024], AX, OP.max,
            apply_absolute_value=True), VV)
        wt(SY, VV, v_am)
        lx_sp = dma(SY, S_SP[c % 2], xn32h[c * 128:(c + 1) * 128, :], msc[:, sln:sln + 1024])
        wt(SC, VV, v_am)
        g_xbf[c % 2].acquire(SC)
        s_ca = op(SC, lambda e, sln=sln: e.activation(
            xbf[:, sln:sln + 1024], msc[:, sln:sln + 1024], AF.Copy), SS)
        nslot_free[c % 2] = (lx_sp, s_ca)
        gPT.acquire(TE)
        wt(TE, SS, s_ca)
        for j in range(8):
            t_tp = op(TE, lambda e, sln=sln, j=j: e.transpose(
                PT[:, j * 128:(j + 1) * 128], xbf[:, sln + j * 128:sln + (j + 1) * 128],
                idbf[:, :]), TT if j == 7 else None)
        g_xbf[c % 2].release(TT, t_tp)
        wt(VE, TT, t_tp)
        dst = a1[:, :].rearrange("p (g t) -> p g t", g=8)[:, :, c * 128:(c + 1) * 128]
        v_ev = op(VE, lambda e, dst=dst: e.tensor_copy(
            dst, PT[:, :].rearrange("p (g t) -> p g t", g=8)), VV)
        gPT.release(VV, v_ev)
    wt(VE, VV, VV.v)
    v_ax = op(VE, lambda e: e.tensor_reduce(pp[:, 15:16], acx[:, :], AX, OP.max), VV)
    wt(GP, VV, v_ax)
    g1 = dma(GP, S_CTL, rowx[0:1, :], pp[:, 15:16])
    wt(VE, S_CTL, g1)
    v_axc = op(VE, lambda e: e.reduce_max(cells[0:1, 0:1], rowx[0:1, :], AX), VV)
    wt(GP, VV, v_axc)
    dma(GP, S_CTL, arx_in.ap(), cells[0:1, 0:1])
    op(GP, lambda e: e.collective_compute(
        "AllReduce", OP.max, replica_groups=[list(range(NC))],
        ins=[arx_in.ap().opt()], outs=[arx_out.ap().opt()]), CC[0])

    barrier()

    g_region = agw_in.ap()[0:1, 0:NW_G].rearrange("a (p c) -> (a p) c", p=128)
    u_region = agw_in.ap()[0:1, NW_G:2 * NW_G].rearrange("a (p c) -> (a p) c", p=128)
    d_region = agw_in.ap()[0:1, 2 * NW_G:NW].rearrange("a (p c) -> (a p) c", p=128)
    last_agw = 0
    g_ttile = Guard()
    g_stats = Guard()
    g_wi32 = Guard()
    g_xbfq = Guard()

    def quant_block(w_slice_ap, tile_dst, is_down, s_col):
        nonlocal last_agw
        width = w_slice_ap.shape[1]
        g_stats.acquire(VE)
        v_amx = op(VE, lambda e: e.tensor_reduce(stats[:, 0:1], w_slice_ap, AX, OP.max,
                                                 apply_absolute_value=True), VV)
        wt(VE, VV, v_amx)
        ve_step(lambda e: e.tensor_scalar(stats[:, 1:2], stats[:, 0:1], 1.0 / 127.0,
                                          None, OP.mult))
        ve_step(lambda e: e.reciprocal(stats[:, 2:3], stats[:, 0:1]))
        v_is = ve_step(lambda e: e.tensor_scalar(stats[:, 2:3], stats[:, 2:3], 127.0,
                                                 None, OP.mult))
        if s_col is not None:
            wt(GP, VV, v_is)
            l_ags = dma(GP, S_AGS, ags_in.ap()[0:1, s_col:s_col + 128], stats[:, 0:1])
            g_stats.release(S_AGS, l_ags)
        nblk = width // 128
        for bb in range(0, nblk, 8):
            hi = min(bb + 8, nblk)
            w = (hi - bb) * 128
            g_wi32.acquire(VE)
            v_q = op(VE, lambda e, bb=bb, w=w: e.tensor_scalar(
                wi32[:, 0:w], w_slice_ap[:, bb * 128:bb * 128 + w],
                stats[:, 2:3], None, OP.mult), VV)
            wt(SC, VV, v_q)
            g_xbfq.acquire(SC)
            if is_down:
                s_q = op(SC, lambda e, w=w: e.activation(
                    xbf[:, 0:w], wi32[:, 0:w], AF.Copy, scale=stats[:, 1:2]), SS)
            else:
                s_q = op(SC, lambda e, w=w: e.activation(
                    xbf[:, 0:w], wi32[:, 0:w], AF.Copy), SS)
            g_wi32.release(SS, s_q)
            if is_down:
                g_stats.release(SS, s_q)
            wt(TE, SS, s_q)
            for j in range(hi - bb):
                gPT.acquire(TE)
                t_tp = op(TE, lambda e, j=j: e.transpose(
                    PT[:, 0:128], xbf[:, j * 128:(j + 1) * 128], idbf[:, :]), TT)
                g_ttile.acquire(VE)
                wt(VE, TT, t_tp)
                v_ev = op(VE, lambda e: e.tensor_copy(ttile[:, :], PT[:, 0:128]), VV)
                gPT.release(VV, v_ev)
                if j == hi - bb - 1:
                    g_xbfq.release(TT, t_tp)
                wt(GP, VV, v_ev)
                last_agw = dma(GP, S_AGW, tile_dst(bb + j), ttile[:, :])
                g_ttile.release(S_AGW, last_agw)

    mslot = Guard()
    for wsel, (w_h, region, soff) in enumerate(
            [(wg_h, g_region, 0), (wu_h, u_region, 384)]):
        for icl in range(ISLOT):
            mslot.acquire(SY)
            lw = dma(SY, S_W, msc[:, 0:1024], w_h[icl * 128:(icl + 1) * 128, :])
            wt(VE, S_W, lw)
            quant_block(msc[:, 0:1024],
                        lambda k, region=region, icl=icl:
                        region[:, k * 384 + icl * 128:k * 384 + (icl + 1) * 128],
                        False, soff + icl * 128)
            mslot.release(VV, VV.v)
    mslot.acquire(SY)
    lw = dma(SY, S_W, msc[:, 0:2816], wd_h)
    wt(VE, S_W, lw)
    quant_block(msc[:, 0:2816],
                lambda k: d_region[:, k * 128:(k + 1) * 128], True, None)
    wt(GP, S_AGW, last_agw)
    wt(GP, S_AGS, S_AGS.v)
    op(GP, lambda e: e.collective_compute(
        "AllGather", OP.bypass, replica_groups=[list(range(NC))],
        ins=[agw_in.ap().opt()], outs=[agw_out.ap().opt()]), CC[1])
    op(GP, lambda e: e.collective_compute(
        "AllGather", OP.bypass, replica_groups=[list(range(NC))],
        ins=[ags_in.ap().opt()], outs=[ags_out.ap().opt()]), CC[2])

    barrier()

    for ei in range(ECH):
        dma(GP, S_VW, vw[:, ei * 1024:(ei + 1) * 1024],
            wvT_h[ei * 128:(ei + 1) * 128, :])
    lg_vw = S_VW.v
    qk_slot = [Guard(), Guard()]
    psq = {0: (PA, gPA), 1: (PB, gPB)}
    for wsel in range(2):
        dst = a2 if wsel == 0 else a3
        brow = bqrb if wsel == 0 else bkrb
        for eo in range(ECH):
            idx = wsel * ECH + eo
            sl = (idx % 2) * 1024
            qk_slot[idx % 2].acquire(GP)
            src = (wqT_h if wsel == 0 else wkT_h)[:, eo * 128:(eo + 1) * 128] \
                .rearrange("(a p) c -> p a c", p=128)
            lqk = dma(GP, S_QK[idx % 2], qkw[:, sl:sl + 1024].rearrange("p (a c) -> p a c", a=8), src)
            wt(TE, S_QK[idx % 2], lqk)
            for t in range(T4):
                ps, g = psq[t % 2]
                g.acquire(TE)
                for ei in range(ECH):
                    op(TE, lambda e, ps=ps, sl=sl, ei=ei, t=t: e.matmul(
                        ps[:, :], qkw[:, sl + ei * 128:sl + (ei + 1) * 128],
                        a1[:, ei * 2048 + t * 512:ei * 2048 + (t + 1) * 512],
                        start=(ei == 0), stop=False))
                tm = op(TE, lambda e, ps=ps, brow=brow, eo=eo: e.matmul(
                    ps[:, :], brow[0:1, eo * 128:(eo + 1) * 128], ones512b[0:1, :],
                    start=False, stop=True), TT)
                if t == T4 - 1:
                    qk_slot[idx % 2].release(TT, tm)
                wt(SC, TT, tm)
                s_ev = op(SC, lambda e, ps=ps, dst=dst, eo=eo, t=t:
                          e.activation(dst[:, eo * 2048 + t * 512:eo * 2048 + (t + 1) * 512],
                                       ps[:, :], AF.Copy), SS)
                g.release(SS, s_ev)
    wt(TE, S_VW, lg_vw)
    for t in range(TCH):
        for eo2 in range(2):
            ps, g = psq[(t * 2 + eo2) % 2]
            g.acquire(TE)
            for ei in range(ECH):
                op(TE, lambda e, ps=ps, t=t, eo2=eo2, ei=ei: e.matmul(
                    ps[:, :], a1[:, ei * 2048 + t * 128:ei * 2048 + (t + 1) * 128],
                    vw[:, ei * 1024 + eo2 * 512:ei * 1024 + (eo2 + 1) * 512],
                    start=(ei == 0), stop=False))
            tm = op(TE, lambda e, ps=ps, eo2=eo2: e.matmul(
                ps[:, :], ones1b[0:1, :], bvrb[0:1, eo2 * 512:(eo2 + 1) * 512],
                start=False, stop=True), TT)
            wt(VE, TT, tm)
            v_ev = op(VE, lambda e, ps=ps, t=t, eo2=eo2: e.tensor_copy(
                a4[:, t * 1024 + eo2 * 512:t * 1024 + (eo2 + 1) * 512], ps[:, :]), VV)
            g.release(VV, v_ev)

    barrier()

    ksq_slot = [Guard(), Guard()]
    g_dcols = Guard()
    g_ek = Guard()
    v_z = op(VE, lambda e: e.memset(msc[0:1, 0:1040], 0.0), VV)
    wt(GP, VV, v_z)
    dma(GP, S_ARP, arp_in.ap()[128:129, 1040:2080], msc[0:1, 0:1040])
    for hc in range(ECH):
        ksq_slot[hc % 2].acquire(SC)
        s_ksq = op(SC, lambda e, hc=hc: e.activation(
            ksq[:, (hc % 2) * 2048:(hc % 2) * 2048 + 2048],
            a3[:, hc * 2048:(hc + 1) * 2048], AF.Square), SS)
        for hh_i in range(2):
            h = 2 * hc + hh_i
            hp = hh_i * 64
            gPC.acquire(TE)
            wt(TE, SS, s_ksq)
            for t in range(TCH):
                tm_d = op(TE, lambda e, hc=hc, hp=hp, t=t: e.matmul(
                    PC[:, t:t + 1],
                    ksq[hp:hp + 64, (hc % 2) * 2048 + t * 128:(hc % 2) * 2048 + (t + 1) * 128],
                    onesn16[hp:hp + 64, 0:1], start=True, stop=True),
                    TT if t == TCH - 1 else None)
            if hh_i == 1:
                ksq_slot[hc % 2].release(TT, tm_d)
            g_dcols.acquire(VE)
            wt(VE, TT, tm_d)
            v_d = op(VE, lambda e: e.tensor_copy(dcols[:, 0:16], PC[:, 0:16]), VV)
            gPC.release(VV, v_d)
            for t in range(TCH):
                ps, g = psq[t % 2]
                g.acquire(TE)
                tm_dd = op(TE, lambda e, ps=ps, hc=hc, hp=hp, t=t: e.matmul(
                    ps[:, 0:256],
                    a3[hp:hp + 64, hc * 2048 + t * 128:hc * 2048 + (t + 1) * 128],
                    projs[hp:hp + 64, :], start=True, stop=True), TT)
                wt(SC, TT, tm_dd)
                if t == 0:
                    wt(SC, VV, v_d)
                g_ek.acquire(SC) if t == 0 else None
                s_ek = op(SC, lambda e, ps=ps, t=t: e.activation(
                    ek[:, t * 256:(t + 1) * 256], ps[:, 0:256], AF.Exp,
                    bias=dcols[:, t:t + 1]), SS)
                g.release(SS, s_ek)
                wt(VE, SS, s_ek)
                v_rm = op(VE, lambda e, h=h, t=t: e.reduce_max(
                    rme[:, t:t + 1], ek[:, t * 256:(t + 1) * 256], AX), VV)
                wt(TE, SS, s_ek)
                if t == 0:
                    gPD.acquire(TE); gPE.acquire(TE); gPF.acquire(TE)
                    gPG.acquire(TE); gPC.acquire(TE)
                v_sl = a4[:, t * 1024 + h * 64:t * 1024 + (h + 1) * 64]
                op(TE, lambda e, t=t, v_sl=v_sl: e.matmul(
                    PD[:, 0:64], ek[:, t * 256:t * 256 + 128], v_sl,
                    start=(t == 0), stop=(t == TCH - 1)))
                op(TE, lambda e, t=t, v_sl=v_sl: e.matmul(
                    PE[:, 0:64], ek[:, t * 256 + 128:t * 256 + 256], v_sl,
                    start=(t == 0), stop=(t == TCH - 1)))
                op(TE, lambda e, t=t: e.matmul(
                    PF[:, 0:1], ek[:, t * 256:t * 256 + 128], onescb[:, 0:1],
                    start=(t == 0), stop=(t == TCH - 1)))
                op(TE, lambda e, t=t: e.matmul(
                    PG[:, 0:1], ek[:, t * 256 + 128:t * 256 + 256], onescb[:, 0:1],
                    start=(t == 0), stop=(t == TCH - 1)))
                tm_vs = op(TE, lambda e, t=t, v_sl=v_sl: e.matmul(
                    PC[0:1, 0:64], onescb[:, 0:1], v_sl,
                    start=(t == 0), stop=(t == TCH - 1)),
                    TT if t == TCH - 1 else None)
                if t == TCH - 1:
                    g_ek.release(TT, tm_vs)
                    g_ek.release(VV, v_rm)
            wt(SC, VV, v_rm)
            s_ln = op(SC, lambda e: e.activation(rme[:, 0:16], rme[:, 0:16], AF.Ln), SS)
            wt(VE, SS, s_ln)
            v_rma = op(VE, lambda e, h=h: e.tensor_tensor(
                rm_all[:, h * 16:h * 16 + 16], rme[:, 0:16], dcols[:, 0:16],
                OP.subtract), VV)
            g_dcols.release(VV, v_rma)
            wt(VE, TT, tm_vs)
            op(VE, lambda e: e.tensor_copy(ust[:, 0:64], PD[:, 0:64]))
            op(VE, lambda e: e.tensor_copy(ust[:, 64:65], PF[:, 0:1]))
            op(VE, lambda e: e.tensor_copy(ust[:, 65:129], PE[:, 0:64]))
            op(VE, lambda e: e.tensor_copy(ust[:, 129:130], PG[:, 0:1]))
            v_u = op(VE, lambda e: e.tensor_copy(vss[0:1, 0:64], PC[0:1, 0:64]), VV)
            gPD.release(VV, v_u); gPE.release(VV, v_u); gPF.release(VV, v_u)
            gPG.release(VV, v_u); gPC.release(VV, v_u)
            wt(GP, VV, v_u)
            w1 = dma(GP, S_ARP, arp_in.ap()[0:128, (2 * h) * 65:(2 * h + 2) * 65],
                     ust[:, 0:130])
            w2 = dma(GP, S_ARP, arp_in.ap()[128:129, h * 65:(h + 1) * 65],
                     vss[0:1, 0:65])
            wt(VE, S_ARP, w2)
    wt(VE, VV, VV.v)
    v_dk = op(VE, lambda e: e.reduce_max(pp[:, 14:15], rm_all[:, :], AX), VV)
    wt(GP, VV, v_dk)
    g2 = dma(GP, S_CTL, rowx[0:1, :], pp[:, 14:15])
    wt(VE, S_CTL, g2)
    v_dkc = op(VE, lambda e: e.reduce_max(cells[0:1, 5:6], rowx[0:1, :], AX), VV)
    wt(GP, VV, v_dkc)
    dma(GP, S_CTL, ars_in.ap(), cells[0:1, 5:6])
    wt(GP, S_ARP, S_ARP.v)
    op(GP, lambda e: e.collective_compute(
        "AllReduce", OP.add, replica_groups=[[0, 1], [2, 3], [4, 5], [6, 7]],
        ins=[arp_in.ap().opt()], outs=[arp_out.ap().opt()]), CC[3])
    op(GP, lambda e: e.collective_compute(
        "AllReduce", OP.max, replica_groups=[list(range(NC))],
        ins=[ars_in.ap().opt()], outs=[ars_out.ap().opt()]), CC[4])

    barrier()

    wt(GP, CC[0], 1)
    g_ax = dma(GP, S_CTL, cells[0:1, 0:1], arx_out.ap())
    wt(SC, S_CTL, g_ax)
    s_sig = op(SC, lambda e: e.activation(cells[0:1, 1:2], cells[0:1, 12:13], AF.Sigmoid), SS)
    wt(VE, SS, s_sig)
    ve_step(lambda e: e.tensor_tensor(cells[0:1, 2:3], cells[0:1, 1:2], cells[0:1, 0:1], OP.mult))
    ve_step(lambda e: e.tensor_scalar(cells[0:1, 2:3], cells[0:1, 2:3], 1.0 / 127.0, None, OP.mult))
    ve_step(lambda e: e.reciprocal(cells[0:1, 3:4], cells[0:1, 2:3]))
    v_sxf = ve_step(lambda e: e.tensor_scalar(cells[0:1, 4:5], cells[0:1, 2:3], 1.0 / 127.0, None, OP.mult))

    def repl128(cell_ap, dst_col):
        ve_step(lambda e: e.tensor_copy(rcell[0:1, 0:1], cell_ap))
        ve_step(lambda e: e.tensor_tensor(vsr[0:1, 70:71], cell_ap, rcell[0:1, 0:1], OP.subtract))
        v_r = ve_step(lambda e: e.tensor_copy(rcell[0:1, 1:2], vsr[0:1, 70:71]))
        gPG.acquire(TE)
        wt(TE, VV, v_r)
        op(TE, lambda e: e.matmul(PG[:, 0:1], ones1b[0:1, :], rcell[0:1, 0:1], start=True, stop=False))
        t_r = op(TE, lambda e: e.matmul(PG[:, 0:1], ones1b[0:1, :], rcell[0:1, 1:2], start=False, stop=True), TT)
        wt(VE, TT, t_r)
        v_f = op(VE, lambda e: e.tensor_copy(dst_col, PG[:, 0:1]), VV)
        wt(VE, VV, v_f)
        gPG.release(VV, v_f)
        return v_f

    wt(VE, VV, v_sxf)
    repl128(cells[0:1, 3:4], pp[:, 0:1])
    v_sxfp = repl128(cells[0:1, 4:5], pp[:, 1:2])

    xq_slot = [None, None]
    g_i32 = [Guard(), Guard()]
    for c in range(TCH):
        sl = (c % 2) * 1024
        if xq_slot[c % 2] is not None:
            wt(SY, VV, xq_slot[c % 2])
        lxn = dma(SY, S_X[c % 2], xnr[:, sl:sl + 1024], xn32h[c * 128:(c + 1) * 128, :])
        wt(VE, S_X[c % 2], lxn)
        op(VE, lambda e, sl=sl: e.tensor_scalar(
            xnr[:, sl:sl + 1024], xnr[:, sl:sl + 1024], pp[:, 0:1], QP, OP.mult, OP.min))
        g_i32[c % 2].acquire(VE)
        v_i = op(VE, lambda e, sl=sl: e.tensor_scalar(
            i32s[:, sl:sl + 1024], xnr[:, sl:sl + 1024], QN, None, OP.max), VV)
        xq_slot[c % 2] = v_i
        wt(SC, VV, v_i)
        s_b = op(SC, lambda e, sl=sl: e.activation(
            xbf2[:, 0:1024], i32s[:, sl:sl + 1024], AF.Copy), SS)
        g_i32[c % 2].release(SS, s_b)
        gPT.acquire(TE)
        wt(TE, SS, s_b)
        for j in range(8):
            t_l = op(TE, lambda e, j=j: e.transpose(
                PT[:, j * 128:(j + 1) * 128], xbf2[:, j * 128:(j + 1) * 128],
                idbf[:, :]), TT if j == 7 else None)
        wt(VE, TT, t_l)
        dst = a4[:, :].rearrange("p (g t) -> p g t", g=8)[:, :, c * 128:(c + 1) * 128]
        v_x = op(VE, lambda e, dst=dst: e.tensor_copy(
            dst, PT[:, :].rearrange("p (g t) -> p g t", g=8)), VV)
        gPT.release(VV, v_x)
        wt(SC, TT, t_l)
    wt(GP, CC[2], 1)
    for gc in range(ICH):
        r, icl = gc // 3, gc % 3
        dma(GP, S_SEF, seffg[:, gc:gc + 1],
            ags_out.ap()[r:r + 1, icl * 128:(icl + 1) * 128])
        g_s = dma(GP, S_SEF, seffu[:, gc:gc + 1],
                  ags_out.ap()[r:r + 1, 384 + icl * 128:384 + (icl + 1) * 128])
    wt(VE, S_SEF, g_s)
    wt(VE, VV, v_sxfp)
    op(VE, lambda e: e.tensor_scalar(seffg[:, :], seffg[:, :], pp[:, 1:2], None, OP.mult))
    op(VE, lambda e: e.tensor_scalar(seffu[:, :], seffu[:, :], pp[:, 1:2], None, OP.mult), VV)

    barrier()

    wt(TE, CC[1], 1)
    wt(SY, CC[1], 1)
    gub_slot = [Guard(), Guard(), Guard(), Guard()]
    hst_slot = [Guard(), Guard(), Guard(), Guard()]
    g_sgs = [Guard(), Guard()]
    psg_m = {0: (PA, gPA), 1: (PB, gPB)}
    psu_m = {0: (PC, gPC), 1: (PD, gPD)}
    def gu_load(gc):
        r, icl = gc // 3, gc % 3
        slg = (gc % 4) * 1024
        src_g = agw_out.ap()[r:r + 1, 0:NW_G] \
            .rearrange("a (p g c) -> (a p) g c", p=128, c=384)[:, :, icl * 128:(icl + 1) * 128]
        src_u = agw_out.ap()[r:r + 1, NW_G:2 * NW_G] \
            .rearrange("a (p g c) -> (a p) g c", p=128, c=384)[:, :, icl * 128:(icl + 1) * 128]
        gub_slot[gc % 4].acquire(SY)
        dma(SY, S_G[gc % 4], gub[:, slg:slg + 1024].rearrange("p (g c) -> p g c", g=8), src_g)
        return dma(SY, S_G[gc % 4], gub[:, 4096 + slg:4096 + slg + 1024]
                   .rearrange("p (g c) -> p g c", g=8), src_u)
    lgu_q = [gu_load(0), gu_load(1), gu_load(2)]
    for gc in range(ICH):
        slg = (gc % 4) * 1024
        sl_h = (gc % 4) * 2048
        if gc + 3 < ICH:
            lgu_q.append(gu_load(gc + 3))
        lgu = lgu_q[gc]
        hst_slot[gc % 4].acquire(VE)
        wt(TE, S_G[gc % 4], lgu)
        for t in range(T4):
            psg, gg = psg_m[t % 2]
            psu, gu = psu_m[t % 2]
            gg.acquire(TE)
            for hcc in range(ECH):
                op(TE, lambda e, psg=psg, slg=slg, hcc=hcc, t=t: e.matmul(
                    psg[:, :], gub[:, slg + hcc * 128:slg + (hcc + 1) * 128],
                    a4[:, hcc * 2048 + t * 512:hcc * 2048 + (t + 1) * 512],
                    start=(hcc == 0), stop=(hcc == ECH - 1)))
            gu.acquire(TE)
            for hcc in range(ECH):
                tm_u = op(TE, lambda e, psu=psu, slg=slg, hcc=hcc, t=t: e.matmul(
                    psu[:, :], gub[:, 4096 + slg + hcc * 128:4096 + slg + (hcc + 1) * 128],
                    a4[:, hcc * 2048 + t * 512:hcc * 2048 + (t + 1) * 512],
                    start=(hcc == 0), stop=(hcc == ECH - 1)),
                    TT if hcc == ECH - 1 else None)
            if t == T4 - 1:
                gub_slot[gc % 4].release(TT, tm_u)
            wt(SC, TT, tm_u)
            g_sgs[t % 2].acquire(SC)
            s_sg = op(SC, lambda e, psg=psg, t=t, gc=gc: e.activation(
                sgs[:, (t % 2) * 512:(t % 2) * 512 + 512], psg[:, :], AF.Silu,
                scale=seffg[:, gc:gc + 1]), SS)
            gg.release(SS, s_sg)
            wt(VE, SS, s_sg)
            op(VE, lambda e, psu=psu, t=t, gc=gc, sl_h=sl_h: e.scalar_tensor_tensor(
                hst[:, sl_h + t * 512:sl_h + (t + 1) * 512], psu[:, :],
                seffu[:, gc:gc + 1], sgs[:, (t % 2) * 512:(t % 2) * 512 + 512],
                OP.mult, OP.mult))
            v_h = op(VE, lambda e, t=t, gc=gc, sl_h=sl_h: e.tensor_reduce(
                ach[:, gc * 4 + t:gc * 4 + t + 1],
                hst[:, sl_h + t * 512:sl_h + (t + 1) * 512], AX, OP.max,
                apply_absolute_value=True), VV)
            gu.release(VV, v_h)
            g_sgs[t % 2].release(VV, v_h)
        wt(SY, VV, v_h)
        l_sp = dma(SY, S_SP[gc % 4], hh[gc, :, :], hst[:, sl_h:sl_h + 2048])
        hst_slot[gc % 4].release(S_SP[gc % 4], l_sp)
    wt(VE, VV, VV.v)
    v_ah = op(VE, lambda e: e.reduce_max(pp[:, 13:14], ach[:, :], AX), VV)
    wt(GP, VV, v_ah)
    g3 = dma(GP, S_CTL, rowx[0:1, :], pp[:, 13:14])
    wt(VE, S_CTL, g3)
    v_ahc = op(VE, lambda e: e.reduce_max(cells[0:1, 8:9], rowx[0:1, :], AX), VV)
    wt(GP, VV, v_ahc)
    dma(GP, S_CTL, arh_in.ap(), cells[0:1, 8:9])
    op(GP, lambda e: e.collective_compute(
        "AllReduce", OP.max, replica_groups=[list(range(NC))],
        ins=[arh_in.ap().opt()], outs=[arh_out.ap().opt()]), CC[5])

    barrier()

    wt(GP, CC[3], 1)
    wt(GP, CC[4], 1)
    g_es = dma(GP, S_CTL, cells[0:1, 6:7], ars_out.ap())
    wt(SC, S_CTL, g_es)
    s_es = op(SC, lambda e: e.activation(cells[0:1, 6:7], cells[0:1, 6:7], AF.Exp, scale=-1.0), SS)
    wt(VE, SS, s_es)
    v_res = ve_step(lambda e: e.tensor_scalar(cells[0:1, 7:8], cells[0:1, 6:7], RATIO, None, OP.mult))
    v_resp = repl128(cells[0:1, 7:8], pp[:, 4:5])

    g_upst = Guard()
    g_vsr = Guard()
    ct_last = None
    for h in range(H):
        g_upst.acquire(GP)
        g_u = dma(GP, S_CTL, upst[:, 0:130], arp_out.ap()[0:128, 2 * h * 65:(2 * h + 2) * 65])
        g_vsr.acquire(GP)
        g_v = dma(GP, S_CTL, vsr[0:1, 0:65], arp_out.ap()[128:129, h * 65:(h + 1) * 65])
        gPF.acquire(TE)
        wt(TE, S_CTL, g_v)
        t_rv = op(TE, lambda e: e.matmul(PF[:, 0:65], onesepsf[0:1, :], vsr[0:1, 0:65],
                                         start=True, stop=True), TT)
        g_vsr.release(TT, t_rv)
        wt(VE, TT, t_rv)
        wt(VE, S_CTL, g_v)
        if h == 0:
            wt(VE, VV, v_resp)
        op(VE, lambda e, h=h: e.scalar_tensor_tensor(
            ct_all[:, (2 * h) * 65:(2 * h) * 65 + 65], upst[:, 0:65], pp[:, 4:5],
            PF[:, 0:65], OP.mult, OP.add))
        v_ct = op(VE, lambda e, h=h: e.scalar_tensor_tensor(
            ct_all[:, (2 * h + 1) * 65:(2 * h + 1) * 65 + 65], upst[:, 65:130], pp[:, 4:5],
            PF[:, 0:65], OP.mult, OP.add), VV)
        g_upst.release(VV, v_ct)
        gPF.release(VV, v_ct)
        gPG.acquire(TE)
        wt(TE, VV, v_ct)
        op(TE, lambda e, h=h: e.matmul(PG[0:1, 0:65], onescb[:, 0:1],
                                       ct_all[:, (2 * h) * 65:(2 * h) * 65 + 65],
                                       start=True, stop=False))
        t_cs = op(TE, lambda e, h=h: e.matmul(PG[0:1, 0:65], onescb[:, 0:1],
                                              ct_all[:, (2 * h + 1) * 65:(2 * h + 1) * 65 + 65],
                                              start=False, stop=True), TT)
        wt(SC, TT, t_cs)
        ct_last = op(SC, lambda e, h=h: e.activation(csr[0:1, h * 65:(h + 1) * 65],
                                                     PG[0:1, 0:65], AF.Copy, scale=EPS), SS)
        gPG.release(SS, ct_last)

    qsq_slot = [Guard(), Guard()]
    g_stq = Guard()
    g_eq = Guard()
    g_dencol = Guard()
    g_densb = Guard()
    g_idrow = Guard()
    g_ibf = Guard()
    g_isb = [Guard(), Guard()]
    psA_m4 = {0: (PD, gPD), 1: (PE, gPE), 2: (PA, gPA), 3: (PB, gPB)}
    g_negrow = Guard()
    g_negtr = Guard()
    g_aqs = [Guard(), Guard()]
    g_idr = [Guard(), Guard()]
    psA_m = {0: (PD, gPD), 1: (PE, gPE)}
    for hc in range(ECH):
        qsq_slot[hc % 2].acquire(SC)
        s_qsq = op(SC, lambda e, hc=hc: e.activation(
            ksq[:, (hc % 2) * 2048:(hc % 2) * 2048 + 2048],
            a2[:, hc * 2048:(hc + 1) * 2048], AF.Square), SS)
        for hh_i in range(2):
            h = 2 * hc + hh_i
            hp = hh_i * 64
            gPC.acquire(TE)
            wt(TE, SS, s_qsq)
            for t in range(TCH):
                tm_d = op(TE, lambda e, hc=hc, hp=hp, t=t: e.matmul(
                    PC[:, t:t + 1],
                    ksq[hp:hp + 64, (hc % 2) * 2048 + t * 128:(hc % 2) * 2048 + (t + 1) * 128],
                    onesn16[hp:hp + 64, 0:1], start=True, stop=True),
                    TT if t == TCH - 1 else None)
            if hh_i == 1:
                qsq_slot[hc % 2].release(TT, tm_d)
            g_stq.acquire(VE)
            for t in range(TCH):
                ps, g = psq[t % 2]
                g.acquire(TE)
                tm_dd = op(TE, lambda e, ps=ps, hc=hc, hp=hp, t=t: e.matmul(
                    ps[:, 0:256],
                    a2[hp:hp + 64, hc * 2048 + t * 128:hc * 2048 + (t + 1) * 128],
                    projs[hp:hp + 64, :], start=True, stop=True), TT)
                wt(VE, TT, tm_dd)
                v_rm = op(VE, lambda e, ps=ps, t=t: e.reduce_max(
                    stq[:, t:t + 1], ps[:, 0:256], AX), VV)
                g.release(VV, v_rm)
            wt(VE, TT, tm_d)
            wt(VE, VV, VV.v)
            v_nb = op(VE, lambda e: e.scalar_tensor_tensor(
                negb[:, :], PC[:, 0:16], 1.0, stq[:, :], OP.mult, OP.subtract), VV)
            gPC.release(VV, v_nb)
            g_stq.release(VV, v_nb)
            gPG.acquire(TE)
            wt(TE, VV, v_nb)
            t_tr = op(TE, lambda e: e.transpose(PG[0:16, 0:128], negb[:, :], idf[:, :]), TT)
            wt(VE, TT, t_tr)
            g_negtr.acquire(VE)
            v_tr = op(VE, lambda e: e.tensor_copy(negtr[0:16, :], PG[0:16, 0:128]), VV)
            gPG.release(VV, v_tr)
            g_negrow.acquire(GP)
            wt(GP, VV, v_tr)
            g_nr = dma(GP, S_CTL, negrow[0:1, :], negtr[0:16, :])
            g_negtr.release(S_CTL, g_nr)
            wt(VE, S_CTL, g_nr)
            op(VE, lambda e: e.tensor_copy(nhi[0:1, :], negrow[0:1, :]))
            op(VE, lambda e: e.tensor_tensor(negrow[0:1, :], negrow[0:1, :], nhi[0:1, :], OP.subtract))
            v_nl = op(VE, lambda e: e.tensor_copy(nlo[0:1, :], negrow[0:1, :]), VV)
            wt(TE, VV, v_nl)
            g_eq.acquire(SC)
            for mc in range(2):
                for t in range(T4):
                    ps, g = psq[(mc * T4 + t) % 2]
                    g.acquire(TE)
                    op(TE, lambda e, ps=ps, hp=hp, mc=mc, t=t, hc=hc: e.matmul(
                        ps[:, :], projs[hp:hp + 64, mc * 128:(mc + 1) * 128],
                        a2[hp:hp + 64, hc * 2048 + t * 512:hc * 2048 + (t + 1) * 512],
                        start=True, stop=False))
                    op(TE, lambda e, ps=ps, t=t: e.matmul(
                        ps[:, :], ones1b[0:1, :], nhi[0:1, t * 512:(t + 1) * 512],
                        start=False, stop=False))
                    tm_q = op(TE, lambda e, ps=ps, t=t: e.matmul(
                        ps[:, :], ones1b[0:1, :], nlo[0:1, t * 512:(t + 1) * 512],
                        start=False, stop=True), TT)
                    if mc == 1 and t == T4 - 1:
                        g_negrow.release(TT, tm_q)
                    wt(SC, TT, tm_q)
                    s_eq = op(SC, lambda e, ps=ps, mc=mc, t=t: e.activation(
                        eq[:, mc * 2048 + t * 512:mc * 2048 + (t + 1) * 512],
                        ps[:, :], AF.Exp), SS)
                    g.release(SS, s_eq)
            if h == 0:
                wt(TE, SS, ct_last)
            wt(TE, SS, s_eq)
            for t in range(T4):
                psA, gA = psA_m[t % 2]
                gA.acquire(TE)
                op(TE, lambda e, psA=psA, h=h, t=t: e.matmul(
                    psA[0:65, :], ct_all[:, (2 * h) * 65:(2 * h) * 65 + 65],
                    eq[:, t * 512:(t + 1) * 512], start=True, stop=False))
                op(TE, lambda e, psA=psA, h=h, t=t: e.matmul(
                    psA[0:65, :], ct_all[:, (2 * h + 1) * 65:(2 * h + 1) * 65 + 65],
                    eq[:, 2048 + t * 512:2048 + (t + 1) * 512], start=False, stop=False))
                tm_a = op(TE, lambda e, psA=psA, h=h: e.matmul(
                    psA[0:65, :], csr[0:1, h * 65:(h + 1) * 65], ones512b[0:1, :],
                    start=False, stop=True), TT)
                if t == T4 - 1:
                    g_eq.release(TT, tm_a)
                g_idr[t % 2].acquire(VE)
                wt(VE, TT, tm_a)
                v_id = op(VE, lambda e, psA=psA, t=t: e.reciprocal(
                    idr[0:1, (t % 2) * 512:(t % 2) * 512 + 512], psA[64:65, :]), VV)
                gPF.acquire(TE)
                wt(TE, VV, v_id)
                t_pi = op(TE, lambda e, t=t: e.matmul(
                    PF[0:64, :], ones64f[0:1, :], idr[0:1, (t % 2) * 512:(t % 2) * 512 + 512],
                    start=True, stop=True), TT)
                g_idr[t % 2].release(TT, t_pi)
                g_aqs[t % 2].acquire(SC)
                wt(SC, TT, tm_a)
                wt(SC, VV, v_id)
                s_aq = op(SC, lambda e, psA=psA, t=t: e.activation(
                    aqs[0:64, (t % 2) * 512:(t % 2) * 512 + 512], psA[0:64, :], AF.Copy), SS)
                gA.release(SS, s_aq)
                wt(VE, SS, s_aq)
                wt(VE, TT, t_pi)
                v_at = op(VE, lambda e, hp=hp, hc=hc, t=t: e.tensor_tensor(
                    a4[hp:hp + 64, hc * 2048 + t * 512:hc * 2048 + (t + 1) * 512],
                    aqs[0:64, (t % 2) * 512:(t % 2) * 512 + 512], PF[0:64, :], OP.mult), VV)
                gPF.release(VV, v_at)
                g_aqs[t % 2].release(VV, v_at)

    barrier()

    wt(GP, CC[5], 1)
    g_ah = dma(GP, S_CTL, cells[0:1, 8:9], arh_out.ap())
    wt(SC, S_CTL, g_ah)
    s_sigd = op(SC, lambda e: e.activation(cells[0:1, 9:10], cells[0:1, 13:14], AF.Sigmoid), SS)
    wt(VE, SS, s_sigd)
    ve_step(lambda e: e.tensor_tensor(cells[0:1, 10:11], cells[0:1, 9:10], cells[0:1, 8:9], OP.mult))
    ve_step(lambda e: e.tensor_scalar(cells[0:1, 10:11], cells[0:1, 10:11], 1.0 / 127.0, None, OP.mult))
    v_sh = ve_step(lambda e: e.reciprocal(cells[0:1, 11:12], cells[0:1, 10:11]))
    repl128(cells[0:1, 11:12], pp[:, 2:3])
    v_shp = repl128(cells[0:1, 10:11], pp[:, 3:4])

    def hq_dst(gc):
        if gc < 8:
            return a1[:, gc * 2048:(gc + 1) * 2048]
        if gc < 16:
            return a2[:, (gc - 8) * 2048:(gc - 7) * 2048]
        return a3[:, (gc - 16) * 2048:(gc - 15) * 2048]

    h_slot = [None, None]
    g_i32s5 = Guard()
    for gc in range(ICH):
        sl = (gc % 2) * 2048
        if h_slot[gc % 2] is not None:
            wt(SY, VV, h_slot[gc % 2])
        lh = dma(SY, S_X[gc % 2], hst[:, sl:sl + 2048], hh[gc, :, :])
        wt(VE, S_X[gc % 2], lh)
        op(VE, lambda e, sl=sl: e.tensor_scalar(
            hst[:, sl:sl + 2048], hst[:, sl:sl + 2048], pp[:, 2:3], QP, OP.mult, OP.min))
        g_i32s5.acquire(VE)
        v_hi = op(VE, lambda e, sl=sl: e.tensor_scalar(
            i32s5[:, :], hst[:, sl:sl + 2048], QN, None, OP.max), VV)
        h_slot[gc % 2] = v_hi
        wt(SC, VV, v_hi)
        s_hq = op(SC, lambda e, gc=gc: e.activation(hq_dst(gc), i32s5[:, :], AF.Copy), SS)
        g_i32s5.release(SS, s_hq)

    barrier()

    agw_d = agw_out.ap()[:, 2 * NW_G:NW].rearrange("r (p c) -> r p c", p=128)
    psW_m = {0: (PA, gPA), 1: (PB, gPB)}
    psD_m = {0: (PC, gPC), 1: (PD, gPD)}
    g_wobuf = Guard()
    g_dbuf = Guard()
    g_xst = [Guard(), Guard()]
    g_tmp = [Guard(), Guard()]
    g_yst = [Guard(), Guard()]
    for eo2 in range(2):
        g_wobuf.acquire(GP)
        for ei in range(ECH):
            lwo = dma(GP, S_WO, wobuf[:, ei * 512:(ei + 1) * 512],
                      woT_h[ei * 128:(ei + 1) * 128, eo2 * 512:(eo2 + 1) * 512])
        g_dbuf.acquire(SY)
        for ic in range(ICH):
            src = agw_d[eo2 * 4:(eo2 + 1) * 4, :, ic * 128:(ic + 1) * 128] \
                .rearrange("r p c -> p r c")
            ldb = dma(SY, S_DB, dbuf[:, ic * 512:(ic + 1) * 512]
                      .rearrange("p (r c) -> p r c", r=4), src)
        wt(TE, S_DB, ldb)
        wt(TE, S_WO, lwo)
        t_last = None
        for t in range(TCH):
            psW, gW = psW_m[t % 2]
            psD, gD = psD_m[t % 2]
            g_xst[t % 2].acquire(SY)
            lx = dma(SY, S_X[t % 2], xst[:, (t % 2) * 512:(t % 2) * 512 + 512],
                     x_h[t * 128:(t + 1) * 128, eo2 * 512:(eo2 + 1) * 512])
            gD.acquire(TE)
            for ic in range(ICH):
                op(TE, lambda e, psD=psD, ic=ic, t=t: e.matmul(
                    psD[:, :], hq_dst(ic)[:, t * 128:(t + 1) * 128],
                    dbuf[:, ic * 512:(ic + 1) * 512],
                    start=(ic == 0), stop=(ic == ICH - 1)))
            gW.acquire(TE)
            for ei in range(ECH):
                op(TE, lambda e, psW=psW, ei=ei, t=t: e.matmul(
                    psW[:, :], a4[:, ei * 2048 + t * 128:ei * 2048 + (t + 1) * 128],
                    wobuf[:, ei * 512:(ei + 1) * 512],
                    start=(ei == 0), stop=False))
            t_last = op(TE, lambda e, psW=psW, eo2=eo2: e.matmul(
                psW[:, :], ones1b[0:1, :], borb[0:1, eo2 * 512:(eo2 + 1) * 512],
                start=False, stop=True), TT)
            g_tmp[t % 2].acquire(SC)
            wt(SC, TT, t_last)
            s_d = op(SC, lambda e, psD=psD, t=t: e.activation(
                tmpst[:, (t % 2) * 512:(t % 2) * 512 + 512], psD[:, :],
                AF.Copy, scale=pp[:, 3:4]), SS)
            gD.release(SS, s_d)
            g_yst[t % 2].acquire(VE)
            wt(VE, SS, s_d)
            op(VE, lambda e, psW=psW, t=t: e.tensor_tensor(
                yst[:, (t % 2) * 512:(t % 2) * 512 + 512],
                psW[:, :], tmpst[:, (t % 2) * 512:(t % 2) * 512 + 512], OP.add))
            wt(VE, S_X[t % 2], lx)
            v_y = op(VE, lambda e, t=t: e.tensor_tensor(
                yst[:, (t % 2) * 512:(t % 2) * 512 + 512],
                yst[:, (t % 2) * 512:(t % 2) * 512 + 512],
                xst[:, (t % 2) * 512:(t % 2) * 512 + 512], OP.add), VV)
            gW.release(VV, v_y)
            g_tmp[t % 2].release(VV, v_y)
            g_xst[t % 2].release(VV, v_y)
            wt(SY, VV, v_y)
            o_d = dma(SY, S_O[t % 2], y_h[t * 128:(t + 1) * 128, eo2 * 512:(eo2 + 1) * 512],
                      yst[:, (t % 2) * 512:(t % 2) * 512 + 512])
            g_yst[t % 2].release(S_O[t % 2], o_d)
        g_wobuf.release(TT, t_last)
        g_dbuf.release(TT, t_last)
    wt(SY, S_O[0], S_O[0].v)
    wt(SY, S_O[1], S_O[1].v)
    wt(GP, S_O[0], S_O[0].v)
    wt(GP, S_O[1], S_O[1].v)
    wt(GP, S_CTL, S_CTL.v)

    with es, nc.Block() as block:
        for name, dec in ((SY, block.sync), (GP, block.gpsimd), (VE, block.vector),
                          (SC, block.scalar), (TE, block.tensor)):
            def runner(e, name=name):
                for eng, fn in PROG:
                    if eng == name:
                        fn(e)
            dec(runner)
    return nc


def kernel(**inputs):
    import ml_dtypes
    x = np.asarray(inputs["L0_attn_x"], dtype=np.float32)
    wq = np.asarray(inputs["L0_attn_Wq"], dtype=np.float32)
    wk = np.asarray(inputs["L0_attn_Wk"], dtype=np.float32)
    wv = np.asarray(inputs["L0_attn_Wv"], dtype=np.float32)
    wo = np.asarray(inputs["L0_attn_Wo"], dtype=np.float32)
    bq = np.asarray(inputs["L0_attn_bq"], dtype=np.float32)
    bk = np.asarray(inputs["L0_attn_bk"], dtype=np.float32)
    bv = np.asarray(inputs["L0_attn_bv"], dtype=np.float32)
    bo = np.asarray(inputs["L0_attn_bo"], dtype=np.float32)
    proj = np.asarray(inputs["L0_attn_proj"], dtype=np.float32)
    clipx = np.asarray(inputs["L0_ffn_clip_x"], dtype=np.float32)
    clipd = np.asarray(inputs["L0_ffn_clip_down"], dtype=np.float32)
    wgate = np.asarray(inputs["L0_ffn_w_gate"], dtype=np.float32)
    wup = np.asarray(inputs["L0_ffn_w_up"], dtype=np.float32)
    wdown = np.asarray(inputs["L0_ffn_w_down"], dtype=np.float32)

    shared = {
        "wqT": np.ascontiguousarray(wq.T), "wkT": np.ascontiguousarray(wk.T),
        "wvT": np.ascontiguousarray(wv.T), "woT": np.ascontiguousarray(wo.T),
        "bqr": bq.reshape(1, E).copy(), "bkr": bk.reshape(1, E).copy(),
        "bvr": bv.reshape(1, E).copy(), "bor": bo.reshape(1, E).copy(),
        "projT": np.ascontiguousarray(proj.T),
        "clipx": clipx.reshape(1, 1).copy(), "clipd": clipd.reshape(1, 1).copy(),
        "idbf": np.eye(128, dtype=ml_dtypes.bfloat16),
        "idf": np.eye(128, dtype=np.float32),
    }
    in_maps = []
    for c in range(NC):
        b_i, hf = c // 2, c % 2
        m = dict(shared)
        m["x"] = np.ascontiguousarray(x[b_i, hf * TOK:(hf + 1) * TOK, :])
        lo = 3 * c * 128
        for nm, w in (("wg_nat", wgate), ("wu_nat", wup)):
            sl = np.ones((ISLOT * 128, E), np.float32)
            real = w[lo:lo + ISLOT * 128, :]
            sl[:real.shape[0]] = real
            m[nm] = sl
        m["wd_nat"] = np.ascontiguousarray(wdown[c * 128:(c + 1) * 128, :])
        in_maps.append(m)

    nc = build_nc()
    res = run_bass_kernel_spmd(nc, in_maps, core_ids=list(range(NC)))
    out = np.empty((B, NSEQ, E), np.float32)
    for c in range(NC):
        b_i, hf = c // 2, c % 2
        out[b_i, hf * TOK:(hf + 1) * TOK, :] = res.results[c]["y"]
    return out


# version 05
# speedup vs baseline: 1.0096x; 1.0096x over previous
import numpy as np
import contextlib
import concourse.bass as bass
import concourse.mybir as mybir
from concourse.alu_op_type import AluOpType as OP
from concourse.bass_utils import run_bass_kernel_spmd

F32 = mybir.dt.float32
BF16 = mybir.dt.bfloat16
I32 = mybir.dt.int32
AF = mybir.ActivationFunctionType
AX = mybir.AxisListType.X

B, NSEQ, E = 4, 4096, 1024
TOK = 2048
TCH = 16
T4 = 4
ECH = 8
H, D, M = 16, 64, 256
IFF, ICH = 2816, 22
ISLOT = 3
NC = 8
RATIO = 0.0625
EPS = 1e-4
SQD = float(64.0 ** -0.25)
LN_EPS = 1e-5
QP = 127.49995
QN = -128.49995
NW_G = 1024 * ISLOT * 128
NW_D = 2816 * 128
NW = 2 * NW_G + NW_D

A1, A2, A3, A4 = 0, 32768, 65536, 98304
R0 = 131072
S0 = R0 + 61440
TOTAL = 212800

SY, GP, VE, SC, TE = "sync", "gpsimd", "vector", "scalar", "tensor"


def build_nc():
    nc = bass.Bass("TRN2", target_bir_lowering=False, detect_race_conditions=False)

    d_in = lambda n, s, dt=F32: nc.dram_tensor(n, list(s), dt, kind="ExternalInput").ap()
    x_h = d_in("x", (TOK, E))
    wqT_h = d_in("wqT", (E, E)); wkT_h = d_in("wkT", (E, E))
    wvT_h = d_in("wvT", (E, E)); woT_h = d_in("woT", (E, E))
    bqr_h = d_in("bqr", (1, E)); bkr_h = d_in("bkr", (1, E))
    bvr_h = d_in("bvr", (1, E)); bor_h = d_in("bor", (1, E))
    projT_h = d_in("projT", (D, M))
    clipx_h = d_in("clipx", (1, 1)); clipd_h = d_in("clipd", (1, 1))
    wg_h = d_in("wg_nat", (ISLOT * 128, E))
    wu_h = d_in("wu_nat", (ISLOT * 128, E))
    wd_h = d_in("wd_nat", (128, IFF))
    idbf_h = d_in("idbf", (128, 128), BF16)
    idf_h = d_in("idf", (128, 128), F32)
    y_h = nc.dram_tensor("y", [TOK, E], F32, kind="ExternalOutput").ap()


    xn32h = nc.dram_tensor("xn32h", [TOK, E], F32).ap()
    hh = nc.dram_tensor("hh", [ICH, 128, TOK], F32).ap()
    agw_in = nc.dram_tensor("agw_in", [1, NW], BF16)
    agw_out = nc.dram_tensor("agw_out", [NC, NW], BF16)
    ags_in = nc.dram_tensor("ags_in", [1, 768], F32)
    ags_out = nc.dram_tensor("ags_out", [NC, 768], F32)
    arx_in = nc.dram_tensor("arx_in", [1, 1], F32)
    arx_out = nc.dram_tensor("arx_out", [1, 1], F32)
    ars_in = nc.dram_tensor("ars_in", [1, 1], F32)
    ars_out = nc.dram_tensor("ars_out", [1, 1], F32)
    arh_in = nc.dram_tensor("arh_in", [1, 1], F32)
    arh_out = nc.dram_tensor("arh_out", [1, 1], F32)
    arp_in = nc.dram_tensor("arp_in", [129, 2080], F32)
    arp_out = nc.dram_tensor("arp_out", [129, 2080], F32)


    es = contextlib.ExitStack()
    big = es.enter_context(nc.sbuf_tensor("big", [128, TOTAL], mybir.dt.uint8))
    base = nc.lookup_mloc(big).addr

    def place(name, shape, dtype, off):
        per_part = int(np.prod(shape[1:])) * mybir.dt.size(dtype)
        assert off % 32 == 0 and off + per_part <= TOTAL, (name, off, per_part)
        h = nc.virtual_sbuf_tensor(name, list(shape), dtype)
        nc._tensor(h.name, h.shape, h.dtype, type="SB")
        mloc = nc.lookup_mloc(h)
        mloc.allocated = True
        mloc.addr = base + off
        return h

    a1 = place("a1", [128, 16384], BF16, A1)
    a2 = place("a2", [128, 16384], BF16, A2)
    a3 = place("a3", [128, 16384], BF16, A3)
    a4 = place("a4", [128, 16384], BF16, A4)

    msc = place("msc", [128, 4096], F32, R0)
    xbf = place("xbf", [128, 2048], BF16, R0 + 16384)
    wi32 = place("wi32", [128, 1024], I32, R0 + 20480)
    ttile = place("ttile", [128, 128], BF16, R0 + 24576)
    vw = place("vw", [128, 8192], BF16, R0)
    qkw = place("qkw", [128, 2048], BF16, R0 + 16384)
    ek = place("ek", [128, 4096], BF16, R0 + 20480)
    ksq = place("ksq", [128, 4096], BF16, R0 + 28672)
    xnr = place("xnr", [128, 2048], F32, R0)
    i32s = place("i32s", [128, 2048], I32, R0 + 8192)
    xbf2 = place("xbf2", [128, 1024], BF16, R0 + 16384)
    hst = place("hst", [128, 8192], F32, R0)
    gub = place("gub", [128, 8192], BF16, R0 + 32768)
    sgs = place("sgs", [128, 1024], F32, R0 + 49152)
    eq = place("eq", [128, 4096], BF16, R0 + 8192)
    aqs = place("aqs", [128, 1024], BF16, R0 + 40960)
    negrow = place("negrow", [1, 2048], F32, R0 + 43008)
    nhi = place("nhi", [1, 2048], BF16, R0 + 36864)
    nlo = place("nlo", [1, 2048], BF16, R0 + 51200)
    idr = place("idr", [1, 2048], F32, R0 + 16384)
    densb = place("densb", [65, 2048], F32, R0 + 51200)
    ibf = place("ibf", [65, 2048], BF16, R0 + 55296)
    i32s5 = place("i32s5", [128, 2048], I32, R0 + 16384)
    dbuf = place("dbuf", [128, 11264], BF16, R0)
    wobuf = place("wobuf", [128, 4096], BF16, R0 + 22528)
    xst = place("xst", [128, 1024], F32, R0 + 30720)
    tmpst = place("tmpst", [128, 1024], F32, R0 + 34816)
    yst = place("yst", [128, 1024], F32, R0 + 38912)
    ust = place("ust", [128, 130], F32, R0 + 36864)
    rm_all = place("rm_all", [128, 256], F32, R0 + 37440)
    upst = place("upst", [128, 130], F32, R0 + 0)
    ct_all = place("ct_all", [128, 2080], BF16, R0 + 1024)

    off = [S0]
    def sp(name, shape, dtype):
        per_part = int(np.prod(shape[1:])) * mybir.dt.size(dtype)
        h = place(name, shape, dtype, off[0])
        off[0] += (per_part + 63) // 64 * 64
        return h
    projs = sp("projs", [128, 256], BF16)
    idbf = sp("idbf_s", [128, 128], BF16)
    idf = sp("idf_s", [128, 128], F32)
    bqrb = sp("bqrb", [1, 1024], BF16)
    bkrb = sp("bkrb", [1, 1024], BF16)
    stats = sp("stats", [128, 96], F32)
    acx = sp("acx", [128, 16], F32)
    ach = sp("ach", [128, 88], F32)
    seffg = sp("seffg", [128, 22], F32)
    seffu = sp("seffu", [128, 22], F32)
    negb = sp("negb", [128, 16], F32)
    stq = sp("stq", [128, 16], F32)
    dcols = sp("dcols", [128, 16], F32)
    negtr = sp("negtr", [16, 128], F32)
    pp = sp("pp", [128, 16], F32)
    dencol = sp("dencol", [128, 16], F32)
    rme = sp("rme", [128, 16], F32)
    ones1b = sp("ones1b", [1, 128], BF16)
    onescb = sp("onescb", [128, 1], BF16)
    onesn16 = sp("onesn16", [128, 1], BF16)
    onesepsf = sp("onesepsf", [1, 128], F32)
    ones64f = sp("ones64f", [1, 64], F32)
    ones512b = sp("ones512b", [1, 512], BF16)
    ones65 = sp("ones65", [65, 128], BF16)
    csr = sp("csr", [1, 1040], BF16)
    rcell = sp("rcell", [1, 16], BF16)
    vss = sp("vss", [1, 80], F32)
    vsr = sp("vsr", [1, 80], F32)
    rowx = sp("rowx", [1, 128], F32)
    cells = sp("cells", [1, 32], F32)
    epscol = sp("epscol", [128, 1], F32)
    bvrb = sp("bvrb", [1, 1024], BF16)
    borb = sp("borb", [1, 1024], BF16)
    prf32 = place("prf32", [128, 256], F32, R0 + 49152)
    assert off[0] <= TOTAL, off[0]

    PA = es.enter_context(nc.psum_tensor("PA", [128, 512], F32))
    PB = es.enter_context(nc.psum_tensor("PB", [128, 512], F32))
    PC = es.enter_context(nc.psum_tensor("PC", [128, 512], F32))
    PD = es.enter_context(nc.psum_tensor("PD", [128, 512], F32))
    PE = es.enter_context(nc.psum_tensor("PE", [128, 512], F32))
    PF = es.enter_context(nc.psum_tensor("PF", [128, 512], F32))
    PG = es.enter_context(nc.psum_tensor("PG", [128, 512], F32))
    PT = es.enter_context(nc.psum_tensor("PT", [128, 1024], BF16))

    class Sem:
        def __init__(self, name):
            self.h = es.enter_context(nc.semaphore(name))
            self.v = 0
    LX, LG, VV, SS, TT = (Sem(n) for n in ("LX", "LG", "VV", "SS", "TT"))
    CC = [Sem(f"CC{i}") for i in range(6)]
    S_X = [Sem("SX0"), Sem("SX1")]
    S_SP = [Sem("SSP0"), Sem("SSP1"), Sem("SSP2"), Sem("SSP3")]
    S_G = [Sem("SG0"), Sem("SG1"), Sem("SG2"), Sem("SG3")]
    S_QK = [Sem("SQK0"), Sem("SQK1")]
    S_O = [Sem("SO0"), Sem("SO1")]
    S_DB = Sem("SDB")
    S_W = Sem("SW")
    S_VW = Sem("SVW")
    S_WO = Sem("SWO")
    S_AGW = Sem("SAGW")
    S_AGS = Sem("SAGS")
    S_ARP = Sem("SARP")
    S_CTL = Sem("SCTL")
    S_SEF = Sem("SSEF")
    BR = Sem("BR")
    SY_SEMS = [LX, S_X[0], S_X[1], S_SP[0], S_SP[1], S_SP[2], S_SP[3],
               S_G[0], S_G[1], S_G[2], S_G[3], S_O[0], S_O[1], S_DB, S_W]
    GP_SEMS = [LG, S_QK[0], S_QK[1], S_VW, S_WO, S_AGW, S_AGS, S_ARP, S_CTL, S_SEF]

    PROG = []
    def em(eng, fn):
        PROG.append((eng, fn))
    def wt(eng, sem, val):
        if val and val > 0:
            em(eng, lambda e, s=sem.h, v=val: e.wait_ge(s, v))
    def dma(eng, sem, out, in_):
        sem.v += 16
        em(eng, lambda e, o=out, i=in_, s=sem.h: e.dma_start(out=o, in_=i).then_inc(s, 16))
        return sem.v
    def op(eng, fn, sem=None):
        if sem is None:
            em(eng, fn)
            return None
        sem.v += 1
        em(eng, lambda e, f=fn, s=sem.h: f(e).then_inc(s, 1))
        return sem.v

    def ve_step(fn):
        v = op(VE, fn, VV)
        wt(VE, VV, v)
        return v

    class Guard:
        def __init__(self):
            self.conds = []
        def acquire(self, eng):
            for s, v in self.conds:
                wt(eng, s, v)
            self.conds = []
        def release(self, sem, val):
            if val:
                self.conds.append((sem, val))
    gPA, gPB, gPC, gPD, gPE, gPF, gPG, gPT = (Guard() for _ in range(8))

    br_round = [0]

    def barrier():
        for s in SY_SEMS:
            wt(SY, s, s.v)
        em(SY, lambda e: e.sem_inc(BR.h, 1)); BR.v += 1
        for s in GP_SEMS:
            wt(GP, s, s.v)
        em(GP, lambda e: e.sem_inc(BR.h, 1)); BR.v += 1
        br_round[0] = BR.v
        vals = [(BR, BR.v), (VV, VV.v), (SS, SS.v), (TT, TT.v)]
        for eng in (SY, GP, VE, SC, TE):
            for s, v in vals:
                wt(eng, s, v)

    op(VE, lambda e: e.memset(ones1b[:, :], 1.0))
    op(VE, lambda e: e.memset(onescb[:, :], 1.0))
    op(VE, lambda e: e.memset(onesn16[:, :], -0.0625))
    op(VE, lambda e: e.memset(onesepsf[:, :], RATIO * EPS))
    op(VE, lambda e: e.memset(ones64f[:, :], 1.0))
    op(VE, lambda e: e.memset(ones512b[:, :], 1.0))
    op(VE, lambda e: e.memset(ones65[:, :], 1.0))
    op(VE, lambda e: e.memset(vss[0:1, 64:65], float(TOK)))
    op(VE, lambda e: e.memset(epscol[:, :], LN_EPS), VV)
    dma(SY, LX, idbf[:, :], idbf_h)
    dma(SY, LX, idf[:, :], idf_h)
    dma(SY, LX, msc[0:1, 2048:3072], bqr_h)
    dma(SY, LX, msc[0:1, 3072:4096], bkr_h)
    dma(SY, LX, prf32[0:64, :], projT_h)
    dma(SY, LX, prf32[64:128, :], projT_h)
    dma(SY, LX, cells[0:1, 12:13], clipx_h)
    dma(SY, LX, cells[0:1, 13:14], clipd_h)
    dma(SY, LX, msc[0:1, 0:1024], bvr_h)
    l_pre = dma(SY, LX, msc[0:1, 1024:2048], bor_h)
    wt(SC, LX, l_pre)
    op(SC, lambda e: e.activation(projs[0:64, :], prf32[0:64, :], AF.Copy, scale=SQD))
    op(SC, lambda e: e.activation(projs[64:128, :], prf32[64:128, :], AF.Copy, scale=SQD))
    op(SC, lambda e: e.activation(bvrb[:, :], msc[0:1, 0:1024], AF.Copy))
    op(SC, lambda e: e.activation(borb[:, :], msc[0:1, 1024:2048], AF.Copy))
    op(SC, lambda e: e.activation(bqrb[:, :], msc[0:1, 2048:3072], AF.Copy))
    s_pre = op(SC, lambda e: e.activation(bkrb[:, :], msc[0:1, 3072:4096], AF.Copy), SS)

    slot_free = [None, None]
    for c in range(TCH):
        sl = (c % 2) * 1024
        if slot_free[c % 2] is not None:
            vv_f, ss_f = slot_free[c % 2]
            wt(SY, VV, vv_f); wt(SY, SS, ss_f)
        if c == 0:
            wt(SY, SS, s_pre)
        lx = dma(SY, S_X[c % 2], msc[:, sl:sl + 1024], x_h[c * 128:(c + 1) * 128, :])
        wt(VE, S_X[c % 2], lx)
        v_r = op(VE, lambda e, sl=sl, c=c: e.reduce_sum(
            stats[:, c:c + 1], msc[:, sl:sl + 1024], AX), VV)
        wt(SC, S_X[c % 2], lx)
        s_r = op(SC, lambda e, sl=sl, c=c: e.activation(
            xbf[:, sl:sl + 1024], msc[:, sl:sl + 1024], AF.Square,
            accum_out=stats[:, 16 + c:17 + c]), SS)
        slot_free[c % 2] = (v_r, s_r)
    wt(VE, SS, SS.v)
    wt(VE, VV, VV.v)
    ve_step(lambda e: e.tensor_scalar(stats[:, 32:48], stats[:, 0:16], 1.0 / E, None, OP.mult))
    ve_step(lambda e: e.tensor_scalar(stats[:, 48:64], stats[:, 16:32], 1.0 / E, None, OP.mult))
    ve_step(lambda e: e.tensor_tensor(stats[:, 64:80], stats[:, 32:48], stats[:, 32:48], OP.mult))
    v_var = ve_step(lambda e: e.tensor_tensor(stats[:, 64:80], stats[:, 48:64], stats[:, 64:80], OP.subtract))
    wt(SC, VV, v_var)
    s_sd = op(SC, lambda e: e.activation(stats[:, 80:96], stats[:, 64:80], AF.Sqrt, bias=epscol[:, 0:1]), SS)
    wt(VE, SS, s_sd)
    v_rstd = ve_step(lambda e: e.reciprocal(stats[:, 80:96], stats[:, 80:96]))

    xslot_free = [None, None]
    nslot_free = [None, None]
    g_xbf = [Guard(), Guard()]
    wt(SY, S_X[0], S_X[0].v)
    wt(SY, S_X[1], S_X[1].v)
    for c in range(TCH):
        slx = 2048 + (c % 2) * 1024
        sln = (c % 2) * 1024
        if xslot_free[c % 2] is not None:
            wt(SY, VV, xslot_free[c % 2])
        lx = dma(SY, S_X[c % 2], msc[:, slx:slx + 1024], x_h[c * 128:(c + 1) * 128, :])
        wt(VE, S_X[c % 2], lx)
        if c == 0:
            wt(VE, VV, v_rstd)
        if nslot_free[c % 2] is not None:
            lx_sp, ss_ca = nslot_free[c % 2]
            wt(VE, S_SP[c % 2], lx_sp); wt(VE, SS, ss_ca)
        v_n = op(VE, lambda e, slx=slx, sln=sln, c=c: e.tensor_scalar(
            msc[:, sln:sln + 1024], msc[:, slx:slx + 1024],
            stats[:, 32 + c:33 + c], stats[:, 80 + c:81 + c],
            OP.subtract, OP.mult), VV)
        xslot_free[c % 2] = v_n
        v_am = op(VE, lambda e, sln=sln, c=c: e.tensor_reduce(
            acx[:, c:c + 1], msc[:, sln:sln + 1024], AX, OP.max,
            apply_absolute_value=True), VV)
        wt(SY, VV, v_am)
        lx_sp = dma(SY, S_SP[c % 2], xn32h[c * 128:(c + 1) * 128, :], msc[:, sln:sln + 1024])
        wt(SC, VV, v_am)
        g_xbf[c % 2].acquire(SC)
        s_ca = op(SC, lambda e, sln=sln: e.activation(
            xbf[:, sln:sln + 1024], msc[:, sln:sln + 1024], AF.Copy), SS)
        nslot_free[c % 2] = (lx_sp, s_ca)
        gPT.acquire(TE)
        wt(TE, SS, s_ca)
        for j in range(8):
            t_tp = op(TE, lambda e, sln=sln, j=j: e.transpose(
                PT[:, j * 128:(j + 1) * 128], xbf[:, sln + j * 128:sln + (j + 1) * 128],
                idbf[:, :]), TT if j == 7 else None)
        g_xbf[c % 2].release(TT, t_tp)
        wt(VE, TT, t_tp)
        dst = a1[:, :].rearrange("p (g t) -> p g t", g=8)[:, :, c * 128:(c + 1) * 128]
        v_ev = op(VE, lambda e, dst=dst: e.tensor_copy(
            dst, PT[:, :].rearrange("p (g t) -> p g t", g=8)), VV)
        gPT.release(VV, v_ev)
    wt(VE, VV, VV.v)
    v_ax = op(VE, lambda e: e.tensor_reduce(pp[:, 15:16], acx[:, :], AX, OP.max), VV)
    wt(GP, VV, v_ax)
    g1 = dma(GP, S_CTL, rowx[0:1, :], pp[:, 15:16])
    wt(VE, S_CTL, g1)
    v_axc = op(VE, lambda e: e.reduce_max(cells[0:1, 0:1], rowx[0:1, :], AX), VV)
    wt(GP, VV, v_axc)
    dma(GP, S_CTL, arx_in.ap(), cells[0:1, 0:1])
    op(GP, lambda e: e.collective_compute(
        "AllReduce", OP.max, replica_groups=[list(range(NC))],
        ins=[arx_in.ap().opt()], outs=[arx_out.ap().opt()]), CC[0])

    barrier()

    g_region = agw_in.ap()[0:1, 0:NW_G].rearrange("a (p c) -> (a p) c", p=128)
    u_region = agw_in.ap()[0:1, NW_G:2 * NW_G].rearrange("a (p c) -> (a p) c", p=128)
    d_region = agw_in.ap()[0:1, 2 * NW_G:NW].rearrange("a (p c) -> (a p) c", p=128)
    last_agw = 0
    g_ttile = Guard()
    g_stats = Guard()
    g_wi32 = Guard()
    g_xbfq = Guard()

    def quant_block(w_slice_ap, tile_dst, is_down, s_col):
        nonlocal last_agw
        width = w_slice_ap.shape[1]
        g_stats.acquire(VE)
        v_amx = op(VE, lambda e: e.tensor_reduce(stats[:, 0:1], w_slice_ap, AX, OP.max,
                                                 apply_absolute_value=True), VV)
        wt(VE, VV, v_amx)
        ve_step(lambda e: e.tensor_scalar(stats[:, 1:2], stats[:, 0:1], 1.0 / 127.0,
                                          None, OP.mult))
        ve_step(lambda e: e.reciprocal(stats[:, 2:3], stats[:, 0:1]))
        v_is = ve_step(lambda e: e.tensor_scalar(stats[:, 2:3], stats[:, 2:3], 127.0,
                                                 None, OP.mult))
        if s_col is not None:
            wt(GP, VV, v_is)
            l_ags = dma(GP, S_AGS, ags_in.ap()[0:1, s_col:s_col + 128], stats[:, 0:1])
            g_stats.release(S_AGS, l_ags)
        nblk = width // 128
        for bb in range(0, nblk, 8):
            hi = min(bb + 8, nblk)
            w = (hi - bb) * 128
            g_wi32.acquire(VE)
            v_q = op(VE, lambda e, bb=bb, w=w: e.tensor_scalar(
                wi32[:, 0:w], w_slice_ap[:, bb * 128:bb * 128 + w],
                stats[:, 2:3], None, OP.mult), VV)
            wt(SC, VV, v_q)
            g_xbfq.acquire(SC)
            if is_down:
                s_q = op(SC, lambda e, w=w: e.activation(
                    xbf[:, 0:w], wi32[:, 0:w], AF.Copy, scale=stats[:, 1:2]), SS)
            else:
                s_q = op(SC, lambda e, w=w: e.activation(
                    xbf[:, 0:w], wi32[:, 0:w], AF.Copy), SS)
            g_wi32.release(SS, s_q)
            if is_down:
                g_stats.release(SS, s_q)
            wt(TE, SS, s_q)
            for j in range(hi - bb):
                gPT.acquire(TE)
                t_tp = op(TE, lambda e, j=j: e.transpose(
                    PT[:, 0:128], xbf[:, j * 128:(j + 1) * 128], idbf[:, :]), TT)
                g_ttile.acquire(VE)
                wt(VE, TT, t_tp)
                v_ev = op(VE, lambda e: e.tensor_copy(ttile[:, :], PT[:, 0:128]), VV)
                gPT.release(VV, v_ev)
                if j == hi - bb - 1:
                    g_xbfq.release(TT, t_tp)
                wt(GP, VV, v_ev)
                last_agw = dma(GP, S_AGW, tile_dst(bb + j), ttile[:, :])
                g_ttile.release(S_AGW, last_agw)

    mslot = Guard()
    for wsel, (w_h, region, soff) in enumerate(
            [(wg_h, g_region, 0), (wu_h, u_region, 384)]):
        for icl in range(ISLOT):
            mslot.acquire(SY)
            lw = dma(SY, S_W, msc[:, 0:1024], w_h[icl * 128:(icl + 1) * 128, :])
            wt(VE, S_W, lw)
            quant_block(msc[:, 0:1024],
                        lambda k, region=region, icl=icl:
                        region[:, k * 384 + icl * 128:k * 384 + (icl + 1) * 128],
                        False, soff + icl * 128)
            mslot.release(VV, VV.v)
    mslot.acquire(SY)
    lw = dma(SY, S_W, msc[:, 0:2816], wd_h)
    wt(VE, S_W, lw)
    quant_block(msc[:, 0:2816],
                lambda k: d_region[:, k * 128:(k + 1) * 128], True, None)
    wt(GP, S_AGW, last_agw)
    wt(GP, S_AGS, S_AGS.v)
    op(GP, lambda e: e.collective_compute(
        "AllGather", OP.bypass, replica_groups=[list(range(NC))],
        ins=[agw_in.ap().opt()], outs=[agw_out.ap().opt()]), CC[1])
    op(GP, lambda e: e.collective_compute(
        "AllGather", OP.bypass, replica_groups=[list(range(NC))],
        ins=[ags_in.ap().opt()], outs=[ags_out.ap().opt()]), CC[2])

    barrier()

    for ei in range(ECH):
        dma(GP, S_VW, vw[:, ei * 1024:(ei + 1) * 1024],
            wvT_h[ei * 128:(ei + 1) * 128, :])
    lg_vw = S_VW.v
    qk_slot = [Guard(), Guard()]
    psq = {0: (PA, gPA), 1: (PB, gPB)}
    for wsel in range(2):
        dst = a2 if wsel == 0 else a3
        brow = bqrb if wsel == 0 else bkrb
        for eo in range(ECH):
            idx = wsel * ECH + eo
            sl = (idx % 2) * 1024
            qk_slot[idx % 2].acquire(GP)
            src = (wqT_h if wsel == 0 else wkT_h)[:, eo * 128:(eo + 1) * 128] \
                .rearrange("(a p) c -> p a c", p=128)
            lqk = dma(GP, S_QK[idx % 2], qkw[:, sl:sl + 1024].rearrange("p (a c) -> p a c", a=8), src)
            wt(TE, S_QK[idx % 2], lqk)
            for t in range(T4):
                ps, g = psq[t % 2]
                g.acquire(TE)
                for ei in range(ECH):
                    op(TE, lambda e, ps=ps, sl=sl, ei=ei, t=t: e.matmul(
                        ps[:, :], qkw[:, sl + ei * 128:sl + (ei + 1) * 128],
                        a1[:, ei * 2048 + t * 512:ei * 2048 + (t + 1) * 512],
                        start=(ei == 0), stop=False))
                tm = op(TE, lambda e, ps=ps, brow=brow, eo=eo: e.matmul(
                    ps[:, :], brow[0:1, eo * 128:(eo + 1) * 128], ones512b[0:1, :],
                    start=False, stop=True), TT)
                if t == T4 - 1:
                    qk_slot[idx % 2].release(TT, tm)
                wt(SC, TT, tm)
                s_ev = op(SC, lambda e, ps=ps, dst=dst, eo=eo, t=t:
                          e.activation(dst[:, eo * 2048 + t * 512:eo * 2048 + (t + 1) * 512],
                                       ps[:, :], AF.Copy), SS)
                g.release(SS, s_ev)
    wt(TE, S_VW, lg_vw)
    for t in range(TCH):
        for eo2 in range(2):
            ps, g = psq[(t * 2 + eo2) % 2]
            g.acquire(TE)
            for ei in range(ECH):
                op(TE, lambda e, ps=ps, t=t, eo2=eo2, ei=ei: e.matmul(
                    ps[:, :], a1[:, ei * 2048 + t * 128:ei * 2048 + (t + 1) * 128],
                    vw[:, ei * 1024 + eo2 * 512:ei * 1024 + (eo2 + 1) * 512],
                    start=(ei == 0), stop=False))
            tm = op(TE, lambda e, ps=ps, eo2=eo2: e.matmul(
                ps[:, :], ones1b[0:1, :], bvrb[0:1, eo2 * 512:(eo2 + 1) * 512],
                start=False, stop=True), TT)
            wt(VE, TT, tm)
            v_ev = op(VE, lambda e, ps=ps, t=t, eo2=eo2: e.tensor_copy(
                a4[:, t * 1024 + eo2 * 512:t * 1024 + (eo2 + 1) * 512], ps[:, :]), VV)
            g.release(VV, v_ev)

    barrier()

    ksq_slot = [Guard(), Guard()]
    g_dcols = Guard()
    g_ek = Guard()
    v_z = op(VE, lambda e: e.memset(msc[0:1, 0:1040], 0.0), VV)
    wt(GP, VV, v_z)
    dma(GP, S_ARP, arp_in.ap()[128:129, 1040:2080], msc[0:1, 0:1040])
    for hc in range(ECH):
        ksq_slot[hc % 2].acquire(SC)
        s_ksq = op(SC, lambda e, hc=hc: e.activation(
            ksq[:, (hc % 2) * 2048:(hc % 2) * 2048 + 2048],
            a3[:, hc * 2048:(hc + 1) * 2048], AF.Square), SS)
        for hh_i in range(2):
            h = 2 * hc + hh_i
            hp = hh_i * 64
            gPC.acquire(TE)
            wt(TE, SS, s_ksq)
            for t in range(TCH):
                tm_d = op(TE, lambda e, hc=hc, hp=hp, t=t: e.matmul(
                    PC[:, t:t + 1],
                    ksq[hp:hp + 64, (hc % 2) * 2048 + t * 128:(hc % 2) * 2048 + (t + 1) * 128],
                    onesn16[hp:hp + 64, 0:1], start=True, stop=True),
                    TT if t == TCH - 1 else None)
            if hh_i == 1:
                ksq_slot[hc % 2].release(TT, tm_d)
            g_dcols.acquire(VE)
            wt(VE, TT, tm_d)
            v_d = op(VE, lambda e: e.tensor_copy(dcols[:, 0:16], PC[:, 0:16]), VV)
            gPC.release(VV, v_d)
            for t in range(TCH):
                ps, g = psq[t % 2]
                g.acquire(TE)
                tm_dd = op(TE, lambda e, ps=ps, hc=hc, hp=hp, t=t: e.matmul(
                    ps[:, 0:256],
                    a3[hp:hp + 64, hc * 2048 + t * 128:hc * 2048 + (t + 1) * 128],
                    projs[hp:hp + 64, :], start=True, stop=True), TT)
                wt(SC, TT, tm_dd)
                if t == 0:
                    wt(SC, VV, v_d)
                g_ek.acquire(SC) if t == 0 else None
                s_ek = op(SC, lambda e, ps=ps, t=t: e.activation(
                    ek[:, t * 256:(t + 1) * 256], ps[:, 0:256], AF.Exp,
                    bias=dcols[:, t:t + 1]), SS)
                g.release(SS, s_ek)
                wt(VE, SS, s_ek)
                v_rm = op(VE, lambda e, h=h, t=t: e.reduce_max(
                    rme[:, t:t + 1], ek[:, t * 256:(t + 1) * 256], AX), VV)
                wt(TE, SS, s_ek)
                if t == 0:
                    gPD.acquire(TE); gPE.acquire(TE); gPF.acquire(TE)
                    gPG.acquire(TE); gPC.acquire(TE)
                v_sl = a4[:, t * 1024 + h * 64:t * 1024 + (h + 1) * 64]
                op(TE, lambda e, t=t, v_sl=v_sl: e.matmul(
                    PD[:, 0:64], ek[:, t * 256:t * 256 + 128], v_sl,
                    start=(t == 0), stop=(t == TCH - 1)))
                op(TE, lambda e, t=t, v_sl=v_sl: e.matmul(
                    PE[:, 0:64], ek[:, t * 256 + 128:t * 256 + 256], v_sl,
                    start=(t == 0), stop=(t == TCH - 1)))
                op(TE, lambda e, t=t: e.matmul(
                    PF[:, 0:1], ek[:, t * 256:t * 256 + 128], onescb[:, 0:1],
                    start=(t == 0), stop=(t == TCH - 1)))
                op(TE, lambda e, t=t: e.matmul(
                    PG[:, 0:1], ek[:, t * 256 + 128:t * 256 + 256], onescb[:, 0:1],
                    start=(t == 0), stop=(t == TCH - 1)))
                tm_vs = op(TE, lambda e, t=t, v_sl=v_sl: e.matmul(
                    PC[0:1, 0:64], onescb[:, 0:1], v_sl,
                    start=(t == 0), stop=(t == TCH - 1)),
                    TT if t == TCH - 1 else None)
                if t == TCH - 1:
                    g_ek.release(TT, tm_vs)
                    g_ek.release(VV, v_rm)
            wt(SC, VV, v_rm)
            s_ln = op(SC, lambda e: e.activation(rme[:, 0:16], rme[:, 0:16], AF.Ln), SS)
            wt(VE, SS, s_ln)
            v_rma = op(VE, lambda e, h=h: e.tensor_tensor(
                rm_all[:, h * 16:h * 16 + 16], rme[:, 0:16], dcols[:, 0:16],
                OP.subtract), VV)
            g_dcols.release(VV, v_rma)
            wt(VE, TT, tm_vs)
            op(VE, lambda e: e.tensor_copy(ust[:, 0:64], PD[:, 0:64]))
            op(VE, lambda e: e.tensor_copy(ust[:, 64:65], PF[:, 0:1]))
            op(VE, lambda e: e.tensor_copy(ust[:, 65:129], PE[:, 0:64]))
            op(VE, lambda e: e.tensor_copy(ust[:, 129:130], PG[:, 0:1]))
            v_u = op(VE, lambda e: e.tensor_copy(vss[0:1, 0:64], PC[0:1, 0:64]), VV)
            gPD.release(VV, v_u); gPE.release(VV, v_u); gPF.release(VV, v_u)
            gPG.release(VV, v_u); gPC.release(VV, v_u)
            wt(GP, VV, v_u)
            w1 = dma(GP, S_ARP, arp_in.ap()[0:128, (2 * h) * 65:(2 * h + 2) * 65],
                     ust[:, 0:130])
            w2 = dma(GP, S_ARP, arp_in.ap()[128:129, h * 65:(h + 1) * 65],
                     vss[0:1, 0:65])
            wt(VE, S_ARP, w2)
    wt(VE, VV, VV.v)
    v_dk = op(VE, lambda e: e.reduce_max(pp[:, 14:15], rm_all[:, :], AX), VV)
    wt(GP, VV, v_dk)
    g2 = dma(GP, S_CTL, rowx[0:1, :], pp[:, 14:15])
    wt(VE, S_CTL, g2)
    v_dkc = op(VE, lambda e: e.reduce_max(cells[0:1, 5:6], rowx[0:1, :], AX), VV)
    wt(GP, VV, v_dkc)
    dma(GP, S_CTL, ars_in.ap(), cells[0:1, 5:6])
    wt(GP, S_ARP, S_ARP.v)
    op(GP, lambda e: e.collective_compute(
        "AllReduce", OP.add, replica_groups=[[0, 1], [2, 3], [4, 5], [6, 7]],
        ins=[arp_in.ap().opt()], outs=[arp_out.ap().opt()]), CC[3])
    op(GP, lambda e: e.collective_compute(
        "AllReduce", OP.max, replica_groups=[list(range(NC))],
        ins=[ars_in.ap().opt()], outs=[ars_out.ap().opt()]), CC[4])

    barrier()

    wt(GP, CC[0], 1)
    g_ax = dma(GP, S_CTL, cells[0:1, 0:1], arx_out.ap())
    wt(SC, S_CTL, g_ax)
    s_sig = op(SC, lambda e: e.activation(cells[0:1, 1:2], cells[0:1, 12:13], AF.Sigmoid), SS)
    wt(VE, SS, s_sig)
    ve_step(lambda e: e.tensor_tensor(cells[0:1, 2:3], cells[0:1, 1:2], cells[0:1, 0:1], OP.mult))
    ve_step(lambda e: e.tensor_scalar(cells[0:1, 2:3], cells[0:1, 2:3], 1.0 / 127.0, None, OP.mult))
    ve_step(lambda e: e.reciprocal(cells[0:1, 3:4], cells[0:1, 2:3]))
    v_sxf = ve_step(lambda e: e.tensor_scalar(cells[0:1, 4:5], cells[0:1, 2:3], 1.0 / 127.0, None, OP.mult))

    def repl128(cell_ap, dst_col):
        ve_step(lambda e: e.tensor_copy(rcell[0:1, 0:1], cell_ap))
        ve_step(lambda e: e.tensor_tensor(vsr[0:1, 70:71], cell_ap, rcell[0:1, 0:1], OP.subtract))
        v_r = ve_step(lambda e: e.tensor_copy(rcell[0:1, 1:2], vsr[0:1, 70:71]))
        gPG.acquire(TE)
        wt(TE, VV, v_r)
        op(TE, lambda e: e.matmul(PG[:, 0:1], ones1b[0:1, :], rcell[0:1, 0:1], start=True, stop=False))
        t_r = op(TE, lambda e: e.matmul(PG[:, 0:1], ones1b[0:1, :], rcell[0:1, 1:2], start=False, stop=True), TT)
        wt(VE, TT, t_r)
        v_f = op(VE, lambda e: e.tensor_copy(dst_col, PG[:, 0:1]), VV)
        wt(VE, VV, v_f)
        gPG.release(VV, v_f)
        return v_f

    wt(VE, VV, v_sxf)
    repl128(cells[0:1, 3:4], pp[:, 0:1])
    v_sxfp = repl128(cells[0:1, 4:5], pp[:, 1:2])

    xq_slot = [None, None]
    g_i32 = [Guard(), Guard()]
    for c in range(TCH):
        sl = (c % 2) * 1024
        if xq_slot[c % 2] is not None:
            wt(SY, VV, xq_slot[c % 2])
        lxn = dma(SY, S_X[c % 2], xnr[:, sl:sl + 1024], xn32h[c * 128:(c + 1) * 128, :])
        wt(VE, S_X[c % 2], lxn)
        op(VE, lambda e, sl=sl: e.tensor_scalar(
            xnr[:, sl:sl + 1024], xnr[:, sl:sl + 1024], pp[:, 0:1], QP, OP.mult, OP.min))
        g_i32[c % 2].acquire(VE)
        v_i = op(VE, lambda e, sl=sl: e.tensor_scalar(
            i32s[:, sl:sl + 1024], xnr[:, sl:sl + 1024], QN, None, OP.max), VV)
        xq_slot[c % 2] = v_i
        wt(SC, VV, v_i)
        s_b = op(SC, lambda e, sl=sl: e.activation(
            xbf2[:, 0:1024], i32s[:, sl:sl + 1024], AF.Copy), SS)
        g_i32[c % 2].release(SS, s_b)
        gPT.acquire(TE)
        wt(TE, SS, s_b)
        for j in range(8):
            t_l = op(TE, lambda e, j=j: e.transpose(
                PT[:, j * 128:(j + 1) * 128], xbf2[:, j * 128:(j + 1) * 128],
                idbf[:, :]), TT if j == 7 else None)
        wt(VE, TT, t_l)
        dst = a4[:, :].rearrange("p (g t) -> p g t", g=8)[:, :, c * 128:(c + 1) * 128]
        v_x = op(VE, lambda e, dst=dst: e.tensor_copy(
            dst, PT[:, :].rearrange("p (g t) -> p g t", g=8)), VV)
        gPT.release(VV, v_x)
        wt(SC, TT, t_l)
    wt(GP, CC[2], 1)
    for gc in range(ICH):
        r, icl = gc // 3, gc % 3
        dma(GP, S_SEF, seffg[:, gc:gc + 1],
            ags_out.ap()[r:r + 1, icl * 128:(icl + 1) * 128])
        g_s = dma(GP, S_SEF, seffu[:, gc:gc + 1],
                  ags_out.ap()[r:r + 1, 384 + icl * 128:384 + (icl + 1) * 128])
    wt(VE, S_SEF, g_s)
    wt(VE, VV, v_sxfp)
    op(VE, lambda e: e.tensor_scalar(seffg[:, :], seffg[:, :], pp[:, 1:2], None, OP.mult))
    op(VE, lambda e: e.tensor_scalar(seffu[:, :], seffu[:, :], pp[:, 1:2], None, OP.mult), VV)

    barrier()

    wt(TE, CC[1], 1)
    wt(SY, CC[1], 1)
    gub_slot = [Guard(), Guard(), Guard(), Guard()]
    hst_slot = [Guard(), Guard(), Guard(), Guard()]
    g_sgs = [Guard(), Guard()]
    psg_m = {0: (PA, gPA), 1: (PB, gPB)}
    psu_m = {0: (PC, gPC), 1: (PD, gPD)}
    def gu_load(gc):
        r, icl = gc // 3, gc % 3
        slg = (gc % 4) * 1024
        src_g = agw_out.ap()[r:r + 1, 0:NW_G] \
            .rearrange("a (p g c) -> (a p) g c", p=128, c=384)[:, :, icl * 128:(icl + 1) * 128]
        src_u = agw_out.ap()[r:r + 1, NW_G:2 * NW_G] \
            .rearrange("a (p g c) -> (a p) g c", p=128, c=384)[:, :, icl * 128:(icl + 1) * 128]
        gub_slot[gc % 4].acquire(SY)
        dma(SY, S_G[gc % 4], gub[:, slg:slg + 1024].rearrange("p (g c) -> p g c", g=8), src_g)
        return dma(SY, S_G[gc % 4], gub[:, 4096 + slg:4096 + slg + 1024]
                   .rearrange("p (g c) -> p g c", g=8), src_u)
    lgu_q = [gu_load(0), gu_load(1), gu_load(2)]
    for gc in range(ICH):
        slg = (gc % 4) * 1024
        sl_h = (gc % 4) * 2048
        if gc + 3 < ICH:
            lgu_q.append(gu_load(gc + 3))
        lgu = lgu_q[gc]
        hst_slot[gc % 4].acquire(VE)
        wt(TE, S_G[gc % 4], lgu)
        for t in range(T4):
            psg, gg = psg_m[t % 2]
            psu, gu = psu_m[t % 2]
            gg.acquire(TE)
            for hcc in range(ECH):
                op(TE, lambda e, psg=psg, slg=slg, hcc=hcc, t=t: e.matmul(
                    psg[:, :], gub[:, slg + hcc * 128:slg + (hcc + 1) * 128],
                    a4[:, hcc * 2048 + t * 512:hcc * 2048 + (t + 1) * 512],
                    start=(hcc == 0), stop=(hcc == ECH - 1)))
            gu.acquire(TE)
            for hcc in range(ECH):
                tm_u = op(TE, lambda e, psu=psu, slg=slg, hcc=hcc, t=t: e.matmul(
                    psu[:, :], gub[:, 4096 + slg + hcc * 128:4096 + slg + (hcc + 1) * 128],
                    a4[:, hcc * 2048 + t * 512:hcc * 2048 + (t + 1) * 512],
                    start=(hcc == 0), stop=(hcc == ECH - 1)),
                    TT if hcc == ECH - 1 else None)
            if t == T4 - 1:
                gub_slot[gc % 4].release(TT, tm_u)
            wt(SC, TT, tm_u)
            g_sgs[t % 2].acquire(SC)
            s_sg = op(SC, lambda e, psg=psg, t=t, gc=gc: e.activation(
                sgs[:, (t % 2) * 512:(t % 2) * 512 + 512], psg[:, :], AF.Silu,
                scale=seffg[:, gc:gc + 1]), SS)
            gg.release(SS, s_sg)
            wt(VE, SS, s_sg)
            op(VE, lambda e, psu=psu, t=t, gc=gc, sl_h=sl_h: e.scalar_tensor_tensor(
                hst[:, sl_h + t * 512:sl_h + (t + 1) * 512], psu[:, :],
                seffu[:, gc:gc + 1], sgs[:, (t % 2) * 512:(t % 2) * 512 + 512],
                OP.mult, OP.mult))
            v_h = op(VE, lambda e, t=t, gc=gc, sl_h=sl_h: e.tensor_reduce(
                ach[:, gc * 4 + t:gc * 4 + t + 1],
                hst[:, sl_h + t * 512:sl_h + (t + 1) * 512], AX, OP.max,
                apply_absolute_value=True), VV)
            gu.release(VV, v_h)
            g_sgs[t % 2].release(VV, v_h)
        wt(SY, VV, v_h)
        l_sp = dma(SY, S_SP[gc % 4], hh[gc, :, :], hst[:, sl_h:sl_h + 2048])
        hst_slot[gc % 4].release(S_SP[gc % 4], l_sp)
    wt(VE, VV, VV.v)
    v_ah = op(VE, lambda e: e.reduce_max(pp[:, 13:14], ach[:, :], AX), VV)
    wt(GP, VV, v_ah)
    g3 = dma(GP, S_CTL, rowx[0:1, :], pp[:, 13:14])
    wt(VE, S_CTL, g3)
    v_ahc = op(VE, lambda e: e.reduce_max(cells[0:1, 8:9], rowx[0:1, :], AX), VV)
    wt(GP, VV, v_ahc)
    dma(GP, S_CTL, arh_in.ap(), cells[0:1, 8:9])
    op(GP, lambda e: e.collective_compute(
        "AllReduce", OP.max, replica_groups=[list(range(NC))],
        ins=[arh_in.ap().opt()], outs=[arh_out.ap().opt()]), CC[5])

    barrier()

    wt(GP, CC[3], 1)
    wt(GP, CC[4], 1)
    g_es = dma(GP, S_CTL, cells[0:1, 6:7], ars_out.ap())
    wt(SC, S_CTL, g_es)
    s_es = op(SC, lambda e: e.activation(cells[0:1, 6:7], cells[0:1, 6:7], AF.Exp, scale=-1.0), SS)
    wt(VE, SS, s_es)
    v_res = ve_step(lambda e: e.tensor_scalar(cells[0:1, 7:8], cells[0:1, 6:7], RATIO, None, OP.mult))
    v_resp = repl128(cells[0:1, 7:8], pp[:, 4:5])

    g_upst = Guard()
    g_vsr = Guard()
    ct_last = None
    for h in range(H):
        g_upst.acquire(GP)
        g_u = dma(GP, S_CTL, upst[:, 0:130], arp_out.ap()[0:128, 2 * h * 65:(2 * h + 2) * 65])
        g_vsr.acquire(GP)
        g_v = dma(GP, S_CTL, vsr[0:1, 0:65], arp_out.ap()[128:129, h * 65:(h + 1) * 65])
        gPF.acquire(TE)
        wt(TE, S_CTL, g_v)
        t_rv = op(TE, lambda e: e.matmul(PF[:, 0:65], onesepsf[0:1, :], vsr[0:1, 0:65],
                                         start=True, stop=True), TT)
        g_vsr.release(TT, t_rv)
        wt(VE, TT, t_rv)
        wt(VE, S_CTL, g_v)
        if h == 0:
            wt(VE, VV, v_resp)
        op(VE, lambda e, h=h: e.scalar_tensor_tensor(
            ct_all[:, (2 * h) * 65:(2 * h) * 65 + 65], upst[:, 0:65], pp[:, 4:5],
            PF[:, 0:65], OP.mult, OP.add))
        v_ct = op(VE, lambda e, h=h: e.scalar_tensor_tensor(
            ct_all[:, (2 * h + 1) * 65:(2 * h + 1) * 65 + 65], upst[:, 65:130], pp[:, 4:5],
            PF[:, 0:65], OP.mult, OP.add), VV)
        g_upst.release(VV, v_ct)
        gPF.release(VV, v_ct)
        gPG.acquire(TE)
        wt(TE, VV, v_ct)
        op(TE, lambda e, h=h: e.matmul(PG[0:1, 0:65], onescb[:, 0:1],
                                       ct_all[:, (2 * h) * 65:(2 * h) * 65 + 65],
                                       start=True, stop=False))
        t_cs = op(TE, lambda e, h=h: e.matmul(PG[0:1, 0:65], onescb[:, 0:1],
                                              ct_all[:, (2 * h + 1) * 65:(2 * h + 1) * 65 + 65],
                                              start=False, stop=True), TT)
        wt(SC, TT, t_cs)
        ct_last = op(SC, lambda e, h=h: e.activation(csr[0:1, h * 65:(h + 1) * 65],
                                                     PG[0:1, 0:65], AF.Copy, scale=EPS), SS)
        gPG.release(SS, ct_last)

    qsq_slot = [Guard(), Guard()]
    g_stq = Guard()
    g_eq = Guard()
    g_dencol = Guard()
    g_densb = Guard()
    g_idrow = Guard()
    g_ibf = Guard()
    g_isb = [Guard(), Guard()]
    psA_m4 = {0: (PD, gPD), 1: (PE, gPE), 2: (PA, gPA), 3: (PB, gPB)}
    g_negrow = Guard()
    g_negtr = Guard()
    g_aqs = [Guard(), Guard()]
    g_idr = [Guard(), Guard()]
    psA_m = {0: (PD, gPD), 1: (PE, gPE)}
    for hc in range(ECH):
        qsq_slot[hc % 2].acquire(SC)
        s_qsq = op(SC, lambda e, hc=hc: e.activation(
            ksq[:, (hc % 2) * 2048:(hc % 2) * 2048 + 2048],
            a2[:, hc * 2048:(hc + 1) * 2048], AF.Square), SS)
        for hh_i in range(2):
            h = 2 * hc + hh_i
            hp = hh_i * 64
            gPC.acquire(TE)
            wt(TE, SS, s_qsq)
            for t in range(TCH):
                tm_d = op(TE, lambda e, hc=hc, hp=hp, t=t: e.matmul(
                    PC[:, t:t + 1],
                    ksq[hp:hp + 64, (hc % 2) * 2048 + t * 128:(hc % 2) * 2048 + (t + 1) * 128],
                    onesn16[hp:hp + 64, 0:1], start=True, stop=True),
                    TT if t == TCH - 1 else None)
            if hh_i == 1:
                qsq_slot[hc % 2].release(TT, tm_d)
            g_stq.acquire(VE)
            for t in range(TCH):
                ps, g = psq[t % 2]
                g.acquire(TE)
                tm_dd = op(TE, lambda e, ps=ps, hc=hc, hp=hp, t=t: e.matmul(
                    ps[:, 0:256],
                    a2[hp:hp + 64, hc * 2048 + t * 128:hc * 2048 + (t + 1) * 128],
                    projs[hp:hp + 64, :], start=True, stop=True), TT)
                wt(VE, TT, tm_dd)
                v_rm = op(VE, lambda e, ps=ps, t=t: e.reduce_max(
                    stq[:, t:t + 1], ps[:, 0:256], AX), VV)
                g.release(VV, v_rm)
            wt(VE, TT, tm_d)
            wt(VE, VV, VV.v)
            v_nb = op(VE, lambda e: e.scalar_tensor_tensor(
                negb[:, :], PC[:, 0:16], 1.0, stq[:, :], OP.mult, OP.subtract), VV)
            gPC.release(VV, v_nb)
            g_stq.release(VV, v_nb)
            gPG.acquire(TE)
            wt(TE, VV, v_nb)
            t_tr = op(TE, lambda e: e.transpose(PG[0:16, 0:128], negb[:, :], idf[:, :]), TT)
            wt(VE, TT, t_tr)
            g_negtr.acquire(VE)
            v_tr = op(VE, lambda e: e.tensor_copy(negtr[0:16, :], PG[0:16, 0:128]), VV)
            gPG.release(VV, v_tr)
            g_negrow.acquire(GP)
            wt(GP, VV, v_tr)
            g_nr = dma(GP, S_CTL, negrow[0:1, :], negtr[0:16, :])
            g_negtr.release(S_CTL, g_nr)
            wt(VE, S_CTL, g_nr)
            op(VE, lambda e: e.tensor_copy(nhi[0:1, :], negrow[0:1, :]))
            op(VE, lambda e: e.tensor_tensor(negrow[0:1, :], negrow[0:1, :], nhi[0:1, :], OP.subtract))
            v_nl = op(VE, lambda e: e.tensor_copy(nlo[0:1, :], negrow[0:1, :]), VV)
            wt(TE, VV, v_nl)
            g_eq.acquire(SC)
            for mc in range(2):
                for t in range(T4):
                    ps, g = psq[(mc * T4 + t) % 2]
                    g.acquire(TE)
                    op(TE, lambda e, ps=ps, hp=hp, mc=mc, t=t, hc=hc: e.matmul(
                        ps[:, :], projs[hp:hp + 64, mc * 128:(mc + 1) * 128],
                        a2[hp:hp + 64, hc * 2048 + t * 512:hc * 2048 + (t + 1) * 512],
                        start=True, stop=False))
                    op(TE, lambda e, ps=ps, t=t: e.matmul(
                        ps[:, :], ones1b[0:1, :], nhi[0:1, t * 512:(t + 1) * 512],
                        start=False, stop=False))
                    tm_q = op(TE, lambda e, ps=ps, t=t: e.matmul(
                        ps[:, :], ones1b[0:1, :], nlo[0:1, t * 512:(t + 1) * 512],
                        start=False, stop=True), TT)
                    if mc == 1 and t == T4 - 1:
                        g_negrow.release(TT, tm_q)
                    wt(SC, TT, tm_q)
                    s_eq = op(SC, lambda e, ps=ps, mc=mc, t=t: e.activation(
                        eq[:, mc * 2048 + t * 512:mc * 2048 + (t + 1) * 512],
                        ps[:, :], AF.Exp), SS)
                    g.release(SS, s_eq)
            if h == 0:
                wt(TE, SS, ct_last)
            wt(TE, SS, s_eq)
            for t in range(T4):
                psA, gA = psA_m4[t]
                gA.acquire(TE)
                op(TE, lambda e, psA=psA, h=h, t=t: e.matmul(
                    psA[0:65, :], ct_all[:, (2 * h) * 65:(2 * h) * 65 + 65],
                    eq[:, t * 512:(t + 1) * 512], start=True, stop=False))
                op(TE, lambda e, psA=psA, h=h, t=t: e.matmul(
                    psA[0:65, :], ct_all[:, (2 * h + 1) * 65:(2 * h + 1) * 65 + 65],
                    eq[:, 2048 + t * 512:2048 + (t + 1) * 512], start=False, stop=False))
                tm_a = op(TE, lambda e, psA=psA, h=h: e.matmul(
                    psA[0:65, :], csr[0:1, h * 65:(h + 1) * 65], ones512b[0:1, :],
                    start=False, stop=True), TT)
                if t == T4 - 1:
                    g_eq.release(TT, tm_a)
                g_idr[t % 2].acquire(VE)
                wt(VE, TT, tm_a)
                v_id = op(VE, lambda e, psA=psA, t=t: e.reciprocal(
                    idr[0:1, (t % 2) * 512:(t % 2) * 512 + 512], psA[64:65, :]), VV)
                gPF.acquire(TE)
                wt(TE, VV, v_id)
                t_pi = op(TE, lambda e, t=t: e.matmul(
                    PF[0:64, :], ones64f[0:1, :], idr[0:1, (t % 2) * 512:(t % 2) * 512 + 512],
                    start=True, stop=True), TT)
                g_idr[t % 2].release(TT, t_pi)
                g_aqs[t % 2].acquire(SC)
                wt(SC, TT, tm_a)
                wt(SC, VV, v_id)
                s_aq = op(SC, lambda e, psA=psA, t=t: e.activation(
                    aqs[0:64, (t % 2) * 512:(t % 2) * 512 + 512], psA[0:64, :], AF.Copy), SS)
                gA.release(SS, s_aq)
                wt(VE, SS, s_aq)
                wt(VE, TT, t_pi)
                v_at = op(VE, lambda e, hp=hp, hc=hc, t=t: e.tensor_tensor(
                    a4[hp:hp + 64, hc * 2048 + t * 512:hc * 2048 + (t + 1) * 512],
                    aqs[0:64, (t % 2) * 512:(t % 2) * 512 + 512], PF[0:64, :], OP.mult), VV)
                gPF.release(VV, v_at)
                g_aqs[t % 2].release(VV, v_at)

    barrier()

    wt(GP, CC[5], 1)
    g_ah = dma(GP, S_CTL, cells[0:1, 8:9], arh_out.ap())
    wt(SC, S_CTL, g_ah)
    s_sigd = op(SC, lambda e: e.activation(cells[0:1, 9:10], cells[0:1, 13:14], AF.Sigmoid), SS)
    wt(VE, SS, s_sigd)
    ve_step(lambda e: e.tensor_tensor(cells[0:1, 10:11], cells[0:1, 9:10], cells[0:1, 8:9], OP.mult))
    ve_step(lambda e: e.tensor_scalar(cells[0:1, 10:11], cells[0:1, 10:11], 1.0 / 127.0, None, OP.mult))
    v_sh = ve_step(lambda e: e.reciprocal(cells[0:1, 11:12], cells[0:1, 10:11]))
    repl128(cells[0:1, 11:12], pp[:, 2:3])
    v_shp = repl128(cells[0:1, 10:11], pp[:, 3:4])

    def hq_dst(gc):
        if gc < 8:
            return a1[:, gc * 2048:(gc + 1) * 2048]
        if gc < 16:
            return a2[:, (gc - 8) * 2048:(gc - 7) * 2048]
        return a3[:, (gc - 16) * 2048:(gc - 15) * 2048]

    h_slot = [None, None]
    g_i32s5 = Guard()
    for gc in range(ICH):
        sl = (gc % 2) * 2048
        if h_slot[gc % 2] is not None:
            wt(SY, VV, h_slot[gc % 2])
        lh = dma(SY, S_X[gc % 2], hst[:, sl:sl + 2048], hh[gc, :, :])
        wt(VE, S_X[gc % 2], lh)
        op(VE, lambda e, sl=sl: e.tensor_scalar(
            hst[:, sl:sl + 2048], hst[:, sl:sl + 2048], pp[:, 2:3], QP, OP.mult, OP.min))
        g_i32s5.acquire(VE)
        v_hi = op(VE, lambda e, sl=sl: e.tensor_scalar(
            i32s5[:, :], hst[:, sl:sl + 2048], QN, None, OP.max), VV)
        h_slot[gc % 2] = v_hi
        wt(SC, VV, v_hi)
        s_hq = op(SC, lambda e, gc=gc: e.activation(hq_dst(gc), i32s5[:, :], AF.Copy), SS)
        g_i32s5.release(SS, s_hq)

    barrier()

    agw_d = agw_out.ap()[:, 2 * NW_G:NW].rearrange("r (p c) -> r p c", p=128)
    psW_m = {0: (PA, gPA), 1: (PB, gPB)}
    psD_m = {0: (PC, gPC), 1: (PD, gPD)}
    g_wobuf = Guard()
    g_dbuf = Guard()
    g_xst = [Guard(), Guard()]
    g_tmp = [Guard(), Guard()]
    g_yst = [Guard(), Guard()]
    for eo2 in range(2):
        g_wobuf.acquire(GP)
        for ei in range(ECH):
            lwo = dma(GP, S_WO, wobuf[:, ei * 512:(ei + 1) * 512],
                      woT_h[ei * 128:(ei + 1) * 128, eo2 * 512:(eo2 + 1) * 512])
        g_dbuf.acquire(SY)
        for ic in range(ICH):
            src = agw_d[eo2 * 4:(eo2 + 1) * 4, :, ic * 128:(ic + 1) * 128] \
                .rearrange("r p c -> p r c")
            ldb = dma(SY, S_DB, dbuf[:, ic * 512:(ic + 1) * 512]
                      .rearrange("p (r c) -> p r c", r=4), src)
        wt(TE, S_DB, ldb)
        wt(TE, S_WO, lwo)
        t_last = None
        for t in range(TCH):
            psW, gW = psW_m[t % 2]
            psD, gD = psD_m[t % 2]
            g_xst[t % 2].acquire(SY)
            lx = dma(SY, S_X[t % 2], xst[:, (t % 2) * 512:(t % 2) * 512 + 512],
                     x_h[t * 128:(t + 1) * 128, eo2 * 512:(eo2 + 1) * 512])
            gD.acquire(TE)
            for ic in range(ICH):
                op(TE, lambda e, psD=psD, ic=ic, t=t: e.matmul(
                    psD[:, :], hq_dst(ic)[:, t * 128:(t + 1) * 128],
                    dbuf[:, ic * 512:(ic + 1) * 512],
                    start=(ic == 0), stop=(ic == ICH - 1)))
            gW.acquire(TE)
            for ei in range(ECH):
                op(TE, lambda e, psW=psW, ei=ei, t=t: e.matmul(
                    psW[:, :], a4[:, ei * 2048 + t * 128:ei * 2048 + (t + 1) * 128],
                    wobuf[:, ei * 512:(ei + 1) * 512],
                    start=(ei == 0), stop=False))
            t_last = op(TE, lambda e, psW=psW, eo2=eo2: e.matmul(
                psW[:, :], ones1b[0:1, :], borb[0:1, eo2 * 512:(eo2 + 1) * 512],
                start=False, stop=True), TT)
            g_tmp[t % 2].acquire(SC)
            wt(SC, TT, t_last)
            s_d = op(SC, lambda e, psD=psD, t=t: e.activation(
                tmpst[:, (t % 2) * 512:(t % 2) * 512 + 512], psD[:, :],
                AF.Copy, scale=pp[:, 3:4]), SS)
            gD.release(SS, s_d)
            g_yst[t % 2].acquire(VE)
            wt(VE, SS, s_d)
            op(VE, lambda e, psW=psW, t=t: e.tensor_tensor(
                yst[:, (t % 2) * 512:(t % 2) * 512 + 512],
                psW[:, :], tmpst[:, (t % 2) * 512:(t % 2) * 512 + 512], OP.add))
            wt(VE, S_X[t % 2], lx)
            v_y = op(VE, lambda e, t=t: e.tensor_tensor(
                yst[:, (t % 2) * 512:(t % 2) * 512 + 512],
                yst[:, (t % 2) * 512:(t % 2) * 512 + 512],
                xst[:, (t % 2) * 512:(t % 2) * 512 + 512], OP.add), VV)
            gW.release(VV, v_y)
            g_tmp[t % 2].release(VV, v_y)
            g_xst[t % 2].release(VV, v_y)
            wt(SY, VV, v_y)
            o_d = dma(SY, S_O[t % 2], y_h[t * 128:(t + 1) * 128, eo2 * 512:(eo2 + 1) * 512],
                      yst[:, (t % 2) * 512:(t % 2) * 512 + 512])
            g_yst[t % 2].release(S_O[t % 2], o_d)
        g_wobuf.release(TT, t_last)
        g_dbuf.release(TT, t_last)
    wt(SY, S_O[0], S_O[0].v)
    wt(SY, S_O[1], S_O[1].v)
    wt(GP, S_O[0], S_O[0].v)
    wt(GP, S_O[1], S_O[1].v)
    wt(GP, S_CTL, S_CTL.v)

    with es, nc.Block() as block:
        for name, dec in ((SY, block.sync), (GP, block.gpsimd), (VE, block.vector),
                          (SC, block.scalar), (TE, block.tensor)):
            def runner(e, name=name):
                for eng, fn in PROG:
                    if eng == name:
                        fn(e)
            dec(runner)
    return nc


def kernel(**inputs):
    import ml_dtypes
    x = np.asarray(inputs["L0_attn_x"], dtype=np.float32)
    wq = np.asarray(inputs["L0_attn_Wq"], dtype=np.float32)
    wk = np.asarray(inputs["L0_attn_Wk"], dtype=np.float32)
    wv = np.asarray(inputs["L0_attn_Wv"], dtype=np.float32)
    wo = np.asarray(inputs["L0_attn_Wo"], dtype=np.float32)
    bq = np.asarray(inputs["L0_attn_bq"], dtype=np.float32)
    bk = np.asarray(inputs["L0_attn_bk"], dtype=np.float32)
    bv = np.asarray(inputs["L0_attn_bv"], dtype=np.float32)
    bo = np.asarray(inputs["L0_attn_bo"], dtype=np.float32)
    proj = np.asarray(inputs["L0_attn_proj"], dtype=np.float32)
    clipx = np.asarray(inputs["L0_ffn_clip_x"], dtype=np.float32)
    clipd = np.asarray(inputs["L0_ffn_clip_down"], dtype=np.float32)
    wgate = np.asarray(inputs["L0_ffn_w_gate"], dtype=np.float32)
    wup = np.asarray(inputs["L0_ffn_w_up"], dtype=np.float32)
    wdown = np.asarray(inputs["L0_ffn_w_down"], dtype=np.float32)

    shared = {
        "wqT": np.ascontiguousarray(wq.T), "wkT": np.ascontiguousarray(wk.T),
        "wvT": np.ascontiguousarray(wv.T), "woT": np.ascontiguousarray(wo.T),
        "bqr": bq.reshape(1, E).copy(), "bkr": bk.reshape(1, E).copy(),
        "bvr": bv.reshape(1, E).copy(), "bor": bo.reshape(1, E).copy(),
        "projT": np.ascontiguousarray(proj.T),
        "clipx": clipx.reshape(1, 1).copy(), "clipd": clipd.reshape(1, 1).copy(),
        "idbf": np.eye(128, dtype=ml_dtypes.bfloat16),
        "idf": np.eye(128, dtype=np.float32),
    }
    in_maps = []
    for c in range(NC):
        b_i, hf = c // 2, c % 2
        m = dict(shared)
        m["x"] = np.ascontiguousarray(x[b_i, hf * TOK:(hf + 1) * TOK, :])
        lo = 3 * c * 128
        for nm, w in (("wg_nat", wgate), ("wu_nat", wup)):
            sl = np.ones((ISLOT * 128, E), np.float32)
            real = w[lo:lo + ISLOT * 128, :]
            sl[:real.shape[0]] = real
            m[nm] = sl
        m["wd_nat"] = np.ascontiguousarray(wdown[c * 128:(c + 1) * 128, :])
        in_maps.append(m)

    nc = build_nc()
    res = run_bass_kernel_spmd(nc, in_maps, core_ids=list(range(NC)))
    out = np.empty((B, NSEQ, E), np.float32)
    for c in range(NC):
        b_i, hf = c // 2, c % 2
        out[b_i, hf * TOK:(hf + 1) * TOK, :] = res.results[c]["y"]
    return out


# version 08
# speedup vs baseline: 1.0727x; 1.0624x over previous
import numpy as np
import contextlib
import concourse.bass as bass
import concourse.mybir as mybir
from concourse.alu_op_type import AluOpType as OP
from concourse.bass_utils import run_bass_kernel_spmd

F32 = mybir.dt.float32
BF16 = mybir.dt.bfloat16
I32 = mybir.dt.int32
AF = mybir.ActivationFunctionType
AX = mybir.AxisListType.X

B, NSEQ, E = 4, 4096, 1024
TOK = 2048
TCH = 16
T4 = 4
ECH = 8
H, D, M = 16, 64, 256
IFF, ICH = 2816, 22
ISLOT = 3
NC = 8
RATIO = 0.0625
EPS = 1e-4
SQD = float(64.0 ** -0.25)
LN_EPS = 1e-5
QP = 127.49995
QN = -128.49995
NW_G = 1024 * ISLOT * 128
NW_D = 2816 * 128
NW = 2 * NW_G + NW_D

A1, A2, A3, A4 = 0, 32768, 65536, 98304
R0 = 131072
S0 = R0 + 61440
TOTAL = 212800

SY, GP, VE, SC, TE = "sync", "gpsimd", "vector", "scalar", "tensor"


def build_nc():
    nc = bass.Bass("TRN2", target_bir_lowering=False, detect_race_conditions=False)

    d_in = lambda n, s, dt=F32: nc.dram_tensor(n, list(s), dt, kind="ExternalInput").ap()
    x_h = d_in("x", (TOK, E))
    wqT_h = d_in("wqT", (E, E)); wkT_h = d_in("wkT", (E, E))
    wvT_h = d_in("wvT", (E, E)); woT_h = d_in("woT", (E, E))
    bqr_h = d_in("bqr", (1, E)); bkr_h = d_in("bkr", (1, E))
    bvr_h = d_in("bvr", (1, E)); bor_h = d_in("bor", (1, E))
    projT_h = d_in("projT", (D, M))
    clipx_h = d_in("clipx", (1, 1)); clipd_h = d_in("clipd", (1, 1))
    wg_h = d_in("wg_nat", (ISLOT * 128, E))
    wu_h = d_in("wu_nat", (ISLOT * 128, E))
    wd_h = d_in("wd_nat", (128, IFF))
    idbf_h = d_in("idbf", (128, 128), BF16)
    idf_h = d_in("idf", (128, 128), F32)
    y_h = nc.dram_tensor("y", [TOK, E], F32, kind="ExternalOutput").ap()


    xn32h = nc.dram_tensor("xn32h", [TOK, E], F32).ap()
    hh = nc.dram_tensor("hh", [ICH, 128, TOK], F32).ap()
    agw_in = nc.dram_tensor("agw_in", [1, NW], BF16)
    agw_out = nc.dram_tensor("agw_out", [NC, NW], BF16)
    ags_in = nc.dram_tensor("ags_in", [1, 768], F32)
    ags_out = nc.dram_tensor("ags_out", [NC, 768], F32)
    arx_in = nc.dram_tensor("arx_in", [1, 1], F32)
    arx_out = nc.dram_tensor("arx_out", [1, 1], F32)
    ars_in = nc.dram_tensor("ars_in", [1, 1], F32)
    ars_out = nc.dram_tensor("ars_out", [1, 1], F32)
    arh_in = nc.dram_tensor("arh_in", [1, 1], F32)
    arh_out = nc.dram_tensor("arh_out", [1, 1], F32)
    arp_in = nc.dram_tensor("arp_in", [129, 2080], F32)
    arp_out = nc.dram_tensor("arp_out", [129, 2080], F32)


    es = contextlib.ExitStack()
    big = es.enter_context(nc.sbuf_tensor("big", [128, TOTAL], mybir.dt.uint8))
    base = nc.lookup_mloc(big).addr

    def place(name, shape, dtype, off):
        per_part = int(np.prod(shape[1:])) * mybir.dt.size(dtype)
        assert off % 32 == 0 and off + per_part <= TOTAL, (name, off, per_part)
        h = nc.virtual_sbuf_tensor(name, list(shape), dtype)
        nc._tensor(h.name, h.shape, h.dtype, type="SB")
        mloc = nc.lookup_mloc(h)
        mloc.allocated = True
        mloc.addr = base + off
        return h

    a1 = place("a1", [128, 16384], BF16, A1)
    a2 = place("a2", [128, 16384], BF16, A2)
    a3 = place("a3", [128, 16384], BF16, A3)
    a4 = place("a4", [128, 16384], BF16, A4)

    msc = place("msc", [128, 4096], F32, R0)
    xbf = place("xbf", [128, 2048], BF16, R0 + 16384)
    wi32 = place("wi32", [128, 1024], I32, R0 + 20480)
    ttile = place("ttile", [128, 128], BF16, R0 + 24576)
    vw = place("vw", [128, 8192], BF16, R0)
    qkw = place("qkw", [128, 2048], BF16, R0 + 16384)
    ek = place("ek", [128, 4096], BF16, R0 + 20480)
    ksq = place("ksq", [128, 4096], BF16, R0 + 28672)
    xnr = place("xnr", [128, 2048], F32, R0)
    i32s = place("i32s", [128, 2048], I32, R0 + 8192)
    xbf2 = place("xbf2", [128, 1024], BF16, R0 + 16384)
    hst = place("hst", [128, 8192], F32, R0)
    gub = place("gub", [128, 8192], BF16, R0 + 32768)
    sgs = place("sgs", [128, 1024], F32, R0 + 49152)
    eq = place("eq", [128, 4096], BF16, R0 + 8192)
    aqs = place("aqs", [128, 1024], BF16, R0 + 40960)
    negrow = place("negrow", [1, 2048], F32, R0 + 43008)
    nhi = place("nhi", [1, 2048], BF16, R0 + 36864)
    nlo = place("nlo", [1, 2048], BF16, R0 + 51200)
    lnrow = place("lnrow", [65, 2048], F32, R0 + 16384)
    densb = place("densb", [65, 2048], F32, R0 + 51200)
    ibf = place("ibf", [65, 2048], BF16, R0 + 24576)
    i32s5 = place("i32s5", [128, 2048], I32, R0 + 16384)
    dbuf = place("dbuf", [128, 11264], BF16, R0)
    wobuf = place("wobuf", [128, 4096], BF16, R0 + 22528)
    xst = place("xst", [128, 1024], F32, R0 + 30720)
    tmpst = place("tmpst", [128, 1024], F32, R0 + 34816)
    yst = place("yst", [128, 1024], F32, R0 + 38912)
    ust = place("ust", [128, 130], F32, R0 + 36864)
    rm_all = place("rm_all", [128, 256], F32, R0 + 37440)
    upst = place("upst", [128, 130], F32, R0 + 0)
    ct_all = place("ct_all", [128, 2080], BF16, R0 + 1024)

    off = [S0]
    def sp(name, shape, dtype):
        per_part = int(np.prod(shape[1:])) * mybir.dt.size(dtype)
        h = place(name, shape, dtype, off[0])
        off[0] += (per_part + 63) // 64 * 64
        return h
    projs = sp("projs", [128, 256], BF16)
    idbf = sp("idbf_s", [128, 128], BF16)
    idf = sp("idf_s", [128, 128], F32)
    bqrb = sp("bqrb", [1, 1024], BF16)
    bkrb = sp("bkrb", [1, 1024], BF16)
    stats = sp("stats", [128, 96], F32)
    acx = sp("acx", [128, 16], F32)
    ach = sp("ach", [128, 88], F32)
    seffg = sp("seffg", [128, 22], F32)
    seffu = sp("seffu", [128, 22], F32)
    negb = sp("negb", [128, 16], F32)
    stq = sp("stq", [128, 16], F32)
    dcols = sp("dcols", [128, 16], F32)
    negtr = sp("negtr", [16, 128], F32)
    pp = sp("pp", [128, 16], F32)
    dencol = sp("dencol", [128, 16], F32)
    rme = sp("rme", [128, 16], F32)
    ones1b = sp("ones1b", [1, 128], BF16)
    onescb = sp("onescb", [128, 1], BF16)
    onesn16 = sp("onesn16", [128, 1], BF16)
    onesepsf = sp("onesepsf", [1, 128], F32)
    ones64f = sp("ones64f", [1, 64], F32)
    ones512b = sp("ones512b", [1, 512], BF16)
    ones65 = sp("ones65", [65, 128], BF16)
    csr = sp("csr", [1, 1040], BF16)
    rcell = sp("rcell", [1, 16], BF16)
    vss = sp("vss", [1, 80], F32)
    vsr = sp("vsr", [1, 80], F32)
    rowx = sp("rowx", [1, 128], F32)
    cells = sp("cells", [1, 32], F32)
    epscol = sp("epscol", [128, 1], F32)
    bvrb = sp("bvrb", [1, 1024], BF16)
    borb = sp("borb", [1, 1024], BF16)
    prf32 = place("prf32", [128, 256], F32, R0 + 49152)
    assert off[0] <= TOTAL, off[0]

    PA = es.enter_context(nc.psum_tensor("PA", [128, 512], F32))
    PB = es.enter_context(nc.psum_tensor("PB", [128, 512], F32))
    PC = es.enter_context(nc.psum_tensor("PC", [128, 512], F32))
    PD = es.enter_context(nc.psum_tensor("PD", [128, 512], F32))
    PE = es.enter_context(nc.psum_tensor("PE", [128, 512], F32))
    PF = es.enter_context(nc.psum_tensor("PF", [128, 512], F32))
    PG = es.enter_context(nc.psum_tensor("PG", [128, 512], F32))
    PT = es.enter_context(nc.psum_tensor("PT", [128, 1024], BF16))

    class Sem:
        def __init__(self, name):
            self.h = es.enter_context(nc.semaphore(name))
            self.v = 0
    LX, LG, VV, SS, TT = (Sem(n) for n in ("LX", "LG", "VV", "SS", "TT"))
    CC = [Sem(f"CC{i}") for i in range(6)]
    S_X = [Sem("SX0"), Sem("SX1")]
    S_SP = [Sem("SSP0"), Sem("SSP1"), Sem("SSP2"), Sem("SSP3")]
    S_G = [Sem("SG0"), Sem("SG1"), Sem("SG2"), Sem("SG3")]
    S_QK = [Sem("SQK0"), Sem("SQK1")]
    S_O = [Sem("SO0"), Sem("SO1")]
    S_DB = Sem("SDB")
    S_W = Sem("SW")
    S_VW = Sem("SVW")
    S_WO = Sem("SWO")
    S_AGW = Sem("SAGW")
    S_AGS = Sem("SAGS")
    S_ARP = Sem("SARP")
    S_CTL = Sem("SCTL")
    S_SEF = Sem("SSEF")
    BR = Sem("BR")
    SY_SEMS = [LX, S_X[0], S_X[1], S_SP[0], S_SP[1], S_SP[2], S_SP[3],
               S_G[0], S_G[1], S_G[2], S_G[3], S_O[0], S_O[1], S_DB, S_W]
    GP_SEMS = [LG, S_QK[0], S_QK[1], S_VW, S_WO, S_AGW, S_AGS, S_ARP, S_CTL, S_SEF]

    PROG = []
    def em(eng, fn):
        PROG.append((eng, fn))
    def wt(eng, sem, val):
        if val and val > 0:
            em(eng, lambda e, s=sem.h, v=val: e.wait_ge(s, v))
    def dma(eng, sem, out, in_):
        sem.v += 16
        em(eng, lambda e, o=out, i=in_, s=sem.h: e.dma_start(out=o, in_=i).then_inc(s, 16))
        return sem.v
    def op(eng, fn, sem=None):
        if sem is None:
            em(eng, fn)
            return None
        sem.v += 1
        em(eng, lambda e, f=fn, s=sem.h: f(e).then_inc(s, 1))
        return sem.v

    def ve_step(fn):
        v = op(VE, fn, VV)
        wt(VE, VV, v)
        return v

    class Guard:
        def __init__(self):
            self.conds = []
        def acquire(self, eng):
            for s, v in self.conds:
                wt(eng, s, v)
            self.conds = []
        def release(self, sem, val):
            if val:
                self.conds.append((sem, val))
    gPA, gPB, gPC, gPD, gPE, gPF, gPG, gPT = (Guard() for _ in range(8))

    br_round = [0]

    def barrier():
        for s in SY_SEMS:
            wt(SY, s, s.v)
        em(SY, lambda e: e.sem_inc(BR.h, 1)); BR.v += 1
        for s in GP_SEMS:
            wt(GP, s, s.v)
        em(GP, lambda e: e.sem_inc(BR.h, 1)); BR.v += 1
        br_round[0] = BR.v
        vals = [(BR, BR.v), (VV, VV.v), (SS, SS.v), (TT, TT.v)]
        for eng in (SY, GP, VE, SC, TE):
            for s, v in vals:
                wt(eng, s, v)

    op(VE, lambda e: e.memset(ones1b[:, :], 1.0))
    op(VE, lambda e: e.memset(onescb[:, :], 1.0))
    op(VE, lambda e: e.memset(onesn16[:, :], -0.0625))
    op(VE, lambda e: e.memset(onesepsf[:, :], RATIO * EPS))
    op(VE, lambda e: e.memset(ones64f[:, :], 1.0))
    op(VE, lambda e: e.memset(ones512b[:, :], 1.0))
    op(VE, lambda e: e.memset(ones65[:, :], 1.0))
    op(VE, lambda e: e.memset(vss[0:1, 64:65], float(TOK)))
    op(VE, lambda e: e.memset(epscol[:, :], LN_EPS), VV)
    dma(SY, LX, idbf[:, :], idbf_h)
    dma(SY, LX, idf[:, :], idf_h)
    dma(SY, LX, msc[0:1, 2048:3072], bqr_h)
    dma(SY, LX, msc[0:1, 3072:4096], bkr_h)
    dma(SY, LX, prf32[0:64, :], projT_h)
    dma(SY, LX, prf32[64:128, :], projT_h)
    dma(SY, LX, cells[0:1, 12:13], clipx_h)
    dma(SY, LX, cells[0:1, 13:14], clipd_h)
    dma(SY, LX, msc[0:1, 0:1024], bvr_h)
    l_pre = dma(SY, LX, msc[0:1, 1024:2048], bor_h)
    wt(SC, LX, l_pre)
    op(SC, lambda e: e.activation(projs[0:64, :], prf32[0:64, :], AF.Copy, scale=SQD))
    op(SC, lambda e: e.activation(projs[64:128, :], prf32[64:128, :], AF.Copy, scale=SQD))
    op(SC, lambda e: e.activation(bvrb[:, :], msc[0:1, 0:1024], AF.Copy))
    op(SC, lambda e: e.activation(borb[:, :], msc[0:1, 1024:2048], AF.Copy))
    op(SC, lambda e: e.activation(bqrb[:, :], msc[0:1, 2048:3072], AF.Copy))
    s_pre = op(SC, lambda e: e.activation(bkrb[:, :], msc[0:1, 3072:4096], AF.Copy), SS)

    slot_free = [None, None]
    for c in range(TCH):
        sl = (c % 2) * 1024
        if slot_free[c % 2] is not None:
            vv_f, ss_f = slot_free[c % 2]
            wt(SY, VV, vv_f); wt(SY, SS, ss_f)
        if c == 0:
            wt(SY, SS, s_pre)
        lx = dma(SY, S_X[c % 2], msc[:, sl:sl + 1024], x_h[c * 128:(c + 1) * 128, :])
        wt(VE, S_X[c % 2], lx)
        v_r = op(VE, lambda e, sl=sl, c=c: e.reduce_sum(
            stats[:, c:c + 1], msc[:, sl:sl + 1024], AX), VV)
        wt(SC, S_X[c % 2], lx)
        s_r = op(SC, lambda e, sl=sl, c=c: e.activation(
            xbf[:, sl:sl + 1024], msc[:, sl:sl + 1024], AF.Square,
            accum_out=stats[:, 16 + c:17 + c]), SS)
        slot_free[c % 2] = (v_r, s_r)
    wt(VE, SS, SS.v)
    wt(VE, VV, VV.v)
    ve_step(lambda e: e.tensor_scalar(stats[:, 32:48], stats[:, 0:16], 1.0 / E, None, OP.mult))
    ve_step(lambda e: e.tensor_scalar(stats[:, 48:64], stats[:, 16:32], 1.0 / E, None, OP.mult))
    ve_step(lambda e: e.tensor_tensor(stats[:, 64:80], stats[:, 32:48], stats[:, 32:48], OP.mult))
    v_var = ve_step(lambda e: e.tensor_tensor(stats[:, 64:80], stats[:, 48:64], stats[:, 64:80], OP.subtract))
    wt(SC, VV, v_var)
    s_sd = op(SC, lambda e: e.activation(stats[:, 80:96], stats[:, 64:80], AF.Sqrt, bias=epscol[:, 0:1]), SS)
    wt(VE, SS, s_sd)
    v_rstd = ve_step(lambda e: e.reciprocal(stats[:, 80:96], stats[:, 80:96]))

    xslot_free = [None, None]
    nslot_free = [None, None]
    g_xbf = [Guard(), Guard()]
    wt(SY, S_X[0], S_X[0].v)
    wt(SY, S_X[1], S_X[1].v)
    for c in range(TCH):
        slx = 2048 + (c % 2) * 1024
        sln = (c % 2) * 1024
        if xslot_free[c % 2] is not None:
            wt(SY, VV, xslot_free[c % 2])
        lx = dma(SY, S_X[c % 2], msc[:, slx:slx + 1024], x_h[c * 128:(c + 1) * 128, :])
        wt(VE, S_X[c % 2], lx)
        if c == 0:
            wt(VE, VV, v_rstd)
        if nslot_free[c % 2] is not None:
            lx_sp, ss_ca = nslot_free[c % 2]
            wt(VE, S_SP[c % 2], lx_sp); wt(VE, SS, ss_ca)
        v_n = op(VE, lambda e, slx=slx, sln=sln, c=c: e.tensor_scalar(
            msc[:, sln:sln + 1024], msc[:, slx:slx + 1024],
            stats[:, 32 + c:33 + c], stats[:, 80 + c:81 + c],
            OP.subtract, OP.mult), VV)
        xslot_free[c % 2] = v_n
        v_am = op(VE, lambda e, sln=sln, c=c: e.tensor_reduce(
            acx[:, c:c + 1], msc[:, sln:sln + 1024], AX, OP.max,
            apply_absolute_value=True), VV)
        wt(SY, VV, v_am)
        lx_sp = dma(SY, S_SP[c % 2], xn32h[c * 128:(c + 1) * 128, :], msc[:, sln:sln + 1024])
        wt(SC, VV, v_am)
        g_xbf[c % 2].acquire(SC)
        s_ca = op(SC, lambda e, sln=sln: e.activation(
            xbf[:, sln:sln + 1024], msc[:, sln:sln + 1024], AF.Copy), SS)
        nslot_free[c % 2] = (lx_sp, s_ca)
        gPT.acquire(TE)
        wt(TE, SS, s_ca)
        for j in range(8):
            t_tp = op(TE, lambda e, sln=sln, j=j: e.transpose(
                PT[:, j * 128:(j + 1) * 128], xbf[:, sln + j * 128:sln + (j + 1) * 128],
                idbf[:, :]), TT if j == 7 else None)
        g_xbf[c % 2].release(TT, t_tp)
        wt(VE, TT, t_tp)
        dst = a1[:, :].rearrange("p (g t) -> p g t", g=8)[:, :, c * 128:(c + 1) * 128]
        v_ev = op(VE, lambda e, dst=dst: e.tensor_copy(
            dst, PT[:, :].rearrange("p (g t) -> p g t", g=8)), VV)
        gPT.release(VV, v_ev)
    wt(VE, VV, VV.v)
    v_ax = op(VE, lambda e: e.tensor_reduce(pp[:, 15:16], acx[:, :], AX, OP.max), VV)
    wt(GP, VV, v_ax)
    g1 = dma(GP, S_CTL, rowx[0:1, :], pp[:, 15:16])
    wt(VE, S_CTL, g1)
    v_axc = op(VE, lambda e: e.reduce_max(cells[0:1, 0:1], rowx[0:1, :], AX), VV)
    wt(GP, VV, v_axc)
    dma(GP, S_CTL, arx_in.ap(), cells[0:1, 0:1])
    op(GP, lambda e: e.collective_compute(
        "AllReduce", OP.max, replica_groups=[list(range(NC))],
        ins=[arx_in.ap().opt()], outs=[arx_out.ap().opt()]), CC[0])

    barrier()

    g_region = agw_in.ap()[0:1, 0:NW_G].rearrange("a (p c) -> (a p) c", p=128)
    u_region = agw_in.ap()[0:1, NW_G:2 * NW_G].rearrange("a (p c) -> (a p) c", p=128)
    d_region = agw_in.ap()[0:1, 2 * NW_G:NW].rearrange("a (p c) -> (a p) c", p=128)
    last_agw = 0
    g_ttile = Guard()
    g_stats = Guard()
    g_wi32 = Guard()
    g_xbfq = Guard()

    def quant_block(w_slice_ap, tile_dst, is_down, s_col):
        nonlocal last_agw
        width = w_slice_ap.shape[1]
        g_stats.acquire(VE)
        v_amx = op(VE, lambda e: e.tensor_reduce(stats[:, 0:1], w_slice_ap, AX, OP.max,
                                                 apply_absolute_value=True), VV)
        wt(VE, VV, v_amx)
        ve_step(lambda e: e.tensor_scalar(stats[:, 1:2], stats[:, 0:1], 1.0 / 127.0,
                                          None, OP.mult))
        ve_step(lambda e: e.reciprocal(stats[:, 2:3], stats[:, 0:1]))
        v_is = ve_step(lambda e: e.tensor_scalar(stats[:, 2:3], stats[:, 2:3], 127.0,
                                                 None, OP.mult))
        if s_col is not None:
            wt(GP, VV, v_is)
            l_ags = dma(GP, S_AGS, ags_in.ap()[0:1, s_col:s_col + 128], stats[:, 0:1])
            g_stats.release(S_AGS, l_ags)
        nblk = width // 128
        for bb in range(0, nblk, 8):
            hi = min(bb + 8, nblk)
            w = (hi - bb) * 128
            g_wi32.acquire(VE)
            v_q = op(VE, lambda e, bb=bb, w=w: e.tensor_scalar(
                wi32[:, 0:w], w_slice_ap[:, bb * 128:bb * 128 + w],
                stats[:, 2:3], None, OP.mult), VV)
            wt(SC, VV, v_q)
            g_xbfq.acquire(SC)
            if is_down:
                s_q = op(SC, lambda e, w=w: e.activation(
                    xbf[:, 0:w], wi32[:, 0:w], AF.Copy, scale=stats[:, 1:2]), SS)
            else:
                s_q = op(SC, lambda e, w=w: e.activation(
                    xbf[:, 0:w], wi32[:, 0:w], AF.Copy), SS)
            g_wi32.release(SS, s_q)
            if is_down:
                g_stats.release(SS, s_q)
            wt(TE, SS, s_q)
            for j in range(hi - bb):
                gPT.acquire(TE)
                t_tp = op(TE, lambda e, j=j: e.transpose(
                    PT[:, 0:128], xbf[:, j * 128:(j + 1) * 128], idbf[:, :]), TT)
                g_ttile.acquire(VE)
                wt(VE, TT, t_tp)
                v_ev = op(VE, lambda e: e.tensor_copy(ttile[:, :], PT[:, 0:128]), VV)
                gPT.release(VV, v_ev)
                if j == hi - bb - 1:
                    g_xbfq.release(TT, t_tp)
                wt(GP, VV, v_ev)
                last_agw = dma(GP, S_AGW, tile_dst(bb + j), ttile[:, :])
                g_ttile.release(S_AGW, last_agw)

    mslot = Guard()
    for wsel, (w_h, region, soff) in enumerate(
            [(wg_h, g_region, 0), (wu_h, u_region, 384)]):
        for icl in range(ISLOT):
            mslot.acquire(SY)
            lw = dma(SY, S_W, msc[:, 0:1024], w_h[icl * 128:(icl + 1) * 128, :])
            wt(VE, S_W, lw)
            quant_block(msc[:, 0:1024],
                        lambda k, region=region, icl=icl:
                        region[:, k * 384 + icl * 128:k * 384 + (icl + 1) * 128],
                        False, soff + icl * 128)
            mslot.release(VV, VV.v)
    mslot.acquire(SY)
    lw = dma(SY, S_W, msc[:, 0:2816], wd_h)
    wt(VE, S_W, lw)
    quant_block(msc[:, 0:2816],
                lambda k: d_region[:, k * 128:(k + 1) * 128], True, None)
    wt(GP, S_AGW, last_agw)
    wt(GP, S_AGS, S_AGS.v)
    op(GP, lambda e: e.collective_compute(
        "AllGather", OP.bypass, replica_groups=[list(range(NC))],
        ins=[agw_in.ap().opt()], outs=[agw_out.ap().opt()]), CC[1])
    op(GP, lambda e: e.collective_compute(
        "AllGather", OP.bypass, replica_groups=[list(range(NC))],
        ins=[ags_in.ap().opt()], outs=[ags_out.ap().opt()]), CC[2])

    barrier()

    for ei in range(ECH):
        dma(GP, S_VW, vw[:, ei * 1024:(ei + 1) * 1024],
            wvT_h[ei * 128:(ei + 1) * 128, :])
    lg_vw = S_VW.v
    qk_slot = [Guard(), Guard()]
    psq = {0: (PA, gPA), 1: (PB, gPB)}
    for wsel in range(2):
        dst = a2 if wsel == 0 else a3
        brow = bqrb if wsel == 0 else bkrb
        for eo in range(ECH):
            idx = wsel * ECH + eo
            sl = (idx % 2) * 1024
            qk_slot[idx % 2].acquire(GP)
            src = (wqT_h if wsel == 0 else wkT_h)[:, eo * 128:(eo + 1) * 128] \
                .rearrange("(a p) c -> p a c", p=128)
            lqk = dma(GP, S_QK[idx % 2], qkw[:, sl:sl + 1024].rearrange("p (a c) -> p a c", a=8), src)
            wt(TE, S_QK[idx % 2], lqk)
            for t in range(T4):
                ps, g = psq[t % 2]
                g.acquire(TE)
                for ei in range(ECH):
                    op(TE, lambda e, ps=ps, sl=sl, ei=ei, t=t: e.matmul(
                        ps[:, :], qkw[:, sl + ei * 128:sl + (ei + 1) * 128],
                        a1[:, ei * 2048 + t * 512:ei * 2048 + (t + 1) * 512],
                        start=(ei == 0), stop=False))
                tm = op(TE, lambda e, ps=ps, brow=brow, eo=eo: e.matmul(
                    ps[:, :], brow[0:1, eo * 128:(eo + 1) * 128], ones512b[0:1, :],
                    start=False, stop=True), TT)
                if t == T4 - 1:
                    qk_slot[idx % 2].release(TT, tm)
                wt(SC, TT, tm)
                s_ev = op(SC, lambda e, ps=ps, dst=dst, eo=eo, t=t:
                          e.activation(dst[:, eo * 2048 + t * 512:eo * 2048 + (t + 1) * 512],
                                       ps[:, :], AF.Copy), SS)
                g.release(SS, s_ev)
    wt(TE, S_VW, lg_vw)
    for t in range(TCH):
        for eo2 in range(2):
            ps, g = psq[(t * 2 + eo2) % 2]
            g.acquire(TE)
            for ei in range(ECH):
                op(TE, lambda e, ps=ps, t=t, eo2=eo2, ei=ei: e.matmul(
                    ps[:, :], a1[:, ei * 2048 + t * 128:ei * 2048 + (t + 1) * 128],
                    vw[:, ei * 1024 + eo2 * 512:ei * 1024 + (eo2 + 1) * 512],
                    start=(ei == 0), stop=False))
            tm = op(TE, lambda e, ps=ps, eo2=eo2: e.matmul(
                ps[:, :], ones1b[0:1, :], bvrb[0:1, eo2 * 512:(eo2 + 1) * 512],
                start=False, stop=True), TT)
            wt(VE, TT, tm)
            v_ev = op(VE, lambda e, ps=ps, t=t, eo2=eo2: e.tensor_copy(
                a4[:, t * 1024 + eo2 * 512:t * 1024 + (eo2 + 1) * 512], ps[:, :]), VV)
            g.release(VV, v_ev)

    barrier()

    ksq_slot = [Guard(), Guard()]
    g_dcols = Guard()
    g_ek = Guard()
    v_z = op(VE, lambda e: e.memset(msc[0:1, 0:1040], 0.0), VV)
    wt(GP, VV, v_z)
    dma(GP, S_ARP, arp_in.ap()[128:129, 1040:2080], msc[0:1, 0:1040])
    for hc in range(ECH):
        ksq_slot[hc % 2].acquire(SC)
        s_ksq = op(SC, lambda e, hc=hc: e.activation(
            ksq[:, (hc % 2) * 2048:(hc % 2) * 2048 + 2048],
            a3[:, hc * 2048:(hc + 1) * 2048], AF.Square), SS)
        for hh_i in range(2):
            h = 2 * hc + hh_i
            hp = hh_i * 64
            gPC.acquire(TE)
            wt(TE, SS, s_ksq)
            for t in range(TCH):
                tm_d = op(TE, lambda e, hc=hc, hp=hp, t=t: e.matmul(
                    PC[:, t:t + 1],
                    ksq[hp:hp + 64, (hc % 2) * 2048 + t * 128:(hc % 2) * 2048 + (t + 1) * 128],
                    onesn16[hp:hp + 64, 0:1], start=True, stop=True),
                    TT if t == TCH - 1 else None)
            if hh_i == 1:
                ksq_slot[hc % 2].release(TT, tm_d)
            g_dcols.acquire(VE)
            wt(VE, TT, tm_d)
            v_d = op(VE, lambda e: e.tensor_copy(dcols[:, 0:16], PC[:, 0:16]), VV)
            gPC.release(VV, v_d)
            for t in range(TCH):
                ps, g = psq[t % 2]
                g.acquire(TE)
                tm_dd = op(TE, lambda e, ps=ps, hc=hc, hp=hp, t=t: e.matmul(
                    ps[:, 0:256],
                    a3[hp:hp + 64, hc * 2048 + t * 128:hc * 2048 + (t + 1) * 128],
                    projs[hp:hp + 64, :], start=True, stop=True), TT)
                wt(SC, TT, tm_dd)
                if t == 0:
                    wt(SC, VV, v_d)
                g_ek.acquire(SC) if t == 0 else None
                s_ek = op(SC, lambda e, ps=ps, t=t: e.activation(
                    ek[:, t * 256:(t + 1) * 256], ps[:, 0:256], AF.Exp,
                    bias=dcols[:, t:t + 1]), SS)
                g.release(SS, s_ek)
                wt(VE, SS, s_ek)
                v_rm = op(VE, lambda e, h=h, t=t: e.reduce_max(
                    rme[:, t:t + 1], ek[:, t * 256:(t + 1) * 256], AX), VV)
                wt(TE, SS, s_ek)
                if t == 0:
                    gPD.acquire(TE); gPE.acquire(TE); gPF.acquire(TE)
                    gPG.acquire(TE); gPC.acquire(TE)
                v_sl = a4[:, t * 1024 + h * 64:t * 1024 + (h + 1) * 64]
                op(TE, lambda e, t=t, v_sl=v_sl: e.matmul(
                    PD[:, 0:64], ek[:, t * 256:t * 256 + 128], v_sl,
                    start=(t == 0), stop=(t == TCH - 1)))
                op(TE, lambda e, t=t, v_sl=v_sl: e.matmul(
                    PE[:, 0:64], ek[:, t * 256 + 128:t * 256 + 256], v_sl,
                    start=(t == 0), stop=(t == TCH - 1)))
                op(TE, lambda e, t=t: e.matmul(
                    PF[:, 0:1], ek[:, t * 256:t * 256 + 128], onescb[:, 0:1],
                    start=(t == 0), stop=(t == TCH - 1)))
                op(TE, lambda e, t=t: e.matmul(
                    PG[:, 0:1], ek[:, t * 256 + 128:t * 256 + 256], onescb[:, 0:1],
                    start=(t == 0), stop=(t == TCH - 1)))
                tm_vs = op(TE, lambda e, t=t, v_sl=v_sl: e.matmul(
                    PC[0:1, 0:64], onescb[:, 0:1], v_sl,
                    start=(t == 0), stop=(t == TCH - 1)),
                    TT if t == TCH - 1 else None)
                if t == TCH - 1:
                    g_ek.release(TT, tm_vs)
                    g_ek.release(VV, v_rm)
            wt(SC, VV, v_rm)
            s_ln = op(SC, lambda e: e.activation(rme[:, 0:16], rme[:, 0:16], AF.Ln), SS)
            wt(VE, SS, s_ln)
            v_rma = op(VE, lambda e, h=h: e.tensor_tensor(
                rm_all[:, h * 16:h * 16 + 16], rme[:, 0:16], dcols[:, 0:16],
                OP.subtract), VV)
            g_dcols.release(VV, v_rma)
            wt(VE, TT, tm_vs)
            op(VE, lambda e: e.tensor_copy(ust[:, 0:64], PD[:, 0:64]))
            op(VE, lambda e: e.tensor_copy(ust[:, 64:65], PF[:, 0:1]))
            op(VE, lambda e: e.tensor_copy(ust[:, 65:129], PE[:, 0:64]))
            op(VE, lambda e: e.tensor_copy(ust[:, 129:130], PG[:, 0:1]))
            v_u = op(VE, lambda e: e.tensor_copy(vss[0:1, 0:64], PC[0:1, 0:64]), VV)
            gPD.release(VV, v_u); gPE.release(VV, v_u); gPF.release(VV, v_u)
            gPG.release(VV, v_u); gPC.release(VV, v_u)
            wt(GP, VV, v_u)
            w1 = dma(GP, S_ARP, arp_in.ap()[0:128, (2 * h) * 65:(2 * h + 2) * 65],
                     ust[:, 0:130])
            w2 = dma(GP, S_ARP, arp_in.ap()[128:129, h * 65:(h + 1) * 65],
                     vss[0:1, 0:65])
            wt(VE, S_ARP, w2)
    wt(VE, VV, VV.v)
    v_dk = op(VE, lambda e: e.reduce_max(pp[:, 14:15], rm_all[:, :], AX), VV)
    wt(GP, VV, v_dk)
    g2 = dma(GP, S_CTL, rowx[0:1, :], pp[:, 14:15])
    wt(VE, S_CTL, g2)
    v_dkc = op(VE, lambda e: e.reduce_max(cells[0:1, 5:6], rowx[0:1, :], AX), VV)
    wt(GP, VV, v_dkc)
    dma(GP, S_CTL, ars_in.ap(), cells[0:1, 5:6])
    wt(GP, S_ARP, S_ARP.v)
    op(GP, lambda e: e.collective_compute(
        "AllReduce", OP.add, replica_groups=[[0, 1], [2, 3], [4, 5], [6, 7]],
        ins=[arp_in.ap().opt()], outs=[arp_out.ap().opt()]), CC[3])
    op(GP, lambda e: e.collective_compute(
        "AllReduce", OP.max, replica_groups=[list(range(NC))],
        ins=[ars_in.ap().opt()], outs=[ars_out.ap().opt()]), CC[4])

    barrier()

    wt(GP, CC[0], 1)
    g_ax = dma(GP, S_CTL, cells[0:1, 0:1], arx_out.ap())
    wt(SC, S_CTL, g_ax)
    s_sig = op(SC, lambda e: e.activation(cells[0:1, 1:2], cells[0:1, 12:13], AF.Sigmoid), SS)
    wt(VE, SS, s_sig)
    ve_step(lambda e: e.tensor_tensor(cells[0:1, 2:3], cells[0:1, 1:2], cells[0:1, 0:1], OP.mult))
    ve_step(lambda e: e.tensor_scalar(cells[0:1, 2:3], cells[0:1, 2:3], 1.0 / 127.0, None, OP.mult))
    ve_step(lambda e: e.reciprocal(cells[0:1, 3:4], cells[0:1, 2:3]))
    v_sxf = ve_step(lambda e: e.tensor_scalar(cells[0:1, 4:5], cells[0:1, 2:3], 1.0 / 127.0, None, OP.mult))

    def repl128(cell_ap, dst_col):
        ve_step(lambda e: e.tensor_copy(rcell[0:1, 0:1], cell_ap))
        ve_step(lambda e: e.tensor_tensor(vsr[0:1, 70:71], cell_ap, rcell[0:1, 0:1], OP.subtract))
        v_r = ve_step(lambda e: e.tensor_copy(rcell[0:1, 1:2], vsr[0:1, 70:71]))
        gPG.acquire(TE)
        wt(TE, VV, v_r)
        op(TE, lambda e: e.matmul(PG[:, 0:1], ones1b[0:1, :], rcell[0:1, 0:1], start=True, stop=False))
        t_r = op(TE, lambda e: e.matmul(PG[:, 0:1], ones1b[0:1, :], rcell[0:1, 1:2], start=False, stop=True), TT)
        wt(VE, TT, t_r)
        v_f = op(VE, lambda e: e.tensor_copy(dst_col, PG[:, 0:1]), VV)
        wt(VE, VV, v_f)
        gPG.release(VV, v_f)
        return v_f

    wt(VE, VV, v_sxf)
    repl128(cells[0:1, 3:4], pp[:, 0:1])
    v_sxfp = repl128(cells[0:1, 4:5], pp[:, 1:2])

    xq_slot = [None, None]
    g_i32 = [Guard(), Guard()]
    for c in range(TCH):
        sl = (c % 2) * 1024
        if xq_slot[c % 2] is not None:
            wt(SY, VV, xq_slot[c % 2])
        lxn = dma(SY, S_X[c % 2], xnr[:, sl:sl + 1024], xn32h[c * 128:(c + 1) * 128, :])
        wt(VE, S_X[c % 2], lxn)
        op(VE, lambda e, sl=sl: e.tensor_scalar(
            xnr[:, sl:sl + 1024], xnr[:, sl:sl + 1024], pp[:, 0:1], QP, OP.mult, OP.min))
        g_i32[c % 2].acquire(VE)
        v_i = op(VE, lambda e, sl=sl: e.tensor_scalar(
            i32s[:, sl:sl + 1024], xnr[:, sl:sl + 1024], QN, None, OP.max), VV)
        xq_slot[c % 2] = v_i
        wt(SC, VV, v_i)
        s_b = op(SC, lambda e, sl=sl: e.activation(
            xbf2[:, 0:1024], i32s[:, sl:sl + 1024], AF.Copy), SS)
        g_i32[c % 2].release(SS, s_b)
        gPT.acquire(TE)
        wt(TE, SS, s_b)
        for j in range(8):
            t_l = op(TE, lambda e, j=j: e.transpose(
                PT[:, j * 128:(j + 1) * 128], xbf2[:, j * 128:(j + 1) * 128],
                idbf[:, :]), TT if j == 7 else None)
        wt(VE, TT, t_l)
        dst = a4[:, :].rearrange("p (g t) -> p g t", g=8)[:, :, c * 128:(c + 1) * 128]
        v_x = op(VE, lambda e, dst=dst: e.tensor_copy(
            dst, PT[:, :].rearrange("p (g t) -> p g t", g=8)), VV)
        gPT.release(VV, v_x)
        wt(SC, TT, t_l)
    wt(GP, CC[2], 1)
    for gc in range(ICH):
        r, icl = gc // 3, gc % 3
        dma(GP, S_SEF, seffg[:, gc:gc + 1],
            ags_out.ap()[r:r + 1, icl * 128:(icl + 1) * 128])
        g_s = dma(GP, S_SEF, seffu[:, gc:gc + 1],
                  ags_out.ap()[r:r + 1, 384 + icl * 128:384 + (icl + 1) * 128])
    wt(VE, S_SEF, g_s)
    wt(VE, VV, v_sxfp)
    op(VE, lambda e: e.tensor_scalar(seffg[:, :], seffg[:, :], pp[:, 1:2], None, OP.mult))
    op(VE, lambda e: e.tensor_scalar(seffu[:, :], seffu[:, :], pp[:, 1:2], None, OP.mult), VV)

    barrier()

    wt(TE, CC[1], 1)
    wt(SY, CC[1], 1)
    gub_slot = [Guard(), Guard(), Guard(), Guard()]
    hst_slot = [Guard(), Guard(), Guard(), Guard()]
    g_sgs = [Guard(), Guard()]
    psg_m = {0: (PA, gPA), 1: (PB, gPB)}
    psu_m = {0: (PC, gPC), 1: (PD, gPD)}
    def gu_load(gc):
        r, icl = gc // 3, gc % 3
        slg = (gc % 4) * 1024
        src_g = agw_out.ap()[r:r + 1, 0:NW_G] \
            .rearrange("a (p g c) -> (a p) g c", p=128, c=384)[:, :, icl * 128:(icl + 1) * 128]
        src_u = agw_out.ap()[r:r + 1, NW_G:2 * NW_G] \
            .rearrange("a (p g c) -> (a p) g c", p=128, c=384)[:, :, icl * 128:(icl + 1) * 128]
        gub_slot[gc % 4].acquire(SY)
        dma(SY, S_G[gc % 4], gub[:, slg:slg + 1024].rearrange("p (g c) -> p g c", g=8), src_g)
        return dma(SY, S_G[gc % 4], gub[:, 4096 + slg:4096 + slg + 1024]
                   .rearrange("p (g c) -> p g c", g=8), src_u)
    lgu_q = [gu_load(0), gu_load(1), gu_load(2)]
    for gc in range(ICH):
        slg = (gc % 4) * 1024
        sl_h = (gc % 4) * 2048
        if gc + 3 < ICH:
            lgu_q.append(gu_load(gc + 3))
        lgu = lgu_q[gc]
        hst_slot[gc % 4].acquire(VE)
        wt(TE, S_G[gc % 4], lgu)
        for t in range(T4):
            psg, gg = psg_m[t % 2]
            psu, gu = psu_m[t % 2]
            gg.acquire(TE)
            for hcc in range(ECH):
                op(TE, lambda e, psg=psg, slg=slg, hcc=hcc, t=t: e.matmul(
                    psg[:, :], gub[:, slg + hcc * 128:slg + (hcc + 1) * 128],
                    a4[:, hcc * 2048 + t * 512:hcc * 2048 + (t + 1) * 512],
                    start=(hcc == 0), stop=(hcc == ECH - 1)))
            gu.acquire(TE)
            for hcc in range(ECH):
                tm_u = op(TE, lambda e, psu=psu, slg=slg, hcc=hcc, t=t: e.matmul(
                    psu[:, :], gub[:, 4096 + slg + hcc * 128:4096 + slg + (hcc + 1) * 128],
                    a4[:, hcc * 2048 + t * 512:hcc * 2048 + (t + 1) * 512],
                    start=(hcc == 0), stop=(hcc == ECH - 1)),
                    TT if hcc == ECH - 1 else None)
            if t == T4 - 1:
                gub_slot[gc % 4].release(TT, tm_u)
            wt(SC, TT, tm_u)
            g_sgs[t % 2].acquire(SC)
            s_sg = op(SC, lambda e, psg=psg, t=t, gc=gc: e.activation(
                sgs[:, (t % 2) * 512:(t % 2) * 512 + 512], psg[:, :], AF.Silu,
                scale=seffg[:, gc:gc + 1]), SS)
            gg.release(SS, s_sg)
            wt(VE, SS, s_sg)
            op(VE, lambda e, psu=psu, t=t, gc=gc, sl_h=sl_h: e.scalar_tensor_tensor(
                hst[:, sl_h + t * 512:sl_h + (t + 1) * 512], psu[:, :],
                seffu[:, gc:gc + 1], sgs[:, (t % 2) * 512:(t % 2) * 512 + 512],
                OP.mult, OP.mult))
            v_h = op(VE, lambda e, t=t, gc=gc, sl_h=sl_h: e.tensor_reduce(
                ach[:, gc * 4 + t:gc * 4 + t + 1],
                hst[:, sl_h + t * 512:sl_h + (t + 1) * 512], AX, OP.max,
                apply_absolute_value=True), VV)
            gu.release(VV, v_h)
            g_sgs[t % 2].release(VV, v_h)
        wt(SY, VV, v_h)
        l_sp = dma(SY, S_SP[gc % 4], hh[gc, :, :], hst[:, sl_h:sl_h + 2048])
        hst_slot[gc % 4].release(S_SP[gc % 4], l_sp)
    wt(VE, VV, VV.v)
    v_ah = op(VE, lambda e: e.reduce_max(pp[:, 13:14], ach[:, :], AX), VV)
    wt(GP, VV, v_ah)
    g3 = dma(GP, S_CTL, rowx[0:1, :], pp[:, 13:14])
    wt(VE, S_CTL, g3)
    v_ahc = op(VE, lambda e: e.reduce_max(cells[0:1, 8:9], rowx[0:1, :], AX), VV)
    wt(GP, VV, v_ahc)
    dma(GP, S_CTL, arh_in.ap(), cells[0:1, 8:9])
    op(GP, lambda e: e.collective_compute(
        "AllReduce", OP.max, replica_groups=[list(range(NC))],
        ins=[arh_in.ap().opt()], outs=[arh_out.ap().opt()]), CC[5])

    barrier()

    wt(GP, CC[3], 1)
    wt(GP, CC[4], 1)
    g_es = dma(GP, S_CTL, cells[0:1, 6:7], ars_out.ap())
    wt(SC, S_CTL, g_es)
    s_es = op(SC, lambda e: e.activation(cells[0:1, 6:7], cells[0:1, 6:7], AF.Exp, scale=-1.0), SS)
    wt(VE, SS, s_es)
    v_res = ve_step(lambda e: e.tensor_scalar(cells[0:1, 7:8], cells[0:1, 6:7], RATIO, None, OP.mult))
    v_resp = repl128(cells[0:1, 7:8], pp[:, 4:5])

    g_upst = Guard()
    g_vsr = Guard()
    ct_last = None
    for h in range(H):
        g_upst.acquire(GP)
        g_u = dma(GP, S_CTL, upst[:, 0:130], arp_out.ap()[0:128, 2 * h * 65:(2 * h + 2) * 65])
        g_vsr.acquire(GP)
        g_v = dma(GP, S_CTL, vsr[0:1, 0:65], arp_out.ap()[128:129, h * 65:(h + 1) * 65])
        gPF.acquire(TE)
        wt(TE, S_CTL, g_v)
        t_rv = op(TE, lambda e: e.matmul(PF[:, 0:65], onesepsf[0:1, :], vsr[0:1, 0:65],
                                         start=True, stop=True), TT)
        g_vsr.release(TT, t_rv)
        wt(VE, TT, t_rv)
        wt(VE, S_CTL, g_v)
        if h == 0:
            wt(VE, VV, v_resp)
        op(VE, lambda e, h=h: e.scalar_tensor_tensor(
            ct_all[:, (2 * h) * 65:(2 * h) * 65 + 65], upst[:, 0:65], pp[:, 4:5],
            PF[:, 0:65], OP.mult, OP.add))
        v_ct = op(VE, lambda e, h=h: e.scalar_tensor_tensor(
            ct_all[:, (2 * h + 1) * 65:(2 * h + 1) * 65 + 65], upst[:, 65:130], pp[:, 4:5],
            PF[:, 0:65], OP.mult, OP.add), VV)
        g_upst.release(VV, v_ct)
        gPF.release(VV, v_ct)
        gPG.acquire(TE)
        wt(TE, VV, v_ct)
        op(TE, lambda e, h=h: e.matmul(PG[0:1, 0:65], onescb[:, 0:1],
                                       ct_all[:, (2 * h) * 65:(2 * h) * 65 + 65],
                                       start=True, stop=False))
        t_cs = op(TE, lambda e, h=h: e.matmul(PG[0:1, 0:65], onescb[:, 0:1],
                                              ct_all[:, (2 * h + 1) * 65:(2 * h + 1) * 65 + 65],
                                              start=False, stop=True), TT)
        wt(SC, TT, t_cs)
        ct_last = op(SC, lambda e, h=h: e.activation(csr[0:1, h * 65:(h + 1) * 65],
                                                     PG[0:1, 0:65], AF.Copy, scale=EPS), SS)
        gPG.release(SS, ct_last)

    qsq_slot = [Guard(), Guard()]
    g_stq = Guard()
    g_eq = Guard()
    g_dencol = Guard()
    g_densb = Guard()
    g_idrow = Guard()
    g_ibf = Guard()
    g_isb = [Guard(), Guard()]
    psA_m4 = {0: (PD, gPD), 1: (PE, gPE), 2: (PA, gPA), 3: (PB, gPB)}
    g_negrow = Guard()
    g_negtr = Guard()
    g_aqs = [Guard(), Guard()]
    g_idr = [Guard(), Guard()]
    psA_m = {0: (PD, gPD), 1: (PE, gPE)}
    for hc in range(ECH):
        qsq_slot[hc % 2].acquire(SC)
        s_qsq = op(SC, lambda e, hc=hc: e.activation(
            ksq[:, (hc % 2) * 2048:(hc % 2) * 2048 + 2048],
            a2[:, hc * 2048:(hc + 1) * 2048], AF.Square), SS)
        for hh_i in range(2):
            h = 2 * hc + hh_i
            hp = hh_i * 64
            gPC.acquire(TE)
            wt(TE, SS, s_qsq)
            for t in range(TCH):
                tm_d = op(TE, lambda e, hc=hc, hp=hp, t=t: e.matmul(
                    PC[:, t:t + 1],
                    ksq[hp:hp + 64, (hc % 2) * 2048 + t * 128:(hc % 2) * 2048 + (t + 1) * 128],
                    onesn16[hp:hp + 64, 0:1], start=True, stop=True),
                    TT if t == TCH - 1 else None)
            if hh_i == 1:
                qsq_slot[hc % 2].release(TT, tm_d)
            g_stq.acquire(VE)
            for t in range(TCH):
                ps, g = psq[t % 2]
                g.acquire(TE)
                tm_dd = op(TE, lambda e, ps=ps, hc=hc, hp=hp, t=t: e.matmul(
                    ps[:, 0:256],
                    a2[hp:hp + 64, hc * 2048 + t * 128:hc * 2048 + (t + 1) * 128],
                    projs[hp:hp + 64, :], start=True, stop=True), TT)
                wt(VE, TT, tm_dd)
                v_rm = op(VE, lambda e, ps=ps, t=t: e.reduce_max(
                    stq[:, t:t + 1], ps[:, 0:256], AX), VV)
                g.release(VV, v_rm)
            wt(VE, TT, tm_d)
            wt(VE, VV, VV.v)
            v_nb = op(VE, lambda e: e.scalar_tensor_tensor(
                negb[:, :], PC[:, 0:16], 1.0, stq[:, :], OP.mult, OP.subtract), VV)
            gPC.release(VV, v_nb)
            g_stq.release(VV, v_nb)
            gPG.acquire(TE)
            wt(TE, VV, v_nb)
            t_tr = op(TE, lambda e: e.transpose(PG[0:16, 0:128], negb[:, :], idf[:, :]), TT)
            wt(VE, TT, t_tr)
            g_negtr.acquire(VE)
            v_tr = op(VE, lambda e: e.tensor_copy(negtr[0:16, :], PG[0:16, 0:128]), VV)
            gPG.release(VV, v_tr)
            g_negrow.acquire(GP)
            wt(GP, VV, v_tr)
            g_nr = dma(GP, S_CTL, negrow[0:1, :], negtr[0:16, :])
            g_negtr.release(S_CTL, g_nr)
            wt(VE, S_CTL, g_nr)
            op(VE, lambda e: e.tensor_copy(nhi[0:1, :], negrow[0:1, :]))
            op(VE, lambda e: e.tensor_tensor(negrow[0:1, :], negrow[0:1, :], nhi[0:1, :], OP.subtract))
            v_nl = op(VE, lambda e: e.tensor_copy(nlo[0:1, :], negrow[0:1, :]), VV)
            wt(TE, VV, v_nl)
            g_eq.acquire(SC)
            for mc in range(2):
                for t in range(T4):
                    ps, g = psq[(mc * T4 + t) % 2]
                    g.acquire(TE)
                    op(TE, lambda e, ps=ps, hp=hp, mc=mc, t=t, hc=hc: e.matmul(
                        ps[:, :], projs[hp:hp + 64, mc * 128:(mc + 1) * 128],
                        a2[hp:hp + 64, hc * 2048 + t * 512:hc * 2048 + (t + 1) * 512],
                        start=True, stop=False))
                    op(TE, lambda e, ps=ps, t=t: e.matmul(
                        ps[:, :], ones1b[0:1, :], nhi[0:1, t * 512:(t + 1) * 512],
                        start=False, stop=False))
                    tm_q = op(TE, lambda e, ps=ps, t=t: e.matmul(
                        ps[:, :], ones1b[0:1, :], nlo[0:1, t * 512:(t + 1) * 512],
                        start=False, stop=True), TT)
                    if mc == 1 and t == T4 - 1:
                        g_negrow.release(TT, tm_q)
                    wt(SC, TT, tm_q)
                    s_eq = op(SC, lambda e, ps=ps, mc=mc, t=t: e.activation(
                        eq[:, mc * 2048 + t * 512:mc * 2048 + (t + 1) * 512],
                        ps[:, :], AF.Exp), SS)
                    g.release(SS, s_eq)
            if h == 0:
                wt(TE, SS, ct_last)
            wt(TE, SS, s_eq)
            psA_l = []
            for t in range(T4):
                psA, gA = psA_m4[t]
                psA_l.append((psA, gA))
                gA.acquire(TE)
                op(TE, lambda e, psA=psA, h=h, t=t: e.matmul(
                    psA[0:65, :], ct_all[:, (2 * h) * 65:(2 * h) * 65 + 65],
                    eq[:, t * 512:(t + 1) * 512], start=True, stop=False))
                op(TE, lambda e, psA=psA, h=h, t=t: e.matmul(
                    psA[0:65, :], ct_all[:, (2 * h + 1) * 65:(2 * h + 1) * 65 + 65],
                    eq[:, 2048 + t * 512:2048 + (t + 1) * 512], start=False, stop=False))
                tm_a = op(TE, lambda e, psA=psA, h=h: e.matmul(
                    psA[0:65, :], csr[0:1, h * 65:(h + 1) * 65], ones512b[0:1, :],
                    start=False, stop=True), TT)
                if t == T4 - 1:
                    g_eq.release(TT, tm_a)
                if t == 0:
                    g_densb.acquire(SC)
                wt(SC, TT, tm_a)
                s_dr = op(SC, lambda e, psA=psA, t=t: e.copy(
                    densb[64:65, t * 512:(t + 1) * 512], psA[64:65, :]), SS)
                gA.release(SS, s_dr)
            g_ibf.acquire(SC)
            wt(SC, SS, s_dr)
            s_ln2 = op(SC, lambda e: e.activation(lnrow[64:65, :], densb[64:65, :], AF.Ln), SS)
            wt(VE, SS, s_ln2)
            v_hop = op(VE, lambda e: e.tensor_copy(densb[64:65, :], lnrow[64:65, :]), VV)
            wt(SC, VV, v_hop)
            s_ib = op(SC, lambda e: e.activation(ibf[64:65, :], densb[64:65, :],
                                                 AF.Exp, scale=-1.0), SS)
            g_densb.release(SS, s_ib)
            for t in range(T4):
                psA, gA = psA_l[t]
                gPF.acquire(TE)
                if t == 0:
                    wt(TE, SS, s_ib)
                t_pi = op(TE, lambda e, t=t: e.matmul(
                    PF[0:64, :], ones65[64:65, 0:64], ibf[64:65, t * 512:(t + 1) * 512],
                    start=True, stop=True), TT)
                if t == T4 - 1:
                    g_ibf.release(TT, t_pi)
                g_isb[t % 2].acquire(VE)
                wt(VE, TT, t_pi)
                v_ie = op(VE, lambda e, t=t: e.tensor_copy(
                    aqs[0:64, (t % 2) * 512:(t % 2) * 512 + 512], PF[0:64, :]), VV)
                gPF.release(VV, v_ie)
                v_at = op(VE, lambda e, psA=psA, hp=hp, hc=hc, t=t: e.tensor_tensor(
                    a4[hp:hp + 64, hc * 2048 + t * 512:hc * 2048 + (t + 1) * 512],
                    psA[0:64, :], aqs[0:64, (t % 2) * 512:(t % 2) * 512 + 512], OP.mult), VV)
                gA.release(VV, v_at)
                g_isb[t % 2].release(VV, v_at)

    barrier()

    wt(GP, CC[5], 1)
    g_ah = dma(GP, S_CTL, cells[0:1, 8:9], arh_out.ap())
    wt(SC, S_CTL, g_ah)
    s_sigd = op(SC, lambda e: e.activation(cells[0:1, 9:10], cells[0:1, 13:14], AF.Sigmoid), SS)
    wt(VE, SS, s_sigd)
    ve_step(lambda e: e.tensor_tensor(cells[0:1, 10:11], cells[0:1, 9:10], cells[0:1, 8:9], OP.mult))
    ve_step(lambda e: e.tensor_scalar(cells[0:1, 10:11], cells[0:1, 10:11], 1.0 / 127.0, None, OP.mult))
    v_sh = ve_step(lambda e: e.reciprocal(cells[0:1, 11:12], cells[0:1, 10:11]))
    repl128(cells[0:1, 11:12], pp[:, 2:3])
    v_shp = repl128(cells[0:1, 10:11], pp[:, 3:4])

    def hq_dst(gc):
        if gc < 8:
            return a1[:, gc * 2048:(gc + 1) * 2048]
        if gc < 16:
            return a2[:, (gc - 8) * 2048:(gc - 7) * 2048]
        return a3[:, (gc - 16) * 2048:(gc - 15) * 2048]

    h_slot = [None, None]
    g_i32s5 = Guard()
    for gc in range(ICH):
        sl = (gc % 2) * 2048
        if h_slot[gc % 2] is not None:
            wt(SY, VV, h_slot[gc % 2])
        lh = dma(SY, S_X[gc % 2], hst[:, sl:sl + 2048], hh[gc, :, :])
        wt(VE, S_X[gc % 2], lh)
        op(VE, lambda e, sl=sl: e.tensor_scalar(
            hst[:, sl:sl + 2048], hst[:, sl:sl + 2048], pp[:, 2:3], QP, OP.mult, OP.min))
        g_i32s5.acquire(VE)
        v_hi = op(VE, lambda e, sl=sl: e.tensor_scalar(
            i32s5[:, :], hst[:, sl:sl + 2048], QN, None, OP.max), VV)
        h_slot[gc % 2] = v_hi
        wt(SC, VV, v_hi)
        s_hq = op(SC, lambda e, gc=gc: e.activation(hq_dst(gc), i32s5[:, :], AF.Copy), SS)
        g_i32s5.release(SS, s_hq)

    barrier()

    agw_d = agw_out.ap()[:, 2 * NW_G:NW].rearrange("r (p c) -> r p c", p=128)
    psW_m = {0: (PA, gPA), 1: (PB, gPB)}
    psD_m = {0: (PC, gPC), 1: (PD, gPD)}
    g_wobuf = Guard()
    g_dbuf = Guard()
    g_xst = [Guard(), Guard()]
    g_tmp = [Guard(), Guard()]
    g_yst = [Guard(), Guard()]
    for eo2 in range(2):
        g_wobuf.acquire(GP)
        for ei in range(ECH):
            lwo = dma(GP, S_WO, wobuf[:, ei * 512:(ei + 1) * 512],
                      woT_h[ei * 128:(ei + 1) * 128, eo2 * 512:(eo2 + 1) * 512])
        g_dbuf.acquire(SY)
        for ic in range(ICH):
            src = agw_d[eo2 * 4:(eo2 + 1) * 4, :, ic * 128:(ic + 1) * 128] \
                .rearrange("r p c -> p r c")
            ldb = dma(SY, S_DB, dbuf[:, ic * 512:(ic + 1) * 512]
                      .rearrange("p (r c) -> p r c", r=4), src)
        wt(TE, S_DB, ldb)
        wt(TE, S_WO, lwo)
        t_last = None
        for t in range(TCH):
            psW, gW = psW_m[t % 2]
            psD, gD = psD_m[t % 2]
            g_xst[t % 2].acquire(SY)
            lx = dma(SY, S_X[t % 2], xst[:, (t % 2) * 512:(t % 2) * 512 + 512],
                     x_h[t * 128:(t + 1) * 128, eo2 * 512:(eo2 + 1) * 512])
            gD.acquire(TE)
            for ic in range(ICH):
                op(TE, lambda e, psD=psD, ic=ic, t=t: e.matmul(
                    psD[:, :], hq_dst(ic)[:, t * 128:(t + 1) * 128],
                    dbuf[:, ic * 512:(ic + 1) * 512],
                    start=(ic == 0), stop=(ic == ICH - 1)))
            gW.acquire(TE)
            for ei in range(ECH):
                op(TE, lambda e, psW=psW, ei=ei, t=t: e.matmul(
                    psW[:, :], a4[:, ei * 2048 + t * 128:ei * 2048 + (t + 1) * 128],
                    wobuf[:, ei * 512:(ei + 1) * 512],
                    start=(ei == 0), stop=False))
            t_last = op(TE, lambda e, psW=psW, eo2=eo2: e.matmul(
                psW[:, :], ones1b[0:1, :], borb[0:1, eo2 * 512:(eo2 + 1) * 512],
                start=False, stop=True), TT)
            g_tmp[t % 2].acquire(SC)
            wt(SC, TT, t_last)
            s_d = op(SC, lambda e, psD=psD, t=t: e.activation(
                tmpst[:, (t % 2) * 512:(t % 2) * 512 + 512], psD[:, :],
                AF.Copy, scale=pp[:, 3:4]), SS)
            gD.release(SS, s_d)
            g_yst[t % 2].acquire(VE)
            wt(VE, SS, s_d)
            op(VE, lambda e, psW=psW, t=t: e.tensor_tensor(
                yst[:, (t % 2) * 512:(t % 2) * 512 + 512],
                psW[:, :], tmpst[:, (t % 2) * 512:(t % 2) * 512 + 512], OP.add))
            wt(VE, S_X[t % 2], lx)
            v_y = op(VE, lambda e, t=t: e.tensor_tensor(
                yst[:, (t % 2) * 512:(t % 2) * 512 + 512],
                yst[:, (t % 2) * 512:(t % 2) * 512 + 512],
                xst[:, (t % 2) * 512:(t % 2) * 512 + 512], OP.add), VV)
            gW.release(VV, v_y)
            g_tmp[t % 2].release(VV, v_y)
            g_xst[t % 2].release(VV, v_y)
            wt(SY, VV, v_y)
            o_d = dma(SY, S_O[t % 2], y_h[t * 128:(t + 1) * 128, eo2 * 512:(eo2 + 1) * 512],
                      yst[:, (t % 2) * 512:(t % 2) * 512 + 512])
            g_yst[t % 2].release(S_O[t % 2], o_d)
        g_wobuf.release(TT, t_last)
        g_dbuf.release(TT, t_last)
    wt(SY, S_O[0], S_O[0].v)
    wt(SY, S_O[1], S_O[1].v)
    wt(GP, S_O[0], S_O[0].v)
    wt(GP, S_O[1], S_O[1].v)
    wt(GP, S_CTL, S_CTL.v)

    with es, nc.Block() as block:
        for name, dec in ((SY, block.sync), (GP, block.gpsimd), (VE, block.vector),
                          (SC, block.scalar), (TE, block.tensor)):
            def runner(e, name=name):
                for eng, fn in PROG:
                    if eng == name:
                        fn(e)
            dec(runner)
    return nc


def kernel(**inputs):
    import ml_dtypes
    x = np.asarray(inputs["L0_attn_x"], dtype=np.float32)
    wq = np.asarray(inputs["L0_attn_Wq"], dtype=np.float32)
    wk = np.asarray(inputs["L0_attn_Wk"], dtype=np.float32)
    wv = np.asarray(inputs["L0_attn_Wv"], dtype=np.float32)
    wo = np.asarray(inputs["L0_attn_Wo"], dtype=np.float32)
    bq = np.asarray(inputs["L0_attn_bq"], dtype=np.float32)
    bk = np.asarray(inputs["L0_attn_bk"], dtype=np.float32)
    bv = np.asarray(inputs["L0_attn_bv"], dtype=np.float32)
    bo = np.asarray(inputs["L0_attn_bo"], dtype=np.float32)
    proj = np.asarray(inputs["L0_attn_proj"], dtype=np.float32)
    clipx = np.asarray(inputs["L0_ffn_clip_x"], dtype=np.float32)
    clipd = np.asarray(inputs["L0_ffn_clip_down"], dtype=np.float32)
    wgate = np.asarray(inputs["L0_ffn_w_gate"], dtype=np.float32)
    wup = np.asarray(inputs["L0_ffn_w_up"], dtype=np.float32)
    wdown = np.asarray(inputs["L0_ffn_w_down"], dtype=np.float32)

    shared = {
        "wqT": np.ascontiguousarray(wq.T), "wkT": np.ascontiguousarray(wk.T),
        "wvT": np.ascontiguousarray(wv.T), "woT": np.ascontiguousarray(wo.T),
        "bqr": bq.reshape(1, E).copy(), "bkr": bk.reshape(1, E).copy(),
        "bvr": bv.reshape(1, E).copy(), "bor": bo.reshape(1, E).copy(),
        "projT": np.ascontiguousarray(proj.T),
        "clipx": clipx.reshape(1, 1).copy(), "clipd": clipd.reshape(1, 1).copy(),
        "idbf": np.eye(128, dtype=ml_dtypes.bfloat16),
        "idf": np.eye(128, dtype=np.float32),
    }
    in_maps = []
    for c in range(NC):
        b_i, hf = c // 2, c % 2
        m = dict(shared)
        m["x"] = np.ascontiguousarray(x[b_i, hf * TOK:(hf + 1) * TOK, :])
        lo = 3 * c * 128
        for nm, w in (("wg_nat", wgate), ("wu_nat", wup)):
            sl = np.ones((ISLOT * 128, E), np.float32)
            real = w[lo:lo + ISLOT * 128, :]
            sl[:real.shape[0]] = real
            m[nm] = sl
        m["wd_nat"] = np.ascontiguousarray(wdown[c * 128:(c + 1) * 128, :])
        in_maps.append(m)

    nc = build_nc()
    res = run_bass_kernel_spmd(nc, in_maps, core_ids=list(range(NC)))
    out = np.empty((B, NSEQ, E), np.float32)
    for c in range(NC):
        b_i, hf = c // 2, c % 2
        out[b_i, hf * TOK:(hf + 1) * TOK, :] = res.results[c]["y"]
    return out


# version 12
# speedup vs baseline: 1.2762x; 1.1897x over previous
import numpy as np
import contextlib
import concourse.bass as bass
import concourse.mybir as mybir
from concourse.alu_op_type import AluOpType as OP
from concourse.bass_utils import run_bass_kernel_spmd

F32 = mybir.dt.float32
BF16 = mybir.dt.bfloat16
I32 = mybir.dt.int32
AF = mybir.ActivationFunctionType
AX = mybir.AxisListType.X

B, NSEQ, E = 4, 4096, 1024
TOK = 2048
TCH = 16
T4 = 4
ECH = 8
H, D, M = 16, 64, 256
IFF, ICH = 2816, 22
ISLOT = 3
NC = 8
RATIO = 0.0625
EPS = 1e-4
SQD = float(64.0 ** -0.25)
LN_EPS = 1e-5
QP = 127.49995
QN = -128.49995
NW_G = 1024 * ISLOT * 128
NW_D = 2816 * 128
NW = 2 * NW_G + NW_D

A1, A2, A3, A4 = 0, 32768, 65536, 98304
R0 = 131072
S0 = R0 + 61440
TOTAL = 212800

SY, GP, VE, SC, TE = "sync", "gpsimd", "vector", "scalar", "tensor"


def build_nc():
    nc = bass.Bass("TRN2", target_bir_lowering=False, detect_race_conditions=False)

    d_in = lambda n, s, dt=F32: nc.dram_tensor(n, list(s), dt, kind="ExternalInput").ap()
    x_h = d_in("x", (TOK, E))
    wqT_h = d_in("wqT", (E, E)); wkT_h = d_in("wkT", (E, E))
    wvT_h = d_in("wvT", (E, E)); woT_h = d_in("woT", (E, E))
    bqr_h = d_in("bqr", (1, E)); bkr_h = d_in("bkr", (1, E))
    bvr_h = d_in("bvr", (1, E)); bor_h = d_in("bor", (1, E))
    projT_h = d_in("projT", (D, M))
    clipx_h = d_in("clipx", (1, 1)); clipd_h = d_in("clipd", (1, 1))
    wg_h = d_in("wg_nat", (ISLOT * 128, E))
    wu_h = d_in("wu_nat", (ISLOT * 128, E))
    wd_h = d_in("wd_nat", (128, IFF))
    idbf_h = d_in("idbf", (128, 128), BF16)
    idf_h = d_in("idf", (128, 128), F32)
    y_h = nc.dram_tensor("y", [TOK, E], F32, kind="ExternalOutput").ap()


    xn32h = nc.dram_tensor("xn32h", [TOK, E], F32).ap()
    hh = nc.dram_tensor("hh", [ICH, 128, TOK], F32).ap()
    agw_in = nc.dram_tensor("agw_in", [1, NW], BF16)
    agw_out = nc.dram_tensor("agw_out", [NC, NW], BF16)
    ags_in = nc.dram_tensor("ags_in", [1, 768], F32)
    ags_out = nc.dram_tensor("ags_out", [NC, 768], F32)
    arx_in = nc.dram_tensor("arx_in", [1, 1], F32)
    arx_out = nc.dram_tensor("arx_out", [1, 1], F32)
    ars_in = nc.dram_tensor("ars_in", [1, 1], F32)
    ars_out = nc.dram_tensor("ars_out", [1, 1], F32)
    arh_in = nc.dram_tensor("arh_in", [1, 1], F32)
    arh_out = nc.dram_tensor("arh_out", [1, 1], F32)
    arp_in = nc.dram_tensor("arp_in", [129, 2080], F32)
    arp_out = nc.dram_tensor("arp_out", [129, 2080], F32)


    es = contextlib.ExitStack()
    big = es.enter_context(nc.sbuf_tensor("big", [128, TOTAL], mybir.dt.uint8))
    base = nc.lookup_mloc(big).addr

    def place(name, shape, dtype, off):
        per_part = int(np.prod(shape[1:])) * mybir.dt.size(dtype)
        assert off % 32 == 0 and off + per_part <= TOTAL, (name, off, per_part)
        h = nc.virtual_sbuf_tensor(name, list(shape), dtype)
        nc._tensor(h.name, h.shape, h.dtype, type="SB")
        mloc = nc.lookup_mloc(h)
        mloc.allocated = True
        mloc.addr = base + off
        return h

    a1 = place("a1", [128, 16384], BF16, A1)
    a2 = place("a2", [128, 16384], BF16, A2)
    a3 = place("a3", [128, 16384], BF16, A3)
    a4 = place("a4", [128, 16384], BF16, A4)

    msc = place("msc", [128, 4096], F32, R0)
    xbf = place("xbf", [128, 2048], BF16, R0 + 16384)
    wi32 = place("wi32", [128, 1024], I32, R0 + 20480)
    ttile = place("ttile", [128, 1024], BF16, R0 + 24576)
    vw = place("vw", [128, 8192], BF16, R0)
    qkw = place("qkw", [128, 2048], BF16, R0 + 39936)
    wstage = place("wstage", [128, 2816], F32, R0 + 26624)
    wxbf = place("wxbf", [128, 1024], BF16, R0 + 37888)
    ek = place("ek", [128, 4096], BF16, R0 + 20480)
    ksq = place("ksq", [128, 4096], BF16, R0 + 28672)
    xnr = place("xnr", [128, 2048], F32, R0)
    i32s = place("i32s", [128, 2048], I32, R0 + 8192)
    xbf2 = place("xbf2", [128, 1024], BF16, R0 + 16384)
    hst = place("hst", [128, 8192], F32, R0)
    gub = place("gub", [128, 8192], BF16, R0 + 32768)
    sgs = place("sgs", [128, 1024], F32, R0 + 49152)
    eq = place("eq", [128, 4096], BF16, R0 + 8192)
    aqs = place("aqs", [128, 1024], BF16, R0 + 40960)
    stk2 = place("stk2", [2, 4096], BF16, R0 + 43008)
    lnrow = place("lnrow", [65, 2048], F32, R0 + 16384)
    densb = place("densb", [65, 2048], F32, R0 + 51200)
    ibf = place("ibf", [65, 2048], BF16, R0 + 24576)
    i32s5 = place("i32s5", [128, 2048], I32, R0 + 16384)
    dbuf = place("dbuf", [128, 11264], BF16, R0)
    wobuf = place("wobuf", [128, 4096], BF16, R0 + 22528)
    xst = place("xst", [128, 1024], F32, R0 + 30720)
    tmpst = place("tmpst", [128, 1024], F32, R0 + 34816)
    yst = place("yst", [128, 1024], F32, R0 + 38912)
    ust = place("ust", [128, 130], F32, R0 + 36864)
    rm_all = place("rm_all", [128, 256], F32, R0 + 37440)
    upst = place("upst", [128, 130], F32, R0 + 0)
    ct_all = place("ct_all", [128, 2080], BF16, R0 + 1024)

    off = [S0]
    def sp(name, shape, dtype):
        per_part = int(np.prod(shape[1:])) * mybir.dt.size(dtype)
        h = place(name, shape, dtype, off[0])
        off[0] += (per_part + 63) // 64 * 64
        return h
    projs = sp("projs", [128, 256], BF16)
    idbf = sp("idbf_s", [128, 128], BF16)
    idf = sp("idf_s", [128, 128], F32)
    bqrb = sp("bqrb", [1, 1024], BF16)
    bkrb = sp("bkrb", [1, 1024], BF16)
    stats = sp("stats", [128, 96], F32)
    qstats = sp("qstats", [128, 4], F32)
    acx = sp("acx", [128, 16], F32)
    ach = sp("ach", [128, 88], F32)
    seffg = sp("seffg", [128, 22], F32)
    seffu = sp("seffu", [128, 22], F32)
    negb = sp("negb", [128, 16], F32)
    nbh = sp("nbh", [128, 16], BF16)
    nbl = sp("nbl", [128, 16], BF16)
    ntr2 = sp("ntr2", [16, 256], BF16)
    ones2b = sp("ones2b", [2, 128], BF16)
    stq = sp("stq", [128, 16], F32)
    dcols = sp("dcols", [128, 16], F32)
    negtr = sp("negtr", [16, 128], F32)
    pp = sp("pp", [128, 16], F32)
    dencol = sp("dencol", [128, 16], F32)
    rme = sp("rme", [128, 16], F32)
    ones1b = sp("ones1b", [1, 128], BF16)
    onescb = sp("onescb", [128, 1], BF16)
    onesn16 = sp("onesn16", [128, 1], BF16)
    onesepsf = sp("onesepsf", [1, 128], F32)
    ones64f = sp("ones64f", [1, 64], F32)
    ones512b = sp("ones512b", [1, 512], BF16)
    ones65 = sp("ones65", [65, 128], BF16)
    csr = sp("csr", [1, 1040], BF16)
    rcell = sp("rcell", [1, 16], BF16)
    vss = sp("vss", [1, 80], F32)
    vsr = sp("vsr", [1, 80], F32)
    rowx = sp("rowx", [1, 128], F32)
    cells = sp("cells", [1, 32], F32)
    epscol = sp("epscol", [128, 1], F32)
    bvrb = sp("bvrb", [1, 1024], BF16)
    borb = sp("borb", [1, 1024], BF16)
    prf32 = place("prf32", [128, 256], F32, R0 + 49152)
    assert off[0] <= TOTAL, off[0]

    PA = es.enter_context(nc.psum_tensor("PA", [128, 512], F32))
    PB = es.enter_context(nc.psum_tensor("PB", [128, 512], F32))
    PC = es.enter_context(nc.psum_tensor("PC", [128, 512], F32))
    PD = es.enter_context(nc.psum_tensor("PD", [128, 512], F32))
    PE = es.enter_context(nc.psum_tensor("PE", [128, 512], F32))
    PF = es.enter_context(nc.psum_tensor("PF", [128, 512], F32))
    PG = es.enter_context(nc.psum_tensor("PG", [128, 512], F32))
    PT = es.enter_context(nc.psum_tensor("PT", [128, 1024], BF16))

    class Sem:
        def __init__(self, name):
            self.h = es.enter_context(nc.semaphore(name))
            self.v = 0
    LX, LG, VV, SS, TT = (Sem(n) for n in ("LX", "LG", "VV", "SS", "TT"))
    CC = [Sem(f"CC{i}") for i in range(6)]
    S_X = [Sem("SX0"), Sem("SX1")]
    S_SP = [Sem("SSP0"), Sem("SSP1"), Sem("SSP2"), Sem("SSP3")]
    S_G = [Sem("SG0"), Sem("SG1"), Sem("SG2"), Sem("SG3")]
    S_QK = [Sem("SQK0"), Sem("SQK1")]
    S_O = [Sem("SO0"), Sem("SO1")]
    S_DB = Sem("SDB")
    S_W = Sem("SW")
    S_VW = Sem("SVW")
    S_WO = Sem("SWO")
    S_AGW = Sem("SAGW")
    S_AGS = Sem("SAGS")
    S_ARP = Sem("SARP")
    S_CTL = Sem("SCTL")
    S_SEF = Sem("SSEF")
    BR = Sem("BR")
    SY_SEMS = [LX, S_X[0], S_X[1], S_SP[0], S_SP[1], S_SP[2], S_SP[3],
               S_G[0], S_G[1], S_G[2], S_G[3], S_O[0], S_O[1], S_DB, S_W]
    GP_SEMS = [LG, S_QK[0], S_QK[1], S_VW, S_WO, S_AGW, S_AGS, S_ARP, S_CTL, S_SEF]

    PROG = []
    def em(eng, fn):
        PROG.append((eng, fn))
    def wt(eng, sem, val):
        if val and val > 0:
            em(eng, lambda e, s=sem.h, v=val: e.wait_ge(s, v))
    def dma(eng, sem, out, in_):
        sem.v += 16
        em(eng, lambda e, o=out, i=in_, s=sem.h: e.dma_start(out=o, in_=i).then_inc(s, 16))
        return sem.v
    def op(eng, fn, sem=None):
        if sem is None:
            em(eng, fn)
            return None
        sem.v += 1
        em(eng, lambda e, f=fn, s=sem.h: f(e).then_inc(s, 1))
        return sem.v

    def ve_step(fn):
        v = op(VE, fn, VV)
        wt(VE, VV, v)
        return v

    class Guard:
        def __init__(self):
            self.conds = []
        def acquire(self, eng):
            for s, v in self.conds:
                wt(eng, s, v)
            self.conds = []
        def release(self, sem, val):
            if val:
                self.conds.append((sem, val))
    gPA, gPB, gPC, gPD, gPE, gPF, gPG, gPT = (Guard() for _ in range(8))

    br_round = [0]

    def barrier():
        for s in SY_SEMS:
            wt(SY, s, s.v)
        em(SY, lambda e: e.sem_inc(BR.h, 1)); BR.v += 1
        for s in GP_SEMS:
            wt(GP, s, s.v)
        em(GP, lambda e: e.sem_inc(BR.h, 1)); BR.v += 1
        br_round[0] = BR.v
        vals = [(BR, BR.v), (VV, VV.v), (SS, SS.v), (TT, TT.v)]
        for eng in (SY, GP, VE, SC, TE):
            for s, v in vals:
                wt(eng, s, v)

    op(VE, lambda e: e.memset(ones1b[:, :], 1.0))
    op(VE, lambda e: e.memset(ones2b[:, :], 1.0))
    op(VE, lambda e: e.memset(onescb[:, :], 1.0))
    op(VE, lambda e: e.memset(onesn16[:, :], -0.0625))
    op(VE, lambda e: e.memset(onesepsf[:, :], RATIO * EPS))
    op(VE, lambda e: e.memset(ones64f[:, :], 1.0))
    op(VE, lambda e: e.memset(ones512b[:, :], 1.0))
    op(VE, lambda e: e.memset(ones65[:, :], 1.0))
    op(VE, lambda e: e.memset(vss[0:1, 64:65], float(TOK)))
    op(VE, lambda e: e.memset(epscol[:, :], LN_EPS), VV)
    dma(SY, LX, idbf[:, :], idbf_h)
    dma(SY, LX, idf[:, :], idf_h)
    dma(SY, LX, msc[0:1, 2048:3072], bqr_h)
    dma(SY, LX, msc[0:1, 3072:4096], bkr_h)
    dma(SY, LX, prf32[0:64, :], projT_h)
    dma(SY, LX, prf32[64:128, :], projT_h)
    dma(SY, LX, cells[0:1, 12:13], clipx_h)
    dma(SY, LX, cells[0:1, 13:14], clipd_h)
    dma(SY, LX, msc[0:1, 0:1024], bvr_h)
    l_pre = dma(SY, LX, msc[0:1, 1024:2048], bor_h)
    wt(SC, LX, l_pre)
    op(SC, lambda e: e.activation(projs[0:64, :], prf32[0:64, :], AF.Copy, scale=SQD))
    op(SC, lambda e: e.activation(projs[64:128, :], prf32[64:128, :], AF.Copy, scale=SQD))
    op(SC, lambda e: e.activation(bvrb[:, :], msc[0:1, 0:1024], AF.Copy))
    op(SC, lambda e: e.activation(borb[:, :], msc[0:1, 1024:2048], AF.Copy))
    op(SC, lambda e: e.activation(bqrb[:, :], msc[0:1, 2048:3072], AF.Copy))
    s_pre = op(SC, lambda e: e.activation(bkrb[:, :], msc[0:1, 3072:4096], AF.Copy), SS)

    slot_free = [None, None]
    for c in range(TCH):
        sl = (c % 2) * 1024
        if slot_free[c % 2] is not None:
            vv_f, ss_f = slot_free[c % 2]
            wt(SY, VV, vv_f); wt(SY, SS, ss_f)
        if c == 0:
            wt(SY, SS, s_pre)
        lx = dma(SY, S_X[c % 2], msc[:, sl:sl + 1024], x_h[c * 128:(c + 1) * 128, :])
        wt(VE, S_X[c % 2], lx)
        v_r = op(VE, lambda e, sl=sl, c=c: e.reduce_sum(
            stats[:, c:c + 1], msc[:, sl:sl + 1024], AX), VV)
        wt(SC, S_X[c % 2], lx)
        s_r = op(SC, lambda e, sl=sl, c=c: e.activation(
            xbf[:, sl:sl + 1024], msc[:, sl:sl + 1024], AF.Square,
            accum_out=stats[:, 16 + c:17 + c]), SS)
        slot_free[c % 2] = (v_r, s_r)
    wt(VE, SS, SS.v)
    wt(VE, VV, VV.v)
    ve_step(lambda e: e.tensor_scalar(stats[:, 32:48], stats[:, 0:16], 1.0 / E, None, OP.mult))
    ve_step(lambda e: e.tensor_scalar(stats[:, 48:64], stats[:, 16:32], 1.0 / E, None, OP.mult))
    ve_step(lambda e: e.tensor_tensor(stats[:, 64:80], stats[:, 32:48], stats[:, 32:48], OP.mult))
    v_var = ve_step(lambda e: e.tensor_tensor(stats[:, 64:80], stats[:, 48:64], stats[:, 64:80], OP.subtract))
    wt(SC, VV, v_var)
    s_sd = op(SC, lambda e: e.activation(stats[:, 80:96], stats[:, 64:80], AF.Sqrt, bias=epscol[:, 0:1]), SS)
    wt(VE, SS, s_sd)
    v_rstd = ve_step(lambda e: e.reciprocal(stats[:, 80:96], stats[:, 80:96]))

    g_region = agw_in.ap()[0:1, 0:NW_G].rearrange("a (p c) -> (a p) c", p=128)
    u_region = agw_in.ap()[0:1, NW_G:2 * NW_G].rearrange("a (p c) -> (a p) c", p=128)
    d_region = agw_in.ap()[0:1, 2 * NW_G:NW].rearrange("a (p c) -> (a p) c", p=128)
    last_agw = 0
    g_ttile = Guard()
    g_qstats = Guard()
    g_wi32 = Guard()
    g_wxbf = Guard()
    wslot = [Guard(), Guard()]

    def quant_block(w_slice_ap, tile_dst_blk, is_down, s_col):
        nonlocal last_agw
        width = w_slice_ap.shape[1]
        g_qstats.acquire(VE)
        v_amx = op(VE, lambda e: e.tensor_reduce(qstats[:, 0:1], w_slice_ap, AX, OP.max,
                                                 apply_absolute_value=True), VV)
        wt(VE, VV, v_amx)
        ve_step(lambda e: e.tensor_scalar(qstats[:, 1:2], qstats[:, 0:1], 1.0 / 127.0,
                                          None, OP.mult))
        ve_step(lambda e: e.reciprocal(qstats[:, 2:3], qstats[:, 0:1]))
        v_is = ve_step(lambda e: e.tensor_scalar(qstats[:, 2:3], qstats[:, 2:3], 127.0,
                                                 None, OP.mult))
        if s_col is not None:
            wt(GP, VV, v_is)
            l_ags = dma(GP, S_AGS, ags_in.ap()[0:1, s_col:s_col + 128], qstats[:, 0:1])
            g_qstats.release(S_AGS, l_ags)
        nblk = width // 128
        for bb in range(0, nblk, 8):
            hi = min(bb + 8, nblk)
            w = (hi - bb) * 128
            g_wi32.acquire(VE)
            v_q = op(VE, lambda e, bb=bb, w=w: e.tensor_scalar(
                wi32[:, 0:w], w_slice_ap[:, bb * 128:bb * 128 + w],
                qstats[:, 2:3], None, OP.mult), VV)
            wt(SC, VV, v_q)
            g_wxbf.acquire(SC)
            if is_down:
                s_q = op(SC, lambda e, w=w: e.activation(
                    wxbf[:, 0:w], wi32[:, 0:w], AF.Copy, scale=qstats[:, 1:2]), SS)
            else:
                s_q = op(SC, lambda e, w=w: e.activation(
                    wxbf[:, 0:w], wi32[:, 0:w], AF.Copy), SS)
            g_wi32.release(SS, s_q)
            if is_down:
                g_qstats.release(SS, s_q)
            gPT.acquire(TE)
            wt(TE, SS, s_q)
            for j in range(hi - bb):
                t_tp = op(TE, lambda e, j=j: e.transpose(
                    PT[:, j * 128:(j + 1) * 128], wxbf[:, j * 128:(j + 1) * 128],
                    idbf[:, :]), TT if j == hi - bb - 1 else None)
            g_wxbf.release(TT, t_tp)
            g_ttile.acquire(VE)
            wt(VE, TT, t_tp)
            v_ev = op(VE, lambda e, w=w: e.tensor_copy(ttile[:, 0:w], PT[:, 0:w]), VV)
            gPT.release(VV, v_ev)
            wt(GP, VV, v_ev)
            last_agw = dma(GP, S_AGW, tile_dst_blk(bb, hi - bb),
                           ttile[:, 0:w].rearrange("p (j c) -> p j c", c=128))
            g_ttile.release(S_AGW, last_agw)

    def mk_dst(region, icl):
        return lambda bb, n: region[:, :].rearrange("p (j c) -> p j c", c=384)\
            [:, bb:bb + n, icl * 128:(icl + 1) * 128]
    p05_tasks = []
    for w_h, region, soff in ((wg_h, g_region, 0), (wu_h, u_region, 384)):
        for icl in range(ISLOT):
            p05_tasks.append((w_h[icl * 128:(icl + 1) * 128, :],
                              mk_dst(region, icl), False, soff + icl * 128))
    p05_tasks.append((wd_h, lambda bb, n: d_region[:, bb * 128:(bb + n) * 128]
                      .rearrange("p (j c) -> p j c", c=128), True, None))

    def p05_load(i):
        src, _, is_down, _ = p05_tasks[i]
        if is_down:
            wslot[0].acquire(SY)
            wslot[1].acquire(SY)
            return dma(SY, S_W, wstage[:, 0:2816], src)
        sl = (i % 2) * 1024
        wslot[i % 2].acquire(SY)
        return dma(SY, S_W, wstage[:, sl:sl + 1024], src)

    def p05_compute(i, lw):
        _, dst, is_down, s_col = p05_tasks[i]
        wt(VE, S_W, lw)
        if is_down:
            quant_block(wstage[:, 0:2816], dst, True, None)
            wslot[0].release(VV, VV.v)
            wslot[1].release(VV, VV.v)
        else:
            sl = (i % 2) * 1024
            quant_block(wstage[:, sl:sl + 1024], dst, False, s_col)
            wslot[i % 2].release(VV, VV.v)

    xslot_free = [None, None]
    nslot_free = [None, None]
    g_xbf = [Guard(), Guard()]
    wt(SY, S_X[0], S_X[0].v)
    wt(SY, S_X[1], S_X[1].v)
    for c in range(TCH):
        slx = 2048 + (c % 2) * 1024
        sln = (c % 2) * 1024
        if xslot_free[c % 2] is not None:
            wt(SY, VV, xslot_free[c % 2])
        lx = dma(SY, S_X[c % 2], msc[:, slx:slx + 1024], x_h[c * 128:(c + 1) * 128, :])
        wt(VE, S_X[c % 2], lx)
        if c == 0:
            wt(VE, VV, v_rstd)
        if nslot_free[c % 2] is not None:
            lx_sp, ss_ca = nslot_free[c % 2]
            wt(VE, S_SP[c % 2], lx_sp); wt(VE, SS, ss_ca)
        v_n = op(VE, lambda e, slx=slx, sln=sln, c=c: e.tensor_scalar(
            msc[:, sln:sln + 1024], msc[:, slx:slx + 1024],
            stats[:, 32 + c:33 + c], stats[:, 80 + c:81 + c],
            OP.subtract, OP.mult), VV)
        xslot_free[c % 2] = v_n
        v_am = op(VE, lambda e, sln=sln, c=c: e.tensor_reduce(
            acx[:, c:c + 1], msc[:, sln:sln + 1024], AX, OP.max,
            apply_absolute_value=True), VV)
        wt(SY, VV, v_am)
        lx_sp = dma(SY, S_SP[c % 2], xn32h[c * 128:(c + 1) * 128, :], msc[:, sln:sln + 1024])
        wt(SC, VV, v_am)
        g_xbf[c % 2].acquire(SC)
        s_ca = op(SC, lambda e, sln=sln: e.activation(
            xbf[:, sln:sln + 1024], msc[:, sln:sln + 1024], AF.Copy), SS)
        nslot_free[c % 2] = (lx_sp, s_ca)
        gPT.acquire(TE)
        wt(TE, SS, s_ca)
        for j in range(8):
            t_tp = op(TE, lambda e, sln=sln, j=j: e.transpose(
                PT[:, j * 128:(j + 1) * 128], xbf[:, sln + j * 128:sln + (j + 1) * 128],
                idbf[:, :]), TT if j == 7 else None)
        g_xbf[c % 2].release(TT, t_tp)
        wt(VE, TT, t_tp)
        dst = a1[:, :].rearrange("p (g t) -> p g t", g=8)[:, :, c * 128:(c + 1) * 128]
        v_ev = op(VE, lambda e, dst=dst: e.tensor_copy(
            dst, PT[:, :].rearrange("p (g t) -> p g t", g=8)), VV)
        gPT.release(VV, v_ev)
        if c % 2 == 0 and c // 2 < len(p05_tasks):
            p05_pending = p05_load(c // 2)
        elif c % 2 == 1 and c // 2 < len(p05_tasks):
            p05_compute(c // 2, p05_pending)
    vv_a1 = VV.v
    ss_msc = SS.v
    sp0, sp1 = S_SP[0].v, S_SP[1].v
    wt(VE, VV, VV.v)
    v_ax = op(VE, lambda e: e.tensor_reduce(pp[:, 15:16], acx[:, :], AX, OP.max), VV)
    wt(GP, VV, v_ax)
    g1 = dma(GP, S_CTL, rowx[0:1, :], pp[:, 15:16])
    wt(VE, S_CTL, g1)
    v_axc = op(VE, lambda e: e.reduce_max(cells[0:1, 0:1], rowx[0:1, :], AX), VV)
    wt(GP, VV, v_axc)
    dma(GP, S_CTL, arx_in.ap(), cells[0:1, 0:1])
    op(GP, lambda e: e.collective_compute(
        "AllReduce", OP.max, replica_groups=[list(range(NC))],
        ins=[arx_in.ap().opt()], outs=[arx_out.ap().opt()]), CC[0])
    wt(GP, S_AGW, last_agw)
    wt(GP, S_AGS, S_AGS.v)
    op(GP, lambda e: e.collective_compute(
        "AllGather", OP.bypass, replica_groups=[list(range(NC))],
        ins=[agw_in.ap().opt()], outs=[agw_out.ap().opt()]), CC[1])
    op(GP, lambda e: e.collective_compute(
        "AllGather", OP.bypass, replica_groups=[list(range(NC))],
        ins=[ags_in.ap().opt()], outs=[ags_out.ap().opt()]), CC[2])

    wt(GP, VV, vv_a1)
    wt(GP, SS, ss_msc)
    wt(GP, S_SP[0], sp0)
    wt(GP, S_SP[1], sp1)
    for ei in range(ECH):
        dma(GP, S_VW, vw[:, ei * 1024:(ei + 1) * 1024],
            wvT_h[ei * 128:(ei + 1) * 128, :])
    lg_vw = S_VW.v
    wt(TE, VV, vv_a1)
    qk_slot = [Guard(), Guard()]
    psq = {0: (PA, gPA), 1: (PB, gPB)}
    for wsel in range(2):
        dst = a2 if wsel == 0 else a3
        brow = bqrb if wsel == 0 else bkrb
        for eo in range(ECH):
            idx = wsel * ECH + eo
            sl = (idx % 2) * 1024
            qk_slot[idx % 2].acquire(GP)
            src = (wqT_h if wsel == 0 else wkT_h)[:, eo * 128:(eo + 1) * 128] \
                .rearrange("(a p) c -> p a c", p=128)
            lqk = dma(GP, S_QK[idx % 2], qkw[:, sl:sl + 1024].rearrange("p (a c) -> p a c", a=8), src)
            wt(TE, S_QK[idx % 2], lqk)
            for t in range(T4):
                ps, g = psq[t % 2]
                g.acquire(TE)
                for ei in range(ECH):
                    op(TE, lambda e, ps=ps, sl=sl, ei=ei, t=t: e.matmul(
                        ps[:, :], qkw[:, sl + ei * 128:sl + (ei + 1) * 128],
                        a1[:, ei * 2048 + t * 512:ei * 2048 + (t + 1) * 512],
                        start=(ei == 0), stop=False))
                tm = op(TE, lambda e, ps=ps, brow=brow, eo=eo: e.matmul(
                    ps[:, :], brow[0:1, eo * 128:(eo + 1) * 128], ones512b[0:1, :],
                    start=False, stop=True), TT)
                if t == T4 - 1:
                    qk_slot[idx % 2].release(TT, tm)
                wt(SC, TT, tm)
                s_ev = op(SC, lambda e, ps=ps, dst=dst, eo=eo, t=t:
                          e.activation(dst[:, eo * 2048 + t * 512:eo * 2048 + (t + 1) * 512],
                                       ps[:, :], AF.Copy), SS)
                g.release(SS, s_ev)
    wt(TE, S_VW, lg_vw)
    for t in range(TCH):
        for eo2 in range(2):
            ps, g = psq[(t * 2 + eo2) % 2]
            g.acquire(TE)
            for ei in range(ECH):
                op(TE, lambda e, ps=ps, t=t, eo2=eo2, ei=ei: e.matmul(
                    ps[:, :], a1[:, ei * 2048 + t * 128:ei * 2048 + (t + 1) * 128],
                    vw[:, ei * 1024 + eo2 * 512:ei * 1024 + (eo2 + 1) * 512],
                    start=(ei == 0), stop=False))
            tm = op(TE, lambda e, ps=ps, eo2=eo2: e.matmul(
                ps[:, :], ones1b[0:1, :], bvrb[0:1, eo2 * 512:(eo2 + 1) * 512],
                start=False, stop=True), TT)
            wt(VE, TT, tm)
            v_ev = op(VE, lambda e, ps=ps, t=t, eo2=eo2: e.tensor_copy(
                a4[:, t * 1024 + eo2 * 512:t * 1024 + (eo2 + 1) * 512], ps[:, :]), VV)
            g.release(VV, v_ev)

    barrier()

    ksq_slot = [Guard(), Guard()]
    g_dcols = Guard()
    g_ek = Guard()
    v_z = op(VE, lambda e: e.memset(msc[0:1, 0:1040], 0.0), VV)
    wt(GP, VV, v_z)
    dma(GP, S_ARP, arp_in.ap()[128:129, 1040:2080], msc[0:1, 0:1040])
    for hc in range(ECH):
        ksq_slot[hc % 2].acquire(SC)
        s_ksq = op(SC, lambda e, hc=hc: e.activation(
            ksq[:, (hc % 2) * 2048:(hc % 2) * 2048 + 2048],
            a3[:, hc * 2048:(hc + 1) * 2048], AF.Square), SS)
        for hh_i in range(2):
            h = 2 * hc + hh_i
            hp = hh_i * 64
            gPC.acquire(TE)
            wt(TE, SS, s_ksq)
            for t in range(TCH):
                tm_d = op(TE, lambda e, hc=hc, hp=hp, t=t: e.matmul(
                    PC[:, t:t + 1],
                    ksq[hp:hp + 64, (hc % 2) * 2048 + t * 128:(hc % 2) * 2048 + (t + 1) * 128],
                    onesn16[hp:hp + 64, 0:1], start=True, stop=True),
                    TT if t == TCH - 1 else None)
            if hh_i == 1:
                ksq_slot[hc % 2].release(TT, tm_d)
            g_dcols.acquire(VE)
            wt(VE, TT, tm_d)
            v_d = op(VE, lambda e: e.tensor_copy(dcols[:, 0:16], PC[:, 0:16]), VV)
            gPC.release(VV, v_d)
            def dd_mm(t):
                ps, g = psq[t % 2]
                g.acquire(TE)
                return ps, g, op(TE, lambda e, ps=ps, hc=hc, hp=hp, t=t: e.matmul(
                    ps[:, 0:256],
                    a3[hp:hp + 64, hc * 2048 + t * 128:hc * 2048 + (t + 1) * 128],
                    projs[hp:hp + 64, :], start=True, stop=True), TT)
            dd_q = [dd_mm(0)]
            for t in range(TCH):
                ps, g, tm_dd = dd_q[t]
                if t + 1 < TCH:
                    dd_q.append(dd_mm(t + 1))
                wt(SC, TT, tm_dd)
                if t == 0:
                    wt(SC, VV, v_d)
                g_ek.acquire(SC) if t == 0 else None
                s_ek = op(SC, lambda e, ps=ps, t=t: e.activation(
                    ek[:, t * 256:(t + 1) * 256], ps[:, 0:256], AF.Exp,
                    bias=dcols[:, t:t + 1]), SS)
                g.release(SS, s_ek)
                wt(VE, SS, s_ek)
                v_rm = op(VE, lambda e, h=h, t=t: e.reduce_max(
                    rme[:, t:t + 1], ek[:, t * 256:(t + 1) * 256], AX), VV)
                wt(TE, SS, s_ek)
                if t == 0:
                    gPD.acquire(TE); gPE.acquire(TE); gPF.acquire(TE)
                    gPG.acquire(TE); gPC.acquire(TE)
                v_sl = a4[:, t * 1024 + h * 64:t * 1024 + (h + 1) * 64]
                op(TE, lambda e, t=t, v_sl=v_sl: e.matmul(
                    PD[:, 0:64], ek[:, t * 256:t * 256 + 128], v_sl,
                    start=(t == 0), stop=(t == TCH - 1)))
                op(TE, lambda e, t=t, v_sl=v_sl: e.matmul(
                    PE[:, 0:64], ek[:, t * 256 + 128:t * 256 + 256], v_sl,
                    start=(t == 0), stop=(t == TCH - 1)))
                op(TE, lambda e, t=t: e.matmul(
                    PF[:, 0:1], ek[:, t * 256:t * 256 + 128], onescb[:, 0:1],
                    start=(t == 0), stop=(t == TCH - 1)))
                op(TE, lambda e, t=t: e.matmul(
                    PG[:, 0:1], ek[:, t * 256 + 128:t * 256 + 256], onescb[:, 0:1],
                    start=(t == 0), stop=(t == TCH - 1)))
                tm_vs = op(TE, lambda e, t=t, v_sl=v_sl: e.matmul(
                    PC[0:1, 0:64], onescb[:, 0:1], v_sl,
                    start=(t == 0), stop=(t == TCH - 1)),
                    TT if t == TCH - 1 else None)
                if t == TCH - 1:
                    g_ek.release(TT, tm_vs)
                    g_ek.release(VV, v_rm)
            wt(SC, VV, v_rm)
            s_ln = op(SC, lambda e: e.activation(rme[:, 0:16], rme[:, 0:16], AF.Ln), SS)
            wt(VE, SS, s_ln)
            v_rma = op(VE, lambda e, h=h: e.tensor_tensor(
                rm_all[:, h * 16:h * 16 + 16], rme[:, 0:16], dcols[:, 0:16],
                OP.subtract), VV)
            g_dcols.release(VV, v_rma)
            wt(VE, TT, tm_vs)
            op(VE, lambda e: e.tensor_copy(ust[:, 0:64], PD[:, 0:64]))
            op(VE, lambda e: e.tensor_copy(ust[:, 64:65], PF[:, 0:1]))
            op(VE, lambda e: e.tensor_copy(ust[:, 65:129], PE[:, 0:64]))
            op(VE, lambda e: e.tensor_copy(ust[:, 129:130], PG[:, 0:1]))
            v_u = op(VE, lambda e: e.tensor_copy(vss[0:1, 0:64], PC[0:1, 0:64]), VV)
            gPD.release(VV, v_u); gPE.release(VV, v_u); gPF.release(VV, v_u)
            gPG.release(VV, v_u); gPC.release(VV, v_u)
            wt(GP, VV, v_u)
            w1 = dma(GP, S_ARP, arp_in.ap()[0:128, (2 * h) * 65:(2 * h + 2) * 65],
                     ust[:, 0:130])
            w2 = dma(GP, S_ARP, arp_in.ap()[128:129, h * 65:(h + 1) * 65],
                     vss[0:1, 0:65])
            wt(VE, S_ARP, w2)
    wt(VE, VV, VV.v)
    v_dk = op(VE, lambda e: e.reduce_max(pp[:, 14:15], rm_all[:, :], AX), VV)
    wt(GP, VV, v_dk)
    g2 = dma(GP, S_CTL, rowx[0:1, :], pp[:, 14:15])
    wt(VE, S_CTL, g2)
    v_dkc = op(VE, lambda e: e.reduce_max(cells[0:1, 5:6], rowx[0:1, :], AX), VV)
    wt(GP, VV, v_dkc)
    dma(GP, S_CTL, ars_in.ap(), cells[0:1, 5:6])
    wt(GP, S_ARP, S_ARP.v)
    op(GP, lambda e: e.collective_compute(
        "AllReduce", OP.add, replica_groups=[[0, 1], [2, 3], [4, 5], [6, 7]],
        ins=[arp_in.ap().opt()], outs=[arp_out.ap().opt()]), CC[3])
    op(GP, lambda e: e.collective_compute(
        "AllReduce", OP.max, replica_groups=[list(range(NC))],
        ins=[ars_in.ap().opt()], outs=[ars_out.ap().opt()]), CC[4])

    barrier()

    wt(GP, CC[0], 1)
    g_ax = dma(GP, S_CTL, cells[0:1, 0:1], arx_out.ap())
    wt(SC, S_CTL, g_ax)
    s_sig = op(SC, lambda e: e.activation(cells[0:1, 1:2], cells[0:1, 12:13], AF.Sigmoid), SS)
    wt(VE, SS, s_sig)
    ve_step(lambda e: e.tensor_tensor(cells[0:1, 2:3], cells[0:1, 1:2], cells[0:1, 0:1], OP.mult))
    ve_step(lambda e: e.tensor_scalar(cells[0:1, 2:3], cells[0:1, 2:3], 1.0 / 127.0, None, OP.mult))
    ve_step(lambda e: e.reciprocal(cells[0:1, 3:4], cells[0:1, 2:3]))
    v_sxf = ve_step(lambda e: e.tensor_scalar(cells[0:1, 4:5], cells[0:1, 2:3], 1.0 / 127.0, None, OP.mult))

    def repl128(cell_ap, dst_col):
        ve_step(lambda e: e.tensor_copy(rcell[0:1, 0:1], cell_ap))
        ve_step(lambda e: e.tensor_tensor(vsr[0:1, 70:71], cell_ap, rcell[0:1, 0:1], OP.subtract))
        v_r = ve_step(lambda e: e.tensor_copy(rcell[0:1, 1:2], vsr[0:1, 70:71]))
        gPG.acquire(TE)
        wt(TE, VV, v_r)
        op(TE, lambda e: e.matmul(PG[:, 0:1], ones1b[0:1, :], rcell[0:1, 0:1], start=True, stop=False))
        t_r = op(TE, lambda e: e.matmul(PG[:, 0:1], ones1b[0:1, :], rcell[0:1, 1:2], start=False, stop=True), TT)
        wt(VE, TT, t_r)
        v_f = op(VE, lambda e: e.tensor_copy(dst_col, PG[:, 0:1]), VV)
        wt(VE, VV, v_f)
        gPG.release(VV, v_f)
        return v_f

    wt(VE, VV, v_sxf)
    repl128(cells[0:1, 3:4], pp[:, 0:1])
    v_sxfp = repl128(cells[0:1, 4:5], pp[:, 1:2])

    xq_slot = [None, None]
    g_i32 = [Guard(), Guard()]
    for c in range(TCH):
        sl = (c % 2) * 1024
        if xq_slot[c % 2] is not None:
            wt(SY, VV, xq_slot[c % 2])
        lxn = dma(SY, S_X[c % 2], xnr[:, sl:sl + 1024], xn32h[c * 128:(c + 1) * 128, :])
        wt(VE, S_X[c % 2], lxn)
        op(VE, lambda e, sl=sl: e.tensor_scalar(
            xnr[:, sl:sl + 1024], xnr[:, sl:sl + 1024], pp[:, 0:1], QP, OP.mult, OP.min))
        g_i32[c % 2].acquire(VE)
        v_i = op(VE, lambda e, sl=sl: e.tensor_scalar(
            i32s[:, sl:sl + 1024], xnr[:, sl:sl + 1024], QN, None, OP.max), VV)
        xq_slot[c % 2] = v_i
        wt(SC, VV, v_i)
        s_b = op(SC, lambda e, sl=sl: e.activation(
            xbf2[:, 0:1024], i32s[:, sl:sl + 1024], AF.Copy), SS)
        g_i32[c % 2].release(SS, s_b)
        gPT.acquire(TE)
        wt(TE, SS, s_b)
        for j in range(8):
            t_l = op(TE, lambda e, j=j: e.transpose(
                PT[:, j * 128:(j + 1) * 128], xbf2[:, j * 128:(j + 1) * 128],
                idbf[:, :]), TT if j == 7 else None)
        wt(VE, TT, t_l)
        dst = a4[:, :].rearrange("p (g t) -> p g t", g=8)[:, :, c * 128:(c + 1) * 128]
        v_x = op(VE, lambda e, dst=dst: e.tensor_copy(
            dst, PT[:, :].rearrange("p (g t) -> p g t", g=8)), VV)
        gPT.release(VV, v_x)
        wt(SC, TT, t_l)
    wt(GP, CC[2], 1)
    for gc in range(ICH):
        r, icl = gc // 3, gc % 3
        dma(GP, S_SEF, seffg[:, gc:gc + 1],
            ags_out.ap()[r:r + 1, icl * 128:(icl + 1) * 128])
        g_s = dma(GP, S_SEF, seffu[:, gc:gc + 1],
                  ags_out.ap()[r:r + 1, 384 + icl * 128:384 + (icl + 1) * 128])
    wt(VE, S_SEF, g_s)
    wt(VE, VV, v_sxfp)
    op(VE, lambda e: e.tensor_scalar(seffg[:, :], seffg[:, :], pp[:, 1:2], None, OP.mult))
    op(VE, lambda e: e.tensor_scalar(seffu[:, :], seffu[:, :], pp[:, 1:2], None, OP.mult), VV)

    barrier()

    wt(TE, CC[1], 1)
    wt(SY, CC[1], 1)
    gub_slot = [Guard(), Guard(), Guard(), Guard()]
    hst_slot = [Guard(), Guard(), Guard(), Guard()]
    g_sgs = [Guard(), Guard()]
    psg_m = {0: (PA, gPA), 1: (PB, gPB)}
    psu_m = {0: (PC, gPC), 1: (PD, gPD)}
    def gu_load(gc):
        r, icl = gc // 3, gc % 3
        slg = (gc % 4) * 1024
        src_g = agw_out.ap()[r:r + 1, 0:NW_G] \
            .rearrange("a (p g c) -> (a p) g c", p=128, c=384)[:, :, icl * 128:(icl + 1) * 128]
        src_u = agw_out.ap()[r:r + 1, NW_G:2 * NW_G] \
            .rearrange("a (p g c) -> (a p) g c", p=128, c=384)[:, :, icl * 128:(icl + 1) * 128]
        gub_slot[gc % 4].acquire(SY)
        dma(SY, S_G[gc % 4], gub[:, slg:slg + 1024].rearrange("p (g c) -> p g c", g=8), src_g)
        return dma(SY, S_G[gc % 4], gub[:, 4096 + slg:4096 + slg + 1024]
                   .rearrange("p (g c) -> p g c", g=8), src_u)
    lgu_q = [gu_load(0), gu_load(1), gu_load(2)]
    for gc in range(ICH):
        slg = (gc % 4) * 1024
        sl_h = (gc % 4) * 2048
        if gc + 3 < ICH:
            lgu_q.append(gu_load(gc + 3))
        lgu = lgu_q[gc]
        hst_slot[gc % 4].acquire(VE)
        wt(TE, S_G[gc % 4], lgu)
        for t in range(T4):
            psg, gg = psg_m[t % 2]
            psu, gu = psu_m[t % 2]
            gg.acquire(TE)
            for hcc in range(ECH):
                op(TE, lambda e, psg=psg, slg=slg, hcc=hcc, t=t: e.matmul(
                    psg[:, :], gub[:, slg + hcc * 128:slg + (hcc + 1) * 128],
                    a4[:, hcc * 2048 + t * 512:hcc * 2048 + (t + 1) * 512],
                    start=(hcc == 0), stop=(hcc == ECH - 1)))
            gu.acquire(TE)
            for hcc in range(ECH):
                tm_u = op(TE, lambda e, psu=psu, slg=slg, hcc=hcc, t=t: e.matmul(
                    psu[:, :], gub[:, 4096 + slg + hcc * 128:4096 + slg + (hcc + 1) * 128],
                    a4[:, hcc * 2048 + t * 512:hcc * 2048 + (t + 1) * 512],
                    start=(hcc == 0), stop=(hcc == ECH - 1)),
                    TT if hcc == ECH - 1 else None)
            if t == T4 - 1:
                gub_slot[gc % 4].release(TT, tm_u)
            wt(SC, TT, tm_u)
            g_sgs[t % 2].acquire(SC)
            s_sg = op(SC, lambda e, psg=psg, t=t, gc=gc: e.activation(
                sgs[:, (t % 2) * 512:(t % 2) * 512 + 512], psg[:, :], AF.Silu,
                scale=seffg[:, gc:gc + 1]), SS)
            gg.release(SS, s_sg)
            wt(VE, SS, s_sg)
            op(VE, lambda e, psu=psu, t=t, gc=gc, sl_h=sl_h: e.scalar_tensor_tensor(
                hst[:, sl_h + t * 512:sl_h + (t + 1) * 512], psu[:, :],
                seffu[:, gc:gc + 1], sgs[:, (t % 2) * 512:(t % 2) * 512 + 512],
                OP.mult, OP.mult))
            v_h = op(VE, lambda e, t=t, gc=gc, sl_h=sl_h: e.tensor_reduce(
                ach[:, gc * 4 + t:gc * 4 + t + 1],
                hst[:, sl_h + t * 512:sl_h + (t + 1) * 512], AX, OP.max,
                apply_absolute_value=True), VV)
            gu.release(VV, v_h)
            g_sgs[t % 2].release(VV, v_h)
        wt(SY, VV, v_h)
        l_sp = dma(SY, S_SP[gc % 4], hh[gc, :, :], hst[:, sl_h:sl_h + 2048])
        hst_slot[gc % 4].release(S_SP[gc % 4], l_sp)
    wt(VE, VV, VV.v)
    v_ah = op(VE, lambda e: e.reduce_max(pp[:, 13:14], ach[:, :], AX), VV)
    wt(GP, VV, v_ah)
    g3 = dma(GP, S_CTL, rowx[0:1, :], pp[:, 13:14])
    wt(VE, S_CTL, g3)
    v_ahc = op(VE, lambda e: e.reduce_max(cells[0:1, 8:9], rowx[0:1, :], AX), VV)
    wt(GP, VV, v_ahc)
    dma(GP, S_CTL, arh_in.ap(), cells[0:1, 8:9])
    op(GP, lambda e: e.collective_compute(
        "AllReduce", OP.max, replica_groups=[list(range(NC))],
        ins=[arh_in.ap().opt()], outs=[arh_out.ap().opt()]), CC[5])

    barrier()

    wt(GP, CC[3], 1)
    wt(GP, CC[4], 1)
    g_es = dma(GP, S_CTL, cells[0:1, 6:7], ars_out.ap())
    wt(SC, S_CTL, g_es)
    s_es = op(SC, lambda e: e.activation(cells[0:1, 6:7], cells[0:1, 6:7], AF.Exp, scale=-1.0), SS)
    wt(VE, SS, s_es)
    v_res = ve_step(lambda e: e.tensor_scalar(cells[0:1, 7:8], cells[0:1, 6:7], RATIO, None, OP.mult))
    v_resp = repl128(cells[0:1, 7:8], pp[:, 4:5])

    g_upst = Guard()
    g_vsr = Guard()
    ct_last = None
    for h in range(H):
        g_upst.acquire(GP)
        g_u = dma(GP, S_CTL, upst[:, 0:130], arp_out.ap()[0:128, 2 * h * 65:(2 * h + 2) * 65])
        g_vsr.acquire(GP)
        g_v = dma(GP, S_CTL, vsr[0:1, 0:65], arp_out.ap()[128:129, h * 65:(h + 1) * 65])
        gPF.acquire(TE)
        wt(TE, S_CTL, g_v)
        t_rv = op(TE, lambda e: e.matmul(PF[:, 0:65], onesepsf[0:1, :], vsr[0:1, 0:65],
                                         start=True, stop=True), TT)
        g_vsr.release(TT, t_rv)
        wt(VE, TT, t_rv)
        wt(VE, S_CTL, g_v)
        if h == 0:
            wt(VE, VV, v_resp)
        op(VE, lambda e, h=h: e.scalar_tensor_tensor(
            ct_all[:, (2 * h) * 65:(2 * h) * 65 + 65], upst[:, 0:65], pp[:, 4:5],
            PF[:, 0:65], OP.mult, OP.add))
        v_ct = op(VE, lambda e, h=h: e.scalar_tensor_tensor(
            ct_all[:, (2 * h + 1) * 65:(2 * h + 1) * 65 + 65], upst[:, 65:130], pp[:, 4:5],
            PF[:, 0:65], OP.mult, OP.add), VV)
        g_upst.release(VV, v_ct)
        gPF.release(VV, v_ct)
        gPG.acquire(TE)
        wt(TE, VV, v_ct)
        op(TE, lambda e, h=h: e.matmul(PG[0:1, 0:65], onescb[:, 0:1],
                                       ct_all[:, (2 * h) * 65:(2 * h) * 65 + 65],
                                       start=True, stop=False))
        t_cs = op(TE, lambda e, h=h: e.matmul(PG[0:1, 0:65], onescb[:, 0:1],
                                              ct_all[:, (2 * h + 1) * 65:(2 * h + 1) * 65 + 65],
                                              start=False, stop=True), TT)
        wt(SC, TT, t_cs)
        ct_last = op(SC, lambda e, h=h: e.activation(csr[0:1, h * 65:(h + 1) * 65],
                                                     PG[0:1, 0:65], AF.Copy, scale=EPS), SS)
        gPG.release(SS, ct_last)

    qsq_slot = [Guard(), Guard()]
    g_stq = Guard()
    g_eq = Guard()
    g_dencol = Guard()
    g_densb = Guard()
    g_idrow = Guard()
    g_ibf = Guard()
    g_isb = [Guard(), Guard()]
    psA_m4 = {0: (PD, gPD), 1: (PE, gPE), 2: (PA, gPA), 3: (PB, gPB)}
    g_stk = [Guard(), Guard()]
    g_nbh = Guard()
    g_nbl = Guard()
    g_ntr2 = Guard()
    g_pt4 = Guard()
    g_aqs = [Guard(), Guard()]
    g_idr = [Guard(), Guard()]
    psA_m = {0: (PD, gPD), 1: (PE, gPE)}
    for hc in range(ECH):
        qsq_slot[hc % 2].acquire(SC)
        s_qsq = op(SC, lambda e, hc=hc: e.activation(
            ksq[:, (hc % 2) * 2048:(hc % 2) * 2048 + 2048],
            a2[:, hc * 2048:(hc + 1) * 2048], AF.Square), SS)
        for hh_i in range(2):
            h = 2 * hc + hh_i
            hp = hh_i * 64
            gPC.acquire(TE)
            wt(TE, SS, s_qsq)
            for t in range(TCH):
                tm_d = op(TE, lambda e, hc=hc, hp=hp, t=t: e.matmul(
                    PC[:, t:t + 1],
                    ksq[hp:hp + 64, (hc % 2) * 2048 + t * 128:(hc % 2) * 2048 + (t + 1) * 128],
                    onesn16[hp:hp + 64, 0:1], start=True, stop=True),
                    TT if t == TCH - 1 else None)
            if hh_i == 1:
                qsq_slot[hc % 2].release(TT, tm_d)
            g_stq.acquire(VE)
            for t in range(TCH):
                ps, g = psA_m4[t % 4]
                g.acquire(TE)
                tm_dd = op(TE, lambda e, ps=ps, hc=hc, hp=hp, t=t: e.matmul(
                    ps[:, 0:256],
                    a2[hp:hp + 64, hc * 2048 + t * 128:hc * 2048 + (t + 1) * 128],
                    projs[hp:hp + 64, :], start=True, stop=True), TT)
                wt(VE, TT, tm_dd)
                v_rm = op(VE, lambda e, ps=ps, t=t: e.reduce_max(
                    stq[:, t:t + 1], ps[:, 0:256], AX), VV)
                g.release(VV, v_rm)
            wt(VE, TT, tm_d)
            wt(VE, VV, VV.v)
            v_nb = op(VE, lambda e: e.scalar_tensor_tensor(
                negb[:, :], PC[:, 0:16], 1.0, stq[:, :], OP.mult, OP.subtract), VV)
            gPC.release(VV, v_nb)
            g_stq.release(VV, v_nb)
            wt(SC, VV, v_nb)
            g_nbh.acquire(SC)
            s_hi = op(SC, lambda e: e.activation(nbh[:, :], negb[:, :], AF.Copy), SS)
            g_nbl.acquire(VE)
            wt(VE, SS, s_hi)
            v_lo = op(VE, lambda e: e.tensor_tensor(nbl[:, :], negb[:, :], nbh[:, :], OP.subtract), VV)
            g_pt4.acquire(TE)
            wt(TE, SS, s_hi)
            wt(TE, VV, v_lo)
            op(TE, lambda e: e.transpose(PT[0:16, 0:128], nbh[:, :], idbf[:, :]))
            t_lo = op(TE, lambda e: e.transpose(PT[0:16, 128:256], nbl[:, :], idbf[:, :]), TT)
            g_nbh.release(TT, t_lo)
            g_nbl.release(TT, t_lo)
            g_ntr2.acquire(VE)
            wt(VE, TT, t_lo)
            v_ev = op(VE, lambda e: e.tensor_copy(ntr2[:, :], PT[0:16, 0:256]), VV)
            g_pt4.release(VV, v_ev)
            off2 = (h % 2) * 2048
            g_stk[h % 2].acquire(GP)
            wt(GP, VV, v_ev)
            dma(GP, S_CTL, stk2[0:1, off2:off2 + 2048], ntr2[:, 0:128])
            g_lo = dma(GP, S_CTL, stk2[1:2, off2:off2 + 2048], ntr2[:, 128:256])
            g_ntr2.release(S_CTL, g_lo)
            wt(TE, S_CTL, g_lo)
            g_eq.acquire(SC)
            for mc in range(2):
                for t in range(T4):
                    ps, g = psq[(mc * T4 + t) % 2]
                    g.acquire(TE)
                    op(TE, lambda e, ps=ps, hp=hp, mc=mc, t=t, hc=hc: e.matmul(
                        ps[:, :], projs[hp:hp + 64, mc * 128:(mc + 1) * 128],
                        a2[hp:hp + 64, hc * 2048 + t * 512:hc * 2048 + (t + 1) * 512],
                        start=True, stop=False))
                    tm_q = op(TE, lambda e, ps=ps, t=t, off2=off2: e.matmul(
                        ps[:, :], ones2b[0:2, :],
                        stk2[0:2, off2 + t * 512:off2 + (t + 1) * 512],
                        start=False, stop=True), TT)
                    if mc == 1 and t == T4 - 1:
                        g_stk[h % 2].release(TT, tm_q)
                    wt(SC, TT, tm_q)
                    s_eq = op(SC, lambda e, ps=ps, mc=mc, t=t: e.activation(
                        eq[:, mc * 2048 + t * 512:mc * 2048 + (t + 1) * 512],
                        ps[:, :], AF.Exp), SS)
                    g.release(SS, s_eq)
            if h == 0:
                wt(TE, SS, ct_last)
            wt(TE, SS, s_eq)
            psA_l = []
            for t in range(T4):
                psA, gA = psA_m4[t]
                psA_l.append((psA, gA))
                gA.acquire(TE)
                op(TE, lambda e, psA=psA, h=h, t=t: e.matmul(
                    psA[0:65, :], ct_all[:, (2 * h) * 65:(2 * h) * 65 + 65],
                    eq[:, t * 512:(t + 1) * 512], start=True, stop=False))
                op(TE, lambda e, psA=psA, h=h, t=t: e.matmul(
                    psA[0:65, :], ct_all[:, (2 * h + 1) * 65:(2 * h + 1) * 65 + 65],
                    eq[:, 2048 + t * 512:2048 + (t + 1) * 512], start=False, stop=False))
                tm_a = op(TE, lambda e, psA=psA, h=h: e.matmul(
                    psA[0:65, :], csr[0:1, h * 65:(h + 1) * 65], ones512b[0:1, :],
                    start=False, stop=True), TT)
                if t == T4 - 1:
                    g_eq.release(TT, tm_a)
                if t == 0:
                    g_densb.acquire(SC)
                wt(SC, TT, tm_a)
                s_dr = op(SC, lambda e, psA=psA, t=t: e.copy(
                    densb[64:65, t * 512:(t + 1) * 512], psA[64:65, :]), SS)
                gA.release(SS, s_dr)
            g_ibf.acquire(SC)
            wt(SC, SS, s_dr)
            s_ln2 = op(SC, lambda e: e.activation(lnrow[64:65, :], densb[64:65, :], AF.Ln), SS)
            wt(VE, SS, s_ln2)
            v_hop = op(VE, lambda e: e.tensor_copy(densb[64:65, :], lnrow[64:65, :]), VV)
            wt(SC, VV, v_hop)
            s_ib = op(SC, lambda e: e.activation(ibf[64:65, :], densb[64:65, :],
                                                 AF.Exp, scale=-1.0), SS)
            g_densb.release(SS, s_ib)
            for t in range(T4):
                psA, gA = psA_l[t]
                gPF.acquire(TE)
                if t == 0:
                    wt(TE, SS, s_ib)
                t_pi = op(TE, lambda e, t=t: e.matmul(
                    PF[0:64, :], ones65[64:65, 0:64], ibf[64:65, t * 512:(t + 1) * 512],
                    start=True, stop=True), TT)
                if t == T4 - 1:
                    g_ibf.release(TT, t_pi)
                g_isb[t % 2].acquire(VE)
                wt(VE, TT, t_pi)
                v_ie = op(VE, lambda e, t=t: e.tensor_copy(
                    aqs[0:64, (t % 2) * 512:(t % 2) * 512 + 512], PF[0:64, :]), VV)
                gPF.release(VV, v_ie)
                v_at = op(VE, lambda e, psA=psA, hp=hp, hc=hc, t=t: e.tensor_tensor(
                    a4[hp:hp + 64, hc * 2048 + t * 512:hc * 2048 + (t + 1) * 512],
                    psA[0:64, :], aqs[0:64, (t % 2) * 512:(t % 2) * 512 + 512], OP.mult), VV)
                gA.release(VV, v_at)
                g_isb[t % 2].release(VV, v_at)

    barrier()

    wt(GP, CC[5], 1)
    g_ah = dma(GP, S_CTL, cells[0:1, 8:9], arh_out.ap())
    wt(SC, S_CTL, g_ah)
    s_sigd = op(SC, lambda e: e.activation(cells[0:1, 9:10], cells[0:1, 13:14], AF.Sigmoid), SS)
    wt(VE, SS, s_sigd)
    ve_step(lambda e: e.tensor_tensor(cells[0:1, 10:11], cells[0:1, 9:10], cells[0:1, 8:9], OP.mult))
    ve_step(lambda e: e.tensor_scalar(cells[0:1, 10:11], cells[0:1, 10:11], 1.0 / 127.0, None, OP.mult))
    v_sh = ve_step(lambda e: e.reciprocal(cells[0:1, 11:12], cells[0:1, 10:11]))
    repl128(cells[0:1, 11:12], pp[:, 2:3])
    v_shp = repl128(cells[0:1, 10:11], pp[:, 3:4])

    def hq_dst(gc):
        if gc < 8:
            return a1[:, gc * 2048:(gc + 1) * 2048]
        if gc < 16:
            return a2[:, (gc - 8) * 2048:(gc - 7) * 2048]
        return a3[:, (gc - 16) * 2048:(gc - 15) * 2048]

    h_slot = [None, None]
    g_i32s5 = Guard()
    for gc in range(ICH):
        sl = (gc % 2) * 2048
        if h_slot[gc % 2] is not None:
            wt(SY, VV, h_slot[gc % 2])
        lh = dma(SY, S_X[gc % 2], hst[:, sl:sl + 2048], hh[gc, :, :])
        wt(VE, S_X[gc % 2], lh)
        op(VE, lambda e, sl=sl: e.tensor_scalar(
            hst[:, sl:sl + 2048], hst[:, sl:sl + 2048], pp[:, 2:3], QP, OP.mult, OP.min))
        g_i32s5.acquire(VE)
        v_hi = op(VE, lambda e, sl=sl: e.tensor_scalar(
            i32s5[:, :], hst[:, sl:sl + 2048], QN, None, OP.max), VV)
        h_slot[gc % 2] = v_hi
        wt(SC, VV, v_hi)
        s_hq = op(SC, lambda e, gc=gc: e.activation(hq_dst(gc), i32s5[:, :], AF.Copy), SS)
        g_i32s5.release(SS, s_hq)

    barrier()

    agw_d = agw_out.ap()[:, 2 * NW_G:NW].rearrange("r (p c) -> r p c", p=128)
    psW_m = {0: (PA, gPA), 1: (PB, gPB)}
    psD_m = {0: (PC, gPC), 1: (PD, gPD)}
    g_wobuf = Guard()
    g_dbuf = Guard()
    g_xst = [Guard(), Guard()]
    g_tmp = [Guard(), Guard()]
    g_yst = [Guard(), Guard()]
    for eo2 in range(2):
        g_wobuf.acquire(GP)
        for ei in range(ECH):
            lwo = dma(GP, S_WO, wobuf[:, ei * 512:(ei + 1) * 512],
                      woT_h[ei * 128:(ei + 1) * 128, eo2 * 512:(eo2 + 1) * 512])
        g_dbuf.acquire(SY)
        for ic in range(ICH):
            src = agw_d[eo2 * 4:(eo2 + 1) * 4, :, ic * 128:(ic + 1) * 128] \
                .rearrange("r p c -> p r c")
            ldb = dma(SY, S_DB, dbuf[:, ic * 512:(ic + 1) * 512]
                      .rearrange("p (r c) -> p r c", r=4), src)
        wt(TE, S_DB, ldb)
        wt(TE, S_WO, lwo)
        t_last = None
        for t in range(TCH):
            psW, gW = psW_m[t % 2]
            psD, gD = psD_m[t % 2]
            g_xst[t % 2].acquire(SY)
            lx = dma(SY, S_X[t % 2], xst[:, (t % 2) * 512:(t % 2) * 512 + 512],
                     x_h[t * 128:(t + 1) * 128, eo2 * 512:(eo2 + 1) * 512])
            gD.acquire(TE)
            for ic in range(ICH):
                op(TE, lambda e, psD=psD, ic=ic, t=t: e.matmul(
                    psD[:, :], hq_dst(ic)[:, t * 128:(t + 1) * 128],
                    dbuf[:, ic * 512:(ic + 1) * 512],
                    start=(ic == 0), stop=(ic == ICH - 1)))
            gW.acquire(TE)
            for ei in range(ECH):
                op(TE, lambda e, psW=psW, ei=ei, t=t: e.matmul(
                    psW[:, :], a4[:, ei * 2048 + t * 128:ei * 2048 + (t + 1) * 128],
                    wobuf[:, ei * 512:(ei + 1) * 512],
                    start=(ei == 0), stop=False))
            t_last = op(TE, lambda e, psW=psW, eo2=eo2: e.matmul(
                psW[:, :], ones1b[0:1, :], borb[0:1, eo2 * 512:(eo2 + 1) * 512],
                start=False, stop=True), TT)
            g_tmp[t % 2].acquire(SC)
            wt(SC, TT, t_last)
            s_d = op(SC, lambda e, psD=psD, t=t: e.activation(
                tmpst[:, (t % 2) * 512:(t % 2) * 512 + 512], psD[:, :],
                AF.Copy, scale=pp[:, 3:4]), SS)
            gD.release(SS, s_d)
            g_yst[t % 2].acquire(VE)
            wt(VE, SS, s_d)
            op(VE, lambda e, psW=psW, t=t: e.tensor_tensor(
                yst[:, (t % 2) * 512:(t % 2) * 512 + 512],
                psW[:, :], tmpst[:, (t % 2) * 512:(t % 2) * 512 + 512], OP.add))
            wt(VE, S_X[t % 2], lx)
            v_y = op(VE, lambda e, t=t: e.tensor_tensor(
                yst[:, (t % 2) * 512:(t % 2) * 512 + 512],
                yst[:, (t % 2) * 512:(t % 2) * 512 + 512],
                xst[:, (t % 2) * 512:(t % 2) * 512 + 512], OP.add), VV)
            gW.release(VV, v_y)
            g_tmp[t % 2].release(VV, v_y)
            g_xst[t % 2].release(VV, v_y)
            wt(SY, VV, v_y)
            o_d = dma(SY, S_O[t % 2], y_h[t * 128:(t + 1) * 128, eo2 * 512:(eo2 + 1) * 512],
                      yst[:, (t % 2) * 512:(t % 2) * 512 + 512])
            g_yst[t % 2].release(S_O[t % 2], o_d)
        g_wobuf.release(TT, t_last)
        g_dbuf.release(TT, t_last)
    wt(SY, S_O[0], S_O[0].v)
    wt(SY, S_O[1], S_O[1].v)
    wt(GP, S_O[0], S_O[0].v)
    wt(GP, S_O[1], S_O[1].v)
    wt(GP, S_CTL, S_CTL.v)

    with es, nc.Block() as block:
        for name, dec in ((SY, block.sync), (GP, block.gpsimd), (VE, block.vector),
                          (SC, block.scalar), (TE, block.tensor)):
            def runner(e, name=name):
                for eng, fn in PROG:
                    if eng == name:
                        fn(e)
            dec(runner)
    return nc


def kernel(**inputs):
    import ml_dtypes
    x = np.asarray(inputs["L0_attn_x"], dtype=np.float32)
    wq = np.asarray(inputs["L0_attn_Wq"], dtype=np.float32)
    wk = np.asarray(inputs["L0_attn_Wk"], dtype=np.float32)
    wv = np.asarray(inputs["L0_attn_Wv"], dtype=np.float32)
    wo = np.asarray(inputs["L0_attn_Wo"], dtype=np.float32)
    bq = np.asarray(inputs["L0_attn_bq"], dtype=np.float32)
    bk = np.asarray(inputs["L0_attn_bk"], dtype=np.float32)
    bv = np.asarray(inputs["L0_attn_bv"], dtype=np.float32)
    bo = np.asarray(inputs["L0_attn_bo"], dtype=np.float32)
    proj = np.asarray(inputs["L0_attn_proj"], dtype=np.float32)
    clipx = np.asarray(inputs["L0_ffn_clip_x"], dtype=np.float32)
    clipd = np.asarray(inputs["L0_ffn_clip_down"], dtype=np.float32)
    wgate = np.asarray(inputs["L0_ffn_w_gate"], dtype=np.float32)
    wup = np.asarray(inputs["L0_ffn_w_up"], dtype=np.float32)
    wdown = np.asarray(inputs["L0_ffn_w_down"], dtype=np.float32)

    shared = {
        "wqT": np.ascontiguousarray(wq.T), "wkT": np.ascontiguousarray(wk.T),
        "wvT": np.ascontiguousarray(wv.T), "woT": np.ascontiguousarray(wo.T),
        "bqr": bq.reshape(1, E).copy(), "bkr": bk.reshape(1, E).copy(),
        "bvr": bv.reshape(1, E).copy(), "bor": bo.reshape(1, E).copy(),
        "projT": np.ascontiguousarray(proj.T),
        "clipx": clipx.reshape(1, 1).copy(), "clipd": clipd.reshape(1, 1).copy(),
        "idbf": np.eye(128, dtype=ml_dtypes.bfloat16),
        "idf": np.eye(128, dtype=np.float32),
    }
    in_maps = []
    for c in range(NC):
        b_i, hf = c // 2, c % 2
        m = dict(shared)
        m["x"] = np.ascontiguousarray(x[b_i, hf * TOK:(hf + 1) * TOK, :])
        lo = 3 * c * 128
        for nm, w in (("wg_nat", wgate), ("wu_nat", wup)):
            sl = np.ones((ISLOT * 128, E), np.float32)
            real = w[lo:lo + ISLOT * 128, :]
            sl[:real.shape[0]] = real
            m[nm] = sl
        m["wd_nat"] = np.ascontiguousarray(wdown[c * 128:(c + 1) * 128, :])
        in_maps.append(m)

    nc = build_nc()
    res = run_bass_kernel_spmd(nc, in_maps, core_ids=list(range(NC)))
    out = np.empty((B, NSEQ, E), np.float32)
    for c in range(NC):
        b_i, hf = c // 2, c % 2
        out[b_i, hf * TOK:(hf + 1) * TOK, :] = res.results[c]["y"]
    return out


# version 13
# speedup vs baseline: 1.3556x; 1.0622x over previous
import numpy as np
import contextlib
import concourse.bass as bass
import concourse.mybir as mybir
from concourse.alu_op_type import AluOpType as OP
from concourse.bass_utils import run_bass_kernel_spmd

F32 = mybir.dt.float32
BF16 = mybir.dt.bfloat16
I32 = mybir.dt.int32
AF = mybir.ActivationFunctionType
AX = mybir.AxisListType.X

B, NSEQ, E = 4, 4096, 1024
TOK = 2048
TCH = 16
T4 = 4
ECH = 8
H, D, M = 16, 64, 256
IFF, ICH = 2816, 22
ISLOT = 3
NC = 8
RATIO = 0.0625
EPS = 1e-4
SQD = float(64.0 ** -0.25)
LN_EPS = 1e-5
QP = 127.49995
QN = -128.49995
NW_G = 1024 * ISLOT * 128
NW_D = 2816 * 128
NW = 2 * NW_G + NW_D

A1, A2, A3, A4 = 0, 32768, 65536, 98304
R0 = 131072
S0 = R0 + 61440
TOTAL = 212800

SY, GP, VE, SC, TE = "sync", "gpsimd", "vector", "scalar", "tensor"


def build_nc():
    nc = bass.Bass("TRN2", target_bir_lowering=False, detect_race_conditions=False)

    d_in = lambda n, s, dt=F32: nc.dram_tensor(n, list(s), dt, kind="ExternalInput").ap()
    x_h = d_in("x", (TOK, E))
    wqT_h = d_in("wqT", (E, E)); wkT_h = d_in("wkT", (E, E))
    wvT_h = d_in("wvT", (E, E)); woT_h = d_in("woT", (E, E))
    bqr_h = d_in("bqr", (1, E)); bkr_h = d_in("bkr", (1, E))
    bvr_h = d_in("bvr", (1, E)); bor_h = d_in("bor", (1, E))
    projT_h = d_in("projT", (D, M))
    clipx_h = d_in("clipx", (1, 1)); clipd_h = d_in("clipd", (1, 1))
    wg_h = d_in("wg_nat", (ISLOT * 128, E))
    wu_h = d_in("wu_nat", (ISLOT * 128, E))
    wd_h = d_in("wd_nat", (128, IFF))
    idbf_h = d_in("idbf", (128, 128), BF16)
    idf_h = d_in("idf", (128, 128), F32)
    y_h = nc.dram_tensor("y", [TOK, E], F32, kind="ExternalOutput").ap()


    xn32h = nc.dram_tensor("xn32h", [TOK, E], F32).ap()
    hh = nc.dram_tensor("hh", [ICH, 128, TOK], F32).ap()
    agw_in = nc.dram_tensor("agw_in", [1, NW], BF16)
    agw_out = nc.dram_tensor("agw_out", [NC, NW], BF16)
    ags_in = nc.dram_tensor("ags_in", [1, 768], F32)
    ags_out = nc.dram_tensor("ags_out", [NC, 768], F32)
    arx_in = nc.dram_tensor("arx_in", [1, 1], F32)
    arx_out = nc.dram_tensor("arx_out", [1, 1], F32)
    ars_in = nc.dram_tensor("ars_in", [1, 1], F32)
    ars_out = nc.dram_tensor("ars_out", [1, 1], F32)
    arh_in = nc.dram_tensor("arh_in", [1, 1], F32)
    arh_out = nc.dram_tensor("arh_out", [1, 1], F32)
    arp_in = nc.dram_tensor("arp_in", [129, 2080], F32)
    arp_out = nc.dram_tensor("arp_out", [129, 2080], F32)


    es = contextlib.ExitStack()
    big = es.enter_context(nc.sbuf_tensor("big", [128, TOTAL], mybir.dt.uint8))
    base = nc.lookup_mloc(big).addr

    def place(name, shape, dtype, off):
        per_part = int(np.prod(shape[1:])) * mybir.dt.size(dtype)
        assert off % 32 == 0 and off + per_part <= TOTAL, (name, off, per_part)
        h = nc.virtual_sbuf_tensor(name, list(shape), dtype)
        nc._tensor(h.name, h.shape, h.dtype, type="SB")
        mloc = nc.lookup_mloc(h)
        mloc.allocated = True
        mloc.addr = base + off
        return h

    a1 = place("a1", [128, 16384], BF16, A1)
    a2 = place("a2", [128, 16384], BF16, A2)
    a3 = place("a3", [128, 16384], BF16, A3)
    a4 = place("a4", [128, 16384], BF16, A4)

    msc = place("msc", [128, 4096], F32, R0)
    xbf = place("xbf", [128, 2048], BF16, R0 + 16384)
    wi32 = place("wi32", [128, 1024], I32, R0 + 20480)
    ttile = place("ttile", [128, 1024], BF16, R0 + 24576)
    vw = place("vw", [128, 8192], BF16, R0)
    qkw = place("qkw", [128, 2048], BF16, R0 + 39936)
    wstage = place("wstage", [128, 2816], F32, R0 + 26624)
    wxbf = place("wxbf", [128, 1024], BF16, R0 + 37888)
    ek = place("ek", [128, 4096], BF16, R0 + 20480)
    ksq = place("ksq", [128, 4096], BF16, R0 + 28672)
    xnr = place("xnr", [128, 2048], F32, R0)
    i32s = place("i32s", [128, 2048], I32, R0 + 8192)
    xbf2 = place("xbf2", [128, 1024], BF16, R0 + 16384)
    hst = place("hst", [128, 8192], F32, R0)
    gub = place("gub", [128, 8192], BF16, R0 + 32768)
    sgs = place("sgs", [128, 1024], F32, R0 + 49152)
    eq = place("eq", [128, 4096], BF16, R0 + 8192)
    aqs = place("aqs", [128, 1024], BF16, R0 + 40960)
    stk2 = place("stk2", [2, 4096], BF16, R0 + 43008)
    lnrow = place("lnrow", [65, 2048], F32, R0 + 16384)
    densb = place("densb", [65, 2048], F32, R0 + 51200)
    ibf = place("ibf", [65, 2048], BF16, R0 + 24576)
    i32s5 = place("i32s5", [128, 2048], I32, R0 + 16384)
    dbuf = place("dbuf", [128, 11264], BF16, R0)
    wobuf = place("wobuf", [128, 4096], BF16, R0 + 22528)
    xst = place("xst", [128, 1024], F32, R0 + 30720)
    tmpst = place("tmpst", [128, 1024], F32, R0 + 34816)
    yst = place("yst", [128, 1024], F32, R0 + 38912)
    ust = place("ust", [128, 130], F32, R0 + 36864)
    rm_all = place("rm_all", [128, 256], F32, R0 + 37440)
    upst = place("upst", [128, 520], F32, R0 + 0)
    ct_all = place("ct_all", [128, 2080], BF16, R0 + 2080)
    vsr_all = place("vsr_all", [1, 1040], F32, R0 + 16384)
    anum0 = place("anum0", [64, 2048], BF16, R0 + 20544)
    anum1 = place("anum1", [64, 2048], BF16, R0 + 36864)
    ibf1 = place("ibf1", [65, 2048], BF16, R0 + 43008)

    off = [S0]
    def sp(name, shape, dtype):
        per_part = int(np.prod(shape[1:])) * mybir.dt.size(dtype)
        h = place(name, shape, dtype, off[0])
        off[0] += (per_part + 63) // 64 * 64
        return h
    projs = sp("projs", [128, 256], BF16)
    idbf = sp("idbf_s", [128, 128], BF16)
    idf = sp("idf_s", [128, 128], F32)
    bqrb = sp("bqrb", [1, 1024], BF16)
    bkrb = sp("bkrb", [1, 1024], BF16)
    stats = sp("stats", [128, 96], F32)
    qstats = sp("qstats", [128, 4], F32)
    acx = sp("acx", [128, 16], F32)
    ach = sp("ach", [128, 88], F32)
    seffg = sp("seffg", [128, 22], F32)
    seffu = sp("seffu", [128, 22], F32)
    negb = sp("negb", [128, 32], F32)
    nbh = sp("nbh", [128, 16], BF16)
    nbl = sp("nbl", [128, 16], BF16)
    ntr2 = sp("ntr2", [16, 256], BF16)
    ones2b = sp("ones2b", [2, 128], BF16)
    stq = sp("stq", [128, 16], F32)
    dcols = sp("dcols", [128, 16], F32)
    negtr = sp("negtr", [16, 128], F32)
    pp = sp("pp", [128, 16], F32)
    dencol = sp("dencol", [128, 16], F32)
    rme = sp("rme", [128, 16], F32)
    ones1b = sp("ones1b", [1, 128], BF16)
    onescb = sp("onescb", [128, 1], BF16)
    onesn16 = sp("onesn16", [128, 1], BF16)
    onesepsf = sp("onesepsf", [1, 128], F32)
    ones64f = sp("ones64f", [1, 64], F32)
    ones512b = sp("ones512b", [1, 512], BF16)
    ones65 = sp("ones65", [65, 128], BF16)
    csr = sp("csr", [1, 1040], BF16)
    rcell = sp("rcell", [1, 16], BF16)
    vss = sp("vss", [1, 80], F32)
    vsr = sp("vsr", [1, 80], F32)
    rowx = sp("rowx", [1, 128], F32)
    cells = sp("cells", [1, 32], F32)
    epscol = sp("epscol", [128, 1], F32)
    bvrb = sp("bvrb", [1, 1024], BF16)
    borb = sp("borb", [1, 1024], BF16)
    prf32 = place("prf32", [128, 256], F32, R0 + 49152)
    assert off[0] <= TOTAL, off[0]

    PA = es.enter_context(nc.psum_tensor("PA", [128, 512], F32))
    PB = es.enter_context(nc.psum_tensor("PB", [128, 512], F32))
    PC = es.enter_context(nc.psum_tensor("PC", [128, 512], F32))
    PD = es.enter_context(nc.psum_tensor("PD", [128, 512], F32))
    PE = es.enter_context(nc.psum_tensor("PE", [128, 512], F32))
    PF = es.enter_context(nc.psum_tensor("PF", [128, 512], F32))
    PG = es.enter_context(nc.psum_tensor("PG", [128, 512], F32))
    PT = es.enter_context(nc.psum_tensor("PT", [128, 1024], BF16))

    class Sem:
        def __init__(self, name):
            self.h = es.enter_context(nc.semaphore(name))
            self.v = 0
    LX, LG, VV, SS, TT = (Sem(n) for n in ("LX", "LG", "VV", "SS", "TT"))
    CC = [Sem(f"CC{i}") for i in range(6)]
    S_X = [Sem("SX0"), Sem("SX1")]
    S_SP = [Sem("SSP0"), Sem("SSP1"), Sem("SSP2"), Sem("SSP3")]
    S_G = [Sem("SG0"), Sem("SG1"), Sem("SG2"), Sem("SG3")]
    S_QK = [Sem("SQK0"), Sem("SQK1")]
    S_O = [Sem("SO0"), Sem("SO1")]
    S_DB = Sem("SDB")
    S_W = Sem("SW")
    S_VW = Sem("SVW")
    S_WO = Sem("SWO")
    S_AGW = Sem("SAGW")
    S_AGS = Sem("SAGS")
    S_ARP = Sem("SARP")
    S_CTL = Sem("SCTL")
    S_SEF = Sem("SSEF")
    BR = Sem("BR")
    SY_SEMS = [LX, S_X[0], S_X[1], S_SP[0], S_SP[1], S_SP[2], S_SP[3],
               S_G[0], S_G[1], S_G[2], S_G[3], S_O[0], S_O[1], S_DB, S_W]
    GP_SEMS = [LG, S_QK[0], S_QK[1], S_VW, S_WO, S_AGW, S_AGS, S_ARP, S_CTL, S_SEF]

    PROG = []
    def em(eng, fn):
        PROG.append((eng, fn))
    def wt(eng, sem, val):
        if val and val > 0:
            em(eng, lambda e, s=sem.h, v=val: e.wait_ge(s, v))
    def dma(eng, sem, out, in_):
        sem.v += 16
        em(eng, lambda e, o=out, i=in_, s=sem.h: e.dma_start(out=o, in_=i).then_inc(s, 16))
        return sem.v
    def op(eng, fn, sem=None):
        if sem is None:
            em(eng, fn)
            return None
        sem.v += 1
        em(eng, lambda e, f=fn, s=sem.h: f(e).then_inc(s, 1))
        return sem.v

    def ve_step(fn):
        v = op(VE, fn, VV)
        wt(VE, VV, v)
        return v

    class Guard:
        def __init__(self):
            self.conds = []
        def acquire(self, eng):
            for s, v in self.conds:
                wt(eng, s, v)
            self.conds = []
        def release(self, sem, val):
            if val:
                self.conds.append((sem, val))
    gPA, gPB, gPC, gPD, gPE, gPF, gPG, gPT = (Guard() for _ in range(8))

    br_round = [0]

    def barrier():
        for s in SY_SEMS:
            wt(SY, s, s.v)
        em(SY, lambda e: e.sem_inc(BR.h, 1)); BR.v += 1
        for s in GP_SEMS:
            wt(GP, s, s.v)
        em(GP, lambda e: e.sem_inc(BR.h, 1)); BR.v += 1
        br_round[0] = BR.v
        vals = [(BR, BR.v), (VV, VV.v), (SS, SS.v), (TT, TT.v)]
        for eng in (SY, GP, VE, SC, TE):
            for s, v in vals:
                wt(eng, s, v)

    op(VE, lambda e: e.memset(ones1b[:, :], 1.0))
    op(VE, lambda e: e.memset(ones2b[:, :], 1.0))
    op(VE, lambda e: e.memset(onescb[:, :], 1.0))
    op(VE, lambda e: e.memset(onesn16[:, :], -0.0625))
    op(VE, lambda e: e.memset(onesepsf[:, :], RATIO * EPS))
    op(VE, lambda e: e.memset(ones64f[:, :], 1.0))
    op(VE, lambda e: e.memset(ones512b[:, :], 1.0))
    op(VE, lambda e: e.memset(ones65[:, :], 1.0))
    op(VE, lambda e: e.memset(vss[0:1, 64:65], float(TOK)))
    op(VE, lambda e: e.memset(epscol[:, :], LN_EPS), VV)
    dma(SY, LX, idbf[:, :], idbf_h)
    dma(SY, LX, idf[:, :], idf_h)
    dma(SY, LX, msc[0:1, 2048:3072], bqr_h)
    dma(SY, LX, msc[0:1, 3072:4096], bkr_h)
    dma(SY, LX, prf32[0:64, :], projT_h)
    dma(SY, LX, prf32[64:128, :], projT_h)
    dma(SY, LX, cells[0:1, 12:13], clipx_h)
    dma(SY, LX, cells[0:1, 13:14], clipd_h)
    dma(SY, LX, msc[0:1, 0:1024], bvr_h)
    l_pre = dma(SY, LX, msc[0:1, 1024:2048], bor_h)
    wt(SC, LX, l_pre)
    op(SC, lambda e: e.activation(projs[0:64, :], prf32[0:64, :], AF.Copy, scale=SQD))
    op(SC, lambda e: e.activation(projs[64:128, :], prf32[64:128, :], AF.Copy, scale=SQD))
    op(SC, lambda e: e.activation(bvrb[:, :], msc[0:1, 0:1024], AF.Copy))
    op(SC, lambda e: e.activation(borb[:, :], msc[0:1, 1024:2048], AF.Copy))
    op(SC, lambda e: e.activation(bqrb[:, :], msc[0:1, 2048:3072], AF.Copy))
    s_pre = op(SC, lambda e: e.activation(bkrb[:, :], msc[0:1, 3072:4096], AF.Copy), SS)

    slot_free = [None, None]
    for c in range(TCH):
        sl = (c % 2) * 1024
        if slot_free[c % 2] is not None:
            vv_f, ss_f = slot_free[c % 2]
            wt(SY, VV, vv_f); wt(SY, SS, ss_f)
        if c == 0:
            wt(SY, SS, s_pre)
        lx = dma(SY, S_X[c % 2], msc[:, sl:sl + 1024], x_h[c * 128:(c + 1) * 128, :])
        wt(VE, S_X[c % 2], lx)
        v_r = op(VE, lambda e, sl=sl, c=c: e.reduce_sum(
            stats[:, c:c + 1], msc[:, sl:sl + 1024], AX), VV)
        wt(SC, S_X[c % 2], lx)
        s_r = op(SC, lambda e, sl=sl, c=c: e.activation(
            xbf[:, sl:sl + 1024], msc[:, sl:sl + 1024], AF.Square,
            accum_out=stats[:, 16 + c:17 + c]), SS)
        slot_free[c % 2] = (v_r, s_r)
    wt(VE, SS, SS.v)
    wt(VE, VV, VV.v)
    ve_step(lambda e: e.tensor_scalar(stats[:, 32:48], stats[:, 0:16], 1.0 / E, None, OP.mult))
    ve_step(lambda e: e.tensor_scalar(stats[:, 48:64], stats[:, 16:32], 1.0 / E, None, OP.mult))
    ve_step(lambda e: e.tensor_tensor(stats[:, 64:80], stats[:, 32:48], stats[:, 32:48], OP.mult))
    v_var = ve_step(lambda e: e.tensor_tensor(stats[:, 64:80], stats[:, 48:64], stats[:, 64:80], OP.subtract))
    wt(SC, VV, v_var)
    s_sd = op(SC, lambda e: e.activation(stats[:, 80:96], stats[:, 64:80], AF.Sqrt, bias=epscol[:, 0:1]), SS)
    wt(VE, SS, s_sd)
    v_rstd = ve_step(lambda e: e.reciprocal(stats[:, 80:96], stats[:, 80:96]))

    g_region = agw_in.ap()[0:1, 0:NW_G].rearrange("a (p c) -> (a p) c", p=128)
    u_region = agw_in.ap()[0:1, NW_G:2 * NW_G].rearrange("a (p c) -> (a p) c", p=128)
    d_region = agw_in.ap()[0:1, 2 * NW_G:NW].rearrange("a (p c) -> (a p) c", p=128)
    last_agw = 0
    g_ttile = Guard()
    g_qstats = Guard()
    g_wi32 = Guard()
    g_wxbf = Guard()
    wslot = [Guard(), Guard()]

    def quant_block(w_slice_ap, tile_dst_blk, is_down, s_col):
        nonlocal last_agw
        width = w_slice_ap.shape[1]
        g_qstats.acquire(VE)
        v_amx = op(VE, lambda e: e.tensor_reduce(qstats[:, 0:1], w_slice_ap, AX, OP.max,
                                                 apply_absolute_value=True), VV)
        wt(VE, VV, v_amx)
        ve_step(lambda e: e.tensor_scalar(qstats[:, 1:2], qstats[:, 0:1], 1.0 / 127.0,
                                          None, OP.mult))
        ve_step(lambda e: e.reciprocal(qstats[:, 2:3], qstats[:, 0:1]))
        v_is = ve_step(lambda e: e.tensor_scalar(qstats[:, 2:3], qstats[:, 2:3], 127.0,
                                                 None, OP.mult))
        if s_col is not None:
            wt(GP, VV, v_is)
            l_ags = dma(GP, S_AGS, ags_in.ap()[0:1, s_col:s_col + 128], qstats[:, 0:1])
            g_qstats.release(S_AGS, l_ags)
        nblk = width // 128
        for bb in range(0, nblk, 8):
            hi = min(bb + 8, nblk)
            w = (hi - bb) * 128
            g_wi32.acquire(VE)
            v_q = op(VE, lambda e, bb=bb, w=w: e.tensor_scalar(
                wi32[:, 0:w], w_slice_ap[:, bb * 128:bb * 128 + w],
                qstats[:, 2:3], None, OP.mult), VV)
            wt(SC, VV, v_q)
            g_wxbf.acquire(SC)
            if is_down:
                s_q = op(SC, lambda e, w=w: e.activation(
                    wxbf[:, 0:w], wi32[:, 0:w], AF.Copy, scale=qstats[:, 1:2]), SS)
            else:
                s_q = op(SC, lambda e, w=w: e.activation(
                    wxbf[:, 0:w], wi32[:, 0:w], AF.Copy), SS)
            g_wi32.release(SS, s_q)
            if is_down:
                g_qstats.release(SS, s_q)
            gPT.acquire(TE)
            wt(TE, SS, s_q)
            for j in range(hi - bb):
                t_tp = op(TE, lambda e, j=j: e.transpose(
                    PT[:, j * 128:(j + 1) * 128], wxbf[:, j * 128:(j + 1) * 128],
                    idbf[:, :]), TT if j == hi - bb - 1 else None)
            g_wxbf.release(TT, t_tp)
            g_ttile.acquire(VE)
            wt(VE, TT, t_tp)
            v_ev = op(VE, lambda e, w=w: e.tensor_copy(ttile[:, 0:w], PT[:, 0:w]), VV)
            gPT.release(VV, v_ev)
            wt(GP, VV, v_ev)
            last_agw = dma(GP, S_AGW, tile_dst_blk(bb, hi - bb),
                           ttile[:, 0:w].rearrange("p (j c) -> p j c", c=128))
            g_ttile.release(S_AGW, last_agw)

    def mk_dst(region, icl):
        return lambda bb, n: region[:, :].rearrange("p (j c) -> p j c", c=384)\
            [:, bb:bb + n, icl * 128:(icl + 1) * 128]
    p05_tasks = []
    for w_h, region, soff in ((wg_h, g_region, 0), (wu_h, u_region, 384)):
        for icl in range(ISLOT):
            p05_tasks.append((w_h[icl * 128:(icl + 1) * 128, :],
                              mk_dst(region, icl), False, soff + icl * 128))
    p05_tasks.append((wd_h, lambda bb, n: d_region[:, bb * 128:(bb + n) * 128]
                      .rearrange("p (j c) -> p j c", c=128), True, None))

    def p05_load(i):
        src, _, is_down, _ = p05_tasks[i]
        if is_down:
            wslot[0].acquire(SY)
            wslot[1].acquire(SY)
            return dma(SY, S_W, wstage[:, 0:2816], src)
        sl = (i % 2) * 1024
        wslot[i % 2].acquire(SY)
        return dma(SY, S_W, wstage[:, sl:sl + 1024], src)

    def p05_compute(i, lw):
        _, dst, is_down, s_col = p05_tasks[i]
        wt(VE, S_W, lw)
        if is_down:
            quant_block(wstage[:, 0:2816], dst, True, None)
            wslot[0].release(VV, VV.v)
            wslot[1].release(VV, VV.v)
        else:
            sl = (i % 2) * 1024
            quant_block(wstage[:, sl:sl + 1024], dst, False, s_col)
            wslot[i % 2].release(VV, VV.v)

    xslot_free = [None, None]
    nslot_free = [None, None]
    g_xbf = [Guard(), Guard()]
    wt(SY, S_X[0], S_X[0].v)
    wt(SY, S_X[1], S_X[1].v)
    for c in range(TCH):
        slx = 2048 + (c % 2) * 1024
        sln = (c % 2) * 1024
        if xslot_free[c % 2] is not None:
            wt(SY, VV, xslot_free[c % 2])
        lx = dma(SY, S_X[c % 2], msc[:, slx:slx + 1024], x_h[c * 128:(c + 1) * 128, :])
        wt(VE, S_X[c % 2], lx)
        if c == 0:
            wt(VE, VV, v_rstd)
        if nslot_free[c % 2] is not None:
            lx_sp, ss_ca = nslot_free[c % 2]
            wt(VE, S_SP[c % 2], lx_sp); wt(VE, SS, ss_ca)
        v_n = op(VE, lambda e, slx=slx, sln=sln, c=c: e.tensor_scalar(
            msc[:, sln:sln + 1024], msc[:, slx:slx + 1024],
            stats[:, 32 + c:33 + c], stats[:, 80 + c:81 + c],
            OP.subtract, OP.mult), VV)
        xslot_free[c % 2] = v_n
        v_am = op(VE, lambda e, sln=sln, c=c: e.tensor_reduce(
            acx[:, c:c + 1], msc[:, sln:sln + 1024], AX, OP.max,
            apply_absolute_value=True), VV)
        wt(SY, VV, v_am)
        lx_sp = dma(SY, S_SP[c % 2], xn32h[c * 128:(c + 1) * 128, :], msc[:, sln:sln + 1024])
        wt(SC, VV, v_am)
        g_xbf[c % 2].acquire(SC)
        s_ca = op(SC, lambda e, sln=sln: e.activation(
            xbf[:, sln:sln + 1024], msc[:, sln:sln + 1024], AF.Copy), SS)
        nslot_free[c % 2] = (lx_sp, s_ca)
        gPT.acquire(TE)
        wt(TE, SS, s_ca)
        for j in range(8):
            t_tp = op(TE, lambda e, sln=sln, j=j: e.transpose(
                PT[:, j * 128:(j + 1) * 128], xbf[:, sln + j * 128:sln + (j + 1) * 128],
                idbf[:, :]), TT if j == 7 else None)
        g_xbf[c % 2].release(TT, t_tp)
        wt(VE, TT, t_tp)
        dst = a1[:, :].rearrange("p (g t) -> p g t", g=8)[:, :, c * 128:(c + 1) * 128]
        v_ev = op(VE, lambda e, dst=dst: e.tensor_copy(
            dst, PT[:, :].rearrange("p (g t) -> p g t", g=8)), VV)
        gPT.release(VV, v_ev)
        if c % 2 == 0 and c // 2 < len(p05_tasks):
            p05_pending = p05_load(c // 2)
        elif c % 2 == 1 and c // 2 < len(p05_tasks):
            p05_compute(c // 2, p05_pending)
    vv_a1 = VV.v
    ss_msc = SS.v
    sp0, sp1 = S_SP[0].v, S_SP[1].v
    wt(VE, VV, VV.v)
    v_ax = op(VE, lambda e: e.tensor_reduce(pp[:, 15:16], acx[:, :], AX, OP.max), VV)
    wt(GP, VV, v_ax)
    g1 = dma(GP, S_CTL, rowx[0:1, :], pp[:, 15:16])
    wt(VE, S_CTL, g1)
    v_axc = op(VE, lambda e: e.reduce_max(cells[0:1, 0:1], rowx[0:1, :], AX), VV)
    wt(GP, VV, v_axc)
    dma(GP, S_CTL, arx_in.ap(), cells[0:1, 0:1])
    op(GP, lambda e: e.collective_compute(
        "AllReduce", OP.max, replica_groups=[list(range(NC))],
        ins=[arx_in.ap().opt()], outs=[arx_out.ap().opt()]), CC[0])
    wt(GP, S_AGW, last_agw)
    wt(GP, S_AGS, S_AGS.v)
    op(GP, lambda e: e.collective_compute(
        "AllGather", OP.bypass, replica_groups=[list(range(NC))],
        ins=[agw_in.ap().opt()], outs=[agw_out.ap().opt()]), CC[1])
    op(GP, lambda e: e.collective_compute(
        "AllGather", OP.bypass, replica_groups=[list(range(NC))],
        ins=[ags_in.ap().opt()], outs=[ags_out.ap().opt()]), CC[2])

    wt(GP, VV, vv_a1)
    wt(GP, SS, ss_msc)
    wt(GP, S_SP[0], sp0)
    wt(GP, S_SP[1], sp1)
    for ei in range(ECH):
        dma(GP, S_VW, vw[:, ei * 1024:(ei + 1) * 1024],
            wvT_h[ei * 128:(ei + 1) * 128, :])
    lg_vw = S_VW.v
    wt(TE, VV, vv_a1)
    qk_slot = [Guard(), Guard()]
    psq = {0: (PA, gPA), 1: (PB, gPB)}
    for wsel in range(2):
        dst = a2 if wsel == 0 else a3
        brow = bqrb if wsel == 0 else bkrb
        for eo in range(ECH):
            idx = wsel * ECH + eo
            sl = (idx % 2) * 1024
            qk_slot[idx % 2].acquire(GP)
            src = (wqT_h if wsel == 0 else wkT_h)[:, eo * 128:(eo + 1) * 128] \
                .rearrange("(a p) c -> p a c", p=128)
            lqk = dma(GP, S_QK[idx % 2], qkw[:, sl:sl + 1024].rearrange("p (a c) -> p a c", a=8), src)
            wt(TE, S_QK[idx % 2], lqk)
            for t in range(T4):
                ps, g = psq[t % 2]
                g.acquire(TE)
                for ei in range(ECH):
                    op(TE, lambda e, ps=ps, sl=sl, ei=ei, t=t: e.matmul(
                        ps[:, :], qkw[:, sl + ei * 128:sl + (ei + 1) * 128],
                        a1[:, ei * 2048 + t * 512:ei * 2048 + (t + 1) * 512],
                        start=(ei == 0), stop=False))
                tm = op(TE, lambda e, ps=ps, brow=brow, eo=eo: e.matmul(
                    ps[:, :], brow[0:1, eo * 128:(eo + 1) * 128], ones512b[0:1, :],
                    start=False, stop=True), TT)
                if t == T4 - 1:
                    qk_slot[idx % 2].release(TT, tm)
                wt(SC, TT, tm)
                s_ev = op(SC, lambda e, ps=ps, dst=dst, eo=eo, t=t:
                          e.activation(dst[:, eo * 2048 + t * 512:eo * 2048 + (t + 1) * 512],
                                       ps[:, :], AF.Copy), SS)
                g.release(SS, s_ev)
    wt(TE, S_VW, lg_vw)
    for t in range(TCH):
        for eo2 in range(2):
            ps, g = psq[(t * 2 + eo2) % 2]
            g.acquire(TE)
            for ei in range(ECH):
                op(TE, lambda e, ps=ps, t=t, eo2=eo2, ei=ei: e.matmul(
                    ps[:, :], a1[:, ei * 2048 + t * 128:ei * 2048 + (t + 1) * 128],
                    vw[:, ei * 1024 + eo2 * 512:ei * 1024 + (eo2 + 1) * 512],
                    start=(ei == 0), stop=False))
            tm = op(TE, lambda e, ps=ps, eo2=eo2: e.matmul(
                ps[:, :], ones1b[0:1, :], bvrb[0:1, eo2 * 512:(eo2 + 1) * 512],
                start=False, stop=True), TT)
            wt(VE, TT, tm)
            v_ev = op(VE, lambda e, ps=ps, t=t, eo2=eo2: e.tensor_copy(
                a4[:, t * 1024 + eo2 * 512:t * 1024 + (eo2 + 1) * 512], ps[:, :]), VV)
            g.release(VV, v_ev)

    barrier()

    ksq_slot = [Guard(), Guard()]
    g_dcols = Guard()
    g_ek = Guard()
    v_z = op(VE, lambda e: e.memset(msc[0:1, 0:1040], 0.0), VV)
    wt(GP, VV, v_z)
    dma(GP, S_ARP, arp_in.ap()[128:129, 1040:2080], msc[0:1, 0:1040])
    for hc in range(ECH):
        ksq_slot[hc % 2].acquire(SC)
        s_ksq = op(SC, lambda e, hc=hc: e.activation(
            ksq[:, (hc % 2) * 2048:(hc % 2) * 2048 + 2048],
            a3[:, hc * 2048:(hc + 1) * 2048], AF.Square), SS)
        for hh_i in range(2):
            h = 2 * hc + hh_i
            hp = hh_i * 64
            gPC.acquire(TE)
            wt(TE, SS, s_ksq)
            for t in range(TCH):
                tm_d = op(TE, lambda e, hc=hc, hp=hp, t=t: e.matmul(
                    PC[:, t:t + 1],
                    ksq[hp:hp + 64, (hc % 2) * 2048 + t * 128:(hc % 2) * 2048 + (t + 1) * 128],
                    onesn16[hp:hp + 64, 0:1], start=True, stop=True),
                    TT if t == TCH - 1 else None)
            if hh_i == 1:
                ksq_slot[hc % 2].release(TT, tm_d)
            g_dcols.acquire(VE)
            wt(VE, TT, tm_d)
            v_d = op(VE, lambda e: e.tensor_copy(dcols[:, 0:16], PC[:, 0:16]), VV)
            gPC.release(VV, v_d)
            def dd_mm(t):
                ps, g = psq[t % 2]
                g.acquire(TE)
                return ps, g, op(TE, lambda e, ps=ps, hc=hc, hp=hp, t=t: e.matmul(
                    ps[:, 0:256],
                    a3[hp:hp + 64, hc * 2048 + t * 128:hc * 2048 + (t + 1) * 128],
                    projs[hp:hp + 64, :], start=True, stop=True), TT)
            dd_q = [dd_mm(0)]
            for t in range(TCH):
                ps, g, tm_dd = dd_q[t]
                if t + 1 < TCH:
                    dd_q.append(dd_mm(t + 1))
                wt(SC, TT, tm_dd)
                if t == 0:
                    wt(SC, VV, v_d)
                g_ek.acquire(SC) if t == 0 else None
                s_ek = op(SC, lambda e, ps=ps, t=t: e.activation(
                    ek[:, t * 256:(t + 1) * 256], ps[:, 0:256], AF.Exp,
                    bias=dcols[:, t:t + 1]), SS)
                g.release(SS, s_ek)
                wt(VE, SS, s_ek)
                v_rm = op(VE, lambda e, h=h, t=t: e.reduce_max(
                    rme[:, t:t + 1], ek[:, t * 256:(t + 1) * 256], AX), VV)
                wt(TE, SS, s_ek)
                if t == 0:
                    gPD.acquire(TE); gPE.acquire(TE); gPF.acquire(TE)
                    gPG.acquire(TE); gPC.acquire(TE)
                v_sl = a4[:, t * 1024 + h * 64:t * 1024 + (h + 1) * 64]
                op(TE, lambda e, t=t, v_sl=v_sl: e.matmul(
                    PD[:, 0:64], ek[:, t * 256:t * 256 + 128], v_sl,
                    start=(t == 0), stop=(t == TCH - 1)))
                op(TE, lambda e, t=t, v_sl=v_sl: e.matmul(
                    PE[:, 0:64], ek[:, t * 256 + 128:t * 256 + 256], v_sl,
                    start=(t == 0), stop=(t == TCH - 1)))
                op(TE, lambda e, t=t: e.matmul(
                    PF[:, 0:1], ek[:, t * 256:t * 256 + 128], onescb[:, 0:1],
                    start=(t == 0), stop=(t == TCH - 1)))
                op(TE, lambda e, t=t: e.matmul(
                    PG[:, 0:1], ek[:, t * 256 + 128:t * 256 + 256], onescb[:, 0:1],
                    start=(t == 0), stop=(t == TCH - 1)))
                tm_vs = op(TE, lambda e, t=t, v_sl=v_sl: e.matmul(
                    PC[0:1, 0:64], onescb[:, 0:1], v_sl,
                    start=(t == 0), stop=(t == TCH - 1)),
                    TT if t == TCH - 1 else None)
                if t == TCH - 1:
                    g_ek.release(TT, tm_vs)
                    g_ek.release(VV, v_rm)
            wt(SC, VV, v_rm)
            s_ln = op(SC, lambda e: e.activation(rme[:, 0:16], rme[:, 0:16], AF.Ln), SS)
            wt(VE, SS, s_ln)
            v_rma = op(VE, lambda e, h=h: e.tensor_tensor(
                rm_all[:, h * 16:h * 16 + 16], rme[:, 0:16], dcols[:, 0:16],
                OP.subtract), VV)
            g_dcols.release(VV, v_rma)
            wt(VE, TT, tm_vs)
            op(VE, lambda e: e.tensor_copy(ust[:, 0:64], PD[:, 0:64]))
            op(VE, lambda e: e.tensor_copy(ust[:, 64:65], PF[:, 0:1]))
            op(VE, lambda e: e.tensor_copy(ust[:, 65:129], PE[:, 0:64]))
            op(VE, lambda e: e.tensor_copy(ust[:, 129:130], PG[:, 0:1]))
            v_u = op(VE, lambda e: e.tensor_copy(vss[0:1, 0:64], PC[0:1, 0:64]), VV)
            gPD.release(VV, v_u); gPE.release(VV, v_u); gPF.release(VV, v_u)
            gPG.release(VV, v_u); gPC.release(VV, v_u)
            wt(GP, VV, v_u)
            w1 = dma(GP, S_ARP, arp_in.ap()[0:128, (2 * h) * 65:(2 * h + 2) * 65],
                     ust[:, 0:130])
            w2 = dma(GP, S_ARP, arp_in.ap()[128:129, h * 65:(h + 1) * 65],
                     vss[0:1, 0:65])
            wt(VE, S_ARP, w2)
    wt(VE, VV, VV.v)
    v_dk = op(VE, lambda e: e.reduce_max(pp[:, 14:15], rm_all[:, :], AX), VV)
    wt(GP, VV, v_dk)
    g2 = dma(GP, S_CTL, rowx[0:1, :], pp[:, 14:15])
    wt(VE, S_CTL, g2)
    v_dkc = op(VE, lambda e: e.reduce_max(cells[0:1, 5:6], rowx[0:1, :], AX), VV)
    wt(GP, VV, v_dkc)
    dma(GP, S_CTL, ars_in.ap(), cells[0:1, 5:6])
    wt(GP, S_ARP, S_ARP.v)
    op(GP, lambda e: e.collective_compute(
        "AllReduce", OP.add, replica_groups=[[0, 1], [2, 3], [4, 5], [6, 7]],
        ins=[arp_in.ap().opt()], outs=[arp_out.ap().opt()]), CC[3])
    op(GP, lambda e: e.collective_compute(
        "AllReduce", OP.max, replica_groups=[list(range(NC))],
        ins=[ars_in.ap().opt()], outs=[ars_out.ap().opt()]), CC[4])

    barrier()

    wt(GP, CC[0], 1)
    g_ax = dma(GP, S_CTL, cells[0:1, 0:1], arx_out.ap())
    wt(SC, S_CTL, g_ax)
    s_sig = op(SC, lambda e: e.activation(cells[0:1, 1:2], cells[0:1, 12:13], AF.Sigmoid), SS)
    wt(VE, SS, s_sig)
    ve_step(lambda e: e.tensor_tensor(cells[0:1, 2:3], cells[0:1, 1:2], cells[0:1, 0:1], OP.mult))
    ve_step(lambda e: e.tensor_scalar(cells[0:1, 2:3], cells[0:1, 2:3], 1.0 / 127.0, None, OP.mult))
    ve_step(lambda e: e.reciprocal(cells[0:1, 3:4], cells[0:1, 2:3]))
    v_sxf = ve_step(lambda e: e.tensor_scalar(cells[0:1, 4:5], cells[0:1, 2:3], 1.0 / 127.0, None, OP.mult))

    def repl128(cell_ap, dst_col):
        ve_step(lambda e: e.tensor_copy(rcell[0:1, 0:1], cell_ap))
        ve_step(lambda e: e.tensor_tensor(vsr[0:1, 70:71], cell_ap, rcell[0:1, 0:1], OP.subtract))
        v_r = ve_step(lambda e: e.tensor_copy(rcell[0:1, 1:2], vsr[0:1, 70:71]))
        gPG.acquire(TE)
        wt(TE, VV, v_r)
        op(TE, lambda e: e.matmul(PG[:, 0:1], ones1b[0:1, :], rcell[0:1, 0:1], start=True, stop=False))
        t_r = op(TE, lambda e: e.matmul(PG[:, 0:1], ones1b[0:1, :], rcell[0:1, 1:2], start=False, stop=True), TT)
        wt(VE, TT, t_r)
        v_f = op(VE, lambda e: e.tensor_copy(dst_col, PG[:, 0:1]), VV)
        wt(VE, VV, v_f)
        gPG.release(VV, v_f)
        return v_f

    wt(VE, VV, v_sxf)
    repl128(cells[0:1, 3:4], pp[:, 0:1])
    v_sxfp = repl128(cells[0:1, 4:5], pp[:, 1:2])

    xq_slot = [None, None]
    g_i32 = [Guard(), Guard()]
    for c in range(TCH):
        sl = (c % 2) * 1024
        if xq_slot[c % 2] is not None:
            wt(SY, VV, xq_slot[c % 2])
        lxn = dma(SY, S_X[c % 2], xnr[:, sl:sl + 1024], xn32h[c * 128:(c + 1) * 128, :])
        wt(VE, S_X[c % 2], lxn)
        op(VE, lambda e, sl=sl: e.tensor_scalar(
            xnr[:, sl:sl + 1024], xnr[:, sl:sl + 1024], pp[:, 0:1], QP, OP.mult, OP.min))
        g_i32[c % 2].acquire(VE)
        v_i = op(VE, lambda e, sl=sl: e.tensor_scalar(
            i32s[:, sl:sl + 1024], xnr[:, sl:sl + 1024], QN, None, OP.max), VV)
        xq_slot[c % 2] = v_i
        wt(SC, VV, v_i)
        s_b = op(SC, lambda e, sl=sl: e.activation(
            xbf2[:, 0:1024], i32s[:, sl:sl + 1024], AF.Copy), SS)
        g_i32[c % 2].release(SS, s_b)
        gPT.acquire(TE)
        wt(TE, SS, s_b)
        for j in range(8):
            t_l = op(TE, lambda e, j=j: e.transpose(
                PT[:, j * 128:(j + 1) * 128], xbf2[:, j * 128:(j + 1) * 128],
                idbf[:, :]), TT if j == 7 else None)
        wt(VE, TT, t_l)
        dst = a4[:, :].rearrange("p (g t) -> p g t", g=8)[:, :, c * 128:(c + 1) * 128]
        v_x = op(VE, lambda e, dst=dst: e.tensor_copy(
            dst, PT[:, :].rearrange("p (g t) -> p g t", g=8)), VV)
        gPT.release(VV, v_x)
        wt(SC, TT, t_l)
    wt(GP, CC[2], 1)
    for gc in range(ICH):
        r, icl = gc // 3, gc % 3
        dma(GP, S_SEF, seffg[:, gc:gc + 1],
            ags_out.ap()[r:r + 1, icl * 128:(icl + 1) * 128])
        g_s = dma(GP, S_SEF, seffu[:, gc:gc + 1],
                  ags_out.ap()[r:r + 1, 384 + icl * 128:384 + (icl + 1) * 128])
    wt(VE, S_SEF, g_s)
    wt(VE, VV, v_sxfp)
    op(VE, lambda e: e.tensor_scalar(seffg[:, :], seffg[:, :], pp[:, 1:2], None, OP.mult))
    op(VE, lambda e: e.tensor_scalar(seffu[:, :], seffu[:, :], pp[:, 1:2], None, OP.mult), VV)

    barrier()

    wt(TE, CC[1], 1)
    wt(SY, CC[1], 1)
    gub_slot = [Guard(), Guard(), Guard(), Guard()]
    hst_slot = [Guard(), Guard(), Guard(), Guard()]
    g_sgs = [Guard(), Guard()]
    psg_m = {0: (PA, gPA), 1: (PB, gPB)}
    psu_m = {0: (PC, gPC), 1: (PD, gPD)}
    def gu_load(gc):
        r, icl = gc // 3, gc % 3
        slg = (gc % 4) * 1024
        src_g = agw_out.ap()[r:r + 1, 0:NW_G] \
            .rearrange("a (p g c) -> (a p) g c", p=128, c=384)[:, :, icl * 128:(icl + 1) * 128]
        src_u = agw_out.ap()[r:r + 1, NW_G:2 * NW_G] \
            .rearrange("a (p g c) -> (a p) g c", p=128, c=384)[:, :, icl * 128:(icl + 1) * 128]
        gub_slot[gc % 4].acquire(SY)
        dma(SY, S_G[gc % 4], gub[:, slg:slg + 1024].rearrange("p (g c) -> p g c", g=8), src_g)
        return dma(SY, S_G[gc % 4], gub[:, 4096 + slg:4096 + slg + 1024]
                   .rearrange("p (g c) -> p g c", g=8), src_u)
    lgu_q = [gu_load(0), gu_load(1), gu_load(2)]
    for gc in range(ICH):
        slg = (gc % 4) * 1024
        sl_h = (gc % 4) * 2048
        if gc + 3 < ICH:
            lgu_q.append(gu_load(gc + 3))
        lgu = lgu_q[gc]
        hst_slot[gc % 4].acquire(VE)
        wt(TE, S_G[gc % 4], lgu)
        for t in range(T4):
            psg, gg = psg_m[t % 2]
            psu, gu = psu_m[t % 2]
            gg.acquire(TE)
            for hcc in range(ECH):
                op(TE, lambda e, psg=psg, slg=slg, hcc=hcc, t=t: e.matmul(
                    psg[:, :], gub[:, slg + hcc * 128:slg + (hcc + 1) * 128],
                    a4[:, hcc * 2048 + t * 512:hcc * 2048 + (t + 1) * 512],
                    start=(hcc == 0), stop=(hcc == ECH - 1)))
            gu.acquire(TE)
            for hcc in range(ECH):
                tm_u = op(TE, lambda e, psu=psu, slg=slg, hcc=hcc, t=t: e.matmul(
                    psu[:, :], gub[:, 4096 + slg + hcc * 128:4096 + slg + (hcc + 1) * 128],
                    a4[:, hcc * 2048 + t * 512:hcc * 2048 + (t + 1) * 512],
                    start=(hcc == 0), stop=(hcc == ECH - 1)),
                    TT if hcc == ECH - 1 else None)
            if t == T4 - 1:
                gub_slot[gc % 4].release(TT, tm_u)
            wt(SC, TT, tm_u)
            g_sgs[t % 2].acquire(SC)
            s_sg = op(SC, lambda e, psg=psg, t=t, gc=gc: e.activation(
                sgs[:, (t % 2) * 512:(t % 2) * 512 + 512], psg[:, :], AF.Silu,
                scale=seffg[:, gc:gc + 1]), SS)
            gg.release(SS, s_sg)
            wt(VE, SS, s_sg)
            op(VE, lambda e, psu=psu, t=t, gc=gc, sl_h=sl_h: e.scalar_tensor_tensor(
                hst[:, sl_h + t * 512:sl_h + (t + 1) * 512], psu[:, :],
                seffu[:, gc:gc + 1], sgs[:, (t % 2) * 512:(t % 2) * 512 + 512],
                OP.mult, OP.mult))
            v_h = op(VE, lambda e, t=t, gc=gc, sl_h=sl_h: e.tensor_reduce(
                ach[:, gc * 4 + t:gc * 4 + t + 1],
                hst[:, sl_h + t * 512:sl_h + (t + 1) * 512], AX, OP.max,
                apply_absolute_value=True), VV)
            gu.release(VV, v_h)
            g_sgs[t % 2].release(VV, v_h)
        wt(SY, VV, v_h)
        l_sp = dma(SY, S_SP[gc % 4], hh[gc, :, :], hst[:, sl_h:sl_h + 2048])
        hst_slot[gc % 4].release(S_SP[gc % 4], l_sp)
    wt(VE, VV, VV.v)
    v_ah = op(VE, lambda e: e.reduce_max(pp[:, 13:14], ach[:, :], AX), VV)
    wt(GP, VV, v_ah)
    g3 = dma(GP, S_CTL, rowx[0:1, :], pp[:, 13:14])
    wt(VE, S_CTL, g3)
    v_ahc = op(VE, lambda e: e.reduce_max(cells[0:1, 8:9], rowx[0:1, :], AX), VV)
    wt(GP, VV, v_ahc)
    dma(GP, S_CTL, arh_in.ap(), cells[0:1, 8:9])
    op(GP, lambda e: e.collective_compute(
        "AllReduce", OP.max, replica_groups=[list(range(NC))],
        ins=[arh_in.ap().opt()], outs=[arh_out.ap().opt()]), CC[5])

    barrier()

    wt(GP, CC[3], 1)
    wt(GP, CC[4], 1)
    g_es = dma(GP, S_CTL, cells[0:1, 6:7], ars_out.ap())
    wt(SC, S_CTL, g_es)
    s_es = op(SC, lambda e: e.activation(cells[0:1, 6:7], cells[0:1, 6:7], AF.Exp, scale=-1.0), SS)
    wt(VE, SS, s_es)
    v_res = ve_step(lambda e: e.tensor_scalar(cells[0:1, 7:8], cells[0:1, 6:7], RATIO, None, OP.mult))
    v_resp = repl128(cells[0:1, 7:8], pp[:, 4:5])

    g_upst4 = [Guard() for _ in range(4)]
    ct_last = None
    g_vall = dma(GP, S_CTL, vsr_all[0:1, :], arp_out.ap()[128:129, 0:1040])
    up_v = {}
    def up_load(h):
        g_upst4[h % 4].acquire(GP)
        up_v[h] = dma(GP, S_CTL, upst[:, (h % 4) * 130:(h % 4) * 130 + 130],
                      arp_out.ap()[0:128, 2 * h * 65:(2 * h + 2) * 65])
    for h in range(3):
        up_load(h)
    for h in range(H):
        if h + 3 < H:
            up_load(h + 3)
        u0 = (h % 4) * 130
        gPF.acquire(TE)
        if h == 0:
            wt(TE, S_CTL, g_vall)
        t_rv = op(TE, lambda e, h=h: e.matmul(PF[:, 0:65], onesepsf[0:1, :],
                                              vsr_all[0:1, h * 65:(h + 1) * 65],
                                              start=True, stop=True), TT)
        wt(VE, TT, t_rv)
        wt(VE, S_CTL, up_v[h])
        if h == 0:
            wt(VE, VV, v_resp)
        op(VE, lambda e, h=h, u0=u0: e.scalar_tensor_tensor(
            ct_all[:, (2 * h) * 65:(2 * h) * 65 + 65], upst[:, u0:u0 + 65], pp[:, 4:5],
            PF[:, 0:65], OP.mult, OP.add))
        v_ct = op(VE, lambda e, h=h, u0=u0: e.scalar_tensor_tensor(
            ct_all[:, (2 * h + 1) * 65:(2 * h + 1) * 65 + 65], upst[:, u0 + 65:u0 + 130],
            pp[:, 4:5], PF[:, 0:65], OP.mult, OP.add), VV)
        g_upst4[h % 4].release(VV, v_ct)
        gPF.release(VV, v_ct)
        gPG.acquire(TE)
        wt(TE, VV, v_ct)
        op(TE, lambda e, h=h: e.matmul(PG[0:1, 0:65], onescb[:, 0:1],
                                       ct_all[:, (2 * h) * 65:(2 * h) * 65 + 65],
                                       start=True, stop=False))
        t_cs = op(TE, lambda e, h=h: e.matmul(PG[0:1, 0:65], onescb[:, 0:1],
                                              ct_all[:, (2 * h + 1) * 65:(2 * h + 1) * 65 + 65],
                                              start=False, stop=True), TT)
        wt(SC, TT, t_cs)
        ct_last = op(SC, lambda e, h=h: e.activation(csr[0:1, h * 65:(h + 1) * 65],
                                                     PG[0:1, 0:65], AF.Copy, scale=EPS), SS)
        gPG.release(SS, ct_last)

    qsq_slot = [Guard(), Guard()]
    g_eq = Guard()
    g_densb = Guard()
    g_isb = [Guard(), Guard()]
    g_stk = [Guard(), Guard()]
    g_nbh = Guard()
    g_nbl = Guard()
    g_ntr2 = Guard()
    g_pt4 = Guard()
    g_negb = [Guard(), Guard()]
    gPCp = [Guard(), Guard()]
    g_ibfp = [Guard(), Guard()]
    dd_m2 = {0: (PD, gPD), 1: (PE, gPE)}
    fin_m2 = {0: (PG, gPG), 1: (PF, gPF)}
    anum_m = {0: anum0, 1: anum1}
    ibf_m = {0: ibf, 1: ibf1}
    s_qsq_v = {}
    g_lo_v = {}
    s_eq_v = {}
    s_ib_v = {}

    def st_dd(h):
        hc = h // 2
        hp = (h % 2) * 64
        pc0 = (h % 2) * 16
        ks0 = (hc % 2) * 2048
        if h % 2 == 0:
            qsq_slot[hc % 2].acquire(SC)
            s_qsq_v[hc] = op(SC, lambda e, hc=hc, ks0=ks0: e.activation(
                ksq[:, ks0:ks0 + 2048],
                a2[:, hc * 2048:(hc + 1) * 2048], AF.Square), SS)
        gPCp[h % 2].acquire(TE)
        wt(TE, SS, s_qsq_v[hc])
        for t in range(TCH):
            tm_d = op(TE, lambda e, ks0=ks0, hp=hp, t=t, pc0=pc0: e.matmul(
                PC[:, pc0 + t:pc0 + t + 1],
                ksq[hp:hp + 64, ks0 + t * 128:ks0 + (t + 1) * 128],
                onesn16[hp:hp + 64, 0:1], start=True, stop=True),
                TT if t == TCH - 1 else None)
        if h % 2 == 1:
            qsq_slot[hc % 2].release(TT, tm_d)
        for t in range(TCH):
            ps, g = dd_m2[t % 2]
            g.acquire(TE)
            tm_dd = op(TE, lambda e, ps=ps, hc=hc, hp=hp, t=t: e.matmul(
                ps[:, 0:256],
                a2[hp:hp + 64, hc * 2048 + t * 128:hc * 2048 + (t + 1) * 128],
                projs[hp:hp + 64, :], start=True, stop=True), TT)
            wt(VE, TT, tm_dd)
            v_rm = op(VE, lambda e, ps=ps, t=t: e.reduce_max(
                stq[:, t:t + 1], ps[:, 0:256], AX), VV)
            g.release(VV, v_rm)
        wt(VE, TT, tm_d)
        wt(VE, VV, VV.v)
        nb0 = (h % 2) * 16
        g_negb[h % 2].acquire(VE)
        v_nb = op(VE, lambda e, pc0=pc0, nb0=nb0: e.scalar_tensor_tensor(
            negb[:, nb0:nb0 + 16], PC[:, pc0:pc0 + 16], 1.0, stq[:, :],
            OP.mult, OP.subtract), VV)
        gPCp[h % 2].release(VV, v_nb)
        wt(SC, VV, v_nb)
        g_nbh.acquire(SC)
        s_hi = op(SC, lambda e, nb0=nb0: e.activation(
            nbh[:, :], negb[:, nb0:nb0 + 16], AF.Copy), SS)
        g_nbl.acquire(VE)
        wt(VE, SS, s_hi)
        v_lo = op(VE, lambda e, nb0=nb0: e.tensor_tensor(
            nbl[:, :], negb[:, nb0:nb0 + 16], nbh[:, :], OP.subtract), VV)
        g_negb[h % 2].release(SS, s_hi)
        g_negb[h % 2].release(VV, v_lo)
        g_pt4.acquire(TE)
        wt(TE, SS, s_hi)
        wt(TE, VV, v_lo)
        op(TE, lambda e: e.transpose(PT[0:16, 0:128], nbh[:, :], idbf[:, :]))
        t_lo = op(TE, lambda e: e.transpose(PT[0:16, 128:256], nbl[:, :], idbf[:, :]), TT)
        g_nbh.release(TT, t_lo)
        g_nbl.release(TT, t_lo)
        g_ntr2.acquire(VE)
        wt(VE, TT, t_lo)
        v_ev = op(VE, lambda e: e.tensor_copy(ntr2[:, :], PT[0:16, 0:256]), VV)
        g_pt4.release(VV, v_ev)
        off2 = (h % 2) * 2048
        g_stk[h % 2].acquire(GP)
        wt(GP, VV, v_ev)
        dma(GP, S_CTL, stk2[0:1, off2:off2 + 2048], ntr2[:, 0:128])
        g_lo_v[h] = dma(GP, S_CTL, stk2[1:2, off2:off2 + 2048], ntr2[:, 128:256])
        g_ntr2.release(S_CTL, g_lo_v[h])

    def st_qa(h):
        hc = h // 2
        hp = (h % 2) * 64
        off2 = (h % 2) * 2048
        g_eq.acquire(SC)
        for mc in range(2):
            for t in range(T4):
                ps, g = psq[(mc * T4 + t) % 2]
                g.acquire(TE)
                if mc == 0 and t == 0:
                    wt(TE, S_CTL, g_lo_v[h])
                op(TE, lambda e, ps=ps, hp=hp, mc=mc, t=t, hc=hc: e.matmul(
                    ps[:, :], projs[hp:hp + 64, mc * 128:(mc + 1) * 128],
                    a2[hp:hp + 64, hc * 2048 + t * 512:hc * 2048 + (t + 1) * 512],
                    start=True, stop=False))
                tm_q = op(TE, lambda e, ps=ps, t=t, off2=off2: e.matmul(
                    ps[:, :], ones2b[0:2, :],
                    stk2[0:2, off2 + t * 512:off2 + (t + 1) * 512],
                    start=False, stop=True), TT)
                if mc == 1 and t == T4 - 1:
                    g_stk[h % 2].release(TT, tm_q)
                wt(SC, TT, tm_q)
                s_eq = op(SC, lambda e, ps=ps, mc=mc, t=t: e.activation(
                    eq[:, mc * 2048 + t * 512:mc * 2048 + (t + 1) * 512],
                    ps[:, :], AF.Exp), SS)
                s_eq_v[(h, mc, t)] = s_eq
                g.release(SS, s_eq)
        if h == 0:
            wt(TE, SS, ct_last)
        anum = anum_m[h % 2]
        for t in range(T4):
            psA, gA = fin_m2[t % 2]
            gA.acquire(TE)
            wt(TE, SS, s_eq_v[(h, 1, t)])
            op(TE, lambda e, psA=psA, h=h, t=t: e.matmul(
                psA[0:65, :], ct_all[:, (2 * h) * 65:(2 * h) * 65 + 65],
                eq[:, t * 512:(t + 1) * 512], start=True, stop=False))
            op(TE, lambda e, psA=psA, h=h, t=t: e.matmul(
                psA[0:65, :], ct_all[:, (2 * h + 1) * 65:(2 * h + 1) * 65 + 65],
                eq[:, 2048 + t * 512:2048 + (t + 1) * 512], start=False, stop=False))
            tm_a = op(TE, lambda e, psA=psA, h=h: e.matmul(
                psA[0:65, :], csr[0:1, h * 65:(h + 1) * 65], ones512b[0:1, :],
                start=False, stop=True), TT)
            if t == T4 - 1:
                g_eq.release(TT, tm_a)
            if t == 0:
                g_densb.acquire(SC)
            wt(SC, TT, tm_a)
            s_dr = op(SC, lambda e, psA=psA, t=t: e.copy(
                densb[64:65, t * 512:(t + 1) * 512], psA[64:65, :]), SS)
            wt(VE, TT, tm_a)
            v_an = op(VE, lambda e, psA=psA, anum=anum, t=t: e.tensor_copy(
                anum[:, t * 512:(t + 1) * 512], psA[0:64, :]), VV)
            gA.release(SS, s_dr)
            gA.release(VV, v_an)
        g_ibfp[h % 2].acquire(SC)
        wt(SC, SS, s_dr)
        s_ln2 = op(SC, lambda e: e.activation(lnrow[64:65, :], densb[64:65, :], AF.Ln), SS)
        wt(VE, SS, s_ln2)
        v_hop = op(VE, lambda e: e.tensor_copy(densb[64:65, :], lnrow[64:65, :]), VV)
        wt(SC, VV, v_hop)
        ib = ibf_m[h % 2]
        s_ib_v[h] = op(SC, lambda e, ib=ib: e.activation(
            ib[64:65, :], densb[64:65, :], AF.Exp, scale=-1.0), SS)
        g_densb.release(SS, s_ib_v[h])

    def st_fin(h):
        hc = h // 2
        hp = (h % 2) * 64
        anum = anum_m[h % 2]
        ib = ibf_m[h % 2]
        for t in range(T4):
            pfin, gf = fin_m2[t % 2]
            gf.acquire(TE)
            if t == 0:
                wt(TE, SS, s_ib_v[h])
            t_pi = op(TE, lambda e, t=t, pfin=pfin, ib=ib: e.matmul(
                pfin[0:64, :], ones65[64:65, 0:64], ib[64:65, t * 512:(t + 1) * 512],
                start=True, stop=True), TT)
            if t == T4 - 1:
                g_ibfp[h % 2].release(TT, t_pi)
            g_isb[t % 2].acquire(VE)
            wt(VE, TT, t_pi)
            v_ie = op(VE, lambda e, t=t, pfin=pfin: e.tensor_copy(
                aqs[0:64, (t % 2) * 512:(t % 2) * 512 + 512], pfin[0:64, :]), VV)
            gf.release(VV, v_ie)
            v_at = op(VE, lambda e, anum=anum, hp=hp, hc=hc, t=t: e.tensor_tensor(
                a4[hp:hp + 64, hc * 2048 + t * 512:hc * 2048 + (t + 1) * 512],
                anum[:, t * 512:(t + 1) * 512],
                aqs[0:64, (t % 2) * 512:(t % 2) * 512 + 512], OP.mult), VV)
            g_isb[t % 2].release(VV, v_at)

    for k in range(H + 2):
        if k < H:
            st_dd(k)
        if 1 <= k <= H:
            st_qa(k - 1)
        if k >= 2:
            st_fin(k - 2)

    barrier()

    wt(GP, CC[5], 1)
    g_ah = dma(GP, S_CTL, cells[0:1, 8:9], arh_out.ap())
    wt(SC, S_CTL, g_ah)
    s_sigd = op(SC, lambda e: e.activation(cells[0:1, 9:10], cells[0:1, 13:14], AF.Sigmoid), SS)
    wt(VE, SS, s_sigd)
    ve_step(lambda e: e.tensor_tensor(cells[0:1, 10:11], cells[0:1, 9:10], cells[0:1, 8:9], OP.mult))
    ve_step(lambda e: e.tensor_scalar(cells[0:1, 10:11], cells[0:1, 10:11], 1.0 / 127.0, None, OP.mult))
    v_sh = ve_step(lambda e: e.reciprocal(cells[0:1, 11:12], cells[0:1, 10:11]))
    repl128(cells[0:1, 11:12], pp[:, 2:3])
    v_shp = repl128(cells[0:1, 10:11], pp[:, 3:4])

    def hq_dst(gc):
        if gc < 8:
            return a1[:, gc * 2048:(gc + 1) * 2048]
        if gc < 16:
            return a2[:, (gc - 8) * 2048:(gc - 7) * 2048]
        return a3[:, (gc - 16) * 2048:(gc - 15) * 2048]

    h_slot = [None, None]
    g_i32s5 = Guard()
    for gc in range(ICH):
        sl = (gc % 2) * 2048
        if h_slot[gc % 2] is not None:
            wt(SY, VV, h_slot[gc % 2])
        lh = dma(SY, S_X[gc % 2], hst[:, sl:sl + 2048], hh[gc, :, :])
        wt(VE, S_X[gc % 2], lh)
        op(VE, lambda e, sl=sl: e.tensor_scalar(
            hst[:, sl:sl + 2048], hst[:, sl:sl + 2048], pp[:, 2:3], QP, OP.mult, OP.min))
        g_i32s5.acquire(VE)
        v_hi = op(VE, lambda e, sl=sl: e.tensor_scalar(
            i32s5[:, :], hst[:, sl:sl + 2048], QN, None, OP.max), VV)
        h_slot[gc % 2] = v_hi
        wt(SC, VV, v_hi)
        s_hq = op(SC, lambda e, gc=gc: e.activation(hq_dst(gc), i32s5[:, :], AF.Copy), SS)
        g_i32s5.release(SS, s_hq)

    barrier()

    agw_d = agw_out.ap()[:, 2 * NW_G:NW].rearrange("r (p c) -> r p c", p=128)
    psW_m = {0: (PA, gPA), 1: (PB, gPB)}
    psD_m = {0: (PC, gPC), 1: (PD, gPD)}
    g_wobuf = Guard()
    g_dbuf = Guard()
    g_xst = [Guard(), Guard()]
    g_tmp = [Guard(), Guard()]
    g_yst = [Guard(), Guard()]
    for eo2 in range(2):
        g_wobuf.acquire(GP)
        for ei in range(ECH):
            lwo = dma(GP, S_WO, wobuf[:, ei * 512:(ei + 1) * 512],
                      woT_h[ei * 128:(ei + 1) * 128, eo2 * 512:(eo2 + 1) * 512])
        g_dbuf.acquire(SY)
        for ic in range(ICH):
            src = agw_d[eo2 * 4:(eo2 + 1) * 4, :, ic * 128:(ic + 1) * 128] \
                .rearrange("r p c -> p r c")
            ldb = dma(SY, S_DB, dbuf[:, ic * 512:(ic + 1) * 512]
                      .rearrange("p (r c) -> p r c", r=4), src)
        wt(TE, S_DB, ldb)
        wt(TE, S_WO, lwo)
        t_last = None
        for t in range(TCH):
            psW, gW = psW_m[t % 2]
            psD, gD = psD_m[t % 2]
            g_xst[t % 2].acquire(SY)
            lx = dma(SY, S_X[t % 2], xst[:, (t % 2) * 512:(t % 2) * 512 + 512],
                     x_h[t * 128:(t + 1) * 128, eo2 * 512:(eo2 + 1) * 512])
            gD.acquire(TE)
            for ic in range(ICH):
                op(TE, lambda e, psD=psD, ic=ic, t=t: e.matmul(
                    psD[:, :], hq_dst(ic)[:, t * 128:(t + 1) * 128],
                    dbuf[:, ic * 512:(ic + 1) * 512],
                    start=(ic == 0), stop=(ic == ICH - 1)))
            gW.acquire(TE)
            for ei in range(ECH):
                op(TE, lambda e, psW=psW, ei=ei, t=t: e.matmul(
                    psW[:, :], a4[:, ei * 2048 + t * 128:ei * 2048 + (t + 1) * 128],
                    wobuf[:, ei * 512:(ei + 1) * 512],
                    start=(ei == 0), stop=False))
            t_last = op(TE, lambda e, psW=psW, eo2=eo2: e.matmul(
                psW[:, :], ones1b[0:1, :], borb[0:1, eo2 * 512:(eo2 + 1) * 512],
                start=False, stop=True), TT)
            g_tmp[t % 2].acquire(SC)
            wt(SC, TT, t_last)
            s_d = op(SC, lambda e, psD=psD, t=t: e.activation(
                tmpst[:, (t % 2) * 512:(t % 2) * 512 + 512], psD[:, :],
                AF.Copy, scale=pp[:, 3:4]), SS)
            gD.release(SS, s_d)
            g_yst[t % 2].acquire(VE)
            wt(VE, SS, s_d)
            op(VE, lambda e, psW=psW, t=t: e.tensor_tensor(
                yst[:, (t % 2) * 512:(t % 2) * 512 + 512],
                psW[:, :], tmpst[:, (t % 2) * 512:(t % 2) * 512 + 512], OP.add))
            wt(VE, S_X[t % 2], lx)
            v_y = op(VE, lambda e, t=t: e.tensor_tensor(
                yst[:, (t % 2) * 512:(t % 2) * 512 + 512],
                yst[:, (t % 2) * 512:(t % 2) * 512 + 512],
                xst[:, (t % 2) * 512:(t % 2) * 512 + 512], OP.add), VV)
            gW.release(VV, v_y)
            g_tmp[t % 2].release(VV, v_y)
            g_xst[t % 2].release(VV, v_y)
            wt(SY, VV, v_y)
            o_d = dma(SY, S_O[t % 2], y_h[t * 128:(t + 1) * 128, eo2 * 512:(eo2 + 1) * 512],
                      yst[:, (t % 2) * 512:(t % 2) * 512 + 512])
            g_yst[t % 2].release(S_O[t % 2], o_d)
        g_wobuf.release(TT, t_last)
        g_dbuf.release(TT, t_last)
    wt(SY, S_O[0], S_O[0].v)
    wt(SY, S_O[1], S_O[1].v)
    wt(GP, S_O[0], S_O[0].v)
    wt(GP, S_O[1], S_O[1].v)
    wt(GP, S_CTL, S_CTL.v)

    with es, nc.Block() as block:
        for name, dec in ((SY, block.sync), (GP, block.gpsimd), (VE, block.vector),
                          (SC, block.scalar), (TE, block.tensor)):
            def runner(e, name=name):
                for eng, fn in PROG:
                    if eng == name:
                        fn(e)
            dec(runner)
    return nc


def kernel(**inputs):
    import ml_dtypes
    x = np.asarray(inputs["L0_attn_x"], dtype=np.float32)
    wq = np.asarray(inputs["L0_attn_Wq"], dtype=np.float32)
    wk = np.asarray(inputs["L0_attn_Wk"], dtype=np.float32)
    wv = np.asarray(inputs["L0_attn_Wv"], dtype=np.float32)
    wo = np.asarray(inputs["L0_attn_Wo"], dtype=np.float32)
    bq = np.asarray(inputs["L0_attn_bq"], dtype=np.float32)
    bk = np.asarray(inputs["L0_attn_bk"], dtype=np.float32)
    bv = np.asarray(inputs["L0_attn_bv"], dtype=np.float32)
    bo = np.asarray(inputs["L0_attn_bo"], dtype=np.float32)
    proj = np.asarray(inputs["L0_attn_proj"], dtype=np.float32)
    clipx = np.asarray(inputs["L0_ffn_clip_x"], dtype=np.float32)
    clipd = np.asarray(inputs["L0_ffn_clip_down"], dtype=np.float32)
    wgate = np.asarray(inputs["L0_ffn_w_gate"], dtype=np.float32)
    wup = np.asarray(inputs["L0_ffn_w_up"], dtype=np.float32)
    wdown = np.asarray(inputs["L0_ffn_w_down"], dtype=np.float32)

    shared = {
        "wqT": np.ascontiguousarray(wq.T), "wkT": np.ascontiguousarray(wk.T),
        "wvT": np.ascontiguousarray(wv.T), "woT": np.ascontiguousarray(wo.T),
        "bqr": bq.reshape(1, E).copy(), "bkr": bk.reshape(1, E).copy(),
        "bvr": bv.reshape(1, E).copy(), "bor": bo.reshape(1, E).copy(),
        "projT": np.ascontiguousarray(proj.T),
        "clipx": clipx.reshape(1, 1).copy(), "clipd": clipd.reshape(1, 1).copy(),
        "idbf": np.eye(128, dtype=ml_dtypes.bfloat16),
        "idf": np.eye(128, dtype=np.float32),
    }
    in_maps = []
    for c in range(NC):
        b_i, hf = c // 2, c % 2
        m = dict(shared)
        m["x"] = np.ascontiguousarray(x[b_i, hf * TOK:(hf + 1) * TOK, :])
        lo = 3 * c * 128
        for nm, w in (("wg_nat", wgate), ("wu_nat", wup)):
            sl = np.ones((ISLOT * 128, E), np.float32)
            real = w[lo:lo + ISLOT * 128, :]
            sl[:real.shape[0]] = real
            m[nm] = sl
        m["wd_nat"] = np.ascontiguousarray(wdown[c * 128:(c + 1) * 128, :])
        in_maps.append(m)

    nc = build_nc()
    res = run_bass_kernel_spmd(nc, in_maps, core_ids=list(range(NC)))
    out = np.empty((B, NSEQ, E), np.float32)
    for c in range(NC):
        b_i, hf = c // 2, c % 2
        out[b_i, hf * TOK:(hf + 1) * TOK, :] = res.results[c]["y"]
    return out


# version 15 (final)
# speedup vs baseline: 1.3775x; 1.0162x over previous
import numpy as np
import contextlib
import concourse.bass as bass
import concourse.mybir as mybir
from concourse.alu_op_type import AluOpType as OP
from concourse.bass_utils import run_bass_kernel_spmd

F32 = mybir.dt.float32
BF16 = mybir.dt.bfloat16
I32 = mybir.dt.int32
AF = mybir.ActivationFunctionType
AX = mybir.AxisListType.X

B, NSEQ, E = 4, 4096, 1024
TOK = 2048
TCH = 16
T4 = 4
ECH = 8
H, D, M = 16, 64, 256
IFF, ICH = 2816, 22
ISLOT = 3
NC = 8
RATIO = 0.0625
EPS = 1e-4
SQD = float(64.0 ** -0.25)
LN_EPS = 1e-5
QP = 127.49995
QN = -128.49995
NW_G = 1024 * ISLOT * 128
NW_D = 2816 * 128
NW = 2 * NW_G + NW_D

A1, A2, A3, A4 = 0, 32768, 65536, 98304
R0 = 131072
S0 = R0 + 61440
TOTAL = 212800

SY, GP, VE, SC, TE = "sync", "gpsimd", "vector", "scalar", "tensor"


def build_nc():
    nc = bass.Bass("TRN2", target_bir_lowering=False, detect_race_conditions=False)

    d_in = lambda n, s, dt=F32: nc.dram_tensor(n, list(s), dt, kind="ExternalInput").ap()
    x_h = d_in("x", (TOK, E))
    wqT_h = d_in("wqT", (E, E)); wkT_h = d_in("wkT", (E, E))
    wvT_h = d_in("wvT", (E, E)); woT_h = d_in("woT", (E, E))
    bqr_h = d_in("bqr", (1, E)); bkr_h = d_in("bkr", (1, E))
    bvr_h = d_in("bvr", (1, E)); bor_h = d_in("bor", (1, E))
    projT_h = d_in("projT", (D, M))
    clipx_h = d_in("clipx", (1, 1)); clipd_h = d_in("clipd", (1, 1))
    wg_h = d_in("wg_nat", (ISLOT * 128, E))
    wu_h = d_in("wu_nat", (ISLOT * 128, E))
    wd_h = d_in("wd_nat", (128, IFF))
    idbf_h = d_in("idbf", (128, 128), BF16)
    idf_h = d_in("idf", (128, 128), F32)
    y_h = nc.dram_tensor("y", [TOK, E], F32, kind="ExternalOutput").ap()


    xn32h = nc.dram_tensor("xn32h", [TOK, E], F32).ap()
    hh = nc.dram_tensor("hh", [ICH, 128, TOK], F32).ap()
    agw_in = nc.dram_tensor("agw_in", [1, NW], BF16)
    agw_out = nc.dram_tensor("agw_out", [NC, NW], BF16)
    ags_in = nc.dram_tensor("ags_in", [1, 768], F32)
    ags_out = nc.dram_tensor("ags_out", [NC, 768], F32)
    arx_in = nc.dram_tensor("arx_in", [1, 1], F32)
    arx_out = nc.dram_tensor("arx_out", [1, 1], F32)
    ars_in = nc.dram_tensor("ars_in", [1, 1], F32)
    ars_out = nc.dram_tensor("ars_out", [1, 1], F32)
    arh_in = nc.dram_tensor("arh_in", [1, 1], F32)
    arh_out = nc.dram_tensor("arh_out", [1, 1], F32)
    arp_in = nc.dram_tensor("arp_in", [129, 2080], F32)
    arp_out = nc.dram_tensor("arp_out", [129, 2080], F32)


    es = contextlib.ExitStack()
    big = es.enter_context(nc.sbuf_tensor("big", [128, TOTAL], mybir.dt.uint8))
    base = nc.lookup_mloc(big).addr

    def place(name, shape, dtype, off):
        per_part = int(np.prod(shape[1:])) * mybir.dt.size(dtype)
        assert off % 32 == 0 and off + per_part <= TOTAL, (name, off, per_part)
        h = nc.virtual_sbuf_tensor(name, list(shape), dtype)
        nc._tensor(h.name, h.shape, h.dtype, type="SB")
        mloc = nc.lookup_mloc(h)
        mloc.allocated = True
        mloc.addr = base + off
        return h

    a1 = place("a1", [128, 16384], BF16, A1)
    a2 = place("a2", [128, 16384], BF16, A2)
    a3 = place("a3", [128, 16384], BF16, A3)
    a4 = place("a4", [128, 16384], BF16, A4)

    msc = place("msc", [128, 4096], F32, R0)
    xbf = place("xbf", [128, 2048], BF16, R0 + 16384)
    wi32 = place("wi32", [128, 1024], I32, R0 + 20480)
    ttile = place("ttile", [128, 1024], BF16, R0 + 24576)
    vw = place("vw", [128, 8192], BF16, R0)
    qkw = place("qkw", [128, 2048], BF16, R0 + 39936)
    wstage = place("wstage", [128, 2816], F32, R0 + 26624)
    wxbf = place("wxbf", [128, 1024], BF16, R0 + 37888)
    ek = place("ek", [128, 4096], BF16, R0 + 20480)
    ksq = place("ksq", [128, 4096], BF16, R0 + 28672)
    xnr = place("xnr", [128, 2048], F32, R0)
    i32s = place("i32s", [128, 2048], I32, R0 + 8192)
    xbf2 = place("xbf2", [128, 1024], BF16, R0 + 16384)
    hst = place("hst", [128, 8192], F32, R0)
    gub = place("gub", [128, 8192], BF16, R0 + 32768)
    sgs = place("sgs", [128, 1024], F32, R0 + 49152)
    eq = place("eq", [128, 4096], BF16, R0 + 8192)
    aqs = place("aqs", [128, 1024], BF16, R0 + 40960)
    stk2 = place("stk2", [2, 4096], BF16, R0 + 43008)
    lnrow = place("lnrow", [65, 2048], F32, R0 + 16384)
    densb = place("densb", [65, 2048], F32, R0 + 51200)
    ibf = place("ibf", [65, 2048], BF16, R0 + 24576)
    i32s5 = place("i32s5", [128, 2048], I32, R0 + 16384)
    dbuf = place("dbuf", [128, 11264], BF16, R0)
    wobuf = place("wobuf", [128, 4096], BF16, R0 + 22528)
    xst = place("xst", [128, 1024], F32, R0 + 30720)
    tmpst = place("tmpst", [128, 1024], F32, R0 + 34816)
    yst = place("yst", [128, 1024], F32, R0 + 38912)
    ust = place("ust", [128, 130], F32, R0 + 36864)
    rm_all = place("rm_all", [128, 256], F32, R0 + 37440)
    upst = place("upst", [128, 520], F32, R0 + 0)
    ct_all = place("ct_all", [128, 2080], BF16, R0 + 2080)
    vsr_all = place("vsr_all", [1, 1040], F32, R0 + 16384)
    anum0 = place("anum0", [64, 2048], BF16, R0 + 20544)
    anum1 = place("anum1", [64, 2048], BF16, R0 + 36864)
    ibf1 = place("ibf1", [65, 2048], BF16, R0 + 43008)

    off = [S0]
    def sp(name, shape, dtype):
        per_part = int(np.prod(shape[1:])) * mybir.dt.size(dtype)
        h = place(name, shape, dtype, off[0])
        off[0] += (per_part + 63) // 64 * 64
        return h
    projs = sp("projs", [128, 256], BF16)
    idbf = sp("idbf_s", [128, 128], BF16)
    idf = sp("idf_s", [128, 128], F32)
    bqrb = sp("bqrb", [1, 1024], BF16)
    bkrb = sp("bkrb", [1, 1024], BF16)
    stats = sp("stats", [128, 96], F32)
    qstats = sp("qstats", [128, 4], F32)
    acx = sp("acx", [128, 16], F32)
    ach = sp("ach", [128, 88], F32)
    seffg = sp("seffg", [128, 22], F32)
    seffu = sp("seffu", [128, 22], F32)
    negb = sp("negb", [128, 32], F32)
    nbh = sp("nbh", [128, 16], BF16)
    nbl = sp("nbl", [128, 16], BF16)
    ntr2 = sp("ntr2", [16, 256], BF16)
    ones2b = sp("ones2b", [2, 128], BF16)
    stq = sp("stq", [128, 16], F32)
    dcols = sp("dcols", [128, 16], F32)
    negtr = sp("negtr", [16, 128], F32)
    pp = sp("pp", [128, 16], F32)
    dencol = sp("dencol", [128, 16], F32)
    rme = sp("rme", [128, 16], F32)
    ones1b = sp("ones1b", [1, 128], BF16)
    onescb = sp("onescb", [128, 1], BF16)
    onesn16 = sp("onesn16", [128, 1], BF16)
    onesepsf = sp("onesepsf", [1, 128], F32)
    ones64f = sp("ones64f", [1, 64], F32)
    ones512b = sp("ones512b", [1, 512], BF16)
    ones65 = sp("ones65", [65, 128], BF16)
    csr = sp("csr", [1, 1040], BF16)
    csrc = sp("csrc", [65, 32], F32)
    rcell = sp("rcell", [1, 16], BF16)
    vss = sp("vss", [1, 80], F32)
    vsr = sp("vsr", [1, 80], F32)
    rowx = sp("rowx", [1, 128], F32)
    cells = sp("cells", [1, 32], F32)
    epscol = sp("epscol", [128, 1], F32)
    bvrb = sp("bvrb", [1, 1024], BF16)
    borb = sp("borb", [1, 1024], BF16)
    prf32 = place("prf32", [128, 256], F32, R0 + 49152)
    assert off[0] <= TOTAL, off[0]

    PA = es.enter_context(nc.psum_tensor("PA", [128, 512], F32))
    PB = es.enter_context(nc.psum_tensor("PB", [128, 512], F32))
    PC = es.enter_context(nc.psum_tensor("PC", [128, 512], F32))
    PD = es.enter_context(nc.psum_tensor("PD", [128, 512], F32))
    PE = es.enter_context(nc.psum_tensor("PE", [128, 512], F32))
    PF = es.enter_context(nc.psum_tensor("PF", [128, 512], F32))
    PG = es.enter_context(nc.psum_tensor("PG", [128, 512], F32))
    PT = es.enter_context(nc.psum_tensor("PT", [128, 1024], BF16))

    class Sem:
        def __init__(self, name):
            self.h = es.enter_context(nc.semaphore(name))
            self.v = 0
    LX, LG, VV, SS, TT = (Sem(n) for n in ("LX", "LG", "VV", "SS", "TT"))
    CC = [Sem(f"CC{i}") for i in range(6)]
    S_X = [Sem("SX0"), Sem("SX1")]
    S_SP = [Sem("SSP0"), Sem("SSP1"), Sem("SSP2"), Sem("SSP3")]
    S_G = [Sem("SG0"), Sem("SG1"), Sem("SG2"), Sem("SG3")]
    S_QK = [Sem("SQK0"), Sem("SQK1")]
    S_O = [Sem("SO0"), Sem("SO1")]
    S_DB = Sem("SDB")
    S_W = Sem("SW")
    S_VW = Sem("SVW")
    S_WO = Sem("SWO")
    S_AGW = Sem("SAGW")
    S_AGS = Sem("SAGS")
    S_ARP = Sem("SARP")
    S_CTL = Sem("SCTL")
    S_SEF = Sem("SSEF")
    BR = Sem("BR")
    SY_SEMS = [LX, S_X[0], S_X[1], S_SP[0], S_SP[1], S_SP[2], S_SP[3],
               S_G[0], S_G[1], S_G[2], S_G[3], S_O[0], S_O[1], S_DB, S_W]
    GP_SEMS = [LG, S_QK[0], S_QK[1], S_VW, S_WO, S_AGW, S_AGS, S_ARP, S_CTL, S_SEF]

    PROG = []
    def em(eng, fn):
        PROG.append((eng, fn))
    def wt(eng, sem, val):
        if val and val > 0:
            em(eng, lambda e, s=sem.h, v=val: e.wait_ge(s, v))
    def dma(eng, sem, out, in_):
        sem.v += 16
        em(eng, lambda e, o=out, i=in_, s=sem.h: e.dma_start(out=o, in_=i).then_inc(s, 16))
        return sem.v
    def op(eng, fn, sem=None):
        if sem is None:
            em(eng, fn)
            return None
        sem.v += 1
        em(eng, lambda e, f=fn, s=sem.h: f(e).then_inc(s, 1))
        return sem.v

    def ve_step(fn):
        v = op(VE, fn, VV)
        wt(VE, VV, v)
        return v

    class Guard:
        def __init__(self):
            self.conds = []
        def acquire(self, eng):
            for s, v in self.conds:
                wt(eng, s, v)
            self.conds = []
        def release(self, sem, val):
            if val:
                self.conds.append((sem, val))
    gPA, gPB, gPC, gPD, gPE, gPF, gPG, gPT = (Guard() for _ in range(8))

    br_round = [0]

    def barrier():
        for s in SY_SEMS:
            wt(SY, s, s.v)
        em(SY, lambda e: e.sem_inc(BR.h, 1)); BR.v += 1
        for s in GP_SEMS:
            wt(GP, s, s.v)
        em(GP, lambda e: e.sem_inc(BR.h, 1)); BR.v += 1
        br_round[0] = BR.v
        vals = [(BR, BR.v), (VV, VV.v), (SS, SS.v), (TT, TT.v)]
        for eng in (SY, GP, VE, SC, TE):
            for s, v in vals:
                wt(eng, s, v)

    op(VE, lambda e: e.memset(ones1b[:, :], 1.0))
    op(VE, lambda e: e.memset(ones2b[:, :], 1.0))
    op(VE, lambda e: e.memset(onescb[:, :], 1.0))
    op(VE, lambda e: e.memset(onesn16[:, :], -0.0625))
    op(VE, lambda e: e.memset(onesepsf[:, :], RATIO * EPS))
    op(VE, lambda e: e.memset(ones64f[:, :], 1.0))
    op(VE, lambda e: e.memset(ones512b[:, :], 1.0))
    op(VE, lambda e: e.memset(ones65[:, :], 1.0))
    op(VE, lambda e: e.memset(vss[0:1, 64:65], float(TOK)))
    op(VE, lambda e: e.memset(epscol[:, :], LN_EPS), VV)
    dma(SY, LX, idbf[:, :], idbf_h)
    dma(SY, LX, idf[:, :], idf_h)
    dma(SY, LX, msc[0:1, 2048:3072], bqr_h)
    dma(SY, LX, msc[0:1, 3072:4096], bkr_h)
    dma(SY, LX, prf32[0:64, :], projT_h)
    dma(SY, LX, prf32[64:128, :], projT_h)
    dma(SY, LX, cells[0:1, 12:13], clipx_h)
    dma(SY, LX, cells[0:1, 13:14], clipd_h)
    dma(SY, LX, msc[0:1, 0:1024], bvr_h)
    l_pre = dma(SY, LX, msc[0:1, 1024:2048], bor_h)
    wt(SC, LX, l_pre)
    op(SC, lambda e: e.activation(projs[0:64, :], prf32[0:64, :], AF.Copy, scale=SQD))
    op(SC, lambda e: e.activation(projs[64:128, :], prf32[64:128, :], AF.Copy, scale=SQD))
    op(SC, lambda e: e.activation(bvrb[:, :], msc[0:1, 0:1024], AF.Copy))
    op(SC, lambda e: e.activation(borb[:, :], msc[0:1, 1024:2048], AF.Copy))
    op(SC, lambda e: e.activation(bqrb[:, :], msc[0:1, 2048:3072], AF.Copy))
    s_pre = op(SC, lambda e: e.activation(bkrb[:, :], msc[0:1, 3072:4096], AF.Copy), SS)

    slot_free = [None, None]
    for c in range(TCH):
        sl = (c % 2) * 1024
        if slot_free[c % 2] is not None:
            vv_f, ss_f = slot_free[c % 2]
            wt(SY, VV, vv_f); wt(SY, SS, ss_f)
        if c == 0:
            wt(SY, SS, s_pre)
        lx = dma(SY, S_X[c % 2], msc[:, sl:sl + 1024], x_h[c * 128:(c + 1) * 128, :])
        wt(VE, S_X[c % 2], lx)
        v_r = op(VE, lambda e, sl=sl, c=c: e.reduce_sum(
            stats[:, c:c + 1], msc[:, sl:sl + 1024], AX), VV)
        wt(SC, S_X[c % 2], lx)
        s_r = op(SC, lambda e, sl=sl, c=c: e.activation(
            xbf[:, sl:sl + 1024], msc[:, sl:sl + 1024], AF.Square,
            accum_out=stats[:, 16 + c:17 + c]), SS)
        slot_free[c % 2] = (v_r, s_r)
    wt(VE, SS, SS.v)
    wt(VE, VV, VV.v)
    ve_step(lambda e: e.tensor_scalar(stats[:, 32:48], stats[:, 0:16], 1.0 / E, None, OP.mult))
    ve_step(lambda e: e.tensor_scalar(stats[:, 48:64], stats[:, 16:32], 1.0 / E, None, OP.mult))
    ve_step(lambda e: e.tensor_tensor(stats[:, 64:80], stats[:, 32:48], stats[:, 32:48], OP.mult))
    v_var = ve_step(lambda e: e.tensor_tensor(stats[:, 64:80], stats[:, 48:64], stats[:, 64:80], OP.subtract))
    wt(SC, VV, v_var)
    s_sd = op(SC, lambda e: e.activation(stats[:, 80:96], stats[:, 64:80], AF.Sqrt, bias=epscol[:, 0:1]), SS)
    wt(VE, SS, s_sd)
    v_rstd = ve_step(lambda e: e.reciprocal(stats[:, 80:96], stats[:, 80:96]))

    g_region = agw_in.ap()[0:1, 0:NW_G].rearrange("a (p c) -> (a p) c", p=128)
    u_region = agw_in.ap()[0:1, NW_G:2 * NW_G].rearrange("a (p c) -> (a p) c", p=128)
    d_region = agw_in.ap()[0:1, 2 * NW_G:NW].rearrange("a (p c) -> (a p) c", p=128)
    last_agw = 0
    g_ttile = Guard()
    g_qstats = Guard()
    g_wi32 = Guard()
    g_wxbf = Guard()
    wslot = [Guard(), Guard()]

    def quant_block(w_slice_ap, tile_dst_blk, is_down, s_col):
        nonlocal last_agw
        width = w_slice_ap.shape[1]
        g_qstats.acquire(VE)
        v_amx = op(VE, lambda e: e.tensor_reduce(qstats[:, 0:1], w_slice_ap, AX, OP.max,
                                                 apply_absolute_value=True), VV)
        wt(VE, VV, v_amx)
        ve_step(lambda e: e.tensor_scalar(qstats[:, 1:2], qstats[:, 0:1], 1.0 / 127.0,
                                          None, OP.mult))
        ve_step(lambda e: e.reciprocal(qstats[:, 2:3], qstats[:, 0:1]))
        v_is = ve_step(lambda e: e.tensor_scalar(qstats[:, 2:3], qstats[:, 2:3], 127.0,
                                                 None, OP.mult))
        if s_col is not None:
            wt(GP, VV, v_is)
            l_ags = dma(GP, S_AGS, ags_in.ap()[0:1, s_col:s_col + 128], qstats[:, 0:1])
            g_qstats.release(S_AGS, l_ags)
        nblk = width // 128
        for bb in range(0, nblk, 8):
            hi = min(bb + 8, nblk)
            w = (hi - bb) * 128
            g_wi32.acquire(VE)
            v_q = op(VE, lambda e, bb=bb, w=w: e.tensor_scalar(
                wi32[:, 0:w], w_slice_ap[:, bb * 128:bb * 128 + w],
                qstats[:, 2:3], None, OP.mult), VV)
            wt(SC, VV, v_q)
            g_wxbf.acquire(SC)
            if is_down:
                s_q = op(SC, lambda e, w=w: e.activation(
                    wxbf[:, 0:w], wi32[:, 0:w], AF.Copy, scale=qstats[:, 1:2]), SS)
            else:
                s_q = op(SC, lambda e, w=w: e.activation(
                    wxbf[:, 0:w], wi32[:, 0:w], AF.Copy), SS)
            g_wi32.release(SS, s_q)
            if is_down:
                g_qstats.release(SS, s_q)
            gPT.acquire(TE)
            wt(TE, SS, s_q)
            for j in range(hi - bb):
                t_tp = op(TE, lambda e, j=j: e.transpose(
                    PT[:, j * 128:(j + 1) * 128], wxbf[:, j * 128:(j + 1) * 128],
                    idbf[:, :]), TT if j == hi - bb - 1 else None)
            g_wxbf.release(TT, t_tp)
            g_ttile.acquire(VE)
            wt(VE, TT, t_tp)
            v_ev = op(VE, lambda e, w=w: e.tensor_copy(ttile[:, 0:w], PT[:, 0:w]), VV)
            gPT.release(VV, v_ev)
            wt(GP, VV, v_ev)
            last_agw = dma(GP, S_AGW, tile_dst_blk(bb, hi - bb),
                           ttile[:, 0:w].rearrange("p (j c) -> p j c", c=128))
            g_ttile.release(S_AGW, last_agw)

    def mk_dst(region, icl):
        return lambda bb, n: region[:, :].rearrange("p (j c) -> p j c", c=384)\
            [:, bb:bb + n, icl * 128:(icl + 1) * 128]
    p05_tasks = []
    for w_h, region, soff in ((wg_h, g_region, 0), (wu_h, u_region, 384)):
        for icl in range(ISLOT):
            p05_tasks.append((w_h[icl * 128:(icl + 1) * 128, :],
                              mk_dst(region, icl), False, soff + icl * 128))
    p05_tasks.append((wd_h, lambda bb, n: d_region[:, bb * 128:(bb + n) * 128]
                      .rearrange("p (j c) -> p j c", c=128), True, None))

    def p05_load(i):
        src, _, is_down, _ = p05_tasks[i]
        if is_down:
            wslot[0].acquire(SY)
            wslot[1].acquire(SY)
            return dma(SY, S_W, wstage[:, 0:2816], src)
        sl = (i % 2) * 1024
        wslot[i % 2].acquire(SY)
        return dma(SY, S_W, wstage[:, sl:sl + 1024], src)

    def p05_compute(i, lw):
        _, dst, is_down, s_col = p05_tasks[i]
        wt(VE, S_W, lw)
        if is_down:
            quant_block(wstage[:, 0:2816], dst, True, None)
            wslot[0].release(VV, VV.v)
            wslot[1].release(VV, VV.v)
        else:
            sl = (i % 2) * 1024
            quant_block(wstage[:, sl:sl + 1024], dst, False, s_col)
            wslot[i % 2].release(VV, VV.v)

    xslot_free = [None, None]
    nslot_free = [None, None]
    g_xbf = [Guard(), Guard()]
    wt(SY, S_X[0], S_X[0].v)
    wt(SY, S_X[1], S_X[1].v)
    for c in range(TCH):
        slx = 2048 + (c % 2) * 1024
        sln = (c % 2) * 1024
        if xslot_free[c % 2] is not None:
            wt(SY, VV, xslot_free[c % 2])
        lx = dma(SY, S_X[c % 2], msc[:, slx:slx + 1024], x_h[c * 128:(c + 1) * 128, :])
        wt(VE, S_X[c % 2], lx)
        if c == 0:
            wt(VE, VV, v_rstd)
        if nslot_free[c % 2] is not None:
            lx_sp, ss_ca = nslot_free[c % 2]
            wt(VE, S_SP[c % 2], lx_sp); wt(VE, SS, ss_ca)
        v_n = op(VE, lambda e, slx=slx, sln=sln, c=c: e.tensor_scalar(
            msc[:, sln:sln + 1024], msc[:, slx:slx + 1024],
            stats[:, 32 + c:33 + c], stats[:, 80 + c:81 + c],
            OP.subtract, OP.mult), VV)
        xslot_free[c % 2] = v_n
        v_am = op(VE, lambda e, sln=sln, c=c: e.tensor_reduce(
            acx[:, c:c + 1], msc[:, sln:sln + 1024], AX, OP.max,
            apply_absolute_value=True), VV)
        wt(SY, VV, v_am)
        lx_sp = dma(SY, S_SP[c % 2], xn32h[c * 128:(c + 1) * 128, :], msc[:, sln:sln + 1024])
        wt(SC, VV, v_am)
        g_xbf[c % 2].acquire(SC)
        s_ca = op(SC, lambda e, sln=sln: e.activation(
            xbf[:, sln:sln + 1024], msc[:, sln:sln + 1024], AF.Copy), SS)
        nslot_free[c % 2] = (lx_sp, s_ca)
        gPT.acquire(TE)
        wt(TE, SS, s_ca)
        for j in range(8):
            t_tp = op(TE, lambda e, sln=sln, j=j: e.transpose(
                PT[:, j * 128:(j + 1) * 128], xbf[:, sln + j * 128:sln + (j + 1) * 128],
                idbf[:, :]), TT if j == 7 else None)
        g_xbf[c % 2].release(TT, t_tp)
        wt(VE, TT, t_tp)
        dst = a1[:, :].rearrange("p (g t) -> p g t", g=8)[:, :, c * 128:(c + 1) * 128]
        v_ev = op(VE, lambda e, dst=dst: e.tensor_copy(
            dst, PT[:, :].rearrange("p (g t) -> p g t", g=8)), VV)
        gPT.release(VV, v_ev)
        if c % 2 == 0 and c // 2 < len(p05_tasks):
            p05_pending = p05_load(c // 2)
        elif c % 2 == 1 and c // 2 < len(p05_tasks):
            p05_compute(c // 2, p05_pending)
    vv_a1 = VV.v
    ss_msc = SS.v
    sp0, sp1 = S_SP[0].v, S_SP[1].v
    wt(VE, VV, VV.v)
    v_ax = op(VE, lambda e: e.tensor_reduce(pp[:, 15:16], acx[:, :], AX, OP.max), VV)
    wt(GP, VV, v_ax)
    g1 = dma(GP, S_CTL, rowx[0:1, :], pp[:, 15:16])
    wt(VE, S_CTL, g1)
    v_axc = op(VE, lambda e: e.reduce_max(cells[0:1, 0:1], rowx[0:1, :], AX), VV)
    wt(GP, VV, v_axc)
    dma(GP, S_CTL, arx_in.ap(), cells[0:1, 0:1])
    op(GP, lambda e: e.collective_compute(
        "AllReduce", OP.max, replica_groups=[list(range(NC))],
        ins=[arx_in.ap().opt()], outs=[arx_out.ap().opt()]), CC[0])
    wt(GP, S_AGW, last_agw)
    wt(GP, S_AGS, S_AGS.v)
    op(GP, lambda e: e.collective_compute(
        "AllGather", OP.bypass, replica_groups=[list(range(NC))],
        ins=[agw_in.ap().opt()], outs=[agw_out.ap().opt()]), CC[1])
    op(GP, lambda e: e.collective_compute(
        "AllGather", OP.bypass, replica_groups=[list(range(NC))],
        ins=[ags_in.ap().opt()], outs=[ags_out.ap().opt()]), CC[2])

    wt(GP, VV, vv_a1)
    wt(GP, SS, ss_msc)
    wt(GP, S_SP[0], sp0)
    wt(GP, S_SP[1], sp1)
    for ei in range(ECH):
        dma(GP, S_VW, vw[:, ei * 1024:(ei + 1) * 1024],
            wvT_h[ei * 128:(ei + 1) * 128, :])
    lg_vw = S_VW.v
    wt(TE, VV, vv_a1)
    qk_slot = [Guard(), Guard()]
    psq = {0: (PA, gPA), 1: (PB, gPB)}
    psq4 = {0: (PA, gPA), 1: (PB, gPB), 2: (PC, gPC), 3: (PD, gPD)}
    for wsel in range(2):
        dst = a2 if wsel == 0 else a3
        brow = bqrb if wsel == 0 else bkrb
        for eo in range(ECH):
            idx = wsel * ECH + eo
            sl = (idx % 2) * 1024
            qk_slot[idx % 2].acquire(GP)
            src = (wqT_h if wsel == 0 else wkT_h)[:, eo * 128:(eo + 1) * 128] \
                .rearrange("(a p) c -> p a c", p=128)
            lqk = dma(GP, S_QK[idx % 2], qkw[:, sl:sl + 1024].rearrange("p (a c) -> p a c", a=8), src)
            wt(TE, S_QK[idx % 2], lqk)
            for t in range(T4):
                ps, g = psq4[(idx * T4 + t) % 4]
                g.acquire(TE)
                for ei in range(ECH):
                    op(TE, lambda e, ps=ps, sl=sl, ei=ei, t=t: e.matmul(
                        ps[:, :], qkw[:, sl + ei * 128:sl + (ei + 1) * 128],
                        a1[:, ei * 2048 + t * 512:ei * 2048 + (t + 1) * 512],
                        start=(ei == 0), stop=False))
                tm = op(TE, lambda e, ps=ps, brow=brow, eo=eo: e.matmul(
                    ps[:, :], brow[0:1, eo * 128:(eo + 1) * 128], ones512b[0:1, :],
                    start=False, stop=True), TT)
                if t == T4 - 1:
                    qk_slot[idx % 2].release(TT, tm)
                wt(SC, TT, tm)
                s_ev = op(SC, lambda e, ps=ps, dst=dst, eo=eo, t=t:
                          e.activation(dst[:, eo * 2048 + t * 512:eo * 2048 + (t + 1) * 512],
                                       ps[:, :], AF.Copy), SS)
                g.release(SS, s_ev)
    wt(TE, S_VW, lg_vw)
    for t in range(TCH):
        for eo2 in range(2):
            ps, g = psq4[(t * 2 + eo2) % 4]
            g.acquire(TE)
            for ei in range(ECH):
                op(TE, lambda e, ps=ps, t=t, eo2=eo2, ei=ei: e.matmul(
                    ps[:, :], a1[:, ei * 2048 + t * 128:ei * 2048 + (t + 1) * 128],
                    vw[:, ei * 1024 + eo2 * 512:ei * 1024 + (eo2 + 1) * 512],
                    start=(ei == 0), stop=False))
            tm = op(TE, lambda e, ps=ps, eo2=eo2: e.matmul(
                ps[:, :], ones1b[0:1, :], bvrb[0:1, eo2 * 512:(eo2 + 1) * 512],
                start=False, stop=True), TT)
            wt(VE, TT, tm)
            v_ev = op(VE, lambda e, ps=ps, t=t, eo2=eo2: e.tensor_copy(
                a4[:, t * 1024 + eo2 * 512:t * 1024 + (eo2 + 1) * 512], ps[:, :]), VV)
            g.release(VV, v_ev)

    barrier()

    ksq_slot = [Guard(), Guard()]
    g_dcols = Guard()
    g_ek = Guard()
    v_z = op(VE, lambda e: e.memset(msc[0:1, 0:1040], 0.0), VV)
    wt(GP, VV, v_z)
    dma(GP, S_ARP, arp_in.ap()[128:129, 1040:2080], msc[0:1, 0:1040])
    for hc in range(ECH):
        ksq_slot[hc % 2].acquire(SC)
        s_ksq = op(SC, lambda e, hc=hc: e.activation(
            ksq[:, (hc % 2) * 2048:(hc % 2) * 2048 + 2048],
            a3[:, hc * 2048:(hc + 1) * 2048], AF.Square), SS)
        for hh_i in range(2):
            h = 2 * hc + hh_i
            hp = hh_i * 64
            gPC.acquire(TE)
            wt(TE, SS, s_ksq)
            for t in range(TCH):
                tm_d = op(TE, lambda e, hc=hc, hp=hp, t=t: e.matmul(
                    PC[:, t:t + 1],
                    ksq[hp:hp + 64, (hc % 2) * 2048 + t * 128:(hc % 2) * 2048 + (t + 1) * 128],
                    onesn16[hp:hp + 64, 0:1], start=True, stop=True),
                    TT if t == TCH - 1 else None)
            if hh_i == 1:
                ksq_slot[hc % 2].release(TT, tm_d)
            g_dcols.acquire(VE)
            wt(VE, TT, tm_d)
            v_d = op(VE, lambda e: e.tensor_copy(dcols[:, 0:16], PC[:, 0:16]), VV)
            gPC.release(VV, v_d)
            def dd_mm(t):
                ps, g = psq[t % 2]
                g.acquire(TE)
                return ps, g, op(TE, lambda e, ps=ps, hc=hc, hp=hp, t=t: e.matmul(
                    ps[:, 0:256],
                    a3[hp:hp + 64, hc * 2048 + t * 128:hc * 2048 + (t + 1) * 128],
                    projs[hp:hp + 64, :], start=True, stop=True), TT)
            dd_q = [dd_mm(0)]
            for t in range(TCH):
                ps, g, tm_dd = dd_q[t]
                if t + 1 < TCH:
                    dd_q.append(dd_mm(t + 1))
                wt(SC, TT, tm_dd)
                if t == 0:
                    wt(SC, VV, v_d)
                g_ek.acquire(SC) if t == 0 else None
                s_ek = op(SC, lambda e, ps=ps, t=t: e.activation(
                    ek[:, t * 256:(t + 1) * 256], ps[:, 0:256], AF.Exp,
                    bias=dcols[:, t:t + 1]), SS)
                g.release(SS, s_ek)
                wt(VE, SS, s_ek)
                v_rm = op(VE, lambda e, h=h, t=t: e.reduce_max(
                    rme[:, t:t + 1], ek[:, t * 256:(t + 1) * 256], AX), VV)
                wt(TE, SS, s_ek)
                if t == 0:
                    gPD.acquire(TE); gPE.acquire(TE); gPF.acquire(TE)
                    gPG.acquire(TE); gPC.acquire(TE)
                v_sl = a4[:, t * 1024 + h * 64:t * 1024 + (h + 1) * 64]
                op(TE, lambda e, t=t, v_sl=v_sl: e.matmul(
                    PD[:, 0:64], ek[:, t * 256:t * 256 + 128], v_sl,
                    start=(t == 0), stop=(t == TCH - 1)))
                op(TE, lambda e, t=t, v_sl=v_sl: e.matmul(
                    PE[:, 0:64], ek[:, t * 256 + 128:t * 256 + 256], v_sl,
                    start=(t == 0), stop=(t == TCH - 1)))
                op(TE, lambda e, t=t: e.matmul(
                    PF[:, 0:1], ek[:, t * 256:t * 256 + 128], onescb[:, 0:1],
                    start=(t == 0), stop=(t == TCH - 1)))
                op(TE, lambda e, t=t: e.matmul(
                    PG[:, 0:1], ek[:, t * 256 + 128:t * 256 + 256], onescb[:, 0:1],
                    start=(t == 0), stop=(t == TCH - 1)))
                tm_vs = op(TE, lambda e, t=t, v_sl=v_sl: e.matmul(
                    PC[0:1, 0:64], onescb[:, 0:1], v_sl,
                    start=(t == 0), stop=(t == TCH - 1)),
                    TT if t == TCH - 1 else None)
                if t == TCH - 1:
                    g_ek.release(TT, tm_vs)
                    g_ek.release(VV, v_rm)
            wt(SC, VV, v_rm)
            s_ln = op(SC, lambda e: e.activation(rme[:, 0:16], rme[:, 0:16], AF.Ln), SS)
            wt(VE, SS, s_ln)
            v_rma = op(VE, lambda e, h=h: e.tensor_tensor(
                rm_all[:, h * 16:h * 16 + 16], rme[:, 0:16], dcols[:, 0:16],
                OP.subtract), VV)
            g_dcols.release(VV, v_rma)
            wt(VE, TT, tm_vs)
            op(VE, lambda e: e.tensor_copy(ust[:, 0:64], PD[:, 0:64]))
            op(VE, lambda e: e.tensor_copy(ust[:, 64:65], PF[:, 0:1]))
            op(VE, lambda e: e.tensor_copy(ust[:, 65:129], PE[:, 0:64]))
            op(VE, lambda e: e.tensor_copy(ust[:, 129:130], PG[:, 0:1]))
            v_u = op(VE, lambda e: e.tensor_copy(vss[0:1, 0:64], PC[0:1, 0:64]), VV)
            gPD.release(VV, v_u); gPE.release(VV, v_u); gPF.release(VV, v_u)
            gPG.release(VV, v_u); gPC.release(VV, v_u)
            wt(GP, VV, v_u)
            w1 = dma(GP, S_ARP, arp_in.ap()[0:128, (2 * h) * 65:(2 * h + 2) * 65],
                     ust[:, 0:130])
            w2 = dma(GP, S_ARP, arp_in.ap()[128:129, h * 65:(h + 1) * 65],
                     vss[0:1, 0:65])
            wt(VE, S_ARP, w2)
    wt(VE, VV, VV.v)
    v_dk = op(VE, lambda e: e.reduce_max(pp[:, 14:15], rm_all[:, :], AX), VV)
    wt(GP, VV, v_dk)
    g2 = dma(GP, S_CTL, rowx[0:1, :], pp[:, 14:15])
    wt(VE, S_CTL, g2)
    v_dkc = op(VE, lambda e: e.reduce_max(cells[0:1, 5:6], rowx[0:1, :], AX), VV)
    wt(GP, VV, v_dkc)
    dma(GP, S_CTL, ars_in.ap(), cells[0:1, 5:6])
    wt(GP, S_ARP, S_ARP.v)
    op(GP, lambda e: e.collective_compute(
        "AllReduce", OP.add, replica_groups=[[0, 1], [2, 3], [4, 5], [6, 7]],
        ins=[arp_in.ap().opt()], outs=[arp_out.ap().opt()]), CC[3])
    op(GP, lambda e: e.collective_compute(
        "AllReduce", OP.max, replica_groups=[list(range(NC))],
        ins=[ars_in.ap().opt()], outs=[ars_out.ap().opt()]), CC[4])

    barrier()

    wt(GP, CC[0], 1)
    g_ax = dma(GP, S_CTL, cells[0:1, 0:1], arx_out.ap())
    wt(SC, S_CTL, g_ax)
    s_sig = op(SC, lambda e: e.activation(cells[0:1, 1:2], cells[0:1, 12:13], AF.Sigmoid), SS)
    wt(VE, SS, s_sig)
    ve_step(lambda e: e.tensor_tensor(cells[0:1, 2:3], cells[0:1, 1:2], cells[0:1, 0:1], OP.mult))
    ve_step(lambda e: e.tensor_scalar(cells[0:1, 2:3], cells[0:1, 2:3], 1.0 / 127.0, None, OP.mult))
    ve_step(lambda e: e.reciprocal(cells[0:1, 3:4], cells[0:1, 2:3]))
    v_sxf = ve_step(lambda e: e.tensor_scalar(cells[0:1, 4:5], cells[0:1, 2:3], 1.0 / 127.0, None, OP.mult))

    def repl128(cell_ap, dst_col):
        ve_step(lambda e: e.tensor_copy(rcell[0:1, 0:1], cell_ap))
        ve_step(lambda e: e.tensor_tensor(vsr[0:1, 70:71], cell_ap, rcell[0:1, 0:1], OP.subtract))
        v_r = ve_step(lambda e: e.tensor_copy(rcell[0:1, 1:2], vsr[0:1, 70:71]))
        gPG.acquire(TE)
        wt(TE, VV, v_r)
        op(TE, lambda e: e.matmul(PG[:, 0:1], ones1b[0:1, :], rcell[0:1, 0:1], start=True, stop=False))
        t_r = op(TE, lambda e: e.matmul(PG[:, 0:1], ones1b[0:1, :], rcell[0:1, 1:2], start=False, stop=True), TT)
        wt(VE, TT, t_r)
        v_f = op(VE, lambda e: e.tensor_copy(dst_col, PG[:, 0:1]), VV)
        wt(VE, VV, v_f)
        gPG.release(VV, v_f)
        return v_f

    wt(VE, VV, v_sxf)
    repl128(cells[0:1, 3:4], pp[:, 0:1])
    v_sxfp = repl128(cells[0:1, 4:5], pp[:, 1:2])

    xq_slot = [None, None]
    g_i32 = [Guard(), Guard()]
    for c in range(TCH):
        sl = (c % 2) * 1024
        if xq_slot[c % 2] is not None:
            wt(SY, VV, xq_slot[c % 2])
        lxn = dma(SY, S_X[c % 2], xnr[:, sl:sl + 1024], xn32h[c * 128:(c + 1) * 128, :])
        wt(VE, S_X[c % 2], lxn)
        op(VE, lambda e, sl=sl: e.tensor_scalar(
            xnr[:, sl:sl + 1024], xnr[:, sl:sl + 1024], pp[:, 0:1], QP, OP.mult, OP.min))
        g_i32[c % 2].acquire(VE)
        v_i = op(VE, lambda e, sl=sl: e.tensor_scalar(
            i32s[:, sl:sl + 1024], xnr[:, sl:sl + 1024], QN, None, OP.max), VV)
        xq_slot[c % 2] = v_i
        wt(SC, VV, v_i)
        s_b = op(SC, lambda e, sl=sl: e.activation(
            xbf2[:, 0:1024], i32s[:, sl:sl + 1024], AF.Copy), SS)
        g_i32[c % 2].release(SS, s_b)
        gPT.acquire(TE)
        wt(TE, SS, s_b)
        for j in range(8):
            t_l = op(TE, lambda e, j=j: e.transpose(
                PT[:, j * 128:(j + 1) * 128], xbf2[:, j * 128:(j + 1) * 128],
                idbf[:, :]), TT if j == 7 else None)
        wt(VE, TT, t_l)
        dst = a4[:, :].rearrange("p (g t) -> p g t", g=8)[:, :, c * 128:(c + 1) * 128]
        v_x = op(VE, lambda e, dst=dst: e.tensor_copy(
            dst, PT[:, :].rearrange("p (g t) -> p g t", g=8)), VV)
        gPT.release(VV, v_x)
        wt(SC, TT, t_l)
    wt(GP, CC[2], 1)
    for gc in range(ICH):
        r, icl = gc // 3, gc % 3
        dma(GP, S_SEF, seffg[:, gc:gc + 1],
            ags_out.ap()[r:r + 1, icl * 128:(icl + 1) * 128])
        g_s = dma(GP, S_SEF, seffu[:, gc:gc + 1],
                  ags_out.ap()[r:r + 1, 384 + icl * 128:384 + (icl + 1) * 128])
    wt(VE, S_SEF, g_s)
    wt(VE, VV, v_sxfp)
    op(VE, lambda e: e.tensor_scalar(seffg[:, :], seffg[:, :], pp[:, 1:2], None, OP.mult))
    op(VE, lambda e: e.tensor_scalar(seffu[:, :], seffu[:, :], pp[:, 1:2], None, OP.mult), VV)

    barrier()

    wt(TE, CC[1], 1)
    wt(SY, CC[1], 1)
    gub_slot = [Guard(), Guard(), Guard(), Guard()]
    hst_slot = [Guard(), Guard(), Guard(), Guard()]
    g_sgs = [Guard(), Guard()]
    psg_m = {0: (PA, gPA), 1: (PB, gPB)}
    psu_m = {0: (PC, gPC), 1: (PD, gPD)}
    def gu_load(gc):
        r, icl = gc // 3, gc % 3
        slg = (gc % 4) * 1024
        src_g = agw_out.ap()[r:r + 1, 0:NW_G] \
            .rearrange("a (p g c) -> (a p) g c", p=128, c=384)[:, :, icl * 128:(icl + 1) * 128]
        src_u = agw_out.ap()[r:r + 1, NW_G:2 * NW_G] \
            .rearrange("a (p g c) -> (a p) g c", p=128, c=384)[:, :, icl * 128:(icl + 1) * 128]
        gub_slot[gc % 4].acquire(SY)
        dma(SY, S_G[gc % 4], gub[:, slg:slg + 1024].rearrange("p (g c) -> p g c", g=8), src_g)
        return dma(SY, S_G[gc % 4], gub[:, 4096 + slg:4096 + slg + 1024]
                   .rearrange("p (g c) -> p g c", g=8), src_u)
    lgu_q = [gu_load(0), gu_load(1), gu_load(2)]
    for gc in range(ICH):
        slg = (gc % 4) * 1024
        sl_h = (gc % 4) * 2048
        if gc + 3 < ICH:
            lgu_q.append(gu_load(gc + 3))
        lgu = lgu_q[gc]
        hst_slot[gc % 4].acquire(VE)
        wt(TE, S_G[gc % 4], lgu)
        for t in range(T4):
            psg, gg = psg_m[t % 2]
            psu, gu = psu_m[t % 2]
            gg.acquire(TE)
            for hcc in range(ECH):
                op(TE, lambda e, psg=psg, slg=slg, hcc=hcc, t=t: e.matmul(
                    psg[:, :], gub[:, slg + hcc * 128:slg + (hcc + 1) * 128],
                    a4[:, hcc * 2048 + t * 512:hcc * 2048 + (t + 1) * 512],
                    start=(hcc == 0), stop=(hcc == ECH - 1)))
            gu.acquire(TE)
            for hcc in range(ECH):
                tm_u = op(TE, lambda e, psu=psu, slg=slg, hcc=hcc, t=t: e.matmul(
                    psu[:, :], gub[:, 4096 + slg + hcc * 128:4096 + slg + (hcc + 1) * 128],
                    a4[:, hcc * 2048 + t * 512:hcc * 2048 + (t + 1) * 512],
                    start=(hcc == 0), stop=(hcc == ECH - 1)),
                    TT if hcc == ECH - 1 else None)
            if t == T4 - 1:
                gub_slot[gc % 4].release(TT, tm_u)
            wt(SC, TT, tm_u)
            g_sgs[t % 2].acquire(SC)
            s_sg = op(SC, lambda e, psg=psg, t=t, gc=gc: e.activation(
                sgs[:, (t % 2) * 512:(t % 2) * 512 + 512], psg[:, :], AF.Silu,
                scale=seffg[:, gc:gc + 1]), SS)
            gg.release(SS, s_sg)
            wt(VE, SS, s_sg)
            op(VE, lambda e, psu=psu, t=t, gc=gc, sl_h=sl_h: e.scalar_tensor_tensor(
                hst[:, sl_h + t * 512:sl_h + (t + 1) * 512], psu[:, :],
                seffu[:, gc:gc + 1], sgs[:, (t % 2) * 512:(t % 2) * 512 + 512],
                OP.mult, OP.mult))
            v_h = op(VE, lambda e, t=t, gc=gc, sl_h=sl_h: e.tensor_reduce(
                ach[:, gc * 4 + t:gc * 4 + t + 1],
                hst[:, sl_h + t * 512:sl_h + (t + 1) * 512], AX, OP.max,
                apply_absolute_value=True), VV)
            gu.release(VV, v_h)
            g_sgs[t % 2].release(VV, v_h)
        wt(SY, VV, v_h)
        l_sp = dma(SY, S_SP[gc % 4], hh[gc, :, :], hst[:, sl_h:sl_h + 2048])
        hst_slot[gc % 4].release(S_SP[gc % 4], l_sp)
    wt(VE, VV, VV.v)
    v_ah = op(VE, lambda e: e.reduce_max(pp[:, 13:14], ach[:, :], AX), VV)
    wt(GP, VV, v_ah)
    g3 = dma(GP, S_CTL, rowx[0:1, :], pp[:, 13:14])
    wt(VE, S_CTL, g3)
    v_ahc = op(VE, lambda e: e.reduce_max(cells[0:1, 8:9], rowx[0:1, :], AX), VV)
    wt(GP, VV, v_ahc)
    dma(GP, S_CTL, arh_in.ap(), cells[0:1, 8:9])
    op(GP, lambda e: e.collective_compute(
        "AllReduce", OP.max, replica_groups=[list(range(NC))],
        ins=[arh_in.ap().opt()], outs=[arh_out.ap().opt()]), CC[5])

    barrier()

    wt(GP, CC[3], 1)
    wt(GP, CC[4], 1)
    g_es = dma(GP, S_CTL, cells[0:1, 6:7], ars_out.ap())
    wt(SC, S_CTL, g_es)
    s_es = op(SC, lambda e: e.activation(cells[0:1, 6:7], cells[0:1, 6:7], AF.Exp, scale=-1.0), SS)
    wt(VE, SS, s_es)
    v_res = ve_step(lambda e: e.tensor_scalar(cells[0:1, 7:8], cells[0:1, 6:7], RATIO, None, OP.mult))
    v_resp = repl128(cells[0:1, 7:8], pp[:, 4:5])

    g_upst4 = [Guard() for _ in range(4)]
    ct_last = None
    g_vall = dma(GP, S_CTL, vsr_all[0:1, :], arp_out.ap()[128:129, 0:1040])
    up_v = {}
    def up_load(h):
        g_upst4[h % 4].acquire(GP)
        up_v[h] = dma(GP, S_CTL, upst[:, (h % 4) * 130:(h % 4) * 130 + 130],
                      arp_out.ap()[0:128, 2 * h * 65:(2 * h + 2) * 65])
    for h in range(3):
        up_load(h)
    for h in range(H):
        if h + 3 < H:
            up_load(h + 3)
        u0 = (h % 4) * 130
        gPF.acquire(TE)
        if h == 0:
            wt(TE, S_CTL, g_vall)
        t_rv = op(TE, lambda e, h=h: e.matmul(PF[:, 0:65], onesepsf[0:1, :],
                                              vsr_all[0:1, h * 65:(h + 1) * 65],
                                              start=True, stop=True), TT)
        wt(VE, TT, t_rv)
        wt(VE, S_CTL, up_v[h])
        if h == 0:
            wt(VE, VV, v_resp)
        op(VE, lambda e, h=h, u0=u0: e.scalar_tensor_tensor(
            ct_all[:, (2 * h) * 65:(2 * h) * 65 + 65], upst[:, u0:u0 + 65], pp[:, 4:5],
            PF[:, 0:65], OP.mult, OP.add))
        v_ct = op(VE, lambda e, h=h, u0=u0: e.scalar_tensor_tensor(
            ct_all[:, (2 * h + 1) * 65:(2 * h + 1) * 65 + 65], upst[:, u0 + 65:u0 + 130],
            pp[:, 4:5], PF[:, 0:65], OP.mult, OP.add), VV)
        g_upst4[h % 4].release(VV, v_ct)
        gPF.release(VV, v_ct)
        gPG.acquire(TE)
        wt(TE, VV, v_ct)
        op(TE, lambda e, h=h: e.matmul(PG[0:1, 0:65], onescb[:, 0:1],
                                       ct_all[:, (2 * h) * 65:(2 * h) * 65 + 65],
                                       start=True, stop=False))
        t_cs = op(TE, lambda e, h=h: e.matmul(PG[0:1, 0:65], onescb[:, 0:1],
                                              ct_all[:, (2 * h + 1) * 65:(2 * h + 1) * 65 + 65],
                                              start=False, stop=True), TT)
        wt(SC, TT, t_cs)
        ct_last = op(SC, lambda e, h=h: e.activation(csr[0:1, h * 65:(h + 1) * 65],
                                                     PG[0:1, 0:65], AF.Copy, scale=EPS), SS)
        gPG.release(SS, ct_last)
        if h == 0:
            gPT.acquire(TE)
        wt(TE, SS, ct_last)
        t_ctc = op(TE, lambda e, h=h: e.transpose(
            PT[0:65, 2 * h:2 * h + 1], csr[0:1, h * 65:(h + 1) * 65], ones1b[0:1, 0:1]),
            TT if h == H - 1 else None)
    wt(VE, TT, t_ctc)
    v_csrc = op(VE, lambda e: e.tensor_copy(csrc[:, :], PT[0:65, 0:32]), VV)
    gPT.release(VV, v_csrc)

    qsq_slot = [Guard(), Guard()]
    g_eq = Guard()
    g_densb = Guard()
    g_isb = [Guard(), Guard()]
    g_stk = [Guard(), Guard()]
    g_nbh = Guard()
    g_nbl = Guard()
    g_ntr2 = Guard()
    g_pt4 = Guard()
    g_pt4.release(VV, v_csrc)
    g_negb = [Guard(), Guard()]
    gPCp = [Guard(), Guard()]
    g_ibfp = [Guard(), Guard()]
    dd_m2 = {0: (PD, gPD), 1: (PE, gPE)}
    fin_m2 = {0: (PG, gPG), 1: (PF, gPF)}
    anum_m = {0: anum0, 1: anum1}
    ibf_m = {0: ibf, 1: ibf1}
    s_qsq_v = {}
    g_lo_v = {}
    s_eq_v = {}
    s_ib_v = {}

    def st_dd(h):
        hc = h // 2
        hp = (h % 2) * 64
        pc0 = (h % 2) * 16
        ks0 = (hc % 2) * 2048
        if h % 2 == 0:
            qsq_slot[hc % 2].acquire(SC)
            s_qsq_v[hc] = op(SC, lambda e, hc=hc, ks0=ks0: e.activation(
                ksq[:, ks0:ks0 + 2048],
                a2[:, hc * 2048:(hc + 1) * 2048], AF.Square), SS)
        gPCp[h % 2].acquire(TE)
        wt(TE, SS, s_qsq_v[hc])
        for t in range(TCH):
            tm_d = op(TE, lambda e, ks0=ks0, hp=hp, t=t, pc0=pc0: e.matmul(
                PC[:, pc0 + t:pc0 + t + 1],
                ksq[hp:hp + 64, ks0 + t * 128:ks0 + (t + 1) * 128],
                onesn16[hp:hp + 64, 0:1], start=True, stop=True),
                TT if t == TCH - 1 else None)
        if h % 2 == 1:
            qsq_slot[hc % 2].release(TT, tm_d)
        for t in range(TCH):
            ps, g = dd_m2[t % 2]
            g.acquire(TE)
            tm_dd = op(TE, lambda e, ps=ps, hc=hc, hp=hp, t=t: e.matmul(
                ps[:, 0:256],
                a2[hp:hp + 64, hc * 2048 + t * 128:hc * 2048 + (t + 1) * 128],
                projs[hp:hp + 64, :], start=True, stop=True), TT)
            wt(VE, TT, tm_dd)
            v_rm = op(VE, lambda e, ps=ps, t=t: e.reduce_max(
                stq[:, t:t + 1], ps[:, 0:256], AX), VV)
            g.release(VV, v_rm)
        wt(VE, TT, tm_d)
        wt(VE, VV, VV.v)
        nb0 = (h % 2) * 16
        g_negb[h % 2].acquire(VE)
        v_nb = op(VE, lambda e, pc0=pc0, nb0=nb0: e.scalar_tensor_tensor(
            negb[:, nb0:nb0 + 16], PC[:, pc0:pc0 + 16], 1.0, stq[:, :],
            OP.mult, OP.subtract), VV)
        gPCp[h % 2].release(VV, v_nb)
        wt(SC, VV, v_nb)
        g_nbh.acquire(SC)
        s_hi = op(SC, lambda e, nb0=nb0: e.activation(
            nbh[:, :], negb[:, nb0:nb0 + 16], AF.Copy), SS)
        g_nbl.acquire(VE)
        wt(VE, SS, s_hi)
        v_lo = op(VE, lambda e, nb0=nb0: e.tensor_tensor(
            nbl[:, :], negb[:, nb0:nb0 + 16], nbh[:, :], OP.subtract), VV)
        g_negb[h % 2].release(SS, s_hi)
        g_negb[h % 2].release(VV, v_lo)
        g_pt4.acquire(TE)
        wt(TE, SS, s_hi)
        wt(TE, VV, v_lo)
        op(TE, lambda e: e.transpose(PT[0:16, 0:128], nbh[:, :], idbf[:, :]))
        t_lo = op(TE, lambda e: e.transpose(PT[0:16, 128:256], nbl[:, :], idbf[:, :]), TT)
        g_nbh.release(TT, t_lo)
        g_nbl.release(TT, t_lo)
        g_ntr2.acquire(VE)
        wt(VE, TT, t_lo)
        v_ev = op(VE, lambda e: e.tensor_copy(ntr2[:, :], PT[0:16, 0:256]), VV)
        g_pt4.release(VV, v_ev)
        off2 = (h % 2) * 2048
        g_stk[h % 2].acquire(GP)
        wt(GP, VV, v_ev)
        dma(GP, S_CTL, stk2[0:1, off2:off2 + 2048], ntr2[:, 0:128])
        g_lo_v[h] = dma(GP, S_CTL, stk2[1:2, off2:off2 + 2048], ntr2[:, 128:256])
        g_ntr2.release(S_CTL, g_lo_v[h])

    def st_qa(h):
        hc = h // 2
        hp = (h % 2) * 64
        off2 = (h % 2) * 2048
        g_eq.acquire(SC)
        for mc in range(2):
            for t in range(T4):
                ps, g = psq[(mc * T4 + t) % 2]
                g.acquire(TE)
                if mc == 0 and t == 0:
                    wt(TE, S_CTL, g_lo_v[h])
                op(TE, lambda e, ps=ps, hp=hp, mc=mc, t=t, hc=hc: e.matmul(
                    ps[:, :], projs[hp:hp + 64, mc * 128:(mc + 1) * 128],
                    a2[hp:hp + 64, hc * 2048 + t * 512:hc * 2048 + (t + 1) * 512],
                    start=True, stop=False))
                tm_q = op(TE, lambda e, ps=ps, t=t, off2=off2: e.matmul(
                    ps[:, :], ones2b[0:2, :],
                    stk2[0:2, off2 + t * 512:off2 + (t + 1) * 512],
                    start=False, stop=True), TT)
                if mc == 1 and t == T4 - 1:
                    g_stk[h % 2].release(TT, tm_q)
                wt(SC, TT, tm_q)
                s_eq = op(SC, lambda e, ps=ps, mc=mc, t=t: e.activation(
                    eq[:, mc * 2048 + t * 512:mc * 2048 + (t + 1) * 512],
                    ps[:, :], AF.Exp), SS)
                s_eq_v[(h, mc, t)] = s_eq
                g.release(SS, s_eq)
        if h == 0:
            wt(TE, SS, ct_last)
        anum = anum_m[h % 2]
        for t in range(T4):
            psA, gA = fin_m2[t % 2]
            gA.acquire(TE)
            wt(TE, SS, s_eq_v[(h, 1, t)])
            op(TE, lambda e, psA=psA, h=h, t=t: e.matmul(
                psA[0:65, :], ct_all[:, (2 * h) * 65:(2 * h) * 65 + 65],
                eq[:, t * 512:(t + 1) * 512], start=True, stop=False))
            tm_a = op(TE, lambda e, psA=psA, h=h, t=t: e.matmul(
                psA[0:65, :], ct_all[:, (2 * h + 1) * 65:(2 * h + 1) * 65 + 65],
                eq[:, 2048 + t * 512:2048 + (t + 1) * 512], start=False, stop=True), TT)
            if t == T4 - 1:
                g_eq.release(TT, tm_a)
            if t == 0:
                g_densb.acquire(SC)
            wt(SC, TT, tm_a)
            s_dr = op(SC, lambda e, psA=psA, t=t: e.copy(
                densb[64:65, t * 512:(t + 1) * 512], psA[64:65, :]), SS)
            wt(VE, TT, tm_a)
            v_an = op(VE, lambda e, psA=psA, anum=anum, t=t: e.tensor_copy(
                anum[:, t * 512:(t + 1) * 512], psA[0:64, :]), VV)
            gA.release(SS, s_dr)
            gA.release(VV, v_an)
        g_ibfp[h % 2].acquire(SC)
        if h == 0:
            wt(SC, VV, v_csrc)
        wt(SC, SS, s_dr)
        s_ln2 = op(SC, lambda e, h=h: e.activation(
            lnrow[64:65, :], densb[64:65, :], AF.Ln, bias=csrc[64:65, 2 * h:2 * h + 1]), SS)
        wt(VE, SS, s_ln2)
        v_hop = op(VE, lambda e: e.tensor_copy(densb[64:65, :], lnrow[64:65, :]), VV)
        wt(SC, VV, v_hop)
        ib = ibf_m[h % 2]
        s_ib_v[h] = op(SC, lambda e, ib=ib: e.activation(
            ib[64:65, :], densb[64:65, :], AF.Exp, scale=-1.0), SS)
        g_densb.release(SS, s_ib_v[h])

    def st_fin(h):
        hc = h // 2
        hp = (h % 2) * 64
        anum = anum_m[h % 2]
        ib = ibf_m[h % 2]
        for t in range(T4):
            pfin, gf = fin_m2[t % 2]
            gf.acquire(TE)
            if t == 0:
                wt(TE, SS, s_ib_v[h])
            t_pi = op(TE, lambda e, t=t, pfin=pfin, ib=ib: e.matmul(
                pfin[0:64, :], ones65[64:65, 0:64], ib[64:65, t * 512:(t + 1) * 512],
                start=True, stop=True), TT)
            if t == T4 - 1:
                g_ibfp[h % 2].release(TT, t_pi)
            g_isb[t % 2].acquire(VE)
            wt(VE, TT, t_pi)
            v_ie = op(VE, lambda e, t=t, pfin=pfin: e.tensor_copy(
                aqs[0:64, (t % 2) * 512:(t % 2) * 512 + 512], pfin[0:64, :]), VV)
            gf.release(VV, v_ie)
            v_at = op(VE, lambda e, anum=anum, hp=hp, hc=hc, t=t, h=h: e.scalar_tensor_tensor(
                a4[hp:hp + 64, hc * 2048 + t * 512:hc * 2048 + (t + 1) * 512],
                anum[:, t * 512:(t + 1) * 512], csrc[0:64, 2 * h:2 * h + 1],
                aqs[0:64, (t % 2) * 512:(t % 2) * 512 + 512], OP.add, OP.mult), VV)
            g_isb[t % 2].release(VV, v_at)

    for k in range(H + 2):
        if k < H:
            st_dd(k)
        if 1 <= k <= H:
            st_qa(k - 1)
        if k >= 2:
            st_fin(k - 2)

    barrier()

    wt(GP, CC[5], 1)
    g_ah = dma(GP, S_CTL, cells[0:1, 8:9], arh_out.ap())
    wt(SC, S_CTL, g_ah)
    s_sigd = op(SC, lambda e: e.activation(cells[0:1, 9:10], cells[0:1, 13:14], AF.Sigmoid), SS)
    wt(VE, SS, s_sigd)
    ve_step(lambda e: e.tensor_tensor(cells[0:1, 10:11], cells[0:1, 9:10], cells[0:1, 8:9], OP.mult))
    ve_step(lambda e: e.tensor_scalar(cells[0:1, 10:11], cells[0:1, 10:11], 1.0 / 127.0, None, OP.mult))
    v_sh = ve_step(lambda e: e.reciprocal(cells[0:1, 11:12], cells[0:1, 10:11]))
    repl128(cells[0:1, 11:12], pp[:, 2:3])
    v_shp = repl128(cells[0:1, 10:11], pp[:, 3:4])

    def hq_dst(gc):
        if gc < 8:
            return a1[:, gc * 2048:(gc + 1) * 2048]
        if gc < 16:
            return a2[:, (gc - 8) * 2048:(gc - 7) * 2048]
        return a3[:, (gc - 16) * 2048:(gc - 15) * 2048]

    h_slot = [None, None]
    g_i32s5 = Guard()
    for gc in range(ICH):
        sl = (gc % 2) * 2048
        if h_slot[gc % 2] is not None:
            wt(SY, VV, h_slot[gc % 2])
        lh = dma(SY, S_X[gc % 2], hst[:, sl:sl + 2048], hh[gc, :, :])
        wt(VE, S_X[gc % 2], lh)
        op(VE, lambda e, sl=sl: e.tensor_scalar(
            hst[:, sl:sl + 2048], hst[:, sl:sl + 2048], pp[:, 2:3], QP, OP.mult, OP.min))
        g_i32s5.acquire(VE)
        v_hi = op(VE, lambda e, sl=sl: e.tensor_scalar(
            i32s5[:, :], hst[:, sl:sl + 2048], QN, None, OP.max), VV)
        h_slot[gc % 2] = v_hi
        wt(SC, VV, v_hi)
        s_hq = op(SC, lambda e, gc=gc: e.activation(hq_dst(gc), i32s5[:, :], AF.Copy), SS)
        g_i32s5.release(SS, s_hq)

    barrier()

    agw_d = agw_out.ap()[:, 2 * NW_G:NW].rearrange("r (p c) -> r p c", p=128)
    psW_m = {0: (PA, gPA), 1: (PB, gPB)}
    psD_m = {0: (PC, gPC), 1: (PD, gPD)}
    g_wobuf = Guard()
    g_dbuf = Guard()
    g_xst = [Guard(), Guard()]
    g_tmp = [Guard(), Guard()]
    g_yst = [Guard(), Guard()]
    for eo2 in range(2):
        g_wobuf.acquire(GP)
        for ei in range(ECH):
            lwo = dma(GP, S_WO, wobuf[:, ei * 512:(ei + 1) * 512],
                      woT_h[ei * 128:(ei + 1) * 128, eo2 * 512:(eo2 + 1) * 512])
        g_dbuf.acquire(SY)
        for ic in range(ICH):
            src = agw_d[eo2 * 4:(eo2 + 1) * 4, :, ic * 128:(ic + 1) * 128] \
                .rearrange("r p c -> p r c")
            ldb = dma(SY, S_DB, dbuf[:, ic * 512:(ic + 1) * 512]
                      .rearrange("p (r c) -> p r c", r=4), src)
        wt(TE, S_DB, ldb)
        wt(TE, S_WO, lwo)
        t_last = None
        for t in range(TCH):
            psW, gW = psW_m[t % 2]
            psD, gD = psD_m[t % 2]
            g_xst[t % 2].acquire(SY)
            lx = dma(SY, S_X[t % 2], xst[:, (t % 2) * 512:(t % 2) * 512 + 512],
                     x_h[t * 128:(t + 1) * 128, eo2 * 512:(eo2 + 1) * 512])
            gD.acquire(TE)
            for ic in range(ICH):
                op(TE, lambda e, psD=psD, ic=ic, t=t: e.matmul(
                    psD[:, :], hq_dst(ic)[:, t * 128:(t + 1) * 128],
                    dbuf[:, ic * 512:(ic + 1) * 512],
                    start=(ic == 0), stop=(ic == ICH - 1)))
            gW.acquire(TE)
            for ei in range(ECH):
                op(TE, lambda e, psW=psW, ei=ei, t=t: e.matmul(
                    psW[:, :], a4[:, ei * 2048 + t * 128:ei * 2048 + (t + 1) * 128],
                    wobuf[:, ei * 512:(ei + 1) * 512],
                    start=(ei == 0), stop=False))
            t_last = op(TE, lambda e, psW=psW, eo2=eo2: e.matmul(
                psW[:, :], ones1b[0:1, :], borb[0:1, eo2 * 512:(eo2 + 1) * 512],
                start=False, stop=True), TT)
            g_tmp[t % 2].acquire(SC)
            wt(SC, TT, t_last)
            s_d = op(SC, lambda e, psD=psD, t=t: e.activation(
                tmpst[:, (t % 2) * 512:(t % 2) * 512 + 512], psD[:, :],
                AF.Copy, scale=pp[:, 3:4]), SS)
            gD.release(SS, s_d)
            g_yst[t % 2].acquire(VE)
            wt(VE, SS, s_d)
            op(VE, lambda e, psW=psW, t=t: e.tensor_tensor(
                yst[:, (t % 2) * 512:(t % 2) * 512 + 512],
                psW[:, :], tmpst[:, (t % 2) * 512:(t % 2) * 512 + 512], OP.add))
            wt(VE, S_X[t % 2], lx)
            v_y = op(VE, lambda e, t=t: e.tensor_tensor(
                yst[:, (t % 2) * 512:(t % 2) * 512 + 512],
                yst[:, (t % 2) * 512:(t % 2) * 512 + 512],
                xst[:, (t % 2) * 512:(t % 2) * 512 + 512], OP.add), VV)
            gW.release(VV, v_y)
            g_tmp[t % 2].release(VV, v_y)
            g_xst[t % 2].release(VV, v_y)
            wt(SY, VV, v_y)
            o_d = dma(SY, S_O[t % 2], y_h[t * 128:(t + 1) * 128, eo2 * 512:(eo2 + 1) * 512],
                      yst[:, (t % 2) * 512:(t % 2) * 512 + 512])
            g_yst[t % 2].release(S_O[t % 2], o_d)
        g_wobuf.release(TT, t_last)
        g_dbuf.release(TT, t_last)
    wt(SY, S_O[0], S_O[0].v)
    wt(SY, S_O[1], S_O[1].v)
    wt(GP, S_O[0], S_O[0].v)
    wt(GP, S_O[1], S_O[1].v)
    wt(GP, S_CTL, S_CTL.v)

    with es, nc.Block() as block:
        for name, dec in ((SY, block.sync), (GP, block.gpsimd), (VE, block.vector),
                          (SC, block.scalar), (TE, block.tensor)):
            def runner(e, name=name):
                for eng, fn in PROG:
                    if eng == name:
                        fn(e)
            dec(runner)
    return nc


def kernel(**inputs):
    import ml_dtypes
    x = np.asarray(inputs["L0_attn_x"], dtype=np.float32)
    wq = np.asarray(inputs["L0_attn_Wq"], dtype=np.float32)
    wk = np.asarray(inputs["L0_attn_Wk"], dtype=np.float32)
    wv = np.asarray(inputs["L0_attn_Wv"], dtype=np.float32)
    wo = np.asarray(inputs["L0_attn_Wo"], dtype=np.float32)
    bq = np.asarray(inputs["L0_attn_bq"], dtype=np.float32)
    bk = np.asarray(inputs["L0_attn_bk"], dtype=np.float32)
    bv = np.asarray(inputs["L0_attn_bv"], dtype=np.float32)
    bo = np.asarray(inputs["L0_attn_bo"], dtype=np.float32)
    proj = np.asarray(inputs["L0_attn_proj"], dtype=np.float32)
    clipx = np.asarray(inputs["L0_ffn_clip_x"], dtype=np.float32)
    clipd = np.asarray(inputs["L0_ffn_clip_down"], dtype=np.float32)
    wgate = np.asarray(inputs["L0_ffn_w_gate"], dtype=np.float32)
    wup = np.asarray(inputs["L0_ffn_w_up"], dtype=np.float32)
    wdown = np.asarray(inputs["L0_ffn_w_down"], dtype=np.float32)

    shared = {
        "wqT": np.ascontiguousarray(wq.T), "wkT": np.ascontiguousarray(wk.T),
        "wvT": np.ascontiguousarray(wv.T), "woT": np.ascontiguousarray(wo.T),
        "bqr": bq.reshape(1, E).copy(), "bkr": bk.reshape(1, E).copy(),
        "bvr": bv.reshape(1, E).copy(), "bor": bo.reshape(1, E).copy(),
        "projT": np.ascontiguousarray(proj.T),
        "clipx": clipx.reshape(1, 1).copy(), "clipd": clipd.reshape(1, 1).copy(),
        "idbf": np.eye(128, dtype=ml_dtypes.bfloat16),
        "idf": np.eye(128, dtype=np.float32),
    }
    in_maps = []
    for c in range(NC):
        b_i, hf = c // 2, c % 2
        m = dict(shared)
        m["x"] = np.ascontiguousarray(x[b_i, hf * TOK:(hf + 1) * TOK, :])
        lo = 3 * c * 128
        for nm, w in (("wg_nat", wgate), ("wu_nat", wup)):
            sl = np.ones((ISLOT * 128, E), np.float32)
            real = w[lo:lo + ISLOT * 128, :]
            sl[:real.shape[0]] = real
            m[nm] = sl
        m["wd_nat"] = np.ascontiguousarray(wdown[c * 128:(c + 1) * 128, :])
        in_maps.append(m)

    nc = build_nc()
    res = run_bass_kernel_spmd(nc, in_maps, core_ids=list(range(NC)))
    out = np.empty((B, NSEQ, E), np.float32)
    for c in range(NC):
        b_i, hf = c // 2, c % 2
        out[b_i, hf * TOK:(hf + 1) * TOK, :] = res.results[c]["y"]
    return out
